# version 02
# baseline (speedup 1.0000x reference)
from contextlib import ExitStack

import numpy as np
import ml_dtypes
import concourse.bass as bass
import concourse.mybir as mybir
from concourse.bass_utils import run_bass_kernel_spmd
import concourse.dve_ops as dve_ops
from concourse.dve_op import DveOp
from concourse.dve_spec import Spec, Src0, Src1, C0, C1, C2, lower as spec_lower

F32 = mybir.dt.float32
BF16 = mybir.dt.bfloat16
AF = mybir.ActivationFunctionType
ALU = mybir.AluOpType
AX = mybir.AxisListType

T = 4096
D = 1024
FF = 4096
I3 = 3072
N16 = 16
NCORES = 8
MAGIC = float(3 << 22)
NEG = -1.0e30
INV32 = 1.0 / 32.0


def _register_dve(name, spec, reference):
    if name in dve_ops._SUB_OPCODE_FOR_NAME:
        for op in dve_ops.OPS:
            if op.name == name:
                return op
    sha = {"v3": spec_lower(spec, ver="v3", perf_en=False, name=name).sha("v3")}
    op = DveOp(name, spec, subdim=False, uops_sha=sha, reference=reference)
    dve_ops.OPS.append(op)
    dve_ops._SUB_OPCODE_FOR_NAME[name] = dve_ops._CUSTOM_DVE_ROW_BASE + len(dve_ops.OPS) - 1
    dve_ops.CUSTOM_DVE_SPECS[name] = spec
    dve_ops.CUSTOM_DVE_REFS[name] = reference
    return op


ROUND_SCALE = _register_dve(
    "ROUND_SCALE",
    Spec(body=((Src0 * C0 + C2) - C2) * C1),
    lambda in0, in1, s0, s1, imm2: np.rint(in0 * s0) * s1,
)


def ts(i, size):
    return slice(i * size, (i + 1) * size)



ENGINES = ("sync", "vector", "scalar", "tensor", "gpsimd")


class Script:
    def __init__(self):
        self.steps = []

    def c(self, eng, fn):
        self.steps.append((eng, "c", fn, 1))

    def d(self, eng, fn, n=1):
        self.steps.append((eng, "d", fn, n))

    def run(self, nc):
        with ExitStack() as es:
            sems = {e: es.enter_context(nc.semaphore("sem_" + e)) for e in ENGINES}
            cnt = {e: 0 for e in ENGINES}
            plan = []
            for eng, kind, fn, n in self.steps:
                cnt[eng] += 1 if kind == "c" else 16 * n
                plan.append((eng, cnt[eng]))

            def section(e, engobj):
                for i, (eng, kind, fn, n) in enumerate(self.steps):
                    if eng != e:
                        continue
                    if i > 0:
                        peng, pval = plan[i - 1]
                        engobj.wait_ge(sems[peng], pval)
                    instrs = fn(engobj)
                    if not isinstance(instrs, (list, tuple)):
                        instrs = [instrs]
                    if kind == "c":
                        instrs[-1].then_inc(sems[e], 1)
                    else:
                        assert len(instrs) == n, f"step {i}: expected {n} dmas, got {len(instrs)}"
                        for ins_ in instrs:
                            ins_.then_inc(sems[e], 16)

            import os
            if os.environ.get("SCRIPT_VERBOSE"):
                from collections import Counter
                print("script steps:", len(self.steps), Counter(s[0] for s in self.steps))
            block = es.enter_context(nc.Block())

            @block.sync
            def _(sync):
                section("sync", sync)
                leng, lval = plan[-1]
                sync.wait_ge(sems[leng], lval)

            @block.vector
            def _(v):
                section("vector", v)

            @block.scalar
            def _(s):
                section("scalar", s)

            @block.tensor
            def _(te):
                section("tensor", te)

            @block.gpsimd
            def _(g):
                section("gpsimd", g)



def add_const_pool(nc, sc, pool, values):
    for i, v in enumerate(values):
        nc.const_aps.register(F32, float(v), pool[:, i:i + 1])

    def f(vec):
        ins = []
        for i, v in enumerate(values):
            ins.append(vec.memset(pool[:, i:i + 1], float(v)))
        return ins[-1]
    sc.c("vector", f)


def add_identity(sc, ident_bf, ident_f32=None):
    def f0(g):
        ins = [g.memset(ident_bf[:], 0.0)]
        if ident_f32 is not None:
            ins.append(g.memset(ident_f32[:], 0.0))
        return ins[-1]
    sc.c("gpsimd", f0)

    def f1(g):
        ins = [g.affine_select(
            out=ident_bf[:], in_=ident_bf[:], compare_op=ALU.not_equal,
            fill=1.0, base=0, pattern=[[-1, 128]], channel_multiplier=1)]
        if ident_f32 is not None:
            ins.append(g.affine_select(
                out=ident_f32[:], in_=ident_f32[:], compare_op=ALU.not_equal,
                fill=1.0, base=0, pattern=[[-1, 128]], channel_multiplier=1))
        return ins[-1]
    sc.c("gpsimd", f1)


def add_scalar_prep(sc, bufs, amax_in, qmax, clip_lo, mode):
    am8row, am, sc2, rsb, scr2 = (bufs["am8row"], bufs["am"], bufs["sc2"],
                                  bufs["rsb"], bufs["scr2"])

    def load(sync):
        return sync.dma_start(out=am8row[:], in_=amax_in.rearrange("a one -> one a"))
    sc.d("sync", load)

    def red(v):
        return v.reduce_max(am[:], am8row[:], axis=AX.X)
    sc.c("vector", red)

    if mode == "a16":
        sc.c("vector", lambda v: v.tensor_scalar(am[:], am[:], clip_lo, None, ALU.max))
        def mk(v):
            v.reciprocal(sc2[:, 0:1], am[:])
            return v.tensor_scalar(sc2[:, 1:2], am[:], 1.0 / qmax, None, ALU.mult)
        sc.c("vector", mk)
        sc.c("vector", lambda v: v.tensor_scalar(
            sc2[:, 0:1], sc2[:, 0:1], qmax, None, ALU.mult))
    else:
        sc.c("vector", lambda v: v.tensor_scalar(
            sc2[:, 1:2], am[:], 1.0 / qmax, clip_lo, ALU.mult, ALU.max))
        sc.c("vector", lambda v: v.reciprocal(sc2[:, 0:1], sc2[:, 1:2]))

    def store(sync):
        return sync.dma_start(out=scr2, in_=sc2[:])
    sc.d("sync", store)

    def bcast(sync):
        return sync.dma_start(out=rsb[:], in_=scr2.broadcast_to([128, 2]))
    sc.d("sync", bcast)


def add_amax_finalize(sc, amax_acc, amT, am1, am_bounce, amax_out):
    def st1(sync):
        return sync.dma_start(out=am_bounce.rearrange("(p one) -> p one", one=1),
                              in_=amax_acc[:])
    sc.d("sync", st1)

    def ld1(sync):
        return sync.dma_start(out=amT[:],
                              in_=am_bounce.rearrange("(one p) -> one p", one=1))
    sc.d("sync", ld1)

    def red(v):
        return v.reduce_max(am1[:], amT[:], axis=AX.X)
    sc.c("vector", red)

    def st2(sync):
        return sync.dma_start(out=amax_out, in_=am1[:])
    sc.d("sync", st2)



def build_l0_attn(dbg=False, dbg_qb=31):
    nc = bass.Bass("TRN2", target_bir_lowering=False, debug=False)
    x_tok = nc.dram_tensor("x_tok", [T, D], F32, kind="ExternalInput").ap()
    wqkv = nc.dram_tensor("wqkv", [D, 3 * D], F32, kind="ExternalInput").ap()
    x1T = nc.dram_tensor("x1T", [D, T], F32, kind="ExternalOutput").ap()
    amax1 = nc.dram_tensor("amax1", [1, 1], F32, kind="ExternalOutput").ap()
    ikind = "ExternalOutput" if dbg else "Internal"
    xt_bf = nc.dram_tensor("xt_bf", [D, T], BF16, kind=ikind).ap()
    wq_bf = nc.dram_tensor("wq_bf", [D, 3 * D], BF16, kind=ikind).ap()
    qt_bf = nc.dram_tensor("qt_bf", [D, T], BF16, kind=ikind).ap()
    am_bounce = nc.dram_tensor("am_bounce", [128], F32, kind="Internal").ap()
    if dbg:
        kT_d = nc.dram_tensor("kT_d", [128, 8 * T], BF16, kind="ExternalOutput").ap()
        vv_d = nc.dram_tensor("vv_d", [128, 32 * 1024], BF16, kind="ExternalOutput").ap()
        sc_d = nc.dram_tensor("sc_d", [128, T], F32, kind="ExternalOutput").ap()
        p_d = nc.dram_tensor("p_d", [128, T], BF16, kind="ExternalOutput").ap()
        y1_d = nc.dram_tensor("y1_d", [128, D], F32, kind="ExternalOutput").ap()
        at_d = nc.dram_tensor("at_d", [128, D], F32, kind="ExternalOutput").ap()
        sm_d = nc.dram_tensor("sm_d", [128, 16], F32, kind="ExternalOutput").ap()

    sc = Script()
    with ExitStack() as es:
        A = lambda n, sh, dt: es.enter_context(nc.sbuf_tensor(n, sh, dt))
        PS = lambda n, sh, dt: es.enter_context(nc.psum_tensor(n, sh, dt))
        stage_f = A("stage_f", [128, 1024], F32)
        stage_b = A("stage_b", [128, 1024], BF16)
        kT = A("kT", [128, 8 * T], BF16)
        vv = A("vv", [128, 32 * 1024], BF16)
        wvb = A("wvb", [128, 8 * 512], BF16)
        wqk_t = A("wqk_t", [128, 8 * 128], BF16)
        xc = A("xc", [128, 8 * 512], BF16)
        scb = A("scb", [128, T], F32)
        pbuf = A("pbuf", [128, T], BF16)
        pT2 = A("pT2", [128, 256], BF16)
        qTb = A("qTb", [128, 8 * 128], BF16)
        xrow = A("xrow", [128, 1024], F32)
        y1 = A("y1", [128, 1024], F32)
        sqv = A("sqv", [128, 1024], F32)
        x1r = A("x1r", [128, 1024], F32)
        x1tr = A("x1tr", [128, 1024], F32)
        sm = A("sm", [128, 16], F32)
        amax_acc = A("amax_acc", [128, 1], F32)
        amT = A("amT", [1, 128], F32)
        am1 = A("am1", [1, 1], F32)
        ident = A("ident", [128, 128], BF16)
        identf = A("identf", [128, 128], F32)
        ps_a = PS("ps_a", [128, 512], F32)
        ps_o = PS("ps_o", [128, 1024], F32)
        ps_tb = PS("ps_tb", [128, 128], BF16)
        ps_tf = PS("ps_tf", [128, 128], F32)

        COL = {k: i for i, k in enumerate(
            ["rowmax", "nbias", "lsum", "linv", "sumy", "sumsq", "m", "ey2",
             "mm", "var", "sd", "rstd", "mb", "negmb"])}

        def col(k):
            return sm[:, COL[k]:COL[k] + 1]

        cpool = A("cpool", [128, 2], F32)
        add_const_pool(nc, sc, cpool, [1e-5])
        add_identity(sc, ident, identf)
        sc.c("vector", lambda v: v.memset(amax_acc[:], 0.0))

        for r in range(8):
            for cc in range(3):
                sc.d("sync", lambda sync, r=r, cc=cc: sync.dma_start(
                    out=stage_f[:], in_=wqkv[ts(r, 128), ts(cc, 1024)]))
                sc.c("scalar", lambda s, r=r, cc=cc: s.activation(
                    out=stage_b[:], in_=stage_f[:], func=AF.Copy))
                sc.d("sync", lambda sync, r=r, cc=cc: sync.dma_start(
                    out=wq_bf[ts(r, 128), ts(cc, 1024)], in_=stage_b[:]))

        for i in range(32):
            sc.d("sync", lambda sync, i=i: sync.dma_start(
                out=xrow[:], in_=x_tok[ts(i, 128), :]))
            sc.c("scalar", lambda s, i=i: s.activation(
                out=stage_b[:], in_=xrow[:], func=AF.Copy))
            for j in range(8):
                sc.c("tensor", lambda te, j=j: te.transpose(
                    ps_tb[:], stage_b[:, ts(j, 128)], ident[:]))
                sc.c("vector", lambda v, j=j: v.tensor_copy(
                    xc.bitcast(BF16)[:, ts(j, 128)] if False else xc[:, ts(j, 128)], ps_tb[:]))
            sc.d("sync", lambda sync, i=i: [sync.dma_start(
                out=xt_bf[ts(j, 128), ts(i, 128)], in_=xc[:, ts(j, 128)])
                for j in range(8)], n=8)

        for dvh in range(2):
            sc.d("sync", lambda sync, dvh=dvh: [sync.dma_start(
                out=wvb[:, ts(db, 512)],
                in_=wq_bf[ts(db, 128), 2 * D + dvh * 512: 2 * D + dvh * 512 + 512])
                for db in range(8)], n=8)
            for tc in range(8):
                sc.d("sync", lambda sync, tc=tc: [sync.dma_start(
                    out=xc[:, ts(db, 512)], in_=xt_bf[ts(db, 128), ts(tc, 512)])
                    for db in range(8)], n=8)
                for sub in range(4):
                    def mmv(te, tc=tc, sub=sub, dvh=dvh):
                        ins = []
                        for db in range(8):
                            ins.append(te.matmul(
                                ps_a[:],
                                xc[:, db * 512 + sub * 128: db * 512 + sub * 128 + 128],
                                wvb[:, ts(db, 512)],
                                start=(db == 0), stop=(db == 7)))
                        return ins[-1]
                    sc.c("tensor", mmv)
                    kt = tc * 4 + sub
                    sc.c("scalar", lambda s, kt=kt, dvh=dvh: s.activation(
                        out=vv[:, kt * 1024 + dvh * 512: kt * 1024 + dvh * 512 + 512],
                        in_=ps_a[:], func=AF.Copy))

        for tc in range(8):
            sc.d("sync", lambda sync, tc=tc: [sync.dma_start(
                out=xc[:, ts(db, 512)], in_=xt_bf[ts(db, 128), ts(tc, 512)])
                for db in range(8)], n=8)
            for qk in range(2):
                for ob in range(8):
                    sc.d("sync", lambda sync, qk=qk, ob=ob: [sync.dma_start(
                        out=wqk_t[:, ts(db, 128)],
                        in_=wq_bf[ts(db, 128), qk * D + ob * 128: qk * D + ob * 128 + 128])
                        for db in range(8)], n=8)

                    def mmqk(te, tc=tc, qk=qk, ob=ob):
                        ins = []
                        for db in range(8):
                            ins.append(te.matmul(
                                ps_a[:], wqk_t[:, ts(db, 128)],
                                xc[:, ts(db, 512)],
                                start=(db == 0), stop=(db == 7)))
                        return ins[-1]
                    sc.c("tensor", mmqk)
                    if qk == 1:
                        sc.c("scalar", lambda s, ob=ob, tc=tc: s.activation(
                            out=kT[:, ob * T + tc * 512: ob * T + tc * 512 + 512],
                            in_=ps_a[:], func=AF.Copy))
                    else:
                        sc.c("scalar", lambda s: s.activation(
                            out=stage_b[:, 0:512], in_=ps_a[:], func=AF.Copy))
                        sc.d("sync", lambda sync, ob=ob, tc=tc: sync.dma_start(
                            out=qt_bf[ts(ob, 128), ts(tc, 512)], in_=stage_b[:, 0:512]))

        for qb in range(32):
            klen = (qb + 1) * 128
            nkc = (klen + 511) // 512
            sc.d("sync", lambda sync, qb=qb: [sync.dma_start(
                out=qTb[:, ts(db, 128)], in_=qt_bf[ts(db, 128), ts(qb, 128)])
                for db in range(8)], n=8)
            sc.d("sync", lambda sync, qb=qb: sync.dma_start(
                out=xrow[:], in_=x_tok[ts(qb, 128), :]))
            for kc in range(nkc):
                w = min(512, klen - kc * 512)

                def mms(te, qb=qb, kc=kc, w=w):
                    ins = []
                    for db in range(8):
                        ins.append(te.matmul(
                            ps_a[:, 0:w], qTb[:, ts(db, 128)],
                            kT[:, db * T + kc * 512: db * T + kc * 512 + w],
                            start=(db == 0), stop=(db == 7)))
                    return ins[-1]
                sc.c("tensor", mms)
                sc.c("scalar", lambda s, kc=kc, w=w: s.activation(
                    out=scb[:, kc * 512: kc * 512 + w], in_=ps_a[:, 0:w],
                    func=AF.Copy))
            sc.c("gpsimd", lambda g, qb=qb: g.affine_select(
                out=scb[:, ts(qb, 128)], in_=scb[:, ts(qb, 128)],
                compare_op=ALU.is_ge, fill=NEG, base=0,
                pattern=[[-1, 128]], channel_multiplier=1))
            sc.c("vector", lambda v, klen=klen: v.reduce_max(
                col("rowmax"), scb[:, 0:klen], axis=AX.X))
            sc.c("vector", lambda v: v.tensor_scalar(
                col("nbias"), col("rowmax"), -INV32, None, ALU.mult))
            sc.c("scalar", lambda s, klen=klen: s.activation(
                out=pbuf[:, 0:klen], in_=scb[:, 0:klen], func=AF.Exp,
                scale=INV32, bias=col("nbias"), accum_out=col("lsum")))
            sc.c("vector", lambda v: v.reciprocal(col("linv"), col("lsum")))
            for kt in range(qb + 1):
                sc.c("tensor", lambda te, kt=kt: te.transpose(
                    ps_tb[:], pbuf[:, ts(kt, 128)], ident[:]))
                sl = (kt % 2) * 128
                sc.c("vector", lambda v, sl=sl: v.tensor_copy(
                    pT2[:, sl:sl + 128], ps_tb[:]))

                def mmav(te, kt=kt, qb=qb, sl=sl):
                    ins = []
                    for dvh in range(2):
                        ins.append(te.matmul(
                            ps_o[:, ts(dvh, 512)], pT2[:, sl:sl + 128],
                            vv[:, kt * 1024 + dvh * 512: kt * 1024 + dvh * 512 + 512],
                            start=(kt == 0), stop=(kt == qb)))
                    return ins[-1]
                sc.c("tensor", mmav)
            if dbg and qb == dbg_qb:
                sc.c("vector", lambda v: v.tensor_scalar(
                    sqv[:], ps_o[:], col("linv"), None, ALU.mult))
                sc.d("sync", lambda sync: sync.dma_start(out=at_d, in_=sqv[:]))
                sc.d("sync", lambda sync: sync.dma_start(out=sc_d, in_=scb[:]))
                sc.d("sync", lambda sync: sync.dma_start(out=p_d, in_=pbuf[:]))
            sc.c("vector", lambda v: v.scalar_tensor_tensor(
                y1[:], ps_o[:], col("linv"), xrow[:], ALU.mult, ALU.add,
                accum_out=col("sumy")))
            sc.c("scalar", lambda s: s.activation(
                out=sqv[:], in_=y1[:], func=AF.Square, accum_out=col("sumsq")))

            def stats(v):
                v.tensor_scalar(col("m"), col("sumy"), 1.0 / D, None, ALU.mult)
                return v.tensor_scalar(col("ey2"), col("sumsq"), 1.0 / D, None, ALU.mult)
            sc.c("vector", stats)
            sc.c("vector", lambda v: v.tensor_tensor(
                col("mm"), col("m"), col("m"), ALU.mult))
            sc.c("vector", lambda v: v.tensor_tensor(
                col("var"), col("ey2"), col("mm"), ALU.subtract))
            sc.c("scalar", lambda s: s.activation(
                out=col("sd"), in_=col("var"), func=AF.Sqrt, bias=1e-5))
            sc.c("vector", lambda v: v.reciprocal(col("rstd"), col("sd")))
            sc.c("vector", lambda v: v.tensor_tensor(
                col("mb"), col("m"), col("rstd"), ALU.mult))
            sc.c("vector", lambda v: v.tensor_scalar(
                col("negmb"), col("mb"), -1.0, None, ALU.mult))
            sc.c("vector", lambda v: v.tensor_scalar(
                x1r[:], y1[:], col("rstd"), col("negmb"), ALU.mult, ALU.add))

            if dbg and qb == dbg_qb:
                sc.d("sync", lambda sync: sync.dma_start(out=y1_d, in_=y1[:]))
                sc.d("sync", lambda sync: sync.dma_start(out=sm_d, in_=sm[:]))

            sc.c("vector", lambda v: v.reduce_max(
                col("rowmax"), x1r[:], axis=AX.X, apply_absolute_value=True))
            sc.c("vector", lambda v: v.tensor_tensor(
                amax_acc[:], amax_acc[:], col("rowmax"), ALU.max))
            for j in range(8):
                sc.c("tensor", lambda te, j=j: te.transpose(
                    ps_tf[:], x1r[:, ts(j, 128)], identf[:]))
                sc.c("vector", lambda v, j=j: v.tensor_copy(
                    x1tr[:, ts(j, 128)], ps_tf[:]))
            sc.d("sync", lambda sync, qb=qb: [sync.dma_start(
                out=x1T[ts(j, 128), ts(qb, 128)], in_=x1tr[:, ts(j, 128)])
                for j in range(8)], n=8)

        if dbg:
            sc.d("sync", lambda sync: sync.dma_start(out=kT_d, in_=kT[:]))
            sc.d("sync", lambda sync: sync.dma_start(out=vv_d, in_=vv[:]))
        add_amax_finalize(sc, amax_acc, amT, am1, am_bounce, amax1)
        sc.run(nc)
    mybir.codegen_inst_isa_subclasses(nc)
    return nc



_BF = ml_dtypes.bfloat16


def add_ln_rows(sc, rows, ps_sum, ps_sq, ps_al, ps_be, ones_row1, inv_n, eps_bias):
    R = lambda i: rows[:, i * 512:(i + 1) * 512]
    sc.c("vector", lambda v: [v.tensor_copy(R(0), ps_sum[:]),
                              v.tensor_copy(R(1), ps_sq[:])][-1])
    sc.c("vector", lambda v: [
        v.tensor_scalar(R(2), R(0), inv_n, None, ALU.mult),
        v.tensor_scalar(R(3), R(1), inv_n, None, ALU.mult)][-1])
    sc.c("vector", lambda v: v.tensor_tensor(R(5), R(2), R(2), ALU.mult))
    sc.c("vector", lambda v: v.tensor_tensor(R(4), R(3), R(5), ALU.subtract))
    sc.c("scalar", lambda s: s.activation(
        out=R(3), in_=R(4), func=AF.Sqrt, bias=eps_bias))
    sc.c("vector", lambda v: v.reciprocal(R(4), R(3)))
    sc.c("vector", lambda v: v.tensor_tensor(R(5), R(2), R(4), ALU.mult))
    sc.c("vector", lambda v: v.tensor_scalar(R(5), R(5), -1.0, None, ALU.mult))

    def bcasts(te):
        te.matmul(ps_al[:], ones_row1[:], R(4), start=True, stop=True)
        return te.matmul(ps_be[:], ones_row1[:], R(5), start=True, stop=True)
    sc.c("tensor", bcasts)


def add_qdq_weight_lpbq(sc, nc, stage_f, wq_t, awb, sbv, rbv, w_hbm, rb, nblk):
    W = nblk * 32

    def load(sync):
        return sync.dma_start(out=stage_f[:, 0:W], in_=w_hbm[ts(rb, 128), :])
    sc.d("sync", load)
    sc.c("vector", lambda v: v.tensor_reduce(
        awb[:, 0:nblk], stage_f[:, 0:W].rearrange("p (a b) -> p a b", b=32),
        axis=AX.X, op=ALU.max, apply_absolute_value=True))
    sc.c("vector", lambda v: v.tensor_scalar(
        sbv[:, 0:nblk], awb[:, 0:nblk], 1.0 / 7.0, 1e-12, ALU.mult, ALU.max))
    sc.c("vector", lambda v: v.reciprocal(rbv[:, 0:nblk], sbv[:, 0:nblk]))

    def rounds(v):
        ins = []
        for b in range(nblk):
            ins.append(v._custom_dve(
                ROUND_SCALE, out=wq_t[:, ts(b, 32)], in0=stage_f[:, ts(b, 32)],
                s0=rbv[:, b:b + 1], s1=sbv[:, b:b + 1], imm2=MAGIC))
        return ins[-1]
    sc.c("vector", rounds)



def build_l0_ffn_a():
    nc = bass.Bass("TRN2", target_bir_lowering=False, debug=False)
    x1T = nc.dram_tensor("x1T", [D, T], F32, kind="ExternalInput").ap()
    amax8 = nc.dram_tensor("amax8", [8, 1], F32, kind="ExternalInput").ap()
    w_inT = nc.dram_tensor("w_inT", [FF, D], F32, kind="ExternalInput").ap()
    b_in_t = nc.dram_tensor("b_in_t", [128, 32], F32, kind="ExternalInput").ap()
    hidT = nc.dram_tensor("hidT", [FF, T], F32, kind="ExternalOutput").ap()
    amax_h = nc.dram_tensor("amax_h", [1, 1], F32, kind="ExternalOutput").ap()
    scr2 = nc.dram_tensor("scr2", [1, 2], F32, kind="Internal").ap()
    am_bounce = nc.dram_tensor("am_bounce", [128], F32, kind="Internal").ap()

    sc = Script()
    with ExitStack() as es:
        A = lambda n, sh, dt: es.enter_context(nc.sbuf_tensor(n, sh, dt))
        PS = lambda n, sh, dt: es.enter_context(nc.psum_tensor(n, sh, dt))
        stage_f = A("stage_f", [128, 1024], F32)
        w_q_t = A("w_q_t", [128, 1024], BF16)
        w_in_q = A("w_in_q", [128, 8 * FF], BF16)
        xi = A("xi", [128, 8 * T], BF16)
        hb = A("hb", [128, 512], F32)
        binb = A("binb", [128, 32], F32)
        sw = A("sw", [128, 32], F32)
        evsc = A("evsc", [128, 32], F32)
        aw = A("aw", [128, 1], F32)
        rw = A("rw", [128, 1], F32)
        aw2 = A("aw2", [128, 1], F32)
        hacc = A("hacc", [128, 1], F32)
        am8row = A("am8row", [1, 8], F32)
        am = A("am", [1, 1], F32)
        sc2 = A("sc2", [1, 2], F32)
        rsb = A("rsb", [128, 2], F32)
        amT = A("amT", [1, 128], F32)
        am1 = A("am1", [1, 1], F32)
        ident = A("ident", [128, 128], BF16)
        ps_m = PS("ps_m", [128, 512], F32)
        ps_tb = PS("ps_tb", [128, 128], BF16)

        add_identity(sc, ident)
        sc.c("vector", lambda v: v.memset(hacc[:], 0.0))
        add_scalar_prep(sc, dict(am8row=am8row, am=am, sc2=sc2, rsb=rsb,
                                 scr2=scr2), amax8, 32767.0, 1e-8, "a16")
        sc.d("sync", lambda sync: sync.dma_start(out=binb[:], in_=b_in_t))

        for fb in range(32):
            sc.d("sync", lambda sync, fb=fb: sync.dma_start(
                out=stage_f[:], in_=w_inT[ts(fb, 128), :]))
            sc.c("vector", lambda v: v.reduce_max(
                aw[:], stage_f[:], axis=AX.X, apply_absolute_value=True))
            sc.c("vector", lambda v: v.tensor_scalar(
                aw[:], aw[:], 1e-8, None, ALU.max))
            sc.c("vector", lambda v: v.reciprocal(rw[:], aw[:]))
            sc.c("vector", lambda v, fb=fb: v.tensor_scalar(
                sw[:, fb:fb + 1], aw[:], 1.0 / 127.0, None, ALU.mult))
            sc.c("vector", lambda v: v.tensor_scalar(
                rw[:], rw[:], 127.0, None, ALU.mult))
            sc.c("vector", lambda v: v._custom_dve(
                ROUND_SCALE, out=w_q_t[:], in0=stage_f[:], s0=rw[:, 0:1],
                s1=1.0, imm2=MAGIC))
            for j in range(8):
                sc.c("tensor", lambda te, j=j: te.transpose(
                    ps_tb[:], w_q_t[:, ts(j, 128)], ident[:]))
                sc.c("vector", lambda v, j=j, fb=fb: v.tensor_copy(
                    w_in_q[:, j * FF + fb * 128: j * FF + fb * 128 + 128], ps_tb[:]))
        sc.c("vector", lambda v: v.tensor_scalar(
            evsc[:], sw[:], rsb[:, 1:2], None, ALU.mult))

        for db in range(8):
            for tc in range(8):
                sc.d("sync", lambda sync, db=db, tc=tc: sync.dma_start(
                    out=stage_f[:, 0:512], in_=x1T[ts(db, 128), ts(tc, 512)]))
                sc.c("vector", lambda v, db=db, tc=tc: v._custom_dve(
                    ROUND_SCALE, out=xi[:, db * T + tc * 512: db * T + tc * 512 + 512],
                    in0=stage_f[:, 0:512], s0=rsb[:, 0:1], s1=1.0, imm2=MAGIC))

        for fb in range(32):
            for tc in range(8):
                def mmh(te, fb=fb, tc=tc):
                    ins = []
                    for db in range(8):
                        ins.append(te.matmul(
                            ps_m[:],
                            w_in_q[:, db * FF + fb * 128: db * FF + fb * 128 + 128],
                            xi[:, db * T + tc * 512: db * T + tc * 512 + 512],
                            start=(db == 0), stop=(db == 7)))
                    return ins[-1]
                sc.c("tensor", mmh)
                sc.c("scalar", lambda s, fb=fb: s.activation(
                    out=hb[:], in_=ps_m[:], func=AF.Gelu,
                    scale=evsc[:, fb:fb + 1], bias=binb[:, fb:fb + 1]))
                sc.c("vector", lambda v: v.reduce_max(
                    aw2[:], hb[:], axis=AX.X, apply_absolute_value=True))
                sc.c("vector", lambda v: v.tensor_tensor(
                    hacc[:], hacc[:], aw2[:], ALU.max))
                sc.d("sync", lambda sync, fb=fb, tc=tc: sync.dma_start(
                    out=hidT[ts(fb, 128), ts(tc, 512)], in_=hb[:]))

        add_amax_finalize(sc, hacc, amT, am1, am_bounce, amax_h)
        sc.run(nc)
    mybir.codegen_inst_isa_subclasses(nc)
    return nc



def build_l0_ffn_b():
    nc = bass.Bass("TRN2", target_bir_lowering=False, debug=False)
    x1T = nc.dram_tensor("x1T", [D, T], F32, kind="ExternalInput").ap()
    hidT = nc.dram_tensor("hidT", [FF, T], F32, kind="ExternalInput").ap()
    amax8 = nc.dram_tensor("amax8", [8, 1], F32, kind="ExternalInput").ap()
    w_out = nc.dram_tensor("w_out", [FF, D], F32, kind="ExternalInput").ap()
    b_out_t = nc.dram_tensor("b_out_t", [128, 8], F32, kind="ExternalInput").ap()
    x2T = nc.dram_tensor("x2T", [D, T], F32, kind="ExternalOutput").ap()
    scr2 = nc.dram_tensor("scr2", [1, 2], F32, kind="Internal").ap()

    sc = Script()
    with ExitStack() as es:
        A = lambda n, sh, dt: es.enter_context(nc.sbuf_tensor(n, sh, dt))
        PS = lambda n, sh, dt: es.enter_context(nc.psum_tensor(n, sh, dt))
        stage_f = A("stage_f", [128, 1024], F32)
        w_out_q = A("w_out_q", [128, 32 * 1024], BF16)
        hop = A("hop", [128, 32 * 512], BF16)
        hld = A("hld", [128, 512], F32)
        yr = A("yr", [128, 8 * 512], F32)
        xtile = A("xtile", [128, 512], F32)
        sq = A("sq", [128, 512], F32)
        x2t = A("x2t", [128, 512], F32)
        tprod = A("tprod", [128, 512], F32)
        bout = A("bout", [128, 8], F32)
        sw = A("sw", [128, 32], F32)
        cf = A("cf", [128, 32], F32)
        aw = A("aw", [128, 1], F32)
        rw = A("rw", [128, 1], F32)
        am8row = A("am8row", [1, 8], F32)
        am = A("am", [1, 1], F32)
        sc2 = A("sc2", [1, 2], F32)
        rsb = A("rsb", [128, 2], F32)
        lnrows = A("lnrows", [1, 512 * 6], F32)
        ones_col = A("ones_col", [128, 1], F32)
        ones_row1 = A("ones_row1", [1, 128], F32)
        cpool = A("cpool", [128, 1], F32)
        ps_m = PS("ps_m", [128, 512], F32)
        ps_sum = PS("ps_sum", [1, 512], F32)
        ps_sq = PS("ps_sq", [1, 512], F32)
        ps_al = PS("ps_al", [128, 512], F32)
        ps_be = PS("ps_be", [128, 512], F32)

        add_const_pool(nc, sc, cpool, [1e-5])
        sc.c("vector", lambda v: v.memset(ones_col[:], 1.0))
        sc.c("vector", lambda v: v.memset(ones_row1[:], 1.0))
        add_scalar_prep(sc, dict(am8row=am8row, am=am, sc2=sc2, rsb=rsb,
                                 scr2=scr2), amax8, 32767.0, 1e-8, "a16")
        sc.d("sync", lambda sync: sync.dma_start(out=bout[:], in_=b_out_t))

        for fb in range(32):
            sc.d("sync", lambda sync, fb=fb: sync.dma_start(
                out=stage_f[:], in_=w_out[ts(fb, 128), :]))
            sc.c("vector", lambda v: v.reduce_max(
                aw[:], stage_f[:], axis=AX.X, apply_absolute_value=True))
            sc.c("vector", lambda v: v.tensor_scalar(
                aw[:], aw[:], 1e-8, None, ALU.max))
            sc.c("vector", lambda v: v.reciprocal(rw[:], aw[:]))
            sc.c("vector", lambda v, fb=fb: v.tensor_scalar(
                sw[:, fb:fb + 1], aw[:], 1.0 / 127.0, None, ALU.mult))
            sc.c("vector", lambda v: v.tensor_scalar(
                rw[:], rw[:], 127.0, None, ALU.mult))
            sc.c("vector", lambda v, fb=fb: v._custom_dve(
                ROUND_SCALE, out=w_out_q[:, ts(fb, 1024)], in0=stage_f[:],
                s0=rw[:, 0:1], s1=1.0, imm2=MAGIC))
        sc.c("vector", lambda v: v.tensor_scalar(
            cf[:], sw[:], rsb[:, 1:2], None, ALU.mult))

        for tc in range(8):
            for fb in range(32):
                sc.d("sync", lambda sync, fb=fb, tc=tc: sync.dma_start(
                    out=hld[:], in_=hidT[ts(fb, 128), ts(tc, 512)]))
                sc.c("vector", lambda v, fb=fb: v._custom_dve(
                    ROUND_SCALE, out=hop[:, ts(fb, 512)], in0=hld[:],
                    s0=rsb[:, 0:1], s1=cf[:, fb:fb + 1], imm2=MAGIC))
            for do in range(8):
                def mmo(te, do=do):
                    ins = []
                    for fb in range(32):
                        ins.append(te.matmul(
                            ps_m[:],
                            w_out_q[:, fb * 1024 + do * 128: fb * 1024 + do * 128 + 128],
                            hop[:, ts(fb, 512)],
                            start=(fb == 0), stop=(fb == 31)))
                    return ins[-1]
                sc.c("tensor", mmo)
                sc.d("sync", lambda sync, do=do, tc=tc: sync.dma_start(
                    out=xtile[:], in_=x1T[ts(do, 128), ts(tc, 512)]))
                sc.c("vector", lambda v, do=do: v.scalar_tensor_tensor(
                    yr[:, ts(do, 512)], ps_m[:], bout[:, do:do + 1], xtile[:],
                    ALU.add, ALU.add))
                sc.c("scalar", lambda s, do=do: s.activation(
                    out=sq[:], in_=yr[:, ts(do, 512)], func=AF.Square))

                def mmstat(te, do=do):
                    i1 = te.matmul(ps_sum[:], ones_col[:], yr[:, ts(do, 512)],
                                   start=(do == 0), stop=(do == 7))
                    i2 = te.matmul(ps_sq[:], ones_col[:], sq[:],
                                   start=(do == 0), stop=(do == 7))
                    return [i1, i2][-1]
                sc.c("tensor", mmstat)
            add_ln_rows(sc, lnrows, ps_sum, ps_sq, ps_al, ps_be, ones_row1,
                        1.0 / D, 1e-5)
            for do in range(8):
                sc.c("vector", lambda v, do=do: v.tensor_tensor(
                    tprod[:], yr[:, ts(do, 512)], ps_al[:], ALU.mult))
                sc.c("vector", lambda v: v.tensor_tensor(
                    x2t[:], tprod[:], ps_be[:], ALU.add))
                sc.d("sync", lambda sync, do=do, tc=tc: sync.dma_start(
                    out=x2T[ts(do, 128), ts(tc, 512)], in_=x2t[:]))
        sc.run(nc)
    mybir.codegen_inst_isa_subclasses(nc)
    return nc



def build_ssm(phases="123"):
    nc = bass.Bass("TRN2", target_bir_lowering=False, debug=False)
    x2T = nc.dram_tensor("x2T", [D, T], F32, kind="ExternalInput").ap()
    a_log = nc.dram_tensor("a_log", [D, N16], F32, kind="ExternalInput").ap()
    w_delta = nc.dram_tensor("w_delta", [D, D], F32, kind="ExternalInput").ap()
    b_delta_t = nc.dram_tensor("b_delta_t", [128, 8], F32, kind="ExternalInput").ap()
    w_b = nc.dram_tensor("w_b", [D, N16], F32, kind="ExternalInput").ap()
    w_c = nc.dram_tensor("w_c", [D, N16], F32, kind="ExternalInput").ap()
    dskip_t = nc.dram_tensor("dskip_t", [128, 8], F32, kind="ExternalInput").ap()
    nw_t = nc.dram_tensor("nw_t", [128, 8], F32, kind="ExternalInput").ap()
    x3T = nc.dram_tensor("x3T", [D, T], F32, kind="ExternalOutput").ap()
    amax3 = nc.dram_tensor("amax3", [1, 1], F32, kind="ExternalOutput").ap()
    xn_hbm = nc.dram_tensor("xn_hbm", [D, T], F32, kind="Internal").ap()
    xnb_hbm = nc.dram_tensor("xnb_hbm", [D, T], BF16, kind="Internal").ap()
    dl_hbm = nc.dram_tensor("dl_hbm", [D, T], F32, kind="Internal").ap()
    bp_hbm = nc.dram_tensor("bp_hbm", [N16, T], F32, kind="Internal").ap()
    c_hbm = nc.dram_tensor("c_hbm", [N16, T], F32, kind="Internal").ap()
    am_bounce = nc.dram_tensor("am_bounce", [128], F32, kind="Internal").ap()

    DT_MAX, DT_MIN = 10.0, 1e-4
    CEN = (DT_MAX + DT_MIN) / 2.0
    HALF = (DT_MAX - DT_MIN) / 2.0

    sc = Script()
    with ExitStack() as es:
        A = lambda n, sh, dt: es.enter_context(nc.sbuf_tensor(n, sh, dt))
        PS = lambda n, sh, dt: es.enter_context(nc.psum_tensor(n, sh, dt))
        stage_f = A("stage_f", [128, 1024], F32)
        wdel_b = A("wdel_b", [128, 8 * 1024], BF16)
        wb_b = A("wb_b", [128, 8 * N16], BF16)
        wc_b = A("wc_b", [128, 8 * N16], BF16)
        ahalf = A("ahalf", [128, 8 * N16], F32)
        aexp = A("aexp", [128, N16], F32)
        bdel = A("bdel", [128, 8], F32)
        dskip = A("dskip", [128, 8], F32)
        nww = A("nww", [128, 8], F32)
        xch = A("xch", [128, 8 * 512], F32)
        xnch = A("xnch", [128, 8 * 512], F32)
        dlch = A("dlch", [128, 8 * 512], F32)
        uch = A("uch", [128, 8 * 512], F32)
        ytc = A("ytc", [128, 8 * 512], F32)
        yrb = A("yrb", [128, 8 * 512], F32)
        sq = A("sq", [128, 512], F32)
        x3t = A("x3t", [128, 512], F32)
        tprod = A("tprod", [128, 512], F32)
        xbf = A("xbf", [128, 512], BF16)
        rhs_bf = A("rhs_bf", [128, 8 * 512], BF16)
        t1 = A("t1", [128, 512], F32)
        t2 = A("t2", [128, 512], F32)
        t3 = A("t3", [128, 512], F32)
        bp1 = A("bp1", [128, 512], F32)
        c1 = A("c1", [128, 512], F32)
        hn = A("hn", [128, 512], F32)
        hst = A("hst", [128, 128], F32)
        bprow = A("bprow", [16, 512], F32)
        lnrows = A("lnrows", [1, 512 * 6], F32)
        ones_col = A("ones_col", [128, 1], F32)
        ones_row1 = A("ones_row1", [1, 128], F32)
        aw2 = A("aw2", [128, 1], F32)
        amax_acc = A("amax_acc", [128, 1], F32)
        amT = A("amT", [1, 128], F32)
        am1 = A("am1", [1, 1], F32)
        cpool = A("cpool", [128, 4], F32)
        ps_m = PS("ps_m", [128, 512], F32)
        ps_bp = PS("ps_bp", [16, 512], F32)
        ps_sum = PS("ps_sum", [1, 512], F32)
        ps_sq = PS("ps_sq", [1, 512], F32)
        ps_al = PS("ps_al", [128, 512], F32)
        ps_be = PS("ps_be", [128, 512], F32)

        add_const_pool(nc, sc, cpool, [1e-5, 1e-6, 1.0, -CEN / HALF])
        sc.c("vector", lambda v: v.memset(ones_col[:], 1.0))
        sc.c("vector", lambda v: v.memset(ones_row1[:], 1.0))
        sc.c("vector", lambda v: v.memset(amax_acc[:], 0.0))
        sc.c("vector", lambda v: v.memset(hst[:], 0.0))
        sc.d("sync", lambda sync: [sync.dma_start(out=bdel[:], in_=b_delta_t),
                                   sync.dma_start(out=dskip[:], in_=dskip_t),
                                   sync.dma_start(out=nww[:], in_=nw_t)], n=3)
        for db in range(8):
            sc.d("sync", lambda sync, db=db: sync.dma_start(
                out=aexp[:], in_=a_log[ts(db, 128), :]))
            sc.c("scalar", lambda s: s.activation(
                out=aexp[:], in_=aexp[:], func=AF.Exp))
            sc.c("vector", lambda v, db=db: v.tensor_scalar(
                ahalf[:, ts(db, N16)], aexp[:], 0.5, None, ALU.mult))
        for db in range(8):
            sc.d("sync", lambda sync, db=db: sync.dma_start(
                out=stage_f[:], in_=w_delta[ts(db, 128), :]))
            sc.c("scalar", lambda s, db=db: s.activation(
                out=wdel_b[:, ts(db, 1024)], in_=stage_f[:], func=AF.Copy))
            sc.d("sync", lambda sync, db=db: sync.dma_start(
                out=stage_f[:, 0:N16], in_=w_b[ts(db, 128), :]))
            sc.c("scalar", lambda s, db=db: s.activation(
                out=wb_b[:, ts(db, N16)], in_=stage_f[:, 0:N16], func=AF.Copy))
            sc.d("sync", lambda sync, db=db: sync.dma_start(
                out=stage_f[:, 0:N16], in_=w_c[ts(db, 128), :]))
            sc.c("scalar", lambda s, db=db: s.activation(
                out=wc_b[:, ts(db, N16)], in_=stage_f[:, 0:N16], func=AF.Copy))

        for tc in (range(8) if "1" in phases else []):
            sc.d("sync", lambda sync, tc=tc: [sync.dma_start(
                out=xch[:, ts(db, 512)], in_=x2T[ts(db, 128), ts(tc, 512)])
                for db in range(8)], n=8)
            for db in range(8):
                sc.c("scalar", lambda s, db=db: s.activation(
                    out=sq[:], in_=xch[:, ts(db, 512)], func=AF.Square))
                sc.c("tensor", lambda te, db=db: te.matmul(
                    ps_sum[:], ones_col[:], sq[:],
                    start=(db == 0), stop=(db == 7)))
            sc.c("vector", lambda v: v.tensor_copy(lnrows[:, 0:512], ps_sum[:]))
            sc.c("vector", lambda v: v.tensor_scalar(
                lnrows[:, 512:1024], lnrows[:, 0:512], 1.0 / D, None, ALU.mult))
            sc.c("scalar", lambda s: s.activation(
                out=lnrows[:, 0:512], in_=lnrows[:, 512:1024], func=AF.Sqrt,
                bias=1e-6))
            sc.c("vector", lambda v: v.reciprocal(
                lnrows[:, 512:1024], lnrows[:, 0:512]))
            sc.c("tensor", lambda te: te.matmul(
                ps_al[:], ones_row1[:], lnrows[:, 512:1024], start=True, stop=True))
            for db in range(8):
                sc.c("vector", lambda v, db=db: v.scalar_tensor_tensor(
                    xnch[:, ts(db, 512)], xch[:, ts(db, 512)], nww[:, db:db + 1],
                    ps_al[:], ALU.mult, ALU.mult))
                sc.c("scalar", lambda s, db=db: s.activation(
                    out=xbf[:], in_=xnch[:, ts(db, 512)], func=AF.Copy))
                sc.d("sync", lambda sync, db=db, tc=tc: [
                    sync.dma_start(out=xn_hbm[ts(db, 128), ts(tc, 512)],
                                   in_=xnch[:, ts(db, 512)]),
                    sync.dma_start(out=xnb_hbm[ts(db, 128), ts(tc, 512)],
                                   in_=xbf[:])], n=2)

        for ob in (range(8) if "2" in phases else []):
            for tc in range(8):
                sc.d("sync", lambda sync, tc=tc: [sync.dma_start(
                    out=rhs_bf[:, ts(db, 512)],
                    in_=xnb_hbm[ts(db, 128), ts(tc, 512)]) for db in range(8)], n=8)

                def mmd(te, ob=ob):
                    ins = []
                    for db in range(8):
                        ins.append(te.matmul(
                            ps_m[:],
                            wdel_b[:, db * 1024 + ob * 128: db * 1024 + ob * 128 + 128],
                            rhs_bf[:, ts(db, 512)],
                            start=(db == 0), stop=(db == 7)))
                    return ins[-1]
                sc.c("tensor", mmd)
                sc.c("vector", lambda v, ob=ob: v.tensor_scalar(
                    t1[:], ps_m[:], bdel[:, ob:ob + 1], None, ALU.add))
                sc.c("scalar", lambda s: s.activation(
                    out=t2[:], in_=t1[:], func=AF.Abs))
                sc.c("scalar", lambda s: s.activation(
                    out=t2[:], in_=t2[:], func=AF.Exp, scale=-1.0))
                sc.c("scalar", lambda s: s.activation(
                    out=t2[:], in_=t2[:], func=AF.Ln, bias=1.0))
                sc.c("vector", lambda v: v.tensor_scalar(
                    t3[:], t1[:], 0.0, None, ALU.max))
                sc.c("vector", lambda v: v.tensor_tensor(
                    t2[:], t2[:], t3[:], ALU.add))
                sc.c("scalar", lambda s: s.activation(
                    out=t3[:], in_=t2[:], func=AF.Tanh, scale=1.0 / HALF,
                    bias=-CEN / HALF))
                sc.c("vector", lambda v: v.tensor_scalar(
                    t3[:], t3[:], HALF, CEN, ALU.mult, ALU.add))
                sc.d("sync", lambda sync, ob=ob, tc=tc: sync.dma_start(
                    out=dl_hbm[ts(ob, 128), ts(tc, 512)], in_=t3[:]))
        for tc in (range(8) if "2" in phases else []):
            sc.d("sync", lambda sync, tc=tc: [sync.dma_start(
                out=rhs_bf[:, ts(db, 512)],
                in_=xnb_hbm[ts(db, 128), ts(tc, 512)]) for db in range(8)], n=8)
            for wsb, hbm in ((wb_b, bp_hbm), (wc_b, c_hbm)):
                def mmb(te, wsb=wsb):
                    ins = []
                    for db in range(8):
                        ins.append(te.matmul(
                            ps_bp[:], wsb[:, ts(db, N16)], rhs_bf[:, ts(db, 512)],
                            start=(db == 0), stop=(db == 7)))
                    return ins[-1]
                sc.c("tensor", mmb)
                sc.c("vector", lambda v: v.tensor_copy(bprow[:], ps_bp[:]))
                sc.d("sync", lambda sync, hbm=hbm, tc=tc: sync.dma_start(
                    out=hbm[:, ts(tc, 512)], in_=bprow[:]))

        for tc in (range(8) if "3" in phases else []):
            sc.d("sync", lambda sync, tc=tc: [sync.dma_start(
                out=dlch[:, ts(db, 512)], in_=dl_hbm[ts(db, 128), ts(tc, 512)])
                for db in range(8)] + [sync.dma_start(
                out=xnch[:, ts(db, 512)], in_=xn_hbm[ts(db, 128), ts(tc, 512)])
                for db in range(8)], n=16)
            for db in range(8):
                sc.c("scalar", lambda s, db=db: s.activation(
                    out=sq[:], in_=dlch[:, ts(db, 512)], func=AF.Sqrt))
                sc.c("vector", lambda v, db=db: v.scalar_tensor_tensor(
                    uch[:, ts(db, 512)], sq[:], 0.5, xnch[:, ts(db, 512)],
                    ALU.mult, ALU.mult))
            for n in range(N16):
                sc.d("sync", lambda sync, n=n, tc=tc: [
                    sync.dma_start(out=bp1[:], in_=bp_hbm[n:n + 1, ts(tc, 512)]
                                   .broadcast_to([128, 512])),
                    sync.dma_start(out=c1[:], in_=c_hbm[n:n + 1, ts(tc, 512)]
                                   .broadcast_to([128, 512]))], n=2)
                for db in range(8):
                    hcol = hst[:, db * N16 + n: db * N16 + n + 1]
                    sc.c("vector", lambda v, db=db, n=n: v.tensor_scalar(
                        t1[:], dlch[:, ts(db, 512)],
                        ahalf[:, db * N16 + n: db * N16 + n + 1], 1.0,
                        ALU.mult, ALU.add))
                    sc.c("vector", lambda v: v.reciprocal_approx_fast(
                        out=t2[:], in_=t1[:]))
                    sc.c("vector", lambda v: v.tensor_scalar(
                        t1[:], t2[:], 2.0, -1.0, ALU.mult, ALU.add))
                    sc.c("vector", lambda v, db=db: v.tensor_tensor(
                        t3[:], uch[:, ts(db, 512)], bp1[:], ALU.mult))
                    sc.c("vector", lambda v: v.scalar_tensor_tensor(
                        t2[:], t1[:], 1.0, t3[:], ALU.add, ALU.mult))
                    sc.c("vector", lambda v, hcol=hcol: v.tensor_tensor_scan(
                        hn[:], t1[:], t2[:], hcol, ALU.mult, ALU.add))
                    sc.c("vector", lambda v, hcol=hcol: v.tensor_copy(
                        hcol, hn[:, 511:512]))
                    sc.c("vector", lambda v: v.tensor_tensor(
                        t3[:], hn[:], c1[:], ALU.mult))
                    if n == 0:
                        sc.c("vector", lambda v, db=db: v.tensor_copy(
                            ytc[:, ts(db, 512)], t3[:]))
                    else:
                        sc.c("vector", lambda v, db=db: v.tensor_tensor(
                            ytc[:, ts(db, 512)], ytc[:, ts(db, 512)], t3[:],
                            ALU.add))
            sc.d("sync", lambda sync, tc=tc: [sync.dma_start(
                out=xch[:, ts(db, 512)], in_=x2T[ts(db, 128), ts(tc, 512)])
                for db in range(8)], n=8)
            for db in range(8):
                sc.c("vector", lambda v, db=db: v.scalar_tensor_tensor(
                    t1[:], xnch[:, ts(db, 512)], dskip[:, db:db + 1],
                    ytc[:, ts(db, 512)], ALU.mult, ALU.add))
                sc.c("vector", lambda v, db=db: v.tensor_tensor(
                    yrb[:, ts(db, 512)], t1[:], xch[:, ts(db, 512)], ALU.add))
                sc.c("scalar", lambda s, db=db: s.activation(
                    out=sq[:], in_=yrb[:, ts(db, 512)], func=AF.Square))

                def mmstat(te, db=db):
                    te.matmul(ps_sum[:], ones_col[:], yrb[:, ts(db, 512)],
                              start=(db == 0), stop=(db == 7))
                    return te.matmul(ps_sq[:], ones_col[:], sq[:],
                                     start=(db == 0), stop=(db == 7))
                sc.c("tensor", mmstat)
            add_ln_rows(sc, lnrows, ps_sum, ps_sq, ps_al, ps_be, ones_row1,
                        1.0 / D, 1e-5)
            for db in range(8):
                sc.c("vector", lambda v, db=db: v.tensor_tensor(
                    tprod[:], yrb[:, ts(db, 512)], ps_al[:], ALU.mult))
                sc.c("vector", lambda v: v.tensor_tensor(
                    x3t[:], tprod[:], ps_be[:], ALU.add))
                sc.c("vector", lambda v: v.reduce_max(
                    aw2[:], x3t[:], axis=AX.X, apply_absolute_value=True))
                sc.c("vector", lambda v: v.tensor_tensor(
                    amax_acc[:], amax_acc[:], aw2[:], ALU.max))
                sc.d("sync", lambda sync, db=db, tc=tc: sync.dma_start(
                    out=x3T[ts(db, 128), ts(tc, 512)], in_=x3t[:]))

        add_amax_finalize(sc, amax_acc, amT, am1, am_bounce, amax3)
        sc.run(nc)
    mybir.codegen_inst_isa_subclasses(nc)
    return nc



def build_l1_ffn_a():
    nc = bass.Bass("TRN2", target_bir_lowering=False, debug=False)
    x3T = nc.dram_tensor("x3T", [D, T], F32, kind="ExternalInput").ap()
    amax8 = nc.dram_tensor("amax8", [8, 1], F32, kind="ExternalInput").ap()
    w_up = nc.dram_tensor("w_up", [I3, D], F32, kind="ExternalInput").ap()
    w_gate = nc.dram_tensor("w_gate", [I3, D], F32, kind="ExternalInput").ap()
    upT = nc.dram_tensor("upT", [I3, T], F32, kind="ExternalOutput").ap()
    gateT = nc.dram_tensor("gateT", [I3, T], F32, kind="ExternalOutput").ap()
    amax_u = nc.dram_tensor("amax_u", [1, 1], F32, kind="ExternalOutput").ap()
    amax_g = nc.dram_tensor("amax_g", [1, 1], F32, kind="ExternalOutput").ap()
    scr2 = nc.dram_tensor("scr2", [1, 2], F32, kind="Internal").ap()
    am_bounce = nc.dram_tensor("am_bounce", [128], F32, kind="Internal").ap()

    sc = Script()
    with ExitStack() as es:
        A = lambda n, sh, dt: es.enter_context(nc.sbuf_tensor(n, sh, dt))
        PS = lambda n, sh, dt: es.enter_context(nc.psum_tensor(n, sh, dt))
        stage_f = A("stage_f", [128, 1024], F32)
        wq_t = A("wq_t", [128, 1024], BF16)
        wuq = A("wuq", [128, 8 * I3], BF16)
        wgq = A("wgq", [128, 8 * I3], BF16)
        xq = A("xq", [128, 8 * T], BF16)
        ot = A("ot", [128, 512], F32)
        awb = A("awb", [128, 32], F32)
        sbv = A("sbv", [128, 32], F32)
        rbv = A("rbv", [128, 32], F32)
        aw2 = A("aw2", [128, 1], F32)
        acc_u = A("acc_u", [128, 1], F32)
        acc_g = A("acc_g", [128, 1], F32)
        am8row = A("am8row", [1, 8], F32)
        am = A("am", [1, 1], F32)
        sc2 = A("sc2", [1, 2], F32)
        rsb = A("rsb", [128, 2], F32)
        amT = A("amT", [1, 128], F32)
        am1 = A("am1", [1, 1], F32)
        ident = A("ident", [128, 128], BF16)
        ps_m = PS("ps_m", [128, 512], F32)
        ps_tb = PS("ps_tb", [128, 128], BF16)

        add_identity(sc, ident)
        sc.c("vector", lambda v: v.memset(acc_u[:], 0.0))
        sc.c("vector", lambda v: v.memset(acc_g[:], 0.0))
        add_scalar_prep(sc, dict(am8row=am8row, am=am, sc2=sc2, rsb=rsb,
                                 scr2=scr2), amax8, 32767.0, 1e-12, "qdq")

        for db in range(8):
            for tc in range(8):
                sc.d("sync", lambda sync, db=db, tc=tc: sync.dma_start(
                    out=stage_f[:, 0:512], in_=x3T[ts(db, 128), ts(tc, 512)]))
                sc.c("vector", lambda v, db=db, tc=tc: v._custom_dve(
                    ROUND_SCALE, out=xq[:, db * T + tc * 512: db * T + tc * 512 + 512],
                    in0=stage_f[:, 0:512], s0=rsb[:, 0:1], s1=1.0, imm2=MAGIC))

        for w_hbm, wq_dst in ((w_up, wuq), (w_gate, wgq)):
            for rb in range(24):
                add_qdq_weight_lpbq(sc, nc, stage_f, wq_t, awb, sbv, rbv,
                                    w_hbm, rb, 32)
                for j in range(8):
                    sc.c("tensor", lambda te, j=j: te.transpose(
                        ps_tb[:], wq_t[:, ts(j, 128)], ident[:]))
                    sc.c("vector", lambda v, j=j, rb=rb, wq_dst=wq_dst: v.tensor_copy(
                        wq_dst[:, j * I3 + rb * 128: j * I3 + rb * 128 + 128],
                        ps_tb[:]))

        for wq_src, out_hbm, acc in ((wuq, upT, acc_u), (wgq, gateT, acc_g)):
            for ib in range(24):
                for tc in range(8):
                    def mmu(te, wq_src=wq_src, ib=ib, tc=tc):
                        ins = []
                        for db in range(8):
                            ins.append(te.matmul(
                                ps_m[:],
                                wq_src[:, db * I3 + ib * 128: db * I3 + ib * 128 + 128],
                                xq[:, db * T + tc * 512: db * T + tc * 512 + 512],
                                start=(db == 0), stop=(db == 7)))
                        return ins[-1]
                    sc.c("tensor", mmu)
                    sc.c("scalar", lambda s: s.activation(
                        out=ot[:], in_=ps_m[:], func=AF.Copy, scale=rsb[:, 1:2]))
                    sc.c("vector", lambda v: v.reduce_max(
                        aw2[:], ot[:], axis=AX.X, apply_absolute_value=True))
                    sc.c("vector", lambda v, acc=acc: v.tensor_tensor(
                        acc[:], acc[:], aw2[:], ALU.max))
                    sc.d("sync", lambda sync, out_hbm=out_hbm, ib=ib, tc=tc:
                         sync.dma_start(out=out_hbm[ts(ib, 128), ts(tc, 512)],
                                        in_=ot[:]))

        add_amax_finalize(sc, acc_u, amT, am1, am_bounce, amax_u)
        add_amax_finalize(sc, acc_g, amT, am1, am_bounce, amax_g)
        sc.run(nc)
    mybir.codegen_inst_isa_subclasses(nc)
    return nc



def build_l1_ffn_b():
    nc = bass.Bass("TRN2", target_bir_lowering=False, debug=False)
    gateT = nc.dram_tensor("gateT", [I3, T], F32, kind="ExternalInput").ap()
    amax8 = nc.dram_tensor("amax8", [8, 1], F32, kind="ExternalInput").ap()
    mT = nc.dram_tensor("mT", [I3, T], F32, kind="ExternalOutput").ap()
    amax_m = nc.dram_tensor("amax_m", [1, 1], F32, kind="ExternalOutput").ap()
    scr2 = nc.dram_tensor("scr2", [1, 2], F32, kind="Internal").ap()
    am_bounce = nc.dram_tensor("am_bounce", [128], F32, kind="Internal").ap()

    sc = Script()
    with ExitStack() as es:
        A = lambda n, sh, dt: es.enter_context(nc.sbuf_tensor(n, sh, dt))
        gt = A("gt", [128, 512], F32)
        gq = A("gq", [128, 512], F32)
        sig = A("sig", [128, 512], F32)
        si2 = A("si2", [128, 512], F32)
        mm = A("mm_", [128, 512], F32)
        mb16 = A("mb16", [128, 512], BF16)
        aw2 = A("aw2", [128, 1], F32)
        macc = A("macc", [128, 1], F32)
        am8row = A("am8row", [1, 8], F32)
        am = A("am", [1, 1], F32)
        sc2 = A("sc2", [1, 2], F32)
        rsb = A("rsb", [128, 2], F32)
        amT = A("amT", [1, 128], F32)
        am1 = A("am1", [1, 1], F32)

        sc.c("vector", lambda v: v.memset(macc[:], 0.0))
        add_scalar_prep(sc, dict(am8row=am8row, am=am, sc2=sc2, rsb=rsb,
                                 scr2=scr2), amax8, 32767.0, 1e-12, "qdq")
        for ib in range(24):
            for tc in range(8):
                sc.d("sync", lambda sync, ib=ib, tc=tc: sync.dma_start(
                    out=gt[:], in_=gateT[ts(ib, 128), ts(tc, 512)]))
                sc.c("vector", lambda v: v._custom_dve(
                    ROUND_SCALE, out=gq[:], in0=gt[:], s0=rsb[:, 0:1],
                    s1=rsb[:, 1:2], imm2=MAGIC))
                sc.c("scalar", lambda s: s.activation(
                    out=sig[:], in_=gq[:], func=AF.Sigmoid))
                sc.c("vector", lambda v: v._custom_dve(
                    ROUND_SCALE, out=si2[:], in0=sig[:], s0=65536.0, s1=1.0,
                    imm2=MAGIC))
                sc.c("vector", lambda v: v.tensor_scalar(
                    si2[:], si2[:], 65535.0, 1.0 / 65536.0, ALU.min, ALU.mult))
                sc.c("vector", lambda v: v.tensor_tensor(
                    mm[:], gq[:], si2[:], ALU.mult))
                sc.c("vector", lambda v: v.reduce_max(
                    aw2[:], mm[:], axis=AX.X, apply_absolute_value=True))
                sc.c("vector", lambda v: v.tensor_tensor(
                    macc[:], macc[:], aw2[:], ALU.max))
                sc.d("sync", lambda sync, ib=ib, tc=tc: sync.dma_start(
                    out=mT[ts(ib, 128), ts(tc, 512)], in_=mm[:]))
        add_amax_finalize(sc, macc, amT, am1, am_bounce, amax_m)
        sc.run(nc)
    mybir.codegen_inst_isa_subclasses(nc)
    return nc



def build_l1_ffn_c():
    nc = bass.Bass("TRN2", target_bir_lowering=False, debug=False)
    upT = nc.dram_tensor("upT", [I3, T], F32, kind="ExternalInput").ap()
    mT = nc.dram_tensor("mT", [I3, T], F32, kind="ExternalInput").ap()
    amax8u = nc.dram_tensor("amax8u", [8, 1], F32, kind="ExternalInput").ap()
    amax8m = nc.dram_tensor("amax8m", [8, 1], F32, kind="ExternalInput").ap()
    oT = nc.dram_tensor("oT", [I3, T], F32, kind="ExternalOutput").ap()
    amax_o = nc.dram_tensor("amax_o", [1, 1], F32, kind="ExternalOutput").ap()
    scr2 = nc.dram_tensor("scr2", [1, 2], F32, kind="Internal").ap()
    scr2b = nc.dram_tensor("scr2b", [1, 2], F32, kind="Internal").ap()
    am_bounce = nc.dram_tensor("am_bounce", [128], F32, kind="Internal").ap()

    sc = Script()
    with ExitStack() as es:
        A = lambda n, sh, dt: es.enter_context(nc.sbuf_tensor(n, sh, dt))
        ut = A("ut", [128, 512], F32)
        mt = A("mt", [128, 512], F32)
        uq = A("uq", [128, 512], F32)
        mq = A("mq", [128, 512], F32)
        oo = A("oo", [128, 512], F32)
        ob16 = A("ob16", [128, 512], BF16)
        aw2 = A("aw2", [128, 1], F32)
        oacc = A("oacc", [128, 1], F32)
        am8row = A("am8row", [1, 8], F32)
        am = A("am", [1, 1], F32)
        sc2 = A("sc2", [1, 2], F32)
        rsbu = A("rsbu", [128, 2], F32)
        rsbm = A("rsbm", [128, 2], F32)
        amT = A("amT", [1, 128], F32)
        am1 = A("am1", [1, 1], F32)

        sc.c("vector", lambda v: v.memset(oacc[:], 0.0))
        add_scalar_prep(sc, dict(am8row=am8row, am=am, sc2=sc2, rsb=rsbu,
                                 scr2=scr2), amax8u, 32767.0, 1e-12, "qdq")
        add_scalar_prep(sc, dict(am8row=am8row, am=am, sc2=sc2, rsb=rsbm,
                                 scr2=scr2b), amax8m, 32767.0, 1e-12, "qdq")
        for ib in range(24):
            for tc in range(8):
                sc.d("sync", lambda sync, ib=ib, tc=tc: [
                    sync.dma_start(out=ut[:], in_=upT[ts(ib, 128), ts(tc, 512)]),
                    sync.dma_start(out=mt[:], in_=mT[ts(ib, 128), ts(tc, 512)])],
                    n=2)
                sc.c("vector", lambda v: v._custom_dve(
                    ROUND_SCALE, out=uq[:], in0=ut[:], s0=rsbu[:, 0:1],
                    s1=rsbu[:, 1:2], imm2=MAGIC))
                sc.c("vector", lambda v: v._custom_dve(
                    ROUND_SCALE, out=mq[:], in0=mt[:], s0=rsbm[:, 0:1],
                    s1=rsbm[:, 1:2], imm2=MAGIC))
                sc.c("vector", lambda v: v.tensor_tensor(
                    oo[:], uq[:], mq[:], ALU.mult))
                sc.c("vector", lambda v: v.reduce_max(
                    aw2[:], oo[:], axis=AX.X, apply_absolute_value=True))
                sc.c("vector", lambda v: v.tensor_tensor(
                    oacc[:], oacc[:], aw2[:], ALU.max))
                sc.d("sync", lambda sync, ib=ib, tc=tc: sync.dma_start(
                    out=oT[ts(ib, 128), ts(tc, 512)], in_=oo[:]))
        add_amax_finalize(sc, oacc, amT, am1, am_bounce, amax_o)
        sc.run(nc)
    mybir.codegen_inst_isa_subclasses(nc)
    return nc



def build_l1_ffn_d():
    nc = bass.Bass("TRN2", target_bir_lowering=False, debug=False)
    x3T = nc.dram_tensor("x3T", [D, T], F32, kind="ExternalInput").ap()
    oT = nc.dram_tensor("oT", [I3, T], F32, kind="ExternalInput").ap()
    amax8 = nc.dram_tensor("amax8", [8, 1], F32, kind="ExternalInput").ap()
    w_down = nc.dram_tensor("w_down", [D, I3], F32, kind="ExternalInput").ap()
    x4T = nc.dram_tensor("x4T", [D, T], F32, kind="ExternalOutput").ap()
    scr2 = nc.dram_tensor("scr2", [1, 2], F32, kind="Internal").ap()

    sc = Script()
    with ExitStack() as es:
        A = lambda n, sh, dt: es.enter_context(nc.sbuf_tensor(n, sh, dt))
        PS = lambda n, sh, dt: es.enter_context(nc.psum_tensor(n, sh, dt))
        stage_f = A("stage_f", [128, 3072], F32)
        wq_t = A("wq_t", [128, 3072], BF16)
        wdq = A("wdq", [128, 24 * 1024], BF16)
        oqc = A("oqc", [128, 24 * 512], BF16)
        old = A("old", [128, 512], F32)
        yr = A("yr", [128, 8 * 512], F32)
        xtile = A("xtile", [128, 512], F32)
        sq = A("sq", [128, 512], F32)
        x4t = A("x4t", [128, 512], F32)
        tprod = A("tprod", [128, 512], F32)
        awb = A("awb", [128, 96], F32)
        sbv = A("sbv", [128, 96], F32)
        rbv = A("rbv", [128, 96], F32)
        am8row = A("am8row", [1, 8], F32)
        am = A("am", [1, 1], F32)
        sc2 = A("sc2", [1, 2], F32)
        rsb = A("rsb", [128, 2], F32)
        lnrows = A("lnrows", [1, 512 * 6], F32)
        ones_col = A("ones_col", [128, 1], F32)
        ones_row1 = A("ones_row1", [1, 128], F32)
        cpool = A("cpool", [128, 1], F32)
        ident = A("ident", [128, 128], BF16)
        ps_m = PS("ps_m", [128, 512], F32)
        ps_tb = PS("ps_tb", [128, 128], BF16)
        ps_sum = PS("ps_sum", [1, 512], F32)
        ps_sq = PS("ps_sq", [1, 512], F32)
        ps_al = PS("ps_al", [128, 512], F32)
        ps_be = PS("ps_be", [128, 512], F32)

        add_const_pool(nc, sc, cpool, [1e-5])
        add_identity(sc, ident)
        sc.c("vector", lambda v: v.memset(ones_col[:], 1.0))
        sc.c("vector", lambda v: v.memset(ones_row1[:], 1.0))
        add_scalar_prep(sc, dict(am8row=am8row, am=am, sc2=sc2, rsb=rsb,
                                 scr2=scr2), amax8, 32767.0, 1e-12, "qdq")

        for rb in range(8):
            add_qdq_weight_lpbq(sc, nc, stage_f, wq_t, awb, sbv, rbv,
                                w_down, rb, 96)
            for j in range(24):
                sc.c("tensor", lambda te, j=j: te.transpose(
                    ps_tb[:], wq_t[:, ts(j, 128)], ident[:]))
                sc.c("vector", lambda v, j=j, rb=rb: v.tensor_copy(
                    wdq[:, j * 1024 + rb * 128: j * 1024 + rb * 128 + 128],
                    ps_tb[:]))

        for tc in range(8):
            for ib in range(24):
                sc.d("sync", lambda sync, ib=ib, tc=tc: sync.dma_start(
                    out=old[:], in_=oT[ts(ib, 128), ts(tc, 512)]))
                sc.c("vector", lambda v, ib=ib: v._custom_dve(
                    ROUND_SCALE, out=oqc[:, ts(ib, 512)], in0=old[:],
                    s0=rsb[:, 0:1], s1=rsb[:, 1:2], imm2=MAGIC))
            for do in range(8):
                def mmo(te, do=do):
                    ins = []
                    for ib in range(24):
                        ins.append(te.matmul(
                            ps_m[:],
                            wdq[:, ib * 1024 + do * 128: ib * 1024 + do * 128 + 128],
                            oqc[:, ts(ib, 512)],
                            start=(ib == 0), stop=(ib == 23)))
                    return ins[-1]
                sc.c("tensor", mmo)
                sc.d("sync", lambda sync, do=do, tc=tc: sync.dma_start(
                    out=xtile[:], in_=x3T[ts(do, 128), ts(tc, 512)]))
                sc.c("vector", lambda v, do=do: v.scalar_tensor_tensor(
                    yr[:, ts(do, 512)], ps_m[:], 1.0, xtile[:],
                    ALU.mult, ALU.add))
                sc.c("scalar", lambda s, do=do: s.activation(
                    out=sq[:], in_=yr[:, ts(do, 512)], func=AF.Square))

                def mmstat(te, do=do):
                    te.matmul(ps_sum[:], ones_col[:], yr[:, ts(do, 512)],
                              start=(do == 0), stop=(do == 7))
                    return te.matmul(ps_sq[:], ones_col[:], sq[:],
                                     start=(do == 0), stop=(do == 7))
                sc.c("tensor", mmstat)
            add_ln_rows(sc, lnrows, ps_sum, ps_sq, ps_al, ps_be, ones_row1,
                        1.0 / D, 1e-5)
            for do in range(8):
                sc.c("vector", lambda v, do=do: v.tensor_tensor(
                    tprod[:], yr[:, ts(do, 512)], ps_al[:], ALU.mult))
                sc.c("vector", lambda v: v.tensor_tensor(
                    x4t[:], tprod[:], ps_be[:], ALU.add))
                sc.d("sync", lambda sync, do=do, tc=tc: sync.dma_start(
                    out=x4T[ts(do, 128), ts(tc, 512)], in_=x4t[:]))
        sc.run(nc)
    mybir.codegen_inst_isa_subclasses(nc)
    return nc



_NC_CACHE = {}


def _get_nc(name, builder):
    if name not in _NC_CACHE:
        _NC_CACHE[name] = builder()
    return _NC_CACHE[name]


def _gather_amax_n(res, key, n):
    return np.ascontiguousarray(np.concatenate(
        [np.asarray(res.results[c][key]).reshape(1, 1) for c in range(n)]
        + [np.asarray(res.results[0][key]).reshape(1, 1)] * (8 - n),
        axis=0)).astype(np.float32)


def _gather_amax(res, key):
    return _gather_amax_n(res, key, NCORES)


def kernel(L0_attn_x, L0_attn_W_qkv, L0_ffn_W_in, L0_ffn_b_in, L0_ffn_W_out,
           L0_ffn_b_out, L1_attn_A_log, L1_attn_W_delta, L1_attn_b_delta,
           L1_attn_W_B, L1_attn_W_C, L1_attn_D_skip, L1_attn_norm_weight,
           L1_ffn_w_gate, L1_ffn_w_up, L1_ffn_w_down):
    cores = list(range(NCORES))
    x = np.ascontiguousarray(np.asarray(L0_attn_x, dtype=np.float32))
    wqkv = np.ascontiguousarray(np.asarray(L0_attn_W_qkv, np.float32))
    w_inT = np.ascontiguousarray(np.asarray(L0_ffn_W_in, np.float32).T)
    b_in_t = np.ascontiguousarray(np.asarray(L0_ffn_b_in, np.float32).reshape(32, 128).T)
    w_out = np.ascontiguousarray(np.asarray(L0_ffn_W_out, np.float32))
    b_out_t = np.ascontiguousarray(np.asarray(L0_ffn_b_out, np.float32).reshape(8, 128).T)
    a_log = np.ascontiguousarray(np.asarray(L1_attn_A_log, np.float32))
    w_delta = np.ascontiguousarray(np.asarray(L1_attn_W_delta, np.float32))
    b_delta_t = np.ascontiguousarray(np.asarray(L1_attn_b_delta, np.float32).reshape(8, 128).T)
    w_b = np.ascontiguousarray(np.asarray(L1_attn_W_B, np.float32))
    w_c = np.ascontiguousarray(np.asarray(L1_attn_W_C, np.float32))
    dskip_t = np.ascontiguousarray(np.asarray(L1_attn_D_skip, np.float32).reshape(8, 128).T)
    nw_t = np.ascontiguousarray(np.asarray(L1_attn_norm_weight, np.float32).reshape(8, 128).T)
    w_gate = np.ascontiguousarray(np.asarray(L1_ffn_w_gate, np.float32))
    w_up = np.ascontiguousarray(np.asarray(L1_ffn_w_up, np.float32))
    w_down = np.ascontiguousarray(np.asarray(L1_ffn_w_down, np.float32))

    nc1 = _get_nc("l0_attn", build_l0_attn)
    r1 = run_bass_kernel_spmd(nc1, [
        {"x_tok": np.ascontiguousarray(x[c]), "wqkv": wqkv} for c in cores],
        core_ids=cores)
    x1 = [np.asarray(r1.results[c]["x1T"]) for c in cores]
    am1 = _gather_amax(r1, "amax1")

    nc2 = _get_nc("l0_ffn_a", build_l0_ffn_a)
    r2 = run_bass_kernel_spmd(nc2, [
        {"x1T": x1[c], "amax8": am1, "w_inT": w_inT, "b_in_t": b_in_t}
        for c in cores], core_ids=cores)
    hid = [np.asarray(r2.results[c]["hidT"]) for c in cores]
    amh = _gather_amax(r2, "amax_h")

    nc3 = _get_nc("l0_ffn_b", build_l0_ffn_b)
    r3 = run_bass_kernel_spmd(nc3, [
        {"x1T": x1[c], "hidT": hid[c], "amax8": amh, "w_out": w_out,
         "b_out_t": b_out_t} for c in cores], core_ids=cores)
    x2 = [np.asarray(r3.results[c]["x2T"]) for c in cores]

    nc4 = _get_nc("ssm", build_ssm)
    r4 = run_bass_kernel_spmd(nc4, [
        {"x2T": x2[c], "a_log": a_log, "w_delta": w_delta,
         "b_delta_t": b_delta_t, "w_b": w_b, "w_c": w_c, "dskip_t": dskip_t,
         "nw_t": nw_t} for c in cores], core_ids=cores)
    x3 = [np.asarray(r4.results[c]["x3T"]) for c in cores]
    am3 = _gather_amax(r4, "amax3")

    nc5 = _get_nc("l1_ffn_a", build_l1_ffn_a)
    r5 = run_bass_kernel_spmd(nc5, [
        {"x3T": x3[c], "amax8": am3, "w_up": w_up, "w_gate": w_gate}
        for c in cores], core_ids=cores)
    up = [np.asarray(r5.results[c]["upT"]) for c in cores]
    gate = [np.asarray(r5.results[c]["gateT"]) for c in cores]
    amu = _gather_amax(r5, "amax_u")
    amg = _gather_amax(r5, "amax_g")

    nc6 = _get_nc("l1_ffn_b", build_l1_ffn_b)
    r6 = run_bass_kernel_spmd(nc6, [
        {"gateT": gate[c], "amax8": amg} for c in cores], core_ids=cores)
    mm_ = [np.asarray(r6.results[c]["mT"]) for c in cores]
    amm = _gather_amax(r6, "amax_m")

    nc7 = _get_nc("l1_ffn_c", build_l1_ffn_c)
    r7 = run_bass_kernel_spmd(nc7, [
        {"upT": up[c], "mT": mm_[c], "amax8u": amu, "amax8m": amm}
        for c in cores], core_ids=cores)
    oo = [np.asarray(r7.results[c]["oT"]) for c in cores]
    amo = _gather_amax(r7, "amax_o")

    nc8 = _get_nc("l1_ffn_d", build_l1_ffn_d)
    r8 = run_bass_kernel_spmd(nc8, [
        {"x3T": x3[c], "oT": oo[c], "amax8": amo, "w_down": w_down}
        for c in cores], core_ids=cores)
    out = np.stack([np.ascontiguousarray(np.asarray(r8.results[c]["x4T"]).T)
                    for c in cores])
    return out.astype(np.float32)


# version 03
# speedup vs baseline: 1.1113x; 1.1113x over previous
from contextlib import ExitStack

import numpy as np
import ml_dtypes
import concourse.bass as bass
import concourse.mybir as mybir
from concourse.bass_utils import run_bass_kernel_spmd
import concourse.dve_ops as dve_ops
from concourse.dve_op import DveOp
from concourse.dve_spec import Spec, Src0, Src1, C0, C1, C2, lower as spec_lower

F32 = mybir.dt.float32
BF16 = mybir.dt.bfloat16
AF = mybir.ActivationFunctionType
ALU = mybir.AluOpType
AX = mybir.AxisListType

T = 4096
D = 1024
FF = 4096
I3 = 3072
N16 = 16
NCORES = 8
MAGIC = float(3 << 22)
NEG = -1.0e30
INV32 = 1.0 / 32.0


def _register_dve(name, spec, reference):
    if name in dve_ops._SUB_OPCODE_FOR_NAME:
        for op in dve_ops.OPS:
            if op.name == name:
                return op
    sha = {"v3": spec_lower(spec, ver="v3", perf_en=False, name=name).sha("v3")}
    op = DveOp(name, spec, subdim=False, uops_sha=sha, reference=reference)
    dve_ops.OPS.append(op)
    dve_ops._SUB_OPCODE_FOR_NAME[name] = dve_ops._CUSTOM_DVE_ROW_BASE + len(dve_ops.OPS) - 1
    dve_ops.CUSTOM_DVE_SPECS[name] = spec
    dve_ops.CUSTOM_DVE_REFS[name] = reference
    return op


ROUND_SCALE = _register_dve(
    "ROUND_SCALE",
    Spec(body=((Src0 * C0 + C2) - C2) * C1),
    lambda in0, in1, s0, s1, imm2: np.rint(in0 * s0) * s1,
)


def ts(i, size):
    return slice(i * size, (i + 1) * size)



ENGINES = ("sync", "vector", "scalar", "tensor", "gpsimd")


class Script:

    def __init__(self):
        self.steps = []

    def c(self, eng, fn, r=None, w=None):
        self.steps.append((eng, "c", fn, 1, r, w))

    def d(self, eng, fn, n=1, r=None, w=None):
        self.steps.append((eng, "d", fn, n, r, w))

    def _compute_deps(self):
        n_steps = len(self.steps)
        deps = [set() for _ in range(n_steps)]
        last_w = {}
        readers = {}
        barrier = -1
        barrier_readers = []
        for i, (eng, kind, fn, n, r, w) in enumerate(self.steps):
            if r is None or w is None:
                for j in barrier_readers:
                    deps[i].add(j)
                if barrier >= 0:
                    deps[i].add(barrier)
                barrier = i
                barrier_readers = []
                last_w = {}
                readers = {}
                continue
            if barrier >= 0:
                deps[i].add(barrier)
            for k in r:
                if k in last_w:
                    deps[i].add(last_w[k])
            for k in w:
                if k in last_w:
                    deps[i].add(last_w[k])
                for rd in readers.get(k, ()):
                    deps[i].add(rd)
            for k in r:
                readers.setdefault(k, []).append(i)
            for k in w:
                last_w[k] = i
                readers[k] = []
            barrier_readers.append(i)
        return deps

    FORCE_SERIAL = False

    def run(self, nc):
        if Script.FORCE_SERIAL:
            self.steps = [(e, k, f, n, None, None) for (e, k, f, n, r, w)
                          in self.steps]
        with ExitStack() as es:
            sems = {e: es.enter_context(nc.semaphore("sem_" + e)) for e in ENGINES}
            cnt = {e: 0 for e in ENGINES}
            plan = []
            for eng, kind, fn, n, r, w in self.steps:
                cnt[eng] += 1 if kind == "c" else 16 * n
                plan.append((eng, cnt[eng]))
            deps = self._compute_deps()
            waits = []
            for i, dd in enumerate(deps):
                per = {}
                for j in dd:
                    e, v = plan[j]
                    per[e] = max(per.get(e, 0), v)
                waits.append(per)

            def section(e, engobj):
                for i, (eng, kind, fn, n, r, w) in enumerate(self.steps):
                    if eng != e:
                        continue
                    for de, dv in sorted(waits[i].items()):
                        engobj.wait_ge(sems[de], dv)
                    instrs = fn(engobj)
                    if not isinstance(instrs, (list, tuple)):
                        instrs = [instrs]
                    if kind == "c":
                        instrs[-1].then_inc(sems[e], 1)
                    else:
                        assert len(instrs) == n, f"step {i}: expected {n} dmas, got {len(instrs)}"
                        for ins_ in instrs:
                            ins_.then_inc(sems[e], 16)

            import os
            if os.environ.get("SCRIPT_VERBOSE"):
                from collections import Counter
                print("script steps:", len(self.steps), Counter(s[0] for s in self.steps))
            block = es.enter_context(nc.Block())

            @block.sync
            def _(sync):
                section("sync", sync)
                for e in ENGINES:
                    if cnt[e] > 0:
                        sync.wait_ge(sems[e], cnt[e])

            @block.vector
            def _(v):
                section("vector", v)

            @block.scalar
            def _(s):
                section("scalar", s)

            @block.tensor
            def _(te):
                section("tensor", te)

            @block.gpsimd
            def _(g):
                section("gpsimd", g)



def add_const_pool(nc, sc, pool, values):
    for i, v in enumerate(values):
        nc.const_aps.register(F32, float(v), pool[:, i:i + 1])

    def f(vec):
        ins = []
        for i, v in enumerate(values):
            ins.append(vec.memset(pool[:, i:i + 1], float(v)))
        return ins[-1]
    sc.c("vector", f)


def add_identity(sc, ident_bf, ident_f32=None):
    def f0(g):
        ins = [g.memset(ident_bf[:], 0.0)]
        if ident_f32 is not None:
            ins.append(g.memset(ident_f32[:], 0.0))
        return ins[-1]
    sc.c("gpsimd", f0)

    def f1(g):
        ins = [g.affine_select(
            out=ident_bf[:], in_=ident_bf[:], compare_op=ALU.not_equal,
            fill=1.0, base=0, pattern=[[-1, 128]], channel_multiplier=1)]
        if ident_f32 is not None:
            ins.append(g.affine_select(
                out=ident_f32[:], in_=ident_f32[:], compare_op=ALU.not_equal,
                fill=1.0, base=0, pattern=[[-1, 128]], channel_multiplier=1))
        return ins[-1]
    sc.c("gpsimd", f1)


def add_scalar_prep(sc, bufs, amax_in, qmax, clip_lo, mode):
    am8row, am, sc2, rsb, scr2 = (bufs["am8row"], bufs["am"], bufs["sc2"],
                                  bufs["rsb"], bufs["scr2"])

    def load(sync):
        return sync.dma_start(out=am8row[:], in_=amax_in.rearrange("a one -> one a"))
    sc.d("sync", load)

    def red(v):
        return v.reduce_max(am[:], am8row[:], axis=AX.X)
    sc.c("vector", red)

    if mode == "a16":
        sc.c("vector", lambda v: v.tensor_scalar(am[:], am[:], clip_lo, None, ALU.max))
        def mk(v):
            v.reciprocal(sc2[:, 0:1], am[:])
            return v.tensor_scalar(sc2[:, 1:2], am[:], 1.0 / qmax, None, ALU.mult)
        sc.c("vector", mk)
        sc.c("vector", lambda v: v.tensor_scalar(
            sc2[:, 0:1], sc2[:, 0:1], qmax, None, ALU.mult))
    else:
        sc.c("vector", lambda v: v.tensor_scalar(
            sc2[:, 1:2], am[:], 1.0 / qmax, clip_lo, ALU.mult, ALU.max))
        sc.c("vector", lambda v: v.reciprocal(sc2[:, 0:1], sc2[:, 1:2]))

    def store(sync):
        return sync.dma_start(out=scr2, in_=sc2[:])
    sc.d("sync", store)

    def bcast(sync):
        return sync.dma_start(out=rsb[:], in_=scr2.broadcast_to([128, 2]))
    sc.d("sync", bcast)


def add_amax_finalize(sc, amax_acc, amT, am1, am_bounce, amax_out):
    def st1(sync):
        return sync.dma_start(out=am_bounce.rearrange("(p one) -> p one", one=1),
                              in_=amax_acc[:])
    sc.d("sync", st1)

    def ld1(sync):
        return sync.dma_start(out=amT[:],
                              in_=am_bounce.rearrange("(one p) -> one p", one=1))
    sc.d("sync", ld1)

    def red(v):
        return v.reduce_max(am1[:], amT[:], axis=AX.X)
    sc.c("vector", red)

    def st2(sync):
        return sync.dma_start(out=amax_out, in_=am1[:])
    sc.d("sync", st2)



def build_l0_attn(dbg=False, dbg_qb=31):
    nc = bass.Bass("TRN2", target_bir_lowering=False, debug=False)
    x_tok = nc.dram_tensor("x_tok", [T, D], F32, kind="ExternalInput").ap()
    wqkv = nc.dram_tensor("wqkv", [D, 3 * D], F32, kind="ExternalInput").ap()
    x1T = nc.dram_tensor("x1T", [D, T], F32, kind="ExternalOutput").ap()
    amax1 = nc.dram_tensor("amax1", [1, 1], F32, kind="ExternalOutput").ap()
    ikind = "ExternalOutput" if dbg else "Internal"
    xt_bf = nc.dram_tensor("xt_bf", [D, T], BF16, kind=ikind).ap()
    wq_bf = nc.dram_tensor("wq_bf", [D, 3 * D], BF16, kind=ikind).ap()
    qt_bf = nc.dram_tensor("qt_bf", [D, T], BF16, kind=ikind).ap()
    am_bounce = nc.dram_tensor("am_bounce", [128], F32, kind="Internal").ap()
    if dbg:
        kT_d = nc.dram_tensor("kT_d", [128, 8 * T], BF16, kind="ExternalOutput").ap()
        vv_d = nc.dram_tensor("vv_d", [128, 32 * 1024], BF16, kind="ExternalOutput").ap()
        sc_d = nc.dram_tensor("sc_d", [128, T], F32, kind="ExternalOutput").ap()
        p_d = nc.dram_tensor("p_d", [128, T], BF16, kind="ExternalOutput").ap()
        y1_d = nc.dram_tensor("y1_d", [128, D], F32, kind="ExternalOutput").ap()
        at_d = nc.dram_tensor("at_d", [128, D], F32, kind="ExternalOutput").ap()
        sm_d = nc.dram_tensor("sm_d", [128, 16], F32, kind="ExternalOutput").ap()

    sc = Script()
    with ExitStack() as es:
        A = lambda n, sh, dt: es.enter_context(nc.sbuf_tensor(n, sh, dt))
        PS = lambda n, sh, dt: es.enter_context(nc.psum_tensor(n, sh, dt))
        stage_f = A("stage_f", [128, 1024], F32)
        stage_b = A("stage_b", [128, 1024], BF16)
        kT = A("kT", [128, 8 * T], BF16)
        vv = A("vv", [128, 32 * 1024], BF16)
        wvb = A("wvb", [128, 8 * 512], BF16)
        wqk_t = A("wqk_t", [128, 8 * 128], BF16)
        xc = A("xc", [128, 8 * 512], BF16)
        scb = A("scb", [128, T], F32)
        pbuf = A("pbuf", [128, T], BF16)
        pT2 = A("pT2", [128, 256], BF16)
        qTb = A("qTb", [128, 8 * 128], BF16)
        xrow = A("xrow", [128, 1024], F32)
        y1 = A("y1", [128, 1024], F32)
        sqv = A("sqv", [128, 1024], F32)
        x1r = A("x1r", [128, 1024], F32)
        x1tr = A("x1tr", [128, 1024], F32)
        sm = A("sm", [128, 16], F32)
        amax_acc = A("amax_acc", [128, 1], F32)
        amT = A("amT", [1, 128], F32)
        am1 = A("am1", [1, 1], F32)
        ident = A("ident", [128, 128], BF16)
        identf = A("identf", [128, 128], F32)
        ps_a = PS("ps_a", [128, 512], F32)
        ps_o = PS("ps_o", [128, 1024], F32)
        ps_tb = PS("ps_tb", [128, 128], BF16)
        ps_tf = PS("ps_tf", [128, 128], F32)

        COL = {k: i for i, k in enumerate(
            ["rowmax", "nbias", "lsum", "linv", "sumy", "sumsq", "m", "ey2",
             "mm", "var", "sd", "rstd", "mb", "negmb"])}

        def col(k):
            return sm[:, COL[k]:COL[k] + 1]

        cpool = A("cpool", [128, 2], F32)
        add_const_pool(nc, sc, cpool, [1e-5])
        add_identity(sc, ident, identf)
        sc.c("vector", lambda v: v.memset(amax_acc[:], 0.0))

        for r in range(8):
            for cc in range(3):
                sc.d("sync", lambda sync, r=r, cc=cc: sync.dma_start(
                    out=stage_f[:], in_=wqkv[ts(r, 128), ts(cc, 1024)]))
                sc.c("scalar", lambda s, r=r, cc=cc: s.activation(
                    out=stage_b[:], in_=stage_f[:], func=AF.Copy))
                sc.d("sync", lambda sync, r=r, cc=cc: sync.dma_start(
                    out=wq_bf[ts(r, 128), ts(cc, 1024)], in_=stage_b[:]))

        for i in range(32):
            sc.d("sync", lambda sync, i=i: sync.dma_start(
                out=xrow[:], in_=x_tok[ts(i, 128), :]))
            sc.c("scalar", lambda s, i=i: s.activation(
                out=stage_b[:], in_=xrow[:], func=AF.Copy))
            for j in range(8):
                sc.c("tensor", lambda te, j=j: te.transpose(
                    ps_tb[:], stage_b[:, ts(j, 128)], ident[:]))
                sc.c("vector", lambda v, j=j: v.tensor_copy(
                    xc.bitcast(BF16)[:, ts(j, 128)] if False else xc[:, ts(j, 128)], ps_tb[:]))
            sc.d("sync", lambda sync, i=i: [sync.dma_start(
                out=xt_bf[ts(j, 128), ts(i, 128)], in_=xc[:, ts(j, 128)])
                for j in range(8)], n=8)

        for dvh in range(2):
            sc.d("sync", lambda sync, dvh=dvh: [sync.dma_start(
                out=wvb[:, ts(db, 512)],
                in_=wq_bf[ts(db, 128), 2 * D + dvh * 512: 2 * D + dvh * 512 + 512])
                for db in range(8)], n=8)
            for tc in range(8):
                sc.d("sync", lambda sync, tc=tc: [sync.dma_start(
                    out=xc[:, ts(db, 512)], in_=xt_bf[ts(db, 128), ts(tc, 512)])
                    for db in range(8)], n=8)
                for sub in range(4):
                    def mmv(te, tc=tc, sub=sub, dvh=dvh):
                        ins = []
                        for db in range(8):
                            ins.append(te.matmul(
                                ps_a[:],
                                xc[:, db * 512 + sub * 128: db * 512 + sub * 128 + 128],
                                wvb[:, ts(db, 512)],
                                start=(db == 0), stop=(db == 7)))
                        return ins[-1]
                    sc.c("tensor", mmv)
                    kt = tc * 4 + sub
                    sc.c("scalar", lambda s, kt=kt, dvh=dvh: s.activation(
                        out=vv[:, kt * 1024 + dvh * 512: kt * 1024 + dvh * 512 + 512],
                        in_=ps_a[:], func=AF.Copy))

        for tc in range(8):
            sc.d("sync", lambda sync, tc=tc: [sync.dma_start(
                out=xc[:, ts(db, 512)], in_=xt_bf[ts(db, 128), ts(tc, 512)])
                for db in range(8)], n=8)
            for qk in range(2):
                for ob in range(8):
                    sc.d("sync", lambda sync, qk=qk, ob=ob: [sync.dma_start(
                        out=wqk_t[:, ts(db, 128)],
                        in_=wq_bf[ts(db, 128), qk * D + ob * 128: qk * D + ob * 128 + 128])
                        for db in range(8)], n=8)

                    def mmqk(te, tc=tc, qk=qk, ob=ob):
                        ins = []
                        for db in range(8):
                            ins.append(te.matmul(
                                ps_a[:], wqk_t[:, ts(db, 128)],
                                xc[:, ts(db, 512)],
                                start=(db == 0), stop=(db == 7)))
                        return ins[-1]
                    sc.c("tensor", mmqk)
                    if qk == 1:
                        sc.c("scalar", lambda s, ob=ob, tc=tc: s.activation(
                            out=kT[:, ob * T + tc * 512: ob * T + tc * 512 + 512],
                            in_=ps_a[:], func=AF.Copy))
                    else:
                        sc.c("scalar", lambda s: s.activation(
                            out=stage_b[:, 0:512], in_=ps_a[:], func=AF.Copy))
                        sc.d("sync", lambda sync, ob=ob, tc=tc: sync.dma_start(
                            out=qt_bf[ts(ob, 128), ts(tc, 512)], in_=stage_b[:, 0:512]))

        for qb in range(32):
            klen = (qb + 1) * 128
            nkc = (klen + 511) // 512
            sc.d("sync", lambda sync, qb=qb: [sync.dma_start(
                out=qTb[:, ts(db, 128)], in_=qt_bf[ts(db, 128), ts(qb, 128)])
                for db in range(8)], n=8)
            sc.d("sync", lambda sync, qb=qb: sync.dma_start(
                out=xrow[:], in_=x_tok[ts(qb, 128), :]))
            for kc in range(nkc):
                w = min(512, klen - kc * 512)

                def mms(te, qb=qb, kc=kc, w=w):
                    ins = []
                    for db in range(8):
                        ins.append(te.matmul(
                            ps_a[:, 0:w], qTb[:, ts(db, 128)],
                            kT[:, db * T + kc * 512: db * T + kc * 512 + w],
                            start=(db == 0), stop=(db == 7)))
                    return ins[-1]
                sc.c("tensor", mms)
                sc.c("scalar", lambda s, kc=kc, w=w: s.activation(
                    out=scb[:, kc * 512: kc * 512 + w], in_=ps_a[:, 0:w],
                    func=AF.Copy))
            sc.c("gpsimd", lambda g, qb=qb: g.affine_select(
                out=scb[:, ts(qb, 128)], in_=scb[:, ts(qb, 128)],
                compare_op=ALU.is_ge, fill=NEG, base=0,
                pattern=[[-1, 128]], channel_multiplier=1))
            sc.c("vector", lambda v, klen=klen: v.reduce_max(
                col("rowmax"), scb[:, 0:klen], axis=AX.X))
            sc.c("vector", lambda v: v.tensor_scalar(
                col("nbias"), col("rowmax"), -INV32, None, ALU.mult))
            sc.c("scalar", lambda s, klen=klen: s.activation(
                out=pbuf[:, 0:klen], in_=scb[:, 0:klen], func=AF.Exp,
                scale=INV32, bias=col("nbias"), accum_out=col("lsum")))
            sc.c("vector", lambda v: v.reciprocal(col("linv"), col("lsum")))
            for kt in range(qb + 1):
                sc.c("tensor", lambda te, kt=kt: te.transpose(
                    ps_tb[:], pbuf[:, ts(kt, 128)], ident[:]))
                sl = (kt % 2) * 128
                sc.c("vector", lambda v, sl=sl: v.tensor_copy(
                    pT2[:, sl:sl + 128], ps_tb[:]))

                def mmav(te, kt=kt, qb=qb, sl=sl):
                    ins = []
                    for dvh in range(2):
                        ins.append(te.matmul(
                            ps_o[:, ts(dvh, 512)], pT2[:, sl:sl + 128],
                            vv[:, kt * 1024 + dvh * 512: kt * 1024 + dvh * 512 + 512],
                            start=(kt == 0), stop=(kt == qb)))
                    return ins[-1]
                sc.c("tensor", mmav)
            if dbg and qb == dbg_qb:
                sc.c("vector", lambda v: v.tensor_scalar(
                    sqv[:], ps_o[:], col("linv"), None, ALU.mult))
                sc.d("sync", lambda sync: sync.dma_start(out=at_d, in_=sqv[:]))
                sc.d("sync", lambda sync: sync.dma_start(out=sc_d, in_=scb[:]))
                sc.d("sync", lambda sync: sync.dma_start(out=p_d, in_=pbuf[:]))
            sc.c("vector", lambda v: v.scalar_tensor_tensor(
                y1[:], ps_o[:], col("linv"), xrow[:], ALU.mult, ALU.add,
                accum_out=col("sumy")))
            sc.c("scalar", lambda s: s.activation(
                out=sqv[:], in_=y1[:], func=AF.Square, accum_out=col("sumsq")))

            def stats(v):
                v.tensor_scalar(col("m"), col("sumy"), 1.0 / D, None, ALU.mult)
                return v.tensor_scalar(col("ey2"), col("sumsq"), 1.0 / D, None, ALU.mult)
            sc.c("vector", stats)
            sc.c("vector", lambda v: v.tensor_tensor(
                col("mm"), col("m"), col("m"), ALU.mult))
            sc.c("vector", lambda v: v.tensor_tensor(
                col("var"), col("ey2"), col("mm"), ALU.subtract))
            sc.c("scalar", lambda s: s.activation(
                out=col("sd"), in_=col("var"), func=AF.Sqrt, bias=1e-5))
            sc.c("vector", lambda v: v.reciprocal(col("rstd"), col("sd")))
            sc.c("vector", lambda v: v.tensor_tensor(
                col("mb"), col("m"), col("rstd"), ALU.mult))
            sc.c("vector", lambda v: v.tensor_scalar(
                col("negmb"), col("mb"), -1.0, None, ALU.mult))
            sc.c("vector", lambda v: v.tensor_scalar(
                x1r[:], y1[:], col("rstd"), col("negmb"), ALU.mult, ALU.add))

            if dbg and qb == dbg_qb:
                sc.d("sync", lambda sync: sync.dma_start(out=y1_d, in_=y1[:]))
                sc.d("sync", lambda sync: sync.dma_start(out=sm_d, in_=sm[:]))

            sc.c("vector", lambda v: v.reduce_max(
                col("rowmax"), x1r[:], axis=AX.X, apply_absolute_value=True))
            sc.c("vector", lambda v: v.tensor_tensor(
                amax_acc[:], amax_acc[:], col("rowmax"), ALU.max))
            for j in range(8):
                sc.c("tensor", lambda te, j=j: te.transpose(
                    ps_tf[:], x1r[:, ts(j, 128)], identf[:]))
                sc.c("vector", lambda v, j=j: v.tensor_copy(
                    x1tr[:, ts(j, 128)], ps_tf[:]))
            sc.d("sync", lambda sync, qb=qb: [sync.dma_start(
                out=x1T[ts(j, 128), ts(qb, 128)], in_=x1tr[:, ts(j, 128)])
                for j in range(8)], n=8)

        if dbg:
            sc.d("sync", lambda sync: sync.dma_start(out=kT_d, in_=kT[:]))
            sc.d("sync", lambda sync: sync.dma_start(out=vv_d, in_=vv[:]))
        add_amax_finalize(sc, amax_acc, amT, am1, am_bounce, amax1)
        sc.run(nc)
    mybir.codegen_inst_isa_subclasses(nc)
    return nc



_BF = ml_dtypes.bfloat16


def add_ln_rows(sc, rows, ps_sum, ps_sq, ps_al, ps_be, ones_row1, inv_n, eps_bias):
    R = lambda i: rows[:, i * 512:(i + 1) * 512]
    sc.c("vector", lambda v: [v.tensor_copy(R(0), ps_sum[:]),
                              v.tensor_copy(R(1), ps_sq[:])][-1])
    sc.c("vector", lambda v: [
        v.tensor_scalar(R(2), R(0), inv_n, None, ALU.mult),
        v.tensor_scalar(R(3), R(1), inv_n, None, ALU.mult)][-1])
    sc.c("vector", lambda v: v.tensor_tensor(R(5), R(2), R(2), ALU.mult))
    sc.c("vector", lambda v: v.tensor_tensor(R(4), R(3), R(5), ALU.subtract))
    sc.c("scalar", lambda s: s.activation(
        out=R(3), in_=R(4), func=AF.Sqrt, bias=eps_bias))
    sc.c("vector", lambda v: v.reciprocal(R(4), R(3)))
    sc.c("vector", lambda v: v.tensor_tensor(R(5), R(2), R(4), ALU.mult))
    sc.c("vector", lambda v: v.tensor_scalar(R(5), R(5), -1.0, None, ALU.mult))

    def bcasts(te):
        te.matmul(ps_al[:], ones_row1[:], R(4), start=True, stop=True)
        return te.matmul(ps_be[:], ones_row1[:], R(5), start=True, stop=True)
    sc.c("tensor", bcasts)


def add_ln_rows_k(sc, rows, ps_sum, ps_sq, ps_al, ps_be, ones_row1, inv_n,
                  eps_bias):
    R = lambda i: rows[:, i * 512:(i + 1) * 512]
    sc.c("vector", lambda v: [v.tensor_copy(R(0), ps_sum[:]),
                              v.tensor_copy(R(1), ps_sq[:])][-1],
         r=["pssum", "pssq"], w=["lnr"])
    sc.c("vector", lambda v: [
        v.tensor_scalar(R(2), R(0), inv_n, None, ALU.mult),
        v.tensor_scalar(R(3), R(1), inv_n, None, ALU.mult)][-1],
        r=["lnr"], w=["lnr"])
    sc.c("vector", lambda v: v.tensor_tensor(R(5), R(2), R(2), ALU.mult),
         r=["lnr"], w=["lnr"])
    sc.c("vector", lambda v: v.tensor_tensor(R(4), R(3), R(5), ALU.subtract),
         r=["lnr"], w=["lnr"])
    sc.c("scalar", lambda s: s.activation(
        out=R(3), in_=R(4), func=AF.Sqrt, bias=eps_bias),
        r=["lnr"], w=["lnr"])
    sc.c("vector", lambda v: v.reciprocal(R(4), R(3)), r=["lnr"], w=["lnr"])
    sc.c("vector", lambda v: v.tensor_tensor(R(5), R(2), R(4), ALU.mult),
         r=["lnr"], w=["lnr"])
    sc.c("vector", lambda v: v.tensor_scalar(R(5), R(5), -1.0, None, ALU.mult),
         r=["lnr"], w=["lnr"])

    def bcasts(te):
        te.matmul(ps_al[:], ones_row1[:], R(4), start=True, stop=True)
        return te.matmul(ps_be[:], ones_row1[:], R(5), start=True, stop=True)
    sc.c("tensor", bcasts, r=["lnr"], w=["psal", "psbe"])


def add_qdq_weight_lpbq(sc, nc, stage_f, wq_t, awb, sbv, rbv, w_hbm, rb, nblk):
    W = nblk * 32

    def load(sync):
        return sync.dma_start(out=stage_f[:, 0:W], in_=w_hbm[ts(rb, 128), :])
    sc.d("sync", load)
    sc.c("vector", lambda v: v.tensor_reduce(
        awb[:, 0:nblk], stage_f[:, 0:W].rearrange("p (a b) -> p a b", b=32),
        axis=AX.X, op=ALU.max, apply_absolute_value=True))
    sc.c("vector", lambda v: v.tensor_scalar(
        sbv[:, 0:nblk], awb[:, 0:nblk], 1.0 / 7.0, 1e-12, ALU.mult, ALU.max))
    sc.c("vector", lambda v: v.reciprocal(rbv[:, 0:nblk], sbv[:, 0:nblk]))

    def rounds(v):
        ins = []
        for b in range(nblk):
            ins.append(v._custom_dve(
                ROUND_SCALE, out=wq_t[:, ts(b, 32)], in0=stage_f[:, ts(b, 32)],
                s0=rbv[:, b:b + 1], s1=sbv[:, b:b + 1], imm2=MAGIC))
        return ins[-1]
    sc.c("vector", rounds)



def build_l0_ffn_a():
    nc = bass.Bass("TRN2", target_bir_lowering=False, debug=False)
    x1T = nc.dram_tensor("x1T", [D, T], F32, kind="ExternalInput").ap()
    amax8 = nc.dram_tensor("amax8", [8, 1], F32, kind="ExternalInput").ap()
    w_inT = nc.dram_tensor("w_inT", [FF, D], F32, kind="ExternalInput").ap()
    b_in_t = nc.dram_tensor("b_in_t", [128, 32], F32, kind="ExternalInput").ap()
    hidT = nc.dram_tensor("hidT", [FF, T], F32, kind="ExternalOutput").ap()
    amax_h = nc.dram_tensor("amax_h", [1, 1], F32, kind="ExternalOutput").ap()
    scr2 = nc.dram_tensor("scr2", [1, 2], F32, kind="Internal").ap()
    am_bounce = nc.dram_tensor("am_bounce", [128], F32, kind="Internal").ap()

    sc = Script()
    with ExitStack() as es:
        A = lambda n, sh, dt: es.enter_context(nc.sbuf_tensor(n, sh, dt))
        PS = lambda n, sh, dt: es.enter_context(nc.psum_tensor(n, sh, dt))
        stage_f = A("stage_f", [128, 1024], F32)
        w_q_t = A("w_q_t", [128, 1024], BF16)
        w_in_q = A("w_in_q", [128, 8 * FF], BF16)
        xi = A("xi", [128, 8 * T], BF16)
        hb = A("hb", [128, 512], F32)
        binb = A("binb", [128, 32], F32)
        sw = A("sw", [128, 32], F32)
        evsc = A("evsc", [128, 32], F32)
        aw = A("aw", [128, 1], F32)
        rw = A("rw", [128, 1], F32)
        aw2 = A("aw2", [128, 1], F32)
        hacc = A("hacc", [128, 1], F32)
        am8row = A("am8row", [1, 8], F32)
        am = A("am", [1, 1], F32)
        sc2 = A("sc2", [1, 2], F32)
        rsb = A("rsb", [128, 2], F32)
        amT = A("amT", [1, 128], F32)
        am1 = A("am1", [1, 1], F32)
        ident = A("ident", [128, 128], BF16)
        ps_m = PS("ps_m", [128, 512], F32)
        ps_tb = PS("ps_tb", [128, 128], BF16)

        add_identity(sc, ident)
        sc.c("vector", lambda v: v.memset(hacc[:], 0.0))
        add_scalar_prep(sc, dict(am8row=am8row, am=am, sc2=sc2, rsb=rsb,
                                 scr2=scr2), amax8, 32767.0, 1e-8, "a16")
        sc.d("sync", lambda sync: sync.dma_start(out=binb[:], in_=b_in_t))

        for fb in range(32):
            sc.d("sync", lambda sync, fb=fb: sync.dma_start(
                out=stage_f[:], in_=w_inT[ts(fb, 128), :]))
            sc.c("vector", lambda v: v.reduce_max(
                aw[:], stage_f[:], axis=AX.X, apply_absolute_value=True))
            sc.c("vector", lambda v: v.tensor_scalar(
                aw[:], aw[:], 1e-8, None, ALU.max))
            sc.c("vector", lambda v: v.reciprocal(rw[:], aw[:]))
            sc.c("vector", lambda v, fb=fb: v.tensor_scalar(
                sw[:, fb:fb + 1], aw[:], 1.0 / 127.0, None, ALU.mult))
            sc.c("vector", lambda v: v.tensor_scalar(
                rw[:], rw[:], 127.0, None, ALU.mult))
            sc.c("vector", lambda v: v._custom_dve(
                ROUND_SCALE, out=w_q_t[:], in0=stage_f[:], s0=rw[:, 0:1],
                s1=1.0, imm2=MAGIC))
            for j in range(8):
                sc.c("tensor", lambda te, j=j: te.transpose(
                    ps_tb[:], w_q_t[:, ts(j, 128)], ident[:]))
                sc.c("vector", lambda v, j=j, fb=fb: v.tensor_copy(
                    w_in_q[:, j * FF + fb * 128: j * FF + fb * 128 + 128], ps_tb[:]))
        sc.c("vector", lambda v: v.tensor_scalar(
            evsc[:], sw[:], rsb[:, 1:2], None, ALU.mult))

        for db in range(8):
            for tc in range(8):
                sc.d("sync", lambda sync, db=db, tc=tc: sync.dma_start(
                    out=stage_f[:, 0:512], in_=x1T[ts(db, 128), ts(tc, 512)]))
                sc.c("vector", lambda v, db=db, tc=tc: v._custom_dve(
                    ROUND_SCALE, out=xi[:, db * T + tc * 512: db * T + tc * 512 + 512],
                    in0=stage_f[:, 0:512], s0=rsb[:, 0:1], s1=1.0, imm2=MAGIC))

        for fb in range(32):
            for tc in range(8):
                def mmh(te, fb=fb, tc=tc):
                    ins = []
                    for db in range(8):
                        ins.append(te.matmul(
                            ps_m[:],
                            w_in_q[:, db * FF + fb * 128: db * FF + fb * 128 + 128],
                            xi[:, db * T + tc * 512: db * T + tc * 512 + 512],
                            start=(db == 0), stop=(db == 7)))
                    return ins[-1]
                sc.c("tensor", mmh)
                sc.c("scalar", lambda s, fb=fb: s.activation(
                    out=hb[:], in_=ps_m[:], func=AF.Gelu,
                    scale=evsc[:, fb:fb + 1], bias=binb[:, fb:fb + 1]))
                sc.c("vector", lambda v: v.reduce_max(
                    aw2[:], hb[:], axis=AX.X, apply_absolute_value=True))
                sc.c("vector", lambda v: v.tensor_tensor(
                    hacc[:], hacc[:], aw2[:], ALU.max))
                sc.d("sync", lambda sync, fb=fb, tc=tc: sync.dma_start(
                    out=hidT[ts(fb, 128), ts(tc, 512)], in_=hb[:]))

        add_amax_finalize(sc, hacc, amT, am1, am_bounce, amax_h)
        sc.run(nc)
    mybir.codegen_inst_isa_subclasses(nc)
    return nc



def build_l0_ffn_b():
    nc = bass.Bass("TRN2", target_bir_lowering=False, debug=False)
    x1T = nc.dram_tensor("x1T", [D, T], F32, kind="ExternalInput").ap()
    hidT = nc.dram_tensor("hidT", [FF, T], F32, kind="ExternalInput").ap()
    amax8 = nc.dram_tensor("amax8", [8, 1], F32, kind="ExternalInput").ap()
    w_out = nc.dram_tensor("w_out", [FF, D], F32, kind="ExternalInput").ap()
    b_out_t = nc.dram_tensor("b_out_t", [128, 8], F32, kind="ExternalInput").ap()
    x2T = nc.dram_tensor("x2T", [D, T], F32, kind="ExternalOutput").ap()
    scr2 = nc.dram_tensor("scr2", [1, 2], F32, kind="Internal").ap()

    sc = Script()
    with ExitStack() as es:
        A = lambda n, sh, dt: es.enter_context(nc.sbuf_tensor(n, sh, dt))
        PS = lambda n, sh, dt: es.enter_context(nc.psum_tensor(n, sh, dt))
        stage_f = A("stage_f", [128, 1024], F32)
        w_out_q = A("w_out_q", [128, 32 * 1024], BF16)
        hop = A("hop", [128, 32 * 512], BF16)
        hld = A("hld", [128, 512], F32)
        yr = A("yr", [128, 8 * 512], F32)
        xtile = A("xtile", [128, 512], F32)
        sq = A("sq", [128, 512], F32)
        x2t = A("x2t", [128, 512], F32)
        tprod = A("tprod", [128, 512], F32)
        bout = A("bout", [128, 8], F32)
        sw = A("sw", [128, 32], F32)
        cf = A("cf", [128, 32], F32)
        aw = A("aw", [128, 1], F32)
        rw = A("rw", [128, 1], F32)
        am8row = A("am8row", [1, 8], F32)
        am = A("am", [1, 1], F32)
        sc2 = A("sc2", [1, 2], F32)
        rsb = A("rsb", [128, 2], F32)
        lnrows = A("lnrows", [1, 512 * 6], F32)
        ones_col = A("ones_col", [128, 1], F32)
        ones_row1 = A("ones_row1", [1, 128], F32)
        cpool = A("cpool", [128, 1], F32)
        ps_m = PS("ps_m", [128, 512], F32)
        ps_sum = PS("ps_sum", [1, 512], F32)
        ps_sq = PS("ps_sq", [1, 512], F32)
        ps_al = PS("ps_al", [128, 512], F32)
        ps_be = PS("ps_be", [128, 512], F32)

        add_const_pool(nc, sc, cpool, [1e-5])
        sc.c("vector", lambda v: v.memset(ones_col[:], 1.0))
        sc.c("vector", lambda v: v.memset(ones_row1[:], 1.0))
        add_scalar_prep(sc, dict(am8row=am8row, am=am, sc2=sc2, rsb=rsb,
                                 scr2=scr2), amax8, 32767.0, 1e-8, "a16")
        sc.d("sync", lambda sync: sync.dma_start(out=bout[:], in_=b_out_t))

        for fb in range(32):
            sc.d("sync", lambda sync, fb=fb: sync.dma_start(
                out=stage_f[:], in_=w_out[ts(fb, 128), :]))
            sc.c("vector", lambda v: v.reduce_max(
                aw[:], stage_f[:], axis=AX.X, apply_absolute_value=True))
            sc.c("vector", lambda v: v.tensor_scalar(
                aw[:], aw[:], 1e-8, None, ALU.max))
            sc.c("vector", lambda v: v.reciprocal(rw[:], aw[:]))
            sc.c("vector", lambda v, fb=fb: v.tensor_scalar(
                sw[:, fb:fb + 1], aw[:], 1.0 / 127.0, None, ALU.mult))
            sc.c("vector", lambda v: v.tensor_scalar(
                rw[:], rw[:], 127.0, None, ALU.mult))
            sc.c("vector", lambda v, fb=fb: v._custom_dve(
                ROUND_SCALE, out=w_out_q[:, ts(fb, 1024)], in0=stage_f[:],
                s0=rw[:, 0:1], s1=1.0, imm2=MAGIC))
        sc.c("vector", lambda v: v.tensor_scalar(
            cf[:], sw[:], rsb[:, 1:2], None, ALU.mult))

        for tc in range(8):
            for fb in range(32):
                sc.d("sync", lambda sync, fb=fb, tc=tc: sync.dma_start(
                    out=hld[:], in_=hidT[ts(fb, 128), ts(tc, 512)]))
                sc.c("vector", lambda v, fb=fb: v._custom_dve(
                    ROUND_SCALE, out=hop[:, ts(fb, 512)], in0=hld[:],
                    s0=rsb[:, 0:1], s1=cf[:, fb:fb + 1], imm2=MAGIC))
            for do in range(8):
                def mmo(te, do=do):
                    ins = []
                    for fb in range(32):
                        ins.append(te.matmul(
                            ps_m[:],
                            w_out_q[:, fb * 1024 + do * 128: fb * 1024 + do * 128 + 128],
                            hop[:, ts(fb, 512)],
                            start=(fb == 0), stop=(fb == 31)))
                    return ins[-1]
                sc.c("tensor", mmo)
                sc.d("sync", lambda sync, do=do, tc=tc: sync.dma_start(
                    out=xtile[:], in_=x1T[ts(do, 128), ts(tc, 512)]))
                sc.c("vector", lambda v, do=do: v.scalar_tensor_tensor(
                    yr[:, ts(do, 512)], ps_m[:], bout[:, do:do + 1], xtile[:],
                    ALU.add, ALU.add))
                sc.c("scalar", lambda s, do=do: s.activation(
                    out=sq[:], in_=yr[:, ts(do, 512)], func=AF.Square))

                def mmstat(te, do=do):
                    i1 = te.matmul(ps_sum[:], ones_col[:], yr[:, ts(do, 512)],
                                   start=(do == 0), stop=(do == 7))
                    i2 = te.matmul(ps_sq[:], ones_col[:], sq[:],
                                   start=(do == 0), stop=(do == 7))
                    return [i1, i2][-1]
                sc.c("tensor", mmstat)
            add_ln_rows(sc, lnrows, ps_sum, ps_sq, ps_al, ps_be, ones_row1,
                        1.0 / D, 1e-5)
            for do in range(8):
                sc.c("vector", lambda v, do=do: v.tensor_tensor(
                    tprod[:], yr[:, ts(do, 512)], ps_al[:], ALU.mult))
                sc.c("vector", lambda v: v.tensor_tensor(
                    x2t[:], tprod[:], ps_be[:], ALU.add))
                sc.d("sync", lambda sync, do=do, tc=tc: sync.dma_start(
                    out=x2T[ts(do, 128), ts(tc, 512)], in_=x2t[:]))
        sc.run(nc)
    mybir.codegen_inst_isa_subclasses(nc)
    return nc



def build_ssm(phases="123", dbg=False):
    nc = bass.Bass("TRN2", target_bir_lowering=False, debug=False)
    x2T = nc.dram_tensor("x2T", [D, T], F32, kind="ExternalInput").ap()
    a_log = nc.dram_tensor("a_log", [D, N16], F32, kind="ExternalInput").ap()
    w_delta = nc.dram_tensor("w_delta", [D, D], F32, kind="ExternalInput").ap()
    b_delta_t = nc.dram_tensor("b_delta_t", [128, 8], F32, kind="ExternalInput").ap()
    w_b = nc.dram_tensor("w_b", [D, N16], F32, kind="ExternalInput").ap()
    w_c = nc.dram_tensor("w_c", [D, N16], F32, kind="ExternalInput").ap()
    dskip_t = nc.dram_tensor("dskip_t", [128, 8], F32, kind="ExternalInput").ap()
    nw_t = nc.dram_tensor("nw_t", [128, 8], F32, kind="ExternalInput").ap()
    x3T = nc.dram_tensor("x3T", [D, T], F32, kind="ExternalOutput").ap()
    amax3 = nc.dram_tensor("amax3", [1, 1], F32, kind="ExternalOutput").ap()
    ik = "ExternalOutput" if dbg else "Internal"
    xn_hbm = nc.dram_tensor("xn_hbm", [D, T], F32, kind=ik).ap()
    xnb_hbm = nc.dram_tensor("xnb_hbm", [D, T], BF16, kind="Internal").ap()
    _dl_hbm = nc.dram_tensor("dl_hbm", [D, T], F32, kind=ik).ap()
    bp_hbm = nc.dram_tensor("bp_hbm", [N16, T], F32, kind=ik).ap()
    c_hbm = nc.dram_tensor("c_hbm", [N16, T], F32, kind=ik).ap()
    am_bounce = nc.dram_tensor("am_bounce", [128], F32, kind="Internal").ap()

    DT_MAX, DT_MIN = 10.0, 1e-4
    CEN = (DT_MAX + DT_MIN) / 2.0
    HALF = (DT_MAX - DT_MIN) / 2.0

    sc = Script()
    with ExitStack() as es:
        A = lambda n, sh, dt: es.enter_context(nc.sbuf_tensor(n, sh, dt))
        PS = lambda n, sh, dt: es.enter_context(nc.psum_tensor(n, sh, dt))
        stage_f = A("stage_f", [128, 1024], F32)
        wdel_b = A("wdel_b", [128, 8 * 1024], BF16)
        wb_b = A("wb_b", [128, 8 * N16], BF16)
        wc_b = A("wc_b", [128, 8 * N16], BF16)
        ahalf = A("ahalf", [128, 8 * N16], F32)
        aexp = A("aexp", [128, N16], F32)
        bdel = A("bdel", [128, 8], F32)
        dskip = A("dskip", [128, 8], F32)
        nww = A("nww", [128, 8], F32)
        xch = A("xch", [128, 8 * 512], F32)
        xnch = A("xnch", [128, 8 * 512], F32)
        dlch = A("dlch", [128, 8 * 512], F32)
        uch = A("uch", [128, 8 * 512], F32)
        ytc = A("ytc", [128, 8 * 512], F32)
        yrb = A("yrb", [128, 8 * 512], F32)
        sq = A("sq", [128, 512], F32)
        sqd = A("sqd", [128, 512], F32)
        x3t = A("x3t", [128, 512], F32)
        tprod = A("tprod", [128, 512], F32)
        xbf = A("xbf", [128, 512], BF16)
        rhs_bf = A("rhs_bf", [128, 2 * 8 * 512], BF16)
        sp1 = A("sp1", [128, 2 * 512], F32)
        sp2 = A("sp2", [128, 2 * 512], F32)
        sp3 = A("sp3", [128, 2 * 512], F32)
        dnm = A("dnm", [128, 2 * 512], F32)
        wre = A("wre", [128, 2 * 512], F32)
        are = A("are", [128, 2 * 512], F32)
        qre = A("qre", [128, 2 * 512], F32)
        bre = A("bre", [128, 2 * 512], F32)
        bp1 = A("bp1", [128, 2 * 512], F32)
        c1 = A("c1", [128, 2 * 512], F32)
        hn = A("hn", [128, 512], F32)
        hst = A("hst", [128, 128], F32)
        bprow = A("bprow", [16, 512], F32)
        lnrows = A("lnrows", [1, 512 * 6], F32)
        ones_col = A("ones_col", [128, 1], F32)
        ones_row1 = A("ones_row1", [1, 128], F32)
        aw2 = A("aw2", [128, 1], F32)
        amax_acc = A("amax_acc", [128, 1], F32)
        amT = A("amT", [1, 128], F32)
        am1 = A("am1", [1, 1], F32)
        cpool = A("cpool", [128, 6], F32)
        ps_m0 = PS("ps_m0", [128, 512], F32)
        ps_m1 = PS("ps_m1", [128, 512], F32)
        ps_bp = PS("ps_bp", [16, 512], F32)
        ps_sum = PS("ps_sum", [1, 512], F32)
        ps_sq = PS("ps_sq", [1, 512], F32)
        ps_al = PS("ps_al", [128, 512], F32)
        ps_be = PS("ps_be", [128, 512], F32)
        psm = [ps_m0, ps_m1]

        S2 = 512

        def sl(buf, p, wdt=S2):
            return buf[:, p * wdt:(p + 1) * wdt]

        add_const_pool(nc, sc, cpool, [1e-5, 1e-6, 1.0, -CEN / HALF, -1.0])
        sc.c("vector", lambda v: v.memset(ones_col[:], 1.0))
        sc.c("vector", lambda v: v.memset(ones_row1[:], 1.0))
        sc.c("vector", lambda v: v.memset(amax_acc[:], 0.0))
        sc.c("vector", lambda v: v.memset(hst[:], 0.0))
        sc.d("sync", lambda sync: [sync.dma_start(out=bdel[:], in_=b_delta_t),
                                   sync.dma_start(out=dskip[:], in_=dskip_t),
                                   sync.dma_start(out=nww[:], in_=nw_t)], n=3)
        for db in range(8):
            sc.d("sync", lambda sync, db=db: sync.dma_start(
                out=aexp[:], in_=a_log[ts(db, 128), :]))
            sc.c("scalar", lambda s: s.activation(
                out=aexp[:], in_=aexp[:], func=AF.Exp))
            sc.c("vector", lambda v, db=db: v.tensor_scalar(
                ahalf[:, ts(db, N16)], aexp[:], 0.5, None, ALU.mult))
        for db in range(8):
            sc.d("sync", lambda sync, db=db: sync.dma_start(
                out=stage_f[:], in_=w_delta[ts(db, 128), :]))
            sc.c("scalar", lambda s, db=db: s.activation(
                out=wdel_b[:, ts(db, 1024)], in_=stage_f[:], func=AF.Copy))
            sc.d("sync", lambda sync, db=db: sync.dma_start(
                out=stage_f[:, 0:N16], in_=w_b[ts(db, 128), :]))
            sc.c("scalar", lambda s, db=db: s.activation(
                out=wb_b[:, ts(db, N16)], in_=stage_f[:, 0:N16], func=AF.Copy))
            sc.d("sync", lambda sync, db=db: sync.dma_start(
                out=stage_f[:, 0:N16], in_=w_c[ts(db, 128), :]))
            sc.c("scalar", lambda s, db=db: s.activation(
                out=wc_b[:, ts(db, N16)], in_=stage_f[:, 0:N16], func=AF.Copy))

        for tc in (range(8) if "1" in phases else []):
            sc.d("sync", lambda sync, tc=tc: [sync.dma_start(
                out=xch[:, ts(db, 512)], in_=x2T[ts(db, 128), ts(tc, 512)])
                for db in range(8)], n=8)
            for db in range(8):
                sc.c("scalar", lambda s, db=db: s.activation(
                    out=sq[:], in_=xch[:, ts(db, 512)], func=AF.Square))
                sc.c("tensor", lambda te, db=db: te.matmul(
                    ps_sum[:], ones_col[:], sq[:],
                    start=(db == 0), stop=(db == 7)))
            sc.c("vector", lambda v: v.tensor_copy(lnrows[:, 0:512], ps_sum[:]))
            sc.c("vector", lambda v: v.tensor_scalar(
                lnrows[:, 512:1024], lnrows[:, 0:512], 1.0 / D, None, ALU.mult))
            sc.c("scalar", lambda s: s.activation(
                out=lnrows[:, 0:512], in_=lnrows[:, 512:1024], func=AF.Sqrt,
                bias=1e-6))
            sc.c("vector", lambda v: v.reciprocal(
                lnrows[:, 512:1024], lnrows[:, 0:512]))
            sc.c("tensor", lambda te: te.matmul(
                ps_al[:], ones_row1[:], lnrows[:, 512:1024], start=True, stop=True))
            for db in range(8):
                sc.c("vector", lambda v, db=db: v.scalar_tensor_tensor(
                    xnch[:, ts(db, 512)], xch[:, ts(db, 512)], nww[:, db:db + 1],
                    ps_al[:], ALU.mult, ALU.mult))
                sc.c("scalar", lambda s, db=db: s.activation(
                    out=xbf[:], in_=xnch[:, ts(db, 512)], func=AF.Copy))
                sc.d("sync", lambda sync, db=db, tc=tc: [
                    sync.dma_start(out=xn_hbm[ts(db, 128), ts(tc, 512)],
                                   in_=xnch[:, ts(db, 512)]),
                    sync.dma_start(out=xnb_hbm[ts(db, 128), ts(tc, 512)],
                                   in_=xbf[:])], n=2)

        DLH = lambda tc: f"dlh:{tc}"

        def emit_p2(tc):
            p2 = tc % 2
            rkey = f"rhs:{p2}"
            sc.d("sync", lambda sync, tc=tc, p2=p2: [sync.dma_start(
                out=rhs_bf[:, (p2 * 8 + db) * 512:(p2 * 8 + db) * 512 + 512],
                in_=xnb_hbm[ts(db, 128), ts(tc, 512)]) for db in range(8)],
                n=8, r=["xnbh"], w=[rkey])
            for ob in range(8):
                obp = ob % 2
                pk = f"psm:{obp}"
                k1, k2, k3 = f"sp1:{obp}", f"sp2:{obp}", f"sp3:{obp}"

                def mmd(te, ob=ob, p2=p2, obp=obp):
                    ins = []
                    for db in range(8):
                        ins.append(te.matmul(
                            psm[obp][:],
                            wdel_b[:, db * 1024 + ob * 128: db * 1024 + ob * 128 + 128],
                            rhs_bf[:, (p2 * 8 + db) * 512:(p2 * 8 + db) * 512 + 512],
                            start=(db == 0), stop=(db == 7)))
                    return ins[-1]
                sc.c("tensor", mmd, r=[rkey], w=[pk])
                sc.c("vector", lambda v, ob=ob, obp=obp: v.tensor_scalar(
                    sl(sp1, obp), psm[obp][:], bdel[:, ob:ob + 1], None, ALU.add),
                    r=[pk], w=[k1])
                sc.c("scalar", lambda s, obp=obp: s.activation(
                    out=sl(sp2, obp), in_=sl(sp1, obp), func=AF.Abs),
                    r=[k1], w=[k2])
                sc.c("scalar", lambda s, obp=obp: s.activation(
                    out=sl(sp2, obp), in_=sl(sp2, obp), func=AF.Exp, scale=-1.0),
                    r=[k2], w=[k2])
                sc.c("scalar", lambda s, obp=obp: s.activation(
                    out=sl(sp2, obp), in_=sl(sp2, obp), func=AF.Ln, bias=1.0),
                    r=[k2], w=[k2])
                sc.c("vector", lambda v, obp=obp: v.tensor_scalar(
                    sl(sp3, obp), sl(sp1, obp), 0.0, None, ALU.max),
                    r=[k1], w=[k3])
                sc.c("vector", lambda v, obp=obp: v.tensor_tensor(
                    sl(sp2, obp), sl(sp2, obp), sl(sp3, obp), ALU.add),
                    r=[k2, k3], w=[k2])
                sc.c("scalar", lambda s, obp=obp: s.activation(
                    out=sl(sp3, obp), in_=sl(sp2, obp), func=AF.Tanh,
                    scale=1.0 / HALF, bias=-CEN / HALF),
                    r=[k2], w=[k3])
                sc.c("vector", lambda v, obp=obp: v.tensor_scalar(
                    sl(sp3, obp), sl(sp3, obp), HALF, CEN, ALU.mult, ALU.add),
                    r=[k3], w=[k3])
                sc.d("sync", lambda sync, ob=ob, tc=tc, obp=obp: sync.dma_start(
                    out=_dl_hbm[ts(ob, 128), ts(tc, 512)], in_=sl(sp3, obp)),
                    r=[k3], w=[DLH(tc)])
            for wsb, hbm in ((wb_b, bp_hbm), (wc_b, c_hbm)):
                def mmb(te, wsb=wsb, p2=p2):
                    ins = []
                    for db in range(8):
                        ins.append(te.matmul(
                            ps_bp[:], wsb[:, ts(db, N16)],
                            rhs_bf[:, (p2 * 8 + db) * 512:(p2 * 8 + db) * 512 + 512],
                            start=(db == 0), stop=(db == 7)))
                    return ins[-1]
                sc.c("tensor", mmb, r=[rkey], w=["psbp"])
                sc.c("vector", lambda v: v.tensor_copy(bprow[:], ps_bp[:]),
                     r=["psbp"], w=["bprow"])
                sc.d("sync", lambda sync, hbm=hbm, tc=tc: sync.dma_start(
                    out=hbm[:, ts(tc, 512)], in_=bprow[:]),
                    r=["bprow"], w=[f"bpch:{tc}"])

        def emit_p3(tc):
            sc.d("sync", lambda sync, tc=tc: [sync.dma_start(
                out=dlch[:, ts(db, 512)], in_=_dl_hbm[ts(db, 128), ts(tc, 512)])
                for db in range(8)], n=8, r=[DLH(tc)], w=["dlch"])
            sc.d("sync", lambda sync, tc=tc: [sync.dma_start(
                out=xnch[:, ts(db, 512)], in_=xn_hbm[ts(db, 128), ts(tc, 512)])
                for db in range(8)], n=8, r=["xnh"], w=["xnch"])
            for db in range(8):
                sc.c("scalar", lambda s, db=db: s.activation(
                    out=sqd[:], in_=dlch[:, ts(db, 512)], func=AF.Sqrt),
                    r=["dlch"], w=["sqd"])
                sc.c("vector", lambda v, db=db: v.scalar_tensor_tensor(
                    uch[:, ts(db, 512)], sqd[:], 0.5, xnch[:, ts(db, 512)],
                    ALU.mult, ALU.mult), r=["sqd", "xnch"], w=["uch"])

            def emit_bcast(n):
                np_ = n % 2
                sc.d("sync", lambda sync, n=n, np_=np_: [
                    sync.dma_start(out=sl(bp1, np_),
                                   in_=bp_hbm[n:n + 1, ts(tc, 512)].broadcast_to([128, 512])),
                    sync.dma_start(out=sl(c1, np_),
                                   in_=c_hbm[n:n + 1, ts(tc, 512)].broadcast_to([128, 512]))],
                    n=2, r=[f"bpch:{tc}"], w=[f"bp:{np_}", f"c:{np_}"])

            def emit_denom(i):
                n, db = divmod(i, 8)
                ip = i % 2
                sc.c("scalar", lambda s, n=n, db=db, ip=ip: s.activation(
                    out=sl(dnm, ip), in_=dlch[:, ts(db, 512)], func=AF.Identity,
                    scale=ahalf[:, db * N16 + n: db * N16 + n + 1], bias=1.0),
                    r=["dlch"], w=[f"dnm:{ip}"])

            emit_bcast(0)
            emit_bcast(1)
            emit_denom(0)
            emit_denom(1)
            for n in range(N16):
                nb = n % 2
                for db in range(8):
                    i = n * 8 + db
                    ip = i % 2
                    kd, kw, ka, kq, kb = (f"dnm:{ip}", f"w:{ip}", f"a:{ip}",
                                          f"q:{ip}", f"b:{ip}")
                    sc.c("vector", lambda v, ip=ip: v.reciprocal_approx_fast(
                        out=sl(wre, ip), in_=sl(dnm, ip)), r=[kd], w=[kw])
                    sc.c("vector", lambda v, db=db, ip=ip, nb=nb: v.tensor_tensor(
                        sl(qre, ip), uch[:, ts(db, 512)], sl(bp1, nb), ALU.mult),
                        r=["uch", f"bp:{nb}"], w=[kq])
                    sc.c("scalar", lambda s, ip=ip: s.activation(
                        out=sl(are, ip), in_=sl(wre, ip), func=AF.Identity,
                        scale=2.0, bias=-1.0), r=[kw], w=[ka])
                    if i + 2 < N16 * 8:
                        emit_denom(i + 2)
                    sc.c("vector", lambda v, ip=ip: v.scalar_tensor_tensor(
                        sl(bre, ip), sl(are, ip), 1.0, sl(qre, ip),
                        ALU.add, ALU.mult), r=[ka, kq], w=[kb])
                    sc.c("vector", lambda v, db=db, n=n, ip=ip: v.tensor_tensor_scan(
                        hn[:], sl(are, ip), sl(bre, ip),
                        hst[:, db * N16 + n: db * N16 + n + 1], ALU.mult, ALU.add),
                        r=[ka, kb, "hst"], w=["hn"])
                    sc.c("vector", lambda v, db=db, n=n: v.tensor_copy(
                        hst[:, db * N16 + n: db * N16 + n + 1], hn[:, 511:512]),
                        r=["hn"], w=["hst"])
                    sc.c("vector", lambda v, ip=ip, nb=nb: v.tensor_tensor(
                        sl(qre, ip), hn[:], sl(c1, nb), ALU.mult),
                        r=["hn", f"c:{nb}"], w=[kq])
                    if n == 0:
                        sc.c("vector", lambda v, db=db, ip=ip: v.tensor_copy(
                            ytc[:, ts(db, 512)], sl(qre, ip)),
                            r=[kq], w=[f"yt:{db}"])
                    else:
                        sc.c("vector", lambda v, db=db, ip=ip: v.tensor_tensor(
                            ytc[:, ts(db, 512)], ytc[:, ts(db, 512)], sl(qre, ip),
                            ALU.add), r=[kq, f"yt:{db}"], w=[f"yt:{db}"])
                if n + 2 < N16:
                    emit_bcast(n + 2)
            sc.d("sync", lambda sync, tc=tc: [sync.dma_start(
                out=xch[:, ts(db, 512)], in_=x2T[ts(db, 128), ts(tc, 512)])
                for db in range(8)], n=8, r=["xch2"], w=["xch2"])
            for db in range(8):
                sc.c("vector", lambda v, db=db: v.scalar_tensor_tensor(
                    tprod[:], xnch[:, ts(db, 512)], dskip[:, db:db + 1],
                    ytc[:, ts(db, 512)], ALU.mult, ALU.add),
                    r=["xnch", f"yt:{db}"], w=["tp"])
                sc.c("vector", lambda v, db=db: v.tensor_tensor(
                    yrb[:, ts(db, 512)], tprod[:], xch[:, ts(db, 512)], ALU.add),
                    r=["tp", "xch2"], w=[f"yr:{db}"])
                sc.c("scalar", lambda s, db=db: s.activation(
                    out=sq[:], in_=yrb[:, ts(db, 512)], func=AF.Square),
                    r=[f"yr:{db}"], w=["sq1"])

                def mmstat(te, db=db):
                    te.matmul(ps_sum[:], ones_col[:], yrb[:, ts(db, 512)],
                              start=(db == 0), stop=(db == 7))
                    return te.matmul(ps_sq[:], ones_col[:], sq[:],
                                     start=(db == 0), stop=(db == 7))
                sc.c("tensor", mmstat, r=[f"yr:{db}", "sq1"], w=["pssum", "pssq"])
            add_ln_rows_k(sc, lnrows, ps_sum, ps_sq, ps_al, ps_be, ones_row1,
                          1.0 / D, 1e-5)
            for db in range(8):
                sc.c("vector", lambda v, db=db: v.tensor_tensor(
                    tprod[:], yrb[:, ts(db, 512)], ps_al[:], ALU.mult),
                    r=[f"yr:{db}", "psal"], w=["tp"])
                sc.c("vector", lambda v: v.tensor_tensor(
                    x3t[:], tprod[:], ps_be[:], ALU.add),
                    r=["tp", "psbe"], w=["x3t"])
                sc.c("vector", lambda v: v.reduce_max(
                    aw2[:], x3t[:], axis=AX.X, apply_absolute_value=True),
                    r=["x3t"], w=["aw2"])
                sc.c("vector", lambda v: v.tensor_tensor(
                    amax_acc[:], amax_acc[:], aw2[:], ALU.max),
                    r=["aw2", "amacc"], w=["amacc"])
                sc.d("sync", lambda sync, db=db, tc=tc: sync.dma_start(
                    out=x3T[ts(db, 128), ts(tc, 512)], in_=x3t[:]),
                    r=["x3t"], w=["x3T"])

        if "3" in phases:
            emit_p2(0)
            for tc in range(8):
                if tc + 1 < 8:
                    emit_p2(tc + 1)
                emit_p3(tc)

        add_amax_finalize(sc, amax_acc, amT, am1, am_bounce, amax3)
        sc.run(nc)
    mybir.codegen_inst_isa_subclasses(nc)
    return nc



def build_l1_ffn_a():
    nc = bass.Bass("TRN2", target_bir_lowering=False, debug=False)
    x3T = nc.dram_tensor("x3T", [D, T], F32, kind="ExternalInput").ap()
    amax8 = nc.dram_tensor("amax8", [8, 1], F32, kind="ExternalInput").ap()
    w_up = nc.dram_tensor("w_up", [I3, D], F32, kind="ExternalInput").ap()
    w_gate = nc.dram_tensor("w_gate", [I3, D], F32, kind="ExternalInput").ap()
    upT = nc.dram_tensor("upT", [I3, T], F32, kind="ExternalOutput").ap()
    gateT = nc.dram_tensor("gateT", [I3, T], F32, kind="ExternalOutput").ap()
    amax_u = nc.dram_tensor("amax_u", [1, 1], F32, kind="ExternalOutput").ap()
    amax_g = nc.dram_tensor("amax_g", [1, 1], F32, kind="ExternalOutput").ap()
    scr2 = nc.dram_tensor("scr2", [1, 2], F32, kind="Internal").ap()
    am_bounce = nc.dram_tensor("am_bounce", [128], F32, kind="Internal").ap()

    sc = Script()
    with ExitStack() as es:
        A = lambda n, sh, dt: es.enter_context(nc.sbuf_tensor(n, sh, dt))
        PS = lambda n, sh, dt: es.enter_context(nc.psum_tensor(n, sh, dt))
        stage_f = A("stage_f", [128, 1024], F32)
        wq_t = A("wq_t", [128, 1024], BF16)
        wuq = A("wuq", [128, 8 * I3], BF16)
        wgq = A("wgq", [128, 8 * I3], BF16)
        xq = A("xq", [128, 8 * T], BF16)
        ot = A("ot", [128, 512], F32)
        awb = A("awb", [128, 32], F32)
        sbv = A("sbv", [128, 32], F32)
        rbv = A("rbv", [128, 32], F32)
        aw2 = A("aw2", [128, 1], F32)
        acc_u = A("acc_u", [128, 1], F32)
        acc_g = A("acc_g", [128, 1], F32)
        am8row = A("am8row", [1, 8], F32)
        am = A("am", [1, 1], F32)
        sc2 = A("sc2", [1, 2], F32)
        rsb = A("rsb", [128, 2], F32)
        amT = A("amT", [1, 128], F32)
        am1 = A("am1", [1, 1], F32)
        ident = A("ident", [128, 128], BF16)
        ps_m = PS("ps_m", [128, 512], F32)
        ps_tb = PS("ps_tb", [128, 128], BF16)

        add_identity(sc, ident)
        sc.c("vector", lambda v: v.memset(acc_u[:], 0.0))
        sc.c("vector", lambda v: v.memset(acc_g[:], 0.0))
        add_scalar_prep(sc, dict(am8row=am8row, am=am, sc2=sc2, rsb=rsb,
                                 scr2=scr2), amax8, 32767.0, 1e-12, "qdq")

        for db in range(8):
            for tc in range(8):
                sc.d("sync", lambda sync, db=db, tc=tc: sync.dma_start(
                    out=stage_f[:, 0:512], in_=x3T[ts(db, 128), ts(tc, 512)]))
                sc.c("vector", lambda v, db=db, tc=tc: v._custom_dve(
                    ROUND_SCALE, out=xq[:, db * T + tc * 512: db * T + tc * 512 + 512],
                    in0=stage_f[:, 0:512], s0=rsb[:, 0:1], s1=1.0, imm2=MAGIC))

        for w_hbm, wq_dst in ((w_up, wuq), (w_gate, wgq)):
            for rb in range(24):
                add_qdq_weight_lpbq(sc, nc, stage_f, wq_t, awb, sbv, rbv,
                                    w_hbm, rb, 32)
                for j in range(8):
                    sc.c("tensor", lambda te, j=j: te.transpose(
                        ps_tb[:], wq_t[:, ts(j, 128)], ident[:]))
                    sc.c("vector", lambda v, j=j, rb=rb, wq_dst=wq_dst: v.tensor_copy(
                        wq_dst[:, j * I3 + rb * 128: j * I3 + rb * 128 + 128],
                        ps_tb[:]))

        for wq_src, out_hbm, acc in ((wuq, upT, acc_u), (wgq, gateT, acc_g)):
            for ib in range(24):
                for tc in range(8):
                    def mmu(te, wq_src=wq_src, ib=ib, tc=tc):
                        ins = []
                        for db in range(8):
                            ins.append(te.matmul(
                                ps_m[:],
                                wq_src[:, db * I3 + ib * 128: db * I3 + ib * 128 + 128],
                                xq[:, db * T + tc * 512: db * T + tc * 512 + 512],
                                start=(db == 0), stop=(db == 7)))
                        return ins[-1]
                    sc.c("tensor", mmu)
                    sc.c("scalar", lambda s: s.activation(
                        out=ot[:], in_=ps_m[:], func=AF.Copy, scale=rsb[:, 1:2]))
                    sc.c("vector", lambda v: v.reduce_max(
                        aw2[:], ot[:], axis=AX.X, apply_absolute_value=True))
                    sc.c("vector", lambda v, acc=acc: v.tensor_tensor(
                        acc[:], acc[:], aw2[:], ALU.max))
                    sc.d("sync", lambda sync, out_hbm=out_hbm, ib=ib, tc=tc:
                         sync.dma_start(out=out_hbm[ts(ib, 128), ts(tc, 512)],
                                        in_=ot[:]))

        add_amax_finalize(sc, acc_u, amT, am1, am_bounce, amax_u)
        add_amax_finalize(sc, acc_g, amT, am1, am_bounce, amax_g)
        sc.run(nc)
    mybir.codegen_inst_isa_subclasses(nc)
    return nc



def build_l1_ffn_b():
    nc = bass.Bass("TRN2", target_bir_lowering=False, debug=False)
    gateT = nc.dram_tensor("gateT", [I3, T], F32, kind="ExternalInput").ap()
    amax8 = nc.dram_tensor("amax8", [8, 1], F32, kind="ExternalInput").ap()
    mT = nc.dram_tensor("mT", [I3, T], F32, kind="ExternalOutput").ap()
    amax_m = nc.dram_tensor("amax_m", [1, 1], F32, kind="ExternalOutput").ap()
    scr2 = nc.dram_tensor("scr2", [1, 2], F32, kind="Internal").ap()
    am_bounce = nc.dram_tensor("am_bounce", [128], F32, kind="Internal").ap()

    sc = Script()
    with ExitStack() as es:
        A = lambda n, sh, dt: es.enter_context(nc.sbuf_tensor(n, sh, dt))
        gt = A("gt", [128, 512], F32)
        gq = A("gq", [128, 512], F32)
        sig = A("sig", [128, 512], F32)
        si2 = A("si2", [128, 512], F32)
        mm = A("mm_", [128, 512], F32)
        mb16 = A("mb16", [128, 512], BF16)
        aw2 = A("aw2", [128, 1], F32)
        macc = A("macc", [128, 1], F32)
        am8row = A("am8row", [1, 8], F32)
        am = A("am", [1, 1], F32)
        sc2 = A("sc2", [1, 2], F32)
        rsb = A("rsb", [128, 2], F32)
        amT = A("amT", [1, 128], F32)
        am1 = A("am1", [1, 1], F32)

        sc.c("vector", lambda v: v.memset(macc[:], 0.0))
        add_scalar_prep(sc, dict(am8row=am8row, am=am, sc2=sc2, rsb=rsb,
                                 scr2=scr2), amax8, 32767.0, 1e-12, "qdq")
        for ib in range(24):
            for tc in range(8):
                sc.d("sync", lambda sync, ib=ib, tc=tc: sync.dma_start(
                    out=gt[:], in_=gateT[ts(ib, 128), ts(tc, 512)]))
                sc.c("vector", lambda v: v._custom_dve(
                    ROUND_SCALE, out=gq[:], in0=gt[:], s0=rsb[:, 0:1],
                    s1=rsb[:, 1:2], imm2=MAGIC))
                sc.c("scalar", lambda s: s.activation(
                    out=sig[:], in_=gq[:], func=AF.Sigmoid))
                sc.c("vector", lambda v: v._custom_dve(
                    ROUND_SCALE, out=si2[:], in0=sig[:], s0=65536.0, s1=1.0,
                    imm2=MAGIC))
                sc.c("vector", lambda v: v.tensor_scalar(
                    si2[:], si2[:], 65535.0, 1.0 / 65536.0, ALU.min, ALU.mult))
                sc.c("vector", lambda v: v.tensor_tensor(
                    mm[:], gq[:], si2[:], ALU.mult))
                sc.c("vector", lambda v: v.reduce_max(
                    aw2[:], mm[:], axis=AX.X, apply_absolute_value=True))
                sc.c("vector", lambda v: v.tensor_tensor(
                    macc[:], macc[:], aw2[:], ALU.max))
                sc.d("sync", lambda sync, ib=ib, tc=tc: sync.dma_start(
                    out=mT[ts(ib, 128), ts(tc, 512)], in_=mm[:]))
        add_amax_finalize(sc, macc, amT, am1, am_bounce, amax_m)
        sc.run(nc)
    mybir.codegen_inst_isa_subclasses(nc)
    return nc



def build_l1_ffn_c():
    nc = bass.Bass("TRN2", target_bir_lowering=False, debug=False)
    upT = nc.dram_tensor("upT", [I3, T], F32, kind="ExternalInput").ap()
    mT = nc.dram_tensor("mT", [I3, T], F32, kind="ExternalInput").ap()
    amax8u = nc.dram_tensor("amax8u", [8, 1], F32, kind="ExternalInput").ap()
    amax8m = nc.dram_tensor("amax8m", [8, 1], F32, kind="ExternalInput").ap()
    oT = nc.dram_tensor("oT", [I3, T], F32, kind="ExternalOutput").ap()
    amax_o = nc.dram_tensor("amax_o", [1, 1], F32, kind="ExternalOutput").ap()
    scr2 = nc.dram_tensor("scr2", [1, 2], F32, kind="Internal").ap()
    scr2b = nc.dram_tensor("scr2b", [1, 2], F32, kind="Internal").ap()
    am_bounce = nc.dram_tensor("am_bounce", [128], F32, kind="Internal").ap()

    sc = Script()
    with ExitStack() as es:
        A = lambda n, sh, dt: es.enter_context(nc.sbuf_tensor(n, sh, dt))
        ut = A("ut", [128, 512], F32)
        mt = A("mt", [128, 512], F32)
        uq = A("uq", [128, 512], F32)
        mq = A("mq", [128, 512], F32)
        oo = A("oo", [128, 512], F32)
        ob16 = A("ob16", [128, 512], BF16)
        aw2 = A("aw2", [128, 1], F32)
        oacc = A("oacc", [128, 1], F32)
        am8row = A("am8row", [1, 8], F32)
        am = A("am", [1, 1], F32)
        sc2 = A("sc2", [1, 2], F32)
        rsbu = A("rsbu", [128, 2], F32)
        rsbm = A("rsbm", [128, 2], F32)
        amT = A("amT", [1, 128], F32)
        am1 = A("am1", [1, 1], F32)

        sc.c("vector", lambda v: v.memset(oacc[:], 0.0))
        add_scalar_prep(sc, dict(am8row=am8row, am=am, sc2=sc2, rsb=rsbu,
                                 scr2=scr2), amax8u, 32767.0, 1e-12, "qdq")
        add_scalar_prep(sc, dict(am8row=am8row, am=am, sc2=sc2, rsb=rsbm,
                                 scr2=scr2b), amax8m, 32767.0, 1e-12, "qdq")
        for ib in range(24):
            for tc in range(8):
                sc.d("sync", lambda sync, ib=ib, tc=tc: [
                    sync.dma_start(out=ut[:], in_=upT[ts(ib, 128), ts(tc, 512)]),
                    sync.dma_start(out=mt[:], in_=mT[ts(ib, 128), ts(tc, 512)])],
                    n=2)
                sc.c("vector", lambda v: v._custom_dve(
                    ROUND_SCALE, out=uq[:], in0=ut[:], s0=rsbu[:, 0:1],
                    s1=rsbu[:, 1:2], imm2=MAGIC))
                sc.c("vector", lambda v: v._custom_dve(
                    ROUND_SCALE, out=mq[:], in0=mt[:], s0=rsbm[:, 0:1],
                    s1=rsbm[:, 1:2], imm2=MAGIC))
                sc.c("vector", lambda v: v.tensor_tensor(
                    oo[:], uq[:], mq[:], ALU.mult))
                sc.c("vector", lambda v: v.reduce_max(
                    aw2[:], oo[:], axis=AX.X, apply_absolute_value=True))
                sc.c("vector", lambda v: v.tensor_tensor(
                    oacc[:], oacc[:], aw2[:], ALU.max))
                sc.d("sync", lambda sync, ib=ib, tc=tc: sync.dma_start(
                    out=oT[ts(ib, 128), ts(tc, 512)], in_=oo[:]))
        add_amax_finalize(sc, oacc, amT, am1, am_bounce, amax_o)
        sc.run(nc)
    mybir.codegen_inst_isa_subclasses(nc)
    return nc



def build_l1_ffn_d():
    nc = bass.Bass("TRN2", target_bir_lowering=False, debug=False)
    x3T = nc.dram_tensor("x3T", [D, T], F32, kind="ExternalInput").ap()
    oT = nc.dram_tensor("oT", [I3, T], F32, kind="ExternalInput").ap()
    amax8 = nc.dram_tensor("amax8", [8, 1], F32, kind="ExternalInput").ap()
    w_down = nc.dram_tensor("w_down", [D, I3], F32, kind="ExternalInput").ap()
    x4T = nc.dram_tensor("x4T", [D, T], F32, kind="ExternalOutput").ap()
    scr2 = nc.dram_tensor("scr2", [1, 2], F32, kind="Internal").ap()

    sc = Script()
    with ExitStack() as es:
        A = lambda n, sh, dt: es.enter_context(nc.sbuf_tensor(n, sh, dt))
        PS = lambda n, sh, dt: es.enter_context(nc.psum_tensor(n, sh, dt))
        stage_f = A("stage_f", [128, 3072], F32)
        wq_t = A("wq_t", [128, 3072], BF16)
        wdq = A("wdq", [128, 24 * 1024], BF16)
        oqc = A("oqc", [128, 24 * 512], BF16)
        old = A("old", [128, 512], F32)
        yr = A("yr", [128, 8 * 512], F32)
        xtile = A("xtile", [128, 512], F32)
        sq = A("sq", [128, 512], F32)
        x4t = A("x4t", [128, 512], F32)
        tprod = A("tprod", [128, 512], F32)
        awb = A("awb", [128, 96], F32)
        sbv = A("sbv", [128, 96], F32)
        rbv = A("rbv", [128, 96], F32)
        am8row = A("am8row", [1, 8], F32)
        am = A("am", [1, 1], F32)
        sc2 = A("sc2", [1, 2], F32)
        rsb = A("rsb", [128, 2], F32)
        lnrows = A("lnrows", [1, 512 * 6], F32)
        ones_col = A("ones_col", [128, 1], F32)
        ones_row1 = A("ones_row1", [1, 128], F32)
        cpool = A("cpool", [128, 1], F32)
        ident = A("ident", [128, 128], BF16)
        ps_m = PS("ps_m", [128, 512], F32)
        ps_tb = PS("ps_tb", [128, 128], BF16)
        ps_sum = PS("ps_sum", [1, 512], F32)
        ps_sq = PS("ps_sq", [1, 512], F32)
        ps_al = PS("ps_al", [128, 512], F32)
        ps_be = PS("ps_be", [128, 512], F32)

        add_const_pool(nc, sc, cpool, [1e-5])
        add_identity(sc, ident)
        sc.c("vector", lambda v: v.memset(ones_col[:], 1.0))
        sc.c("vector", lambda v: v.memset(ones_row1[:], 1.0))
        add_scalar_prep(sc, dict(am8row=am8row, am=am, sc2=sc2, rsb=rsb,
                                 scr2=scr2), amax8, 32767.0, 1e-12, "qdq")

        for rb in range(8):
            add_qdq_weight_lpbq(sc, nc, stage_f, wq_t, awb, sbv, rbv,
                                w_down, rb, 96)
            for j in range(24):
                sc.c("tensor", lambda te, j=j: te.transpose(
                    ps_tb[:], wq_t[:, ts(j, 128)], ident[:]))
                sc.c("vector", lambda v, j=j, rb=rb: v.tensor_copy(
                    wdq[:, j * 1024 + rb * 128: j * 1024 + rb * 128 + 128],
                    ps_tb[:]))

        for tc in range(8):
            for ib in range(24):
                sc.d("sync", lambda sync, ib=ib, tc=tc: sync.dma_start(
                    out=old[:], in_=oT[ts(ib, 128), ts(tc, 512)]))
                sc.c("vector", lambda v, ib=ib: v._custom_dve(
                    ROUND_SCALE, out=oqc[:, ts(ib, 512)], in0=old[:],
                    s0=rsb[:, 0:1], s1=rsb[:, 1:2], imm2=MAGIC))
            for do in range(8):
                def mmo(te, do=do):
                    ins = []
                    for ib in range(24):
                        ins.append(te.matmul(
                            ps_m[:],
                            wdq[:, ib * 1024 + do * 128: ib * 1024 + do * 128 + 128],
                            oqc[:, ts(ib, 512)],
                            start=(ib == 0), stop=(ib == 23)))
                    return ins[-1]
                sc.c("tensor", mmo)
                sc.d("sync", lambda sync, do=do, tc=tc: sync.dma_start(
                    out=xtile[:], in_=x3T[ts(do, 128), ts(tc, 512)]))
                sc.c("vector", lambda v, do=do: v.scalar_tensor_tensor(
                    yr[:, ts(do, 512)], ps_m[:], 1.0, xtile[:],
                    ALU.mult, ALU.add))
                sc.c("scalar", lambda s, do=do: s.activation(
                    out=sq[:], in_=yr[:, ts(do, 512)], func=AF.Square))

                def mmstat(te, do=do):
                    te.matmul(ps_sum[:], ones_col[:], yr[:, ts(do, 512)],
                              start=(do == 0), stop=(do == 7))
                    return te.matmul(ps_sq[:], ones_col[:], sq[:],
                                     start=(do == 0), stop=(do == 7))
                sc.c("tensor", mmstat)
            add_ln_rows(sc, lnrows, ps_sum, ps_sq, ps_al, ps_be, ones_row1,
                        1.0 / D, 1e-5)
            for do in range(8):
                sc.c("vector", lambda v, do=do: v.tensor_tensor(
                    tprod[:], yr[:, ts(do, 512)], ps_al[:], ALU.mult))
                sc.c("vector", lambda v: v.tensor_tensor(
                    x4t[:], tprod[:], ps_be[:], ALU.add))
                sc.d("sync", lambda sync, do=do, tc=tc: sync.dma_start(
                    out=x4T[ts(do, 128), ts(tc, 512)], in_=x4t[:]))
        sc.run(nc)
    mybir.codegen_inst_isa_subclasses(nc)
    return nc



_NC_CACHE = {}


def _get_nc(name, builder):
    if name not in _NC_CACHE:
        _NC_CACHE[name] = builder()
    return _NC_CACHE[name]


def _gather_amax_n(res, key, n):
    return np.ascontiguousarray(np.concatenate(
        [np.asarray(res.results[c][key]).reshape(1, 1) for c in range(n)]
        + [np.asarray(res.results[0][key]).reshape(1, 1)] * (8 - n),
        axis=0)).astype(np.float32)


def _gather_amax(res, key):
    return _gather_amax_n(res, key, NCORES)


def kernel(L0_attn_x, L0_attn_W_qkv, L0_ffn_W_in, L0_ffn_b_in, L0_ffn_W_out,
           L0_ffn_b_out, L1_attn_A_log, L1_attn_W_delta, L1_attn_b_delta,
           L1_attn_W_B, L1_attn_W_C, L1_attn_D_skip, L1_attn_norm_weight,
           L1_ffn_w_gate, L1_ffn_w_up, L1_ffn_w_down):
    cores = list(range(NCORES))
    x = np.ascontiguousarray(np.asarray(L0_attn_x, dtype=np.float32))
    wqkv = np.ascontiguousarray(np.asarray(L0_attn_W_qkv, np.float32))
    w_inT = np.ascontiguousarray(np.asarray(L0_ffn_W_in, np.float32).T)
    b_in_t = np.ascontiguousarray(np.asarray(L0_ffn_b_in, np.float32).reshape(32, 128).T)
    w_out = np.ascontiguousarray(np.asarray(L0_ffn_W_out, np.float32))
    b_out_t = np.ascontiguousarray(np.asarray(L0_ffn_b_out, np.float32).reshape(8, 128).T)
    a_log = np.ascontiguousarray(np.asarray(L1_attn_A_log, np.float32))
    w_delta = np.ascontiguousarray(np.asarray(L1_attn_W_delta, np.float32))
    b_delta_t = np.ascontiguousarray(np.asarray(L1_attn_b_delta, np.float32).reshape(8, 128).T)
    w_b = np.ascontiguousarray(np.asarray(L1_attn_W_B, np.float32))
    w_c = np.ascontiguousarray(np.asarray(L1_attn_W_C, np.float32))
    dskip_t = np.ascontiguousarray(np.asarray(L1_attn_D_skip, np.float32).reshape(8, 128).T)
    nw_t = np.ascontiguousarray(np.asarray(L1_attn_norm_weight, np.float32).reshape(8, 128).T)
    w_gate = np.ascontiguousarray(np.asarray(L1_ffn_w_gate, np.float32))
    w_up = np.ascontiguousarray(np.asarray(L1_ffn_w_up, np.float32))
    w_down = np.ascontiguousarray(np.asarray(L1_ffn_w_down, np.float32))

    nc1 = _get_nc("l0_attn", build_l0_attn)
    r1 = run_bass_kernel_spmd(nc1, [
        {"x_tok": np.ascontiguousarray(x[c]), "wqkv": wqkv} for c in cores],
        core_ids=cores)
    x1 = [np.asarray(r1.results[c]["x1T"]) for c in cores]
    am1 = _gather_amax(r1, "amax1")

    nc2 = _get_nc("l0_ffn_a", build_l0_ffn_a)
    r2 = run_bass_kernel_spmd(nc2, [
        {"x1T": x1[c], "amax8": am1, "w_inT": w_inT, "b_in_t": b_in_t}
        for c in cores], core_ids=cores)
    hid = [np.asarray(r2.results[c]["hidT"]) for c in cores]
    amh = _gather_amax(r2, "amax_h")

    nc3 = _get_nc("l0_ffn_b", build_l0_ffn_b)
    r3 = run_bass_kernel_spmd(nc3, [
        {"x1T": x1[c], "hidT": hid[c], "amax8": amh, "w_out": w_out,
         "b_out_t": b_out_t} for c in cores], core_ids=cores)
    x2 = [np.asarray(r3.results[c]["x2T"]) for c in cores]

    nc4 = _get_nc("ssm", build_ssm)
    r4 = run_bass_kernel_spmd(nc4, [
        {"x2T": x2[c], "a_log": a_log, "w_delta": w_delta,
         "b_delta_t": b_delta_t, "w_b": w_b, "w_c": w_c, "dskip_t": dskip_t,
         "nw_t": nw_t} for c in cores], core_ids=cores)
    x3 = [np.asarray(r4.results[c]["x3T"]) for c in cores]
    am3 = _gather_amax(r4, "amax3")

    nc5 = _get_nc("l1_ffn_a", build_l1_ffn_a)
    r5 = run_bass_kernel_spmd(nc5, [
        {"x3T": x3[c], "amax8": am3, "w_up": w_up, "w_gate": w_gate}
        for c in cores], core_ids=cores)
    up = [np.asarray(r5.results[c]["upT"]) for c in cores]
    gate = [np.asarray(r5.results[c]["gateT"]) for c in cores]
    amu = _gather_amax(r5, "amax_u")
    amg = _gather_amax(r5, "amax_g")

    nc6 = _get_nc("l1_ffn_b", build_l1_ffn_b)
    r6 = run_bass_kernel_spmd(nc6, [
        {"gateT": gate[c], "amax8": amg} for c in cores], core_ids=cores)
    mm_ = [np.asarray(r6.results[c]["mT"]) for c in cores]
    amm = _gather_amax(r6, "amax_m")

    nc7 = _get_nc("l1_ffn_c", build_l1_ffn_c)
    r7 = run_bass_kernel_spmd(nc7, [
        {"upT": up[c], "mT": mm_[c], "amax8u": amu, "amax8m": amm}
        for c in cores], core_ids=cores)
    oo = [np.asarray(r7.results[c]["oT"]) for c in cores]
    amo = _gather_amax(r7, "amax_o")

    nc8 = _get_nc("l1_ffn_d", build_l1_ffn_d)
    r8 = run_bass_kernel_spmd(nc8, [
        {"x3T": x3[c], "oT": oo[c], "amax8": amo, "w_down": w_down}
        for c in cores], core_ids=cores)
    out = np.stack([np.ascontiguousarray(np.asarray(r8.results[c]["x4T"]).T)
                    for c in cores])
    return out.astype(np.float32)


# version 05
# speedup vs baseline: 1.3847x; 1.2461x over previous
from contextlib import ExitStack

import numpy as np
import ml_dtypes
import concourse.bass as bass
import concourse.mybir as mybir
from concourse.bass_utils import run_bass_kernel_spmd
import concourse.dve_ops as dve_ops
from concourse.dve_op import DveOp
from concourse.dve_spec import Spec, Src0, Src1, C0, C1, C2, lower as spec_lower

F32 = mybir.dt.float32
BF16 = mybir.dt.bfloat16
AF = mybir.ActivationFunctionType
ALU = mybir.AluOpType
AX = mybir.AxisListType

T = 4096
D = 1024
FF = 4096
I3 = 3072
N16 = 16
NCORES = 8
MAGIC = float(3 << 22)
NEG = -1.0e30
INV32 = 1.0 / 32.0


def _register_dve(name, spec, reference):
    if name in dve_ops._SUB_OPCODE_FOR_NAME:
        for op in dve_ops.OPS:
            if op.name == name:
                return op
    sha = {"v3": spec_lower(spec, ver="v3", perf_en=False, name=name).sha("v3")}
    op = DveOp(name, spec, subdim=False, uops_sha=sha, reference=reference)
    dve_ops.OPS.append(op)
    dve_ops._SUB_OPCODE_FOR_NAME[name] = dve_ops._CUSTOM_DVE_ROW_BASE + len(dve_ops.OPS) - 1
    dve_ops.CUSTOM_DVE_SPECS[name] = spec
    dve_ops.CUSTOM_DVE_REFS[name] = reference
    return op


ROUND_SCALE = _register_dve(
    "ROUND_SCALE",
    Spec(body=((Src0 * C0 + C2) - C2) * C1),
    lambda in0, in1, s0, s1, imm2: np.rint(in0 * s0) * s1,
)


def ts(i, size):
    return slice(i * size, (i + 1) * size)



ENGINES = ("sync", "vector", "scalar", "tensor", "gpsimd")


class Script:

    def __init__(self):
        self.steps = []

    def c(self, eng, fn, r=None, w=None):
        self.steps.append((eng, "c", fn, 1, r, w))

    def d(self, eng, fn, n=1, r=None, w=None):
        self.steps.append((eng, "d", fn, n, r, w))

    def _compute_deps(self):
        n_steps = len(self.steps)
        deps = [set() for _ in range(n_steps)]
        last_w = {}
        readers = {}
        barrier = -1
        barrier_readers = []
        last_d = {}
        for i, (eng, kind, fn, n, r, w) in enumerate(self.steps):
            if kind == "d":
                if eng in last_d:
                    deps[i].add(last_d[eng])
                last_d[eng] = i
            if r is None or w is None:
                for j in barrier_readers:
                    deps[i].add(j)
                if barrier >= 0:
                    deps[i].add(barrier)
                barrier = i
                barrier_readers = []
                last_w = {}
                readers = {}
                continue
            if barrier >= 0:
                deps[i].add(barrier)
            for k in r:
                if k in last_w:
                    deps[i].add(last_w[k])
            for k in w:
                if k in last_w:
                    deps[i].add(last_w[k])
                for rd in readers.get(k, ()):
                    deps[i].add(rd)
            for k in r:
                readers.setdefault(k, []).append(i)
            for k in w:
                last_w[k] = i
                readers[k] = []
            barrier_readers.append(i)
        return deps

    FORCE_SERIAL = False

    def run(self, nc):
        if Script.FORCE_SERIAL:
            self.steps = [(e, k, f, n, None, None) for (e, k, f, n, r, w)
                          in self.steps]
        with ExitStack() as es:
            sems = {e: es.enter_context(nc.semaphore("sem_" + e)) for e in ENGINES}
            cnt = {e: 0 for e in ENGINES}
            plan = []
            for eng, kind, fn, n, r, w in self.steps:
                cnt[eng] += 1 if kind == "c" else 16 * n
                plan.append((eng, cnt[eng]))
            deps = self._compute_deps()
            waits = []
            for i, dd in enumerate(deps):
                per = {}
                for j in dd:
                    e, v = plan[j]
                    per[e] = max(per.get(e, 0), v)
                waits.append(per)

            def section(e, engobj):
                for i, (eng, kind, fn, n, r, w) in enumerate(self.steps):
                    if eng != e:
                        continue
                    for de, dv in sorted(waits[i].items()):
                        engobj.wait_ge(sems[de], dv)
                    instrs = fn(engobj)
                    if not isinstance(instrs, (list, tuple)):
                        instrs = [instrs]
                    if kind == "c":
                        instrs[-1].then_inc(sems[e], 1)
                    else:
                        assert len(instrs) == n, f"step {i}: expected {n} dmas, got {len(instrs)}"
                        for ins_ in instrs:
                            ins_.then_inc(sems[e], 16)

            import os
            if os.environ.get("SCRIPT_VERBOSE"):
                from collections import Counter
                print("script steps:", len(self.steps), Counter(s[0] for s in self.steps))
            block = es.enter_context(nc.Block())

            @block.sync
            def _(sync):
                section("sync", sync)
                for e in ENGINES:
                    if cnt[e] > 0:
                        sync.wait_ge(sems[e], cnt[e])

            @block.vector
            def _(v):
                section("vector", v)

            @block.scalar
            def _(s):
                section("scalar", s)

            @block.tensor
            def _(te):
                section("tensor", te)

            @block.gpsimd
            def _(g):
                section("gpsimd", g)



def add_const_pool(nc, sc, pool, values):
    for i, v in enumerate(values):
        nc.const_aps.register(F32, float(v), pool[:, i:i + 1])

    def f(vec):
        ins = []
        for i, v in enumerate(values):
            ins.append(vec.memset(pool[:, i:i + 1], float(v)))
        return ins[-1]
    sc.c("vector", f)


def add_identity(sc, ident_bf, ident_f32=None):
    def f0(g):
        ins = [g.memset(ident_bf[:], 0.0)]
        if ident_f32 is not None:
            ins.append(g.memset(ident_f32[:], 0.0))
        return ins[-1]
    sc.c("gpsimd", f0)

    def f1(g):
        ins = [g.affine_select(
            out=ident_bf[:], in_=ident_bf[:], compare_op=ALU.not_equal,
            fill=1.0, base=0, pattern=[[-1, 128]], channel_multiplier=1)]
        if ident_f32 is not None:
            ins.append(g.affine_select(
                out=ident_f32[:], in_=ident_f32[:], compare_op=ALU.not_equal,
                fill=1.0, base=0, pattern=[[-1, 128]], channel_multiplier=1))
        return ins[-1]
    sc.c("gpsimd", f1)


def add_scalar_prep(sc, bufs, amax_in, qmax, clip_lo, mode):
    am8row, am, sc2, rsb, scr2 = (bufs["am8row"], bufs["am"], bufs["sc2"],
                                  bufs["rsb"], bufs["scr2"])

    def load(sync):
        return sync.dma_start(out=am8row[:], in_=amax_in.rearrange("a one -> one a"))
    sc.d("sync", load)

    def red(v):
        return v.reduce_max(am[:], am8row[:], axis=AX.X)
    sc.c("vector", red)

    if mode == "a16":
        sc.c("vector", lambda v: v.tensor_scalar(am[:], am[:], clip_lo, None, ALU.max))
        def mk(v):
            v.reciprocal(sc2[:, 0:1], am[:])
            return v.tensor_scalar(sc2[:, 1:2], am[:], 1.0 / qmax, None, ALU.mult)
        sc.c("vector", mk)
        sc.c("vector", lambda v: v.tensor_scalar(
            sc2[:, 0:1], sc2[:, 0:1], qmax, None, ALU.mult))
    else:
        sc.c("vector", lambda v: v.tensor_scalar(
            sc2[:, 1:2], am[:], 1.0 / qmax, clip_lo, ALU.mult, ALU.max))
        sc.c("vector", lambda v: v.reciprocal(sc2[:, 0:1], sc2[:, 1:2]))

    def store(sync):
        return sync.dma_start(out=scr2, in_=sc2[:])
    sc.d("sync", store)

    def bcast(sync):
        return sync.dma_start(out=rsb[:], in_=scr2.broadcast_to([128, 2]))
    sc.d("sync", bcast)


def add_amax_finalize(sc, amax_acc, amT, am1, am_bounce, amax_out):
    def st1(sync):
        return sync.dma_start(out=am_bounce.rearrange("(p one) -> p one", one=1),
                              in_=amax_acc[:])
    sc.d("sync", st1)

    def ld1(sync):
        return sync.dma_start(out=amT[:],
                              in_=am_bounce.rearrange("(one p) -> one p", one=1))
    sc.d("sync", ld1)

    def red(v):
        return v.reduce_max(am1[:], amT[:], axis=AX.X)
    sc.c("vector", red)

    def st2(sync):
        return sync.dma_start(out=amax_out, in_=am1[:])
    sc.d("sync", st2)



def build_l0_attn(dbg=False, dbg_qb=31):
    nc = bass.Bass("TRN2", target_bir_lowering=False, debug=False)
    x_tok = nc.dram_tensor("x_tok", [T, D], F32, kind="ExternalInput").ap()
    wqkv = nc.dram_tensor("wqkv", [D, 3 * D], F32, kind="ExternalInput").ap()
    x1T = nc.dram_tensor("x1T", [D, T], F32, kind="ExternalOutput").ap()
    amax1 = nc.dram_tensor("amax1", [1, 1], F32, kind="ExternalOutput").ap()
    ikind = "ExternalOutput" if dbg else "Internal"
    xt_bf = nc.dram_tensor("xt_bf", [D, T], BF16, kind=ikind).ap()
    wq_bf = nc.dram_tensor("wq_bf", [D, 3 * D], BF16, kind=ikind).ap()
    qt_bf = nc.dram_tensor("qt_bf", [D, T], BF16, kind=ikind).ap()
    am_bounce = nc.dram_tensor("am_bounce", [128], F32, kind="Internal").ap()
    if dbg:
        kT_d = nc.dram_tensor("kT_d", [128, 8 * T], BF16, kind="ExternalOutput").ap()
        vv_d = nc.dram_tensor("vv_d", [128, 32 * 1024], BF16, kind="ExternalOutput").ap()
        sc_d = nc.dram_tensor("sc_d", [128, T], F32, kind="ExternalOutput").ap()
        p_d = nc.dram_tensor("p_d", [128, T], BF16, kind="ExternalOutput").ap()
        y1_d = nc.dram_tensor("y1_d", [128, D], F32, kind="ExternalOutput").ap()
        at_d = nc.dram_tensor("at_d", [128, D], F32, kind="ExternalOutput").ap()
        sm_d = nc.dram_tensor("sm_d", [128, 16], F32, kind="ExternalOutput").ap()

    sc = Script()
    with ExitStack() as es:
        A = lambda n, sh, dt: es.enter_context(nc.sbuf_tensor(n, sh, dt))
        PS = lambda n, sh, dt: es.enter_context(nc.psum_tensor(n, sh, dt))
        stage_f = A("stage_f", [128, 1024], F32)
        stage_b = A("stage_b", [128, 1024], BF16)
        kT = A("kT", [128, 8 * T], BF16)
        vv = A("vv", [128, 32 * 1024], BF16)
        wvb = A("wvb", [128, 8 * 512], BF16)
        wqk_t = A("wqk_t", [128, 8 * 128], BF16)
        xc = A("xc", [128, 8 * 512], BF16)
        scb = A("scb", [128, T], F32)
        pbuf = A("pbuf", [128, T], BF16)
        pT2 = A("pT2", [128, 256], BF16)
        qTb = A("qTb", [128, 8 * 128], BF16)
        xrow = A("xrow", [128, 1024], F32)
        y1 = A("y1", [128, 1024], F32)
        sqv = A("sqv", [128, 1024], F32)
        x1r = A("x1r", [128, 1024], F32)
        x1tr = A("x1tr", [128, 1024], F32)
        sm = A("sm", [128, 16], F32)
        amax_acc = A("amax_acc", [128, 1], F32)
        amT = A("amT", [1, 128], F32)
        am1 = A("am1", [1, 1], F32)
        ident = A("ident", [128, 128], BF16)
        identf = A("identf", [128, 128], F32)
        ps_a = PS("ps_a", [128, 512], F32)
        ps_o = PS("ps_o", [128, 1024], F32)
        ps_tb = PS("ps_tb", [128, 128], BF16)
        ps_tf = PS("ps_tf", [128, 128], F32)

        COL = {k: i for i, k in enumerate(
            ["rowmax", "nbias", "lsum", "linv", "sumy", "sumsq", "m", "ey2",
             "mm", "var", "sd", "rstd", "mb", "negmb"])}

        def col(k):
            return sm[:, COL[k]:COL[k] + 1]

        cpool = A("cpool", [128, 2], F32)
        add_const_pool(nc, sc, cpool, [1e-5])
        add_identity(sc, ident, identf)
        sc.c("vector", lambda v: v.memset(amax_acc[:], 0.0))

        for r in range(8):
            for cc in range(3):
                sc.d("sync", lambda sync, r=r, cc=cc: sync.dma_start(
                    out=stage_f[:], in_=wqkv[ts(r, 128), ts(cc, 1024)]))
                sc.c("scalar", lambda s, r=r, cc=cc: s.activation(
                    out=stage_b[:], in_=stage_f[:], func=AF.Copy))
                sc.d("sync", lambda sync, r=r, cc=cc: sync.dma_start(
                    out=wq_bf[ts(r, 128), ts(cc, 1024)], in_=stage_b[:]))

        for i in range(32):
            sc.d("sync", lambda sync, i=i: sync.dma_start(
                out=xrow[:], in_=x_tok[ts(i, 128), :]))
            sc.c("scalar", lambda s, i=i: s.activation(
                out=stage_b[:], in_=xrow[:], func=AF.Copy))
            for j in range(8):
                sc.c("tensor", lambda te, j=j: te.transpose(
                    ps_tb[:], stage_b[:, ts(j, 128)], ident[:]))
                sc.c("vector", lambda v, j=j: v.tensor_copy(
                    xc.bitcast(BF16)[:, ts(j, 128)] if False else xc[:, ts(j, 128)], ps_tb[:]))
            sc.d("sync", lambda sync, i=i: [sync.dma_start(
                out=xt_bf[ts(j, 128), ts(i, 128)], in_=xc[:, ts(j, 128)])
                for j in range(8)], n=8)

        for dvh in range(2):
            sc.d("sync", lambda sync, dvh=dvh: [sync.dma_start(
                out=wvb[:, ts(db, 512)],
                in_=wq_bf[ts(db, 128), 2 * D + dvh * 512: 2 * D + dvh * 512 + 512])
                for db in range(8)], n=8)
            for tc in range(8):
                sc.d("sync", lambda sync, tc=tc: [sync.dma_start(
                    out=xc[:, ts(db, 512)], in_=xt_bf[ts(db, 128), ts(tc, 512)])
                    for db in range(8)], n=8)
                for sub in range(4):
                    def mmv(te, tc=tc, sub=sub, dvh=dvh):
                        ins = []
                        for db in range(8):
                            ins.append(te.matmul(
                                ps_a[:],
                                xc[:, db * 512 + sub * 128: db * 512 + sub * 128 + 128],
                                wvb[:, ts(db, 512)],
                                start=(db == 0), stop=(db == 7)))
                        return ins[-1]
                    sc.c("tensor", mmv)
                    kt = tc * 4 + sub
                    sc.c("scalar", lambda s, kt=kt, dvh=dvh: s.activation(
                        out=vv[:, kt * 1024 + dvh * 512: kt * 1024 + dvh * 512 + 512],
                        in_=ps_a[:], func=AF.Copy))

        for tc in range(8):
            sc.d("sync", lambda sync, tc=tc: [sync.dma_start(
                out=xc[:, ts(db, 512)], in_=xt_bf[ts(db, 128), ts(tc, 512)])
                for db in range(8)], n=8)
            for qk in range(2):
                for ob in range(8):
                    sc.d("sync", lambda sync, qk=qk, ob=ob: [sync.dma_start(
                        out=wqk_t[:, ts(db, 128)],
                        in_=wq_bf[ts(db, 128), qk * D + ob * 128: qk * D + ob * 128 + 128])
                        for db in range(8)], n=8)

                    def mmqk(te, tc=tc, qk=qk, ob=ob):
                        ins = []
                        for db in range(8):
                            ins.append(te.matmul(
                                ps_a[:], wqk_t[:, ts(db, 128)],
                                xc[:, ts(db, 512)],
                                start=(db == 0), stop=(db == 7)))
                        return ins[-1]
                    sc.c("tensor", mmqk)
                    if qk == 1:
                        sc.c("scalar", lambda s, ob=ob, tc=tc: s.activation(
                            out=kT[:, ob * T + tc * 512: ob * T + tc * 512 + 512],
                            in_=ps_a[:], func=AF.Copy))
                    else:
                        sc.c("scalar", lambda s: s.activation(
                            out=stage_b[:, 0:512], in_=ps_a[:], func=AF.Copy))
                        sc.d("sync", lambda sync, ob=ob, tc=tc: sync.dma_start(
                            out=qt_bf[ts(ob, 128), ts(tc, 512)], in_=stage_b[:, 0:512]))

        for qb in range(32):
            klen = (qb + 1) * 128
            nkc = (klen + 511) // 512
            sc.d("sync", lambda sync, qb=qb: [sync.dma_start(
                out=qTb[:, ts(db, 128)], in_=qt_bf[ts(db, 128), ts(qb, 128)])
                for db in range(8)], n=8)
            sc.d("sync", lambda sync, qb=qb: sync.dma_start(
                out=xrow[:], in_=x_tok[ts(qb, 128), :]))
            for kc in range(nkc):
                w = min(512, klen - kc * 512)

                def mms(te, qb=qb, kc=kc, w=w):
                    ins = []
                    for db in range(8):
                        ins.append(te.matmul(
                            ps_a[:, 0:w], qTb[:, ts(db, 128)],
                            kT[:, db * T + kc * 512: db * T + kc * 512 + w],
                            start=(db == 0), stop=(db == 7)))
                    return ins[-1]
                sc.c("tensor", mms)
                sc.c("scalar", lambda s, kc=kc, w=w: s.activation(
                    out=scb[:, kc * 512: kc * 512 + w], in_=ps_a[:, 0:w],
                    func=AF.Copy))
            sc.c("gpsimd", lambda g, qb=qb: g.affine_select(
                out=scb[:, ts(qb, 128)], in_=scb[:, ts(qb, 128)],
                compare_op=ALU.is_ge, fill=NEG, base=0,
                pattern=[[-1, 128]], channel_multiplier=1))
            sc.c("vector", lambda v, klen=klen: v.reduce_max(
                col("rowmax"), scb[:, 0:klen], axis=AX.X))
            sc.c("vector", lambda v: v.tensor_scalar(
                col("nbias"), col("rowmax"), -INV32, None, ALU.mult))
            sc.c("scalar", lambda s, klen=klen: s.activation(
                out=pbuf[:, 0:klen], in_=scb[:, 0:klen], func=AF.Exp,
                scale=INV32, bias=col("nbias"), accum_out=col("lsum")))
            sc.c("vector", lambda v: v.reciprocal(col("linv"), col("lsum")))
            for kt in range(qb + 1):
                sc.c("tensor", lambda te, kt=kt: te.transpose(
                    ps_tb[:], pbuf[:, ts(kt, 128)], ident[:]))
                sl = (kt % 2) * 128
                sc.c("vector", lambda v, sl=sl: v.tensor_copy(
                    pT2[:, sl:sl + 128], ps_tb[:]))

                def mmav(te, kt=kt, qb=qb, sl=sl):
                    ins = []
                    for dvh in range(2):
                        ins.append(te.matmul(
                            ps_o[:, ts(dvh, 512)], pT2[:, sl:sl + 128],
                            vv[:, kt * 1024 + dvh * 512: kt * 1024 + dvh * 512 + 512],
                            start=(kt == 0), stop=(kt == qb)))
                    return ins[-1]
                sc.c("tensor", mmav)
            if dbg and qb == dbg_qb:
                sc.c("vector", lambda v: v.tensor_scalar(
                    sqv[:], ps_o[:], col("linv"), None, ALU.mult))
                sc.d("sync", lambda sync: sync.dma_start(out=at_d, in_=sqv[:]))
                sc.d("sync", lambda sync: sync.dma_start(out=sc_d, in_=scb[:]))
                sc.d("sync", lambda sync: sync.dma_start(out=p_d, in_=pbuf[:]))
            sc.c("vector", lambda v: v.scalar_tensor_tensor(
                y1[:], ps_o[:], col("linv"), xrow[:], ALU.mult, ALU.add,
                accum_out=col("sumy")))
            sc.c("scalar", lambda s: s.activation(
                out=sqv[:], in_=y1[:], func=AF.Square, accum_out=col("sumsq")))

            def stats(v):
                v.tensor_scalar(col("m"), col("sumy"), 1.0 / D, None, ALU.mult)
                return v.tensor_scalar(col("ey2"), col("sumsq"), 1.0 / D, None, ALU.mult)
            sc.c("vector", stats)
            sc.c("vector", lambda v: v.tensor_tensor(
                col("mm"), col("m"), col("m"), ALU.mult))
            sc.c("vector", lambda v: v.tensor_tensor(
                col("var"), col("ey2"), col("mm"), ALU.subtract))
            sc.c("scalar", lambda s: s.activation(
                out=col("sd"), in_=col("var"), func=AF.Sqrt, bias=1e-5))
            sc.c("vector", lambda v: v.reciprocal(col("rstd"), col("sd")))
            sc.c("vector", lambda v: v.tensor_tensor(
                col("mb"), col("m"), col("rstd"), ALU.mult))
            sc.c("vector", lambda v: v.tensor_scalar(
                col("negmb"), col("mb"), -1.0, None, ALU.mult))
            sc.c("vector", lambda v: v.tensor_scalar(
                x1r[:], y1[:], col("rstd"), col("negmb"), ALU.mult, ALU.add))

            if dbg and qb == dbg_qb:
                sc.d("sync", lambda sync: sync.dma_start(out=y1_d, in_=y1[:]))
                sc.d("sync", lambda sync: sync.dma_start(out=sm_d, in_=sm[:]))

            sc.c("vector", lambda v: v.reduce_max(
                col("rowmax"), x1r[:], axis=AX.X, apply_absolute_value=True))
            sc.c("vector", lambda v: v.tensor_tensor(
                amax_acc[:], amax_acc[:], col("rowmax"), ALU.max))
            for j in range(8):
                sc.c("tensor", lambda te, j=j: te.transpose(
                    ps_tf[:], x1r[:, ts(j, 128)], identf[:]))
                sc.c("vector", lambda v, j=j: v.tensor_copy(
                    x1tr[:, ts(j, 128)], ps_tf[:]))
            sc.d("sync", lambda sync, qb=qb: [sync.dma_start(
                out=x1T[ts(j, 128), ts(qb, 128)], in_=x1tr[:, ts(j, 128)])
                for j in range(8)], n=8)

        if dbg:
            sc.d("sync", lambda sync: sync.dma_start(out=kT_d, in_=kT[:]))
            sc.d("sync", lambda sync: sync.dma_start(out=vv_d, in_=vv[:]))
        add_amax_finalize(sc, amax_acc, amT, am1, am_bounce, amax1)
        sc.run(nc)
    mybir.codegen_inst_isa_subclasses(nc)
    return nc



_BF = ml_dtypes.bfloat16


def add_ln_rows(sc, rows, ps_sum, ps_sq, ps_al, ps_be, ones_row1, inv_n, eps_bias):
    R = lambda i: rows[:, i * 512:(i + 1) * 512]
    sc.c("vector", lambda v: [v.tensor_copy(R(0), ps_sum[:]),
                              v.tensor_copy(R(1), ps_sq[:])][-1])
    sc.c("vector", lambda v: [
        v.tensor_scalar(R(2), R(0), inv_n, None, ALU.mult),
        v.tensor_scalar(R(3), R(1), inv_n, None, ALU.mult)][-1])
    sc.c("vector", lambda v: v.tensor_tensor(R(5), R(2), R(2), ALU.mult))
    sc.c("vector", lambda v: v.tensor_tensor(R(4), R(3), R(5), ALU.subtract))
    sc.c("scalar", lambda s: s.activation(
        out=R(3), in_=R(4), func=AF.Sqrt, bias=eps_bias))
    sc.c("vector", lambda v: v.reciprocal(R(4), R(3)))
    sc.c("vector", lambda v: v.tensor_tensor(R(5), R(2), R(4), ALU.mult))
    sc.c("vector", lambda v: v.tensor_scalar(R(5), R(5), -1.0, None, ALU.mult))

    def bcasts(te):
        te.matmul(ps_al[:], ones_row1[:], R(4), start=True, stop=True)
        return te.matmul(ps_be[:], ones_row1[:], R(5), start=True, stop=True)
    sc.c("tensor", bcasts)


def add_ln_rows_k(sc, rows, ps_sum, ps_sq, ps_al, ps_be, ones_row1, inv_n,
                  eps_bias):
    R = lambda i: rows[:, i * 512:(i + 1) * 512]
    sc.c("vector", lambda v: [v.tensor_copy(R(0), ps_sum[:]),
                              v.tensor_copy(R(1), ps_sq[:])][-1],
         r=["pssum", "pssq"], w=["lnr"])
    sc.c("vector", lambda v: [
        v.tensor_scalar(R(2), R(0), inv_n, None, ALU.mult),
        v.tensor_scalar(R(3), R(1), inv_n, None, ALU.mult)][-1],
        r=["lnr"], w=["lnr"])
    sc.c("vector", lambda v: v.tensor_tensor(R(5), R(2), R(2), ALU.mult),
         r=["lnr"], w=["lnr"])
    sc.c("vector", lambda v: v.tensor_tensor(R(4), R(3), R(5), ALU.subtract),
         r=["lnr"], w=["lnr"])
    sc.c("scalar", lambda s: s.activation(
        out=R(3), in_=R(4), func=AF.Sqrt, bias=eps_bias),
        r=["lnr"], w=["lnr"])
    sc.c("vector", lambda v: v.reciprocal(R(4), R(3)), r=["lnr"], w=["lnr"])
    sc.c("vector", lambda v: v.tensor_tensor(R(5), R(2), R(4), ALU.mult),
         r=["lnr"], w=["lnr"])
    sc.c("vector", lambda v: v.tensor_scalar(R(5), R(5), -1.0, None, ALU.mult),
         r=["lnr"], w=["lnr"])

    def bcasts(te):
        te.matmul(ps_al[:], ones_row1[:], R(4), start=True, stop=True)
        return te.matmul(ps_be[:], ones_row1[:], R(5), start=True, stop=True)
    sc.c("tensor", bcasts, r=["lnr"], w=["psal", "psbe"])


def add_qdq_weight_lpbq(sc, nc, stage_f, wq_t, awb, sbv, rbv, w_hbm, rb, nblk):
    W = nblk * 32

    def load(sync):
        return sync.dma_start(out=stage_f[:, 0:W], in_=w_hbm[ts(rb, 128), :])
    sc.d("sync", load)
    sc.c("vector", lambda v: v.tensor_reduce(
        awb[:, 0:nblk], stage_f[:, 0:W].rearrange("p (a b) -> p a b", b=32),
        axis=AX.X, op=ALU.max, apply_absolute_value=True))
    sc.c("vector", lambda v: v.tensor_scalar(
        sbv[:, 0:nblk], awb[:, 0:nblk], 1.0 / 7.0, 1e-12, ALU.mult, ALU.max))
    sc.c("vector", lambda v: v.reciprocal(rbv[:, 0:nblk], sbv[:, 0:nblk]))

    def rounds(v):
        ins = []
        for b in range(nblk):
            ins.append(v._custom_dve(
                ROUND_SCALE, out=wq_t[:, ts(b, 32)], in0=stage_f[:, ts(b, 32)],
                s0=rbv[:, b:b + 1], s1=sbv[:, b:b + 1], imm2=MAGIC))
        return ins[-1]
    sc.c("vector", rounds)



def build_l0_ffn_a():
    nc = bass.Bass("TRN2", target_bir_lowering=False, debug=False)
    x1T = nc.dram_tensor("x1T", [D, T], F32, kind="ExternalInput").ap()
    amax8 = nc.dram_tensor("amax8", [8, 1], F32, kind="ExternalInput").ap()
    w_inT = nc.dram_tensor("w_inT", [FF, D], F32, kind="ExternalInput").ap()
    b_in_t = nc.dram_tensor("b_in_t", [128, 32], F32, kind="ExternalInput").ap()
    hidT = nc.dram_tensor("hidT", [FF, T], F32, kind="ExternalOutput").ap()
    amax_h = nc.dram_tensor("amax_h", [1, 1], F32, kind="ExternalOutput").ap()
    scr2 = nc.dram_tensor("scr2", [1, 2], F32, kind="Internal").ap()
    am_bounce = nc.dram_tensor("am_bounce", [128], F32, kind="Internal").ap()

    sc = Script()
    with ExitStack() as es:
        A = lambda n, sh, dt: es.enter_context(nc.sbuf_tensor(n, sh, dt))
        PS = lambda n, sh, dt: es.enter_context(nc.psum_tensor(n, sh, dt))
        stage_f = A("stage_f", [128, 2 * 1024], F32)
        w_q_t = A("w_q_t", [128, 2 * 1024], BF16)
        w_in_q = A("w_in_q", [128, 8 * FF], BF16)
        xi = A("xi", [128, 8 * T], BF16)
        hb = A("hb", [128, 2 * 512], F32)

        def sl2f(buf, p):
            return buf[:, p * 1024:(p + 1) * 1024]

        def sl2b(buf, p):
            return buf[:, p * 1024:(p + 1) * 1024]

        def hbsl(p):
            return hb[:, p * 512:(p + 1) * 512]
        binb = A("binb", [128, 32], F32)
        sw = A("sw", [128, 32], F32)
        evsc = A("evsc", [128, 32], F32)
        aw = A("aw", [128, 2], F32)
        rw = A("rw", [128, 2], F32)
        aw2 = A("aw2", [128, 1], F32)
        hacc = A("hacc", [128, 1], F32)
        am8row = A("am8row", [1, 8], F32)
        am = A("am", [1, 1], F32)
        sc2 = A("sc2", [1, 2], F32)
        rsb = A("rsb", [128, 2], F32)
        amT = A("amT", [1, 128], F32)
        am1 = A("am1", [1, 1], F32)
        ident = A("ident", [128, 128], BF16)
        ps_m0 = PS("ps_m0", [128, 512], F32)
        ps_m1 = PS("ps_m1", [128, 512], F32)
        ps_tb0 = PS("ps_tb0", [128, 128], BF16)
        ps_tb1 = PS("ps_tb1", [128, 128], BF16)
        psm = [ps_m0, ps_m1]
        pstb = [ps_tb0, ps_tb1]

        add_identity(sc, ident)
        sc.c("vector", lambda v: v.memset(hacc[:], 0.0))
        add_scalar_prep(sc, dict(am8row=am8row, am=am, sc2=sc2, rsb=rsb,
                                 scr2=scr2), amax8, 32767.0, 1e-8, "a16")
        sc.d("sync", lambda sync: sync.dma_start(out=binb[:], in_=b_in_t))

        for fb in range(32):
            p = fb % 2
            kst, kaw, krw, kwq = f"st:{p}", f"aw:{p}", f"rw:{p}", f"wq:{p}"
            sc.d("sync", lambda sync, fb=fb, p=p: sync.dma_start(
                out=sl2f(stage_f, p), in_=w_inT[ts(fb, 128), :]),
                r=[], w=[kst])
            sc.c("vector", lambda v, p=p: v.reduce_max(
                aw[:, p:p + 1], sl2f(stage_f, p), axis=AX.X,
                apply_absolute_value=True), r=[kst], w=[kaw])
            sc.c("vector", lambda v, p=p: v.tensor_scalar(
                aw[:, p:p + 1], aw[:, p:p + 1], 1e-8, None, ALU.max),
                r=[kaw], w=[kaw])
            sc.c("vector", lambda v, p=p: v.reciprocal(
                rw[:, p:p + 1], aw[:, p:p + 1]), r=[kaw], w=[krw])
            sc.c("vector", lambda v, fb=fb, p=p: v.tensor_scalar(
                sw[:, fb:fb + 1], aw[:, p:p + 1], 1.0 / 127.0, None, ALU.mult),
                r=[kaw], w=["sw"])
            sc.c("vector", lambda v, p=p: v.tensor_scalar(
                rw[:, p:p + 1], rw[:, p:p + 1], 127.0, None, ALU.mult),
                r=[krw], w=[krw])
            sc.c("vector", lambda v, p=p: v._custom_dve(
                ROUND_SCALE, out=sl2b(w_q_t, p), in0=sl2f(stage_f, p),
                s0=rw[:, p:p + 1], s1=1.0, imm2=MAGIC),
                r=[kst, krw], w=[kwq])
            for j in range(8):
                jp = j % 2
                sc.c("tensor", lambda te, j=j, p=p, jp=jp: te.transpose(
                    pstb[jp][:], sl2b(w_q_t, p)[:, ts(j, 128)], ident[:]),
                    r=[kwq], w=[f"pst:{jp}"])
                sc.c("vector", lambda v, j=j, fb=fb, jp=jp: v.tensor_copy(
                    w_in_q[:, j * FF + fb * 128: j * FF + fb * 128 + 128],
                    pstb[jp][:]), r=[f"pst:{jp}"], w=["winq"])
        sc.c("vector", lambda v: v.tensor_scalar(
            evsc[:], sw[:], rsb[:, 1:2], None, ALU.mult),
            r=["sw"], w=["evsc"])

        for idx in range(64):
            db, tc = divmod(idx, 8)
            p = idx % 2
            sc.d("sync", lambda sync, db=db, tc=tc, p=p: sync.dma_start(
                out=sl2f(stage_f, p)[:, 0:512],
                in_=x1T[ts(db, 128), ts(tc, 512)]), r=[], w=[f"st:{p}"])
            sc.c("vector", lambda v, db=db, tc=tc, p=p: v._custom_dve(
                ROUND_SCALE, out=xi[:, db * T + tc * 512: db * T + tc * 512 + 512],
                in0=sl2f(stage_f, p)[:, 0:512], s0=rsb[:, 0:1], s1=1.0,
                imm2=MAGIC), r=[f"st:{p}"], w=["xi"])

        for idx in range(32 * 8):
            fb, tc = divmod(idx, 8)
            p = idx % 2
            def mmh(te, fb=fb, tc=tc, p=p):
                ins = []
                for db in range(8):
                    ins.append(te.matmul(
                        psm[p][:],
                        w_in_q[:, db * FF + fb * 128: db * FF + fb * 128 + 128],
                        xi[:, db * T + tc * 512: db * T + tc * 512 + 512],
                        start=(db == 0), stop=(db == 7)))
                return ins[-1]
            sc.c("tensor", mmh, r=["winq", "xi"], w=[f"psm:{p}"])
            sc.c("scalar", lambda s, fb=fb, p=p: s.activation(
                out=hbsl(p), in_=psm[p][:], func=AF.Gelu,
                scale=evsc[:, fb:fb + 1], bias=binb[:, fb:fb + 1]),
                r=[f"psm:{p}", "evsc"], w=[f"hb:{p}"])
            sc.c("vector", lambda v, p=p: v.reduce_max(
                aw2[:], hbsl(p), axis=AX.X,
                apply_absolute_value=True), r=[f"hb:{p}"], w=["aw2"])
            sc.c("vector", lambda v: v.tensor_tensor(
                hacc[:], hacc[:], aw2[:], ALU.max), r=["aw2", "hacc"], w=["hacc"])
            sc.d("sync", lambda sync, fb=fb, tc=tc, p=p: sync.dma_start(
                out=hidT[ts(fb, 128), ts(tc, 512)], in_=hbsl(p)),
                r=[f"hb:{p}"], w=["hidT"])

        add_amax_finalize(sc, hacc, amT, am1, am_bounce, amax_h)
        sc.run(nc)
    mybir.codegen_inst_isa_subclasses(nc)
    return nc



def build_l0_ffn_b():
    nc = bass.Bass("TRN2", target_bir_lowering=False, debug=False)
    x1T = nc.dram_tensor("x1T", [D, T], F32, kind="ExternalInput").ap()
    hidT = nc.dram_tensor("hidT", [FF, T], F32, kind="ExternalInput").ap()
    amax8 = nc.dram_tensor("amax8", [8, 1], F32, kind="ExternalInput").ap()
    w_out = nc.dram_tensor("w_out", [FF, D], F32, kind="ExternalInput").ap()
    b_out_t = nc.dram_tensor("b_out_t", [128, 8], F32, kind="ExternalInput").ap()
    x2T = nc.dram_tensor("x2T", [D, T], F32, kind="ExternalOutput").ap()
    scr2 = nc.dram_tensor("scr2", [1, 2], F32, kind="Internal").ap()

    sc = Script()
    with ExitStack() as es:
        A = lambda n, sh, dt: es.enter_context(nc.sbuf_tensor(n, sh, dt))
        PS = lambda n, sh, dt: es.enter_context(nc.psum_tensor(n, sh, dt))
        stage_f = A("stage_f", [128, 2 * 1024], F32)
        w_out_q = A("w_out_q", [128, 32 * 1024], BF16)
        hop = A("hop", [128, 32 * 512], BF16)
        hld = A("hld", [128, 2 * 512], F32)
        yr = A("yr", [128, 8 * 512], F32)
        xtile = A("xtile", [128, 2 * 512], F32)
        sq = A("sq", [128, 2 * 512], F32)
        x2t = A("x2t", [128, 2 * 512], F32)
        tprod = A("tprod", [128, 2 * 512], F32)

        def sl2f(buf, p):
            return buf[:, p * 1024:(p + 1) * 1024]

        def sl2h(buf, p):
            return buf[:, p * 512:(p + 1) * 512]
        bout = A("bout", [128, 8], F32)
        sw = A("sw", [128, 32], F32)
        cf = A("cf", [128, 32], F32)
        aw = A("aw", [128, 2], F32)
        rw = A("rw", [128, 2], F32)
        am8row = A("am8row", [1, 8], F32)
        am = A("am", [1, 1], F32)
        sc2 = A("sc2", [1, 2], F32)
        rsb = A("rsb", [128, 2], F32)
        lnrows = A("lnrows", [1, 512 * 6], F32)
        ones_col = A("ones_col", [128, 1], F32)
        ones_row1 = A("ones_row1", [1, 128], F32)
        cpool = A("cpool", [128, 1], F32)
        ps_m0 = PS("ps_m0", [128, 512], F32)
        ps_m1 = PS("ps_m1", [128, 512], F32)
        psm = [ps_m0, ps_m1]
        ps_sum = PS("ps_sum", [1, 512], F32)
        ps_sq = PS("ps_sq", [1, 512], F32)
        ps_al = PS("ps_al", [128, 512], F32)
        ps_be = PS("ps_be", [128, 512], F32)

        add_const_pool(nc, sc, cpool, [1e-5])
        sc.c("vector", lambda v: v.memset(ones_col[:], 1.0))
        sc.c("vector", lambda v: v.memset(ones_row1[:], 1.0))
        add_scalar_prep(sc, dict(am8row=am8row, am=am, sc2=sc2, rsb=rsb,
                                 scr2=scr2), amax8, 32767.0, 1e-8, "a16")
        sc.d("sync", lambda sync: sync.dma_start(out=bout[:], in_=b_out_t))

        for fb in range(32):
            p = fb % 2
            kst, kaw, krw = f"st:{p}", f"aw:{p}", f"rw:{p}"
            sc.d("sync", lambda sync, fb=fb, p=p: sync.dma_start(
                out=sl2f(stage_f, p), in_=w_out[ts(fb, 128), :]),
                r=[], w=[kst])
            sc.c("vector", lambda v, p=p: v.reduce_max(
                aw[:, p:p + 1], sl2f(stage_f, p), axis=AX.X,
                apply_absolute_value=True), r=[kst], w=[kaw])
            sc.c("vector", lambda v, p=p: v.tensor_scalar(
                aw[:, p:p + 1], aw[:, p:p + 1], 1e-8, None, ALU.max),
                r=[kaw], w=[kaw])
            sc.c("vector", lambda v, p=p: v.reciprocal(
                rw[:, p:p + 1], aw[:, p:p + 1]), r=[kaw], w=[krw])
            sc.c("vector", lambda v, fb=fb, p=p: v.tensor_scalar(
                sw[:, fb:fb + 1], aw[:, p:p + 1], 1.0 / 127.0, None, ALU.mult),
                r=[kaw], w=["sw"])
            sc.c("vector", lambda v, p=p: v.tensor_scalar(
                rw[:, p:p + 1], rw[:, p:p + 1], 127.0, None, ALU.mult),
                r=[krw], w=[krw])
            sc.c("vector", lambda v, fb=fb, p=p: v._custom_dve(
                ROUND_SCALE, out=w_out_q[:, ts(fb, 1024)], in0=sl2f(stage_f, p),
                s0=rw[:, p:p + 1], s1=1.0, imm2=MAGIC),
                r=[kst, krw], w=["woq"])
        sc.c("vector", lambda v: v.tensor_scalar(
            cf[:], sw[:], rsb[:, 1:2], None, ALU.mult), r=["sw"], w=["cf"])

        for tc in range(8):
            for fb in range(32):
                p = fb % 2
                sc.d("sync", lambda sync, fb=fb, tc=tc, p=p: sync.dma_start(
                    out=sl2h(hld, p), in_=hidT[ts(fb, 128), ts(tc, 512)]),
                    r=[], w=[f"hld:{p}"])
                sc.c("vector", lambda v, fb=fb, p=p: v._custom_dve(
                    ROUND_SCALE, out=hop[:, ts(fb, 512)], in0=sl2h(hld, p),
                    s0=rsb[:, 0:1], s1=cf[:, fb:fb + 1], imm2=MAGIC),
                    r=[f"hld:{p}", "cf"], w=[f"hop:{fb}"])
            for do in range(8):
                p = do % 2

                def mmo(te, do=do, p=p):
                    ins = []
                    for fb in range(32):
                        ins.append(te.matmul(
                            psm[p][:],
                            w_out_q[:, fb * 1024 + do * 128: fb * 1024 + do * 128 + 128],
                            hop[:, ts(fb, 512)],
                            start=(fb == 0), stop=(fb == 31)))
                    return ins[-1]
                sc.c("tensor", mmo, r=["woq"] + [f"hop:{fb}" for fb in range(32)],
                     w=[f"psm:{p}"])
                sc.d("sync", lambda sync, do=do, tc=tc, p=p: sync.dma_start(
                    out=sl2h(xtile, p), in_=x1T[ts(do, 128), ts(tc, 512)]),
                    r=[], w=[f"xt:{p}"])
                sc.c("vector", lambda v, do=do, p=p: v.scalar_tensor_tensor(
                    yr[:, ts(do, 512)], psm[p][:], bout[:, do:do + 1],
                    sl2h(xtile, p), ALU.add, ALU.add),
                    r=[f"psm:{p}", f"xt:{p}"], w=[f"yr:{do}"])
                sc.c("scalar", lambda s, do=do, p=p: s.activation(
                    out=sl2h(sq, p), in_=yr[:, ts(do, 512)], func=AF.Square),
                    r=[f"yr:{do}"], w=[f"sq:{p}"])

                def mmstat(te, do=do, p=p):
                    te.matmul(ps_sum[:], ones_col[:], yr[:, ts(do, 512)],
                              start=(do == 0), stop=(do == 7))
                    return te.matmul(ps_sq[:], ones_col[:], sl2h(sq, p),
                                     start=(do == 0), stop=(do == 7))
                sc.c("tensor", mmstat, r=[f"yr:{do}", f"sq:{p}"],
                     w=["pssum", "pssq"])
            add_ln_rows_k(sc, lnrows, ps_sum, ps_sq, ps_al, ps_be, ones_row1,
                          1.0 / D, 1e-5)
            for do in range(8):
                p = do % 2
                sc.c("vector", lambda v, do=do, p=p: v.tensor_tensor(
                    sl2h(tprod, p), yr[:, ts(do, 512)], ps_al[:], ALU.mult),
                    r=[f"yr:{do}", "psal"], w=[f"tp:{p}"])
                sc.c("vector", lambda v, p=p: v.tensor_tensor(
                    sl2h(x2t, p), sl2h(tprod, p), ps_be[:], ALU.add),
                    r=[f"tp:{p}", "psbe"], w=[f"x2t:{p}"])
                sc.d("sync", lambda sync, do=do, tc=tc, p=p: sync.dma_start(
                    out=x2T[ts(do, 128), ts(tc, 512)], in_=sl2h(x2t, p)),
                    r=[f"x2t:{p}"], w=["x2T"])
        sc.run(nc)
    mybir.codegen_inst_isa_subclasses(nc)
    return nc



def build_ssm(phases="123", dbg=False):
    nc = bass.Bass("TRN2", target_bir_lowering=False, debug=False)
    x2T = nc.dram_tensor("x2T", [D, T], F32, kind="ExternalInput").ap()
    a_log = nc.dram_tensor("a_log", [D, N16], F32, kind="ExternalInput").ap()
    w_delta = nc.dram_tensor("w_delta", [D, D], F32, kind="ExternalInput").ap()
    b_delta_t = nc.dram_tensor("b_delta_t", [128, 8], F32, kind="ExternalInput").ap()
    w_b = nc.dram_tensor("w_b", [D, N16], F32, kind="ExternalInput").ap()
    w_c = nc.dram_tensor("w_c", [D, N16], F32, kind="ExternalInput").ap()
    dskip_t = nc.dram_tensor("dskip_t", [128, 8], F32, kind="ExternalInput").ap()
    nw_t = nc.dram_tensor("nw_t", [128, 8], F32, kind="ExternalInput").ap()
    x3T = nc.dram_tensor("x3T", [D, T], F32, kind="ExternalOutput").ap()
    amax3 = nc.dram_tensor("amax3", [1, 1], F32, kind="ExternalOutput").ap()
    ik = "ExternalOutput" if dbg else "Internal"
    xn_hbm = nc.dram_tensor("xn_hbm", [D, T], F32, kind=ik).ap()
    xnb_hbm = nc.dram_tensor("xnb_hbm", [D, T], BF16, kind="Internal").ap()
    _dl_hbm = nc.dram_tensor("dl_hbm", [D, T], F32, kind=ik).ap()
    bp_hbm = nc.dram_tensor("bp_hbm", [N16, T], F32, kind=ik).ap()
    c_hbm = nc.dram_tensor("c_hbm", [N16, T], F32, kind=ik).ap()
    am_bounce = nc.dram_tensor("am_bounce", [128], F32, kind="Internal").ap()

    DT_MAX, DT_MIN = 10.0, 1e-4
    CEN = (DT_MAX + DT_MIN) / 2.0
    HALF = (DT_MAX - DT_MIN) / 2.0

    sc = Script()
    with ExitStack() as es:
        A = lambda n, sh, dt: es.enter_context(nc.sbuf_tensor(n, sh, dt))
        PS = lambda n, sh, dt: es.enter_context(nc.psum_tensor(n, sh, dt))
        stage_f = A("stage_f", [128, 1024], F32)
        wdel_b = A("wdel_b", [128, 8 * 1024], BF16)
        wb_b = A("wb_b", [128, 8 * N16], BF16)
        wc_b = A("wc_b", [128, 8 * N16], BF16)
        ahalf = A("ahalf", [128, 8 * N16], F32)
        aexp = A("aexp", [128, N16], F32)
        bdel = A("bdel", [128, 8], F32)
        dskip = A("dskip", [128, 8], F32)
        nww = A("nww", [128, 8], F32)
        xch = A("xch", [128, 8 * 512], F32)
        xnch = A("xnch", [128, 8 * 512], F32)
        dlch = A("dlch", [128, 8 * 512], F32)
        uch = A("uch", [128, 8 * 512], F32)
        ytc = A("ytc", [128, 8 * 512], F32)
        yrb = A("yrb", [128, 8 * 512], F32)
        sq = A("sq", [128, 512], F32)
        sqd = A("sqd", [128, 512], F32)
        x3t = A("x3t", [128, 512], F32)
        tprod = A("tprod", [128, 512], F32)
        xbf = A("xbf", [128, 512], BF16)
        rhs_bf = A("rhs_bf", [128, 2 * 8 * 512], BF16)
        sp1 = A("sp1", [128, 2 * 512], F32)
        sp2 = A("sp2", [128, 2 * 512], F32)
        sp3 = A("sp3", [128, 2 * 512], F32)
        dnm = A("dnm", [128, 2 * 512], F32)
        wre = A("wre", [128, 2 * 512], F32)
        are = A("are", [128, 2 * 512], F32)
        qre = A("qre", [128, 2 * 512], F32)
        bre = A("bre", [128, 2 * 512], F32)
        bp1 = A("bp1", [128, 2 * 512], F32)
        c1 = A("c1", [128, 2 * 512], F32)
        hn = A("hn", [128, 512], F32)
        hst = A("hst", [128, 128], F32)
        bprow = A("bprow", [16, 512], F32)
        lnrows = A("lnrows", [1, 512 * 6], F32)
        ones_col = A("ones_col", [128, 1], F32)
        ones_row1 = A("ones_row1", [1, 128], F32)
        aw2 = A("aw2", [128, 1], F32)
        amax_acc = A("amax_acc", [128, 1], F32)
        amT = A("amT", [1, 128], F32)
        am1 = A("am1", [1, 1], F32)
        cpool = A("cpool", [128, 6], F32)
        ps_m0 = PS("ps_m0", [128, 512], F32)
        ps_m1 = PS("ps_m1", [128, 512], F32)
        ps_bp = PS("ps_bp", [16, 512], F32)
        ps_sum = PS("ps_sum", [1, 512], F32)
        ps_sq = PS("ps_sq", [1, 512], F32)
        ps_al = PS("ps_al", [128, 512], F32)
        ps_be = PS("ps_be", [128, 512], F32)
        psm = [ps_m0, ps_m1]

        S2 = 512

        def sl(buf, p, wdt=S2):
            return buf[:, p * wdt:(p + 1) * wdt]

        add_const_pool(nc, sc, cpool, [1e-5, 1e-6, 1.0, -CEN / HALF, -1.0])
        sc.c("vector", lambda v: v.memset(ones_col[:], 1.0))
        sc.c("vector", lambda v: v.memset(ones_row1[:], 1.0))
        sc.c("vector", lambda v: v.memset(amax_acc[:], 0.0))
        sc.c("vector", lambda v: v.memset(hst[:], 0.0))
        sc.d("sync", lambda sync: [sync.dma_start(out=bdel[:], in_=b_delta_t),
                                   sync.dma_start(out=dskip[:], in_=dskip_t),
                                   sync.dma_start(out=nww[:], in_=nw_t)], n=3)
        for db in range(8):
            sc.d("sync", lambda sync, db=db: sync.dma_start(
                out=aexp[:], in_=a_log[ts(db, 128), :]))
            sc.c("scalar", lambda s: s.activation(
                out=aexp[:], in_=aexp[:], func=AF.Exp))
            sc.c("vector", lambda v, db=db: v.tensor_scalar(
                ahalf[:, ts(db, N16)], aexp[:], 0.5, None, ALU.mult))
        for db in range(8):
            sc.d("sync", lambda sync, db=db: sync.dma_start(
                out=stage_f[:], in_=w_delta[ts(db, 128), :]))
            sc.c("scalar", lambda s, db=db: s.activation(
                out=wdel_b[:, ts(db, 1024)], in_=stage_f[:], func=AF.Copy))
            sc.d("sync", lambda sync, db=db: sync.dma_start(
                out=stage_f[:, 0:N16], in_=w_b[ts(db, 128), :]))
            sc.c("scalar", lambda s, db=db: s.activation(
                out=wb_b[:, ts(db, N16)], in_=stage_f[:, 0:N16], func=AF.Copy))
            sc.d("sync", lambda sync, db=db: sync.dma_start(
                out=stage_f[:, 0:N16], in_=w_c[ts(db, 128), :]))
            sc.c("scalar", lambda s, db=db: s.activation(
                out=wc_b[:, ts(db, N16)], in_=stage_f[:, 0:N16], func=AF.Copy))

        for tc in (range(8) if "1" in phases else []):
            sc.d("sync", lambda sync, tc=tc: [sync.dma_start(
                out=xch[:, ts(db, 512)], in_=x2T[ts(db, 128), ts(tc, 512)])
                for db in range(8)], n=8)
            for db in range(8):
                sc.c("scalar", lambda s, db=db: s.activation(
                    out=sq[:], in_=xch[:, ts(db, 512)], func=AF.Square))
                sc.c("tensor", lambda te, db=db: te.matmul(
                    ps_sum[:], ones_col[:], sq[:],
                    start=(db == 0), stop=(db == 7)))
            sc.c("vector", lambda v: v.tensor_copy(lnrows[:, 0:512], ps_sum[:]))
            sc.c("vector", lambda v: v.tensor_scalar(
                lnrows[:, 512:1024], lnrows[:, 0:512], 1.0 / D, None, ALU.mult))
            sc.c("scalar", lambda s: s.activation(
                out=lnrows[:, 0:512], in_=lnrows[:, 512:1024], func=AF.Sqrt,
                bias=1e-6))
            sc.c("vector", lambda v: v.reciprocal(
                lnrows[:, 512:1024], lnrows[:, 0:512]))
            sc.c("tensor", lambda te: te.matmul(
                ps_al[:], ones_row1[:], lnrows[:, 512:1024], start=True, stop=True))
            for db in range(8):
                sc.c("vector", lambda v, db=db: v.scalar_tensor_tensor(
                    xnch[:, ts(db, 512)], xch[:, ts(db, 512)], nww[:, db:db + 1],
                    ps_al[:], ALU.mult, ALU.mult))
                sc.c("scalar", lambda s, db=db: s.activation(
                    out=xbf[:], in_=xnch[:, ts(db, 512)], func=AF.Copy))
                sc.d("sync", lambda sync, db=db, tc=tc: [
                    sync.dma_start(out=xn_hbm[ts(db, 128), ts(tc, 512)],
                                   in_=xnch[:, ts(db, 512)]),
                    sync.dma_start(out=xnb_hbm[ts(db, 128), ts(tc, 512)],
                                   in_=xbf[:])], n=2)

        DLH = lambda tc: f"dlh:{tc}"

        def emit_p2(tc):
            p2 = tc % 2
            rkey = f"rhs:{p2}"
            sc.d("sync", lambda sync, tc=tc, p2=p2: [sync.dma_start(
                out=rhs_bf[:, (p2 * 8 + db) * 512:(p2 * 8 + db) * 512 + 512],
                in_=xnb_hbm[ts(db, 128), ts(tc, 512)]) for db in range(8)],
                n=8, r=["xnbh"], w=[rkey])
            for ob in range(8):
                obp = ob % 2
                pk = f"psm:{obp}"
                k1, k2, k3 = f"sp1:{obp}", f"sp2:{obp}", f"sp3:{obp}"

                def mmd(te, ob=ob, p2=p2, obp=obp):
                    ins = []
                    for db in range(8):
                        ins.append(te.matmul(
                            psm[obp][:],
                            wdel_b[:, db * 1024 + ob * 128: db * 1024 + ob * 128 + 128],
                            rhs_bf[:, (p2 * 8 + db) * 512:(p2 * 8 + db) * 512 + 512],
                            start=(db == 0), stop=(db == 7)))
                    return ins[-1]
                sc.c("tensor", mmd, r=[rkey], w=[pk])
                sc.c("vector", lambda v, ob=ob, obp=obp: v.tensor_scalar(
                    sl(sp1, obp), psm[obp][:], bdel[:, ob:ob + 1], None, ALU.add),
                    r=[pk], w=[k1])
                sc.c("scalar", lambda s, obp=obp: s.activation(
                    out=sl(sp2, obp), in_=sl(sp1, obp), func=AF.Abs),
                    r=[k1], w=[k2])
                sc.c("scalar", lambda s, obp=obp: s.activation(
                    out=sl(sp2, obp), in_=sl(sp2, obp), func=AF.Exp, scale=-1.0),
                    r=[k2], w=[k2])
                sc.c("scalar", lambda s, obp=obp: s.activation(
                    out=sl(sp2, obp), in_=sl(sp2, obp), func=AF.Ln, bias=1.0),
                    r=[k2], w=[k2])
                sc.c("vector", lambda v, obp=obp: v.tensor_scalar(
                    sl(sp3, obp), sl(sp1, obp), 0.0, None, ALU.max),
                    r=[k1], w=[k3])
                sc.c("vector", lambda v, obp=obp: v.tensor_tensor(
                    sl(sp2, obp), sl(sp2, obp), sl(sp3, obp), ALU.add),
                    r=[k2, k3], w=[k2])
                sc.c("scalar", lambda s, obp=obp: s.activation(
                    out=sl(sp3, obp), in_=sl(sp2, obp), func=AF.Tanh,
                    scale=1.0 / HALF, bias=-CEN / HALF),
                    r=[k2], w=[k3])
                sc.c("vector", lambda v, obp=obp: v.tensor_scalar(
                    sl(sp3, obp), sl(sp3, obp), HALF, CEN, ALU.mult, ALU.add),
                    r=[k3], w=[k3])
                sc.d("sync", lambda sync, ob=ob, tc=tc, obp=obp: sync.dma_start(
                    out=_dl_hbm[ts(ob, 128), ts(tc, 512)], in_=sl(sp3, obp)),
                    r=[k3], w=[DLH(tc)])
            for wsb, hbm in ((wb_b, bp_hbm), (wc_b, c_hbm)):
                def mmb(te, wsb=wsb, p2=p2):
                    ins = []
                    for db in range(8):
                        ins.append(te.matmul(
                            ps_bp[:], wsb[:, ts(db, N16)],
                            rhs_bf[:, (p2 * 8 + db) * 512:(p2 * 8 + db) * 512 + 512],
                            start=(db == 0), stop=(db == 7)))
                    return ins[-1]
                sc.c("tensor", mmb, r=[rkey], w=["psbp"])
                sc.c("vector", lambda v: v.tensor_copy(bprow[:], ps_bp[:]),
                     r=["psbp"], w=["bprow"])
                sc.d("sync", lambda sync, hbm=hbm, tc=tc: sync.dma_start(
                    out=hbm[:, ts(tc, 512)], in_=bprow[:]),
                    r=["bprow"], w=[f"bpch:{tc}"])

        def emit_p3(tc):
            sc.d("sync", lambda sync, tc=tc: [sync.dma_start(
                out=dlch[:, ts(db, 512)], in_=_dl_hbm[ts(db, 128), ts(tc, 512)])
                for db in range(8)], n=8, r=[DLH(tc)], w=["dlch"])
            sc.d("sync", lambda sync, tc=tc: [sync.dma_start(
                out=xnch[:, ts(db, 512)], in_=xn_hbm[ts(db, 128), ts(tc, 512)])
                for db in range(8)], n=8, r=["xnh"], w=["xnch"])
            for db in range(8):
                sc.c("scalar", lambda s, db=db: s.activation(
                    out=sqd[:], in_=dlch[:, ts(db, 512)], func=AF.Sqrt),
                    r=["dlch"], w=["sqd"])
                sc.c("vector", lambda v, db=db: v.scalar_tensor_tensor(
                    uch[:, ts(db, 512)], sqd[:], 0.5, xnch[:, ts(db, 512)],
                    ALU.mult, ALU.mult), r=["sqd", "xnch"], w=["uch"])

            def emit_bcast(n):
                np_ = n % 2
                sc.d("sync", lambda sync, n=n, np_=np_: [
                    sync.dma_start(out=sl(bp1, np_),
                                   in_=bp_hbm[n:n + 1, ts(tc, 512)].broadcast_to([128, 512])),
                    sync.dma_start(out=sl(c1, np_),
                                   in_=c_hbm[n:n + 1, ts(tc, 512)].broadcast_to([128, 512]))],
                    n=2, r=[f"bpch:{tc}"], w=[f"bp:{np_}", f"c:{np_}"])

            def emit_denom(i):
                n, db = divmod(i, 8)
                ip = i % 2
                sc.c("scalar", lambda s, n=n, db=db, ip=ip: s.activation(
                    out=sl(dnm, ip), in_=dlch[:, ts(db, 512)], func=AF.Identity,
                    scale=ahalf[:, db * N16 + n: db * N16 + n + 1], bias=1.0),
                    r=["dlch"], w=[f"dnm:{ip}"])

            emit_bcast(0)
            emit_bcast(1)
            emit_denom(0)
            emit_denom(1)
            for n in range(N16):
                nb = n % 2
                for db in range(8):
                    i = n * 8 + db
                    ip = i % 2
                    kd, kw, ka, kq, kb = (f"dnm:{ip}", f"w:{ip}", f"a:{ip}",
                                          f"q:{ip}", f"b:{ip}")
                    sc.c("vector", lambda v, ip=ip: v.reciprocal_approx_fast(
                        out=sl(wre, ip), in_=sl(dnm, ip)), r=[kd], w=[kw])
                    sc.c("vector", lambda v, db=db, ip=ip, nb=nb: v.tensor_tensor(
                        sl(qre, ip), uch[:, ts(db, 512)], sl(bp1, nb), ALU.mult),
                        r=["uch", f"bp:{nb}"], w=[kq])
                    sc.c("scalar", lambda s, ip=ip: s.activation(
                        out=sl(are, ip), in_=sl(wre, ip), func=AF.Identity,
                        scale=2.0, bias=-1.0), r=[kw], w=[ka])
                    if i + 2 < N16 * 8:
                        emit_denom(i + 2)
                    sc.c("vector", lambda v, ip=ip: v.scalar_tensor_tensor(
                        sl(bre, ip), sl(are, ip), 1.0, sl(qre, ip),
                        ALU.add, ALU.mult), r=[ka, kq], w=[kb])
                    sc.c("vector", lambda v, db=db, n=n, ip=ip: v.tensor_tensor_scan(
                        hn[:], sl(are, ip), sl(bre, ip),
                        hst[:, db * N16 + n: db * N16 + n + 1], ALU.mult, ALU.add),
                        r=[ka, kb, "hst"], w=["hn"])
                    sc.c("vector", lambda v, db=db, n=n: v.tensor_copy(
                        hst[:, db * N16 + n: db * N16 + n + 1], hn[:, 511:512]),
                        r=["hn"], w=["hst"])
                    sc.c("vector", lambda v, ip=ip, nb=nb: v.tensor_tensor(
                        sl(qre, ip), hn[:], sl(c1, nb), ALU.mult),
                        r=["hn", f"c:{nb}"], w=[kq])
                    if n == 0:
                        sc.c("vector", lambda v, db=db, ip=ip: v.tensor_copy(
                            ytc[:, ts(db, 512)], sl(qre, ip)),
                            r=[kq], w=[f"yt:{db}"])
                    else:
                        sc.c("vector", lambda v, db=db, ip=ip: v.tensor_tensor(
                            ytc[:, ts(db, 512)], ytc[:, ts(db, 512)], sl(qre, ip),
                            ALU.add), r=[kq, f"yt:{db}"], w=[f"yt:{db}"])
                if n + 2 < N16:
                    emit_bcast(n + 2)
            sc.d("sync", lambda sync, tc=tc: [sync.dma_start(
                out=xch[:, ts(db, 512)], in_=x2T[ts(db, 128), ts(tc, 512)])
                for db in range(8)], n=8, r=["xch2"], w=["xch2"])
            for db in range(8):
                sc.c("vector", lambda v, db=db: v.scalar_tensor_tensor(
                    tprod[:], xnch[:, ts(db, 512)], dskip[:, db:db + 1],
                    ytc[:, ts(db, 512)], ALU.mult, ALU.add),
                    r=["xnch", f"yt:{db}"], w=["tp"])
                sc.c("vector", lambda v, db=db: v.tensor_tensor(
                    yrb[:, ts(db, 512)], tprod[:], xch[:, ts(db, 512)], ALU.add),
                    r=["tp", "xch2"], w=[f"yr:{db}"])
                sc.c("scalar", lambda s, db=db: s.activation(
                    out=sq[:], in_=yrb[:, ts(db, 512)], func=AF.Square),
                    r=[f"yr:{db}"], w=["sq1"])

                def mmstat(te, db=db):
                    te.matmul(ps_sum[:], ones_col[:], yrb[:, ts(db, 512)],
                              start=(db == 0), stop=(db == 7))
                    return te.matmul(ps_sq[:], ones_col[:], sq[:],
                                     start=(db == 0), stop=(db == 7))
                sc.c("tensor", mmstat, r=[f"yr:{db}", "sq1"], w=["pssum", "pssq"])
            add_ln_rows_k(sc, lnrows, ps_sum, ps_sq, ps_al, ps_be, ones_row1,
                          1.0 / D, 1e-5)
            for db in range(8):
                sc.c("vector", lambda v, db=db: v.tensor_tensor(
                    tprod[:], yrb[:, ts(db, 512)], ps_al[:], ALU.mult),
                    r=[f"yr:{db}", "psal"], w=["tp"])
                sc.c("vector", lambda v: v.tensor_tensor(
                    x3t[:], tprod[:], ps_be[:], ALU.add),
                    r=["tp", "psbe"], w=["x3t"])
                sc.c("vector", lambda v: v.reduce_max(
                    aw2[:], x3t[:], axis=AX.X, apply_absolute_value=True),
                    r=["x3t"], w=["aw2"])
                sc.c("vector", lambda v: v.tensor_tensor(
                    amax_acc[:], amax_acc[:], aw2[:], ALU.max),
                    r=["aw2", "amacc"], w=["amacc"])
                sc.d("sync", lambda sync, db=db, tc=tc: sync.dma_start(
                    out=x3T[ts(db, 128), ts(tc, 512)], in_=x3t[:]),
                    r=["x3t"], w=["x3T"])

        if "3" in phases:
            emit_p2(0)
            for tc in range(8):
                if tc + 1 < 8:
                    emit_p2(tc + 1)
                emit_p3(tc)

        add_amax_finalize(sc, amax_acc, amT, am1, am_bounce, amax3)
        sc.run(nc)
    mybir.codegen_inst_isa_subclasses(nc)
    return nc



def build_l1_ffn_a():
    nc = bass.Bass("TRN2", target_bir_lowering=False, debug=False)
    x3T = nc.dram_tensor("x3T", [D, T], F32, kind="ExternalInput").ap()
    amax8 = nc.dram_tensor("amax8", [8, 1], F32, kind="ExternalInput").ap()
    w_up = nc.dram_tensor("w_up", [I3, D], F32, kind="ExternalInput").ap()
    w_gate = nc.dram_tensor("w_gate", [I3, D], F32, kind="ExternalInput").ap()
    upT = nc.dram_tensor("upT", [I3, T], F32, kind="ExternalOutput").ap()
    gateT = nc.dram_tensor("gateT", [I3, T], F32, kind="ExternalOutput").ap()
    amax_u = nc.dram_tensor("amax_u", [1, 1], F32, kind="ExternalOutput").ap()
    amax_g = nc.dram_tensor("amax_g", [1, 1], F32, kind="ExternalOutput").ap()
    scr2 = nc.dram_tensor("scr2", [1, 2], F32, kind="Internal").ap()
    am_bounce = nc.dram_tensor("am_bounce", [128], F32, kind="Internal").ap()

    sc = Script()
    with ExitStack() as es:
        A = lambda n, sh, dt: es.enter_context(nc.sbuf_tensor(n, sh, dt))
        PS = lambda n, sh, dt: es.enter_context(nc.psum_tensor(n, sh, dt))
        stage_f = A("stage_f", [128, 2 * 1024], F32)
        wq_t = A("wq_t", [128, 2 * 1024], BF16)
        wuq = A("wuq", [128, 8 * I3], BF16)
        wgq = A("wgq", [128, 8 * I3], BF16)
        xq = A("xq", [128, 8 * T], BF16)
        ot = A("ot", [128, 2 * 512], F32)
        awb = A("awb", [128, 2 * 32], F32)
        sbv = A("sbv", [128, 2 * 32], F32)
        rbv = A("rbv", [128, 2 * 32], F32)

        def sl5(buf, p):
            return buf[:, p * 1024:(p + 1) * 1024]

        def sl5b(buf, p):
            return buf[:, p * 1024:(p + 1) * 1024]

        def sl5h(buf, p):
            return buf[:, p * 512:(p + 1) * 512]

        def sl32(buf, p):
            return buf[:, p * 32:(p + 1) * 32]
        aw2 = A("aw2", [128, 1], F32)
        acc_u = A("acc_u", [128, 1], F32)
        acc_g = A("acc_g", [128, 1], F32)
        am8row = A("am8row", [1, 8], F32)
        am = A("am", [1, 1], F32)
        sc2 = A("sc2", [1, 2], F32)
        rsb = A("rsb", [128, 2], F32)
        amT = A("amT", [1, 128], F32)
        am1 = A("am1", [1, 1], F32)
        ident = A("ident", [128, 128], BF16)
        ps_m0 = PS("ps_m0", [128, 512], F32)
        ps_m1 = PS("ps_m1", [128, 512], F32)
        ps_tb0 = PS("ps_tb0", [128, 128], BF16)
        ps_tb1 = PS("ps_tb1", [128, 128], BF16)
        psm = [ps_m0, ps_m1]
        pstb = [ps_tb0, ps_tb1]

        add_identity(sc, ident)
        sc.c("vector", lambda v: v.memset(acc_u[:], 0.0))
        sc.c("vector", lambda v: v.memset(acc_g[:], 0.0))
        add_scalar_prep(sc, dict(am8row=am8row, am=am, sc2=sc2, rsb=rsb,
                                 scr2=scr2), amax8, 32767.0, 1e-12, "qdq")

        for idx in range(64):
            db, tc = divmod(idx, 8)
            p = idx % 2
            sc.d("sync", lambda sync, db=db, tc=tc, p=p: sync.dma_start(
                out=sl5(stage_f, p)[:, 0:512],
                in_=x3T[ts(db, 128), ts(tc, 512)]), r=[], w=[f"st:{p}"])
            sc.c("vector", lambda v, db=db, tc=tc, p=p: v._custom_dve(
                ROUND_SCALE, out=xq[:, db * T + tc * 512: db * T + tc * 512 + 512],
                in0=sl5(stage_f, p)[:, 0:512], s0=rsb[:, 0:1], s1=1.0,
                imm2=MAGIC), r=[f"st:{p}"], w=["xq"])

        for wi, (w_hbm, wq_dst) in enumerate(((w_up, wuq), (w_gate, wgq))):
            for rb in range(24):
                p = (wi * 24 + rb) % 2
                kst, kaw, kwq = f"st:{p}", f"aw:{p}", f"wq:{p}"
                sc.d("sync", lambda sync, rb=rb, p=p, w_hbm=w_hbm: sync.dma_start(
                    out=sl5(stage_f, p), in_=w_hbm[ts(rb, 128), :]),
                    r=[], w=[kst])
                sc.c("vector", lambda v, p=p: v.tensor_reduce(
                    sl32(awb, p), sl5(stage_f, p).rearrange("p (a b) -> p a b", b=32),
                    axis=AX.X, op=ALU.max, apply_absolute_value=True),
                    r=[kst], w=[kaw])
                sc.c("vector", lambda v, p=p: v.tensor_scalar(
                    sl32(sbv, p), sl32(awb, p), 1.0 / 7.0, 1e-12, ALU.mult,
                    ALU.max), r=[kaw], w=[kaw + "s"])
                sc.c("vector", lambda v, p=p: v.reciprocal(
                    sl32(rbv, p), sl32(sbv, p)), r=[kaw + "s"], w=[kaw + "r"])

                def rounds(v, p=p):
                    ins = []
                    for bk in range(32):
                        ins.append(v._custom_dve(
                            ROUND_SCALE, out=sl5b(wq_t, p)[:, ts(bk, 32)],
                            in0=sl5(stage_f, p)[:, ts(bk, 32)],
                            s0=sl32(rbv, p)[:, bk:bk + 1],
                            s1=sl32(sbv, p)[:, bk:bk + 1], imm2=MAGIC))
                    return ins[-1]
                sc.c("vector", rounds, r=[kst, kaw + "s", kaw + "r"], w=[kwq])
                for j in range(8):
                    jp = j % 2
                    sc.c("tensor", lambda te, j=j, p=p, jp=jp: te.transpose(
                        pstb[jp][:], sl5b(wq_t, p)[:, ts(j, 128)], ident[:]),
                        r=[kwq], w=[f"pst:{jp}"])
                    sc.c("vector", lambda v, j=j, rb=rb, jp=jp, wq_dst=wq_dst:
                         v.tensor_copy(
                             wq_dst[:, j * I3 + rb * 128: j * I3 + rb * 128 + 128],
                             pstb[jp][:]), r=[f"pst:{jp}"], w=["wq_res"])

        for wi, (wq_src, out_hbm, acck) in enumerate(((wuq, upT, "accu"),
                                                      (wgq, gateT, "accg"))):
            accb = acc_u if wi == 0 else acc_g
            for idx in range(24 * 8):
                ib, tc = divmod(idx, 8)
                p = idx % 2

                def mmu(te, wq_src=wq_src, ib=ib, tc=tc, p=p):
                    ins = []
                    for db in range(8):
                        ins.append(te.matmul(
                            psm[p][:],
                            wq_src[:, db * I3 + ib * 128: db * I3 + ib * 128 + 128],
                            xq[:, db * T + tc * 512: db * T + tc * 512 + 512],
                            start=(db == 0), stop=(db == 7)))
                    return ins[-1]
                sc.c("tensor", mmu, r=["wq_res", "xq"], w=[f"psm:{p}"])
                sc.c("scalar", lambda s, p=p: s.activation(
                    out=sl5h(ot, p), in_=psm[p][:], func=AF.Copy,
                    scale=rsb[:, 1:2]), r=[f"psm:{p}"], w=[f"ot:{p}"])
                sc.c("vector", lambda v, p=p: v.reduce_max(
                    aw2[:], sl5h(ot, p), axis=AX.X, apply_absolute_value=True),
                    r=[f"ot:{p}"], w=["aw2"])
                sc.c("vector", lambda v, accb=accb, acck=acck: v.tensor_tensor(
                    accb[:], accb[:], aw2[:], ALU.max),
                    r=["aw2", acck], w=[acck])
                sc.d("sync", lambda sync, out_hbm=out_hbm, ib=ib, tc=tc, p=p:
                     sync.dma_start(out=out_hbm[ts(ib, 128), ts(tc, 512)],
                                    in_=sl5h(ot, p)), r=[f"ot:{p}"], w=["outh"])

        add_amax_finalize(sc, acc_u, amT, am1, am_bounce, amax_u)
        add_amax_finalize(sc, acc_g, amT, am1, am_bounce, amax_g)
        sc.run(nc)
    mybir.codegen_inst_isa_subclasses(nc)
    return nc



def build_l1_ffn_b():
    nc = bass.Bass("TRN2", target_bir_lowering=False, debug=False)
    gateT = nc.dram_tensor("gateT", [I3, T], F32, kind="ExternalInput").ap()
    amax8 = nc.dram_tensor("amax8", [8, 1], F32, kind="ExternalInput").ap()
    mT = nc.dram_tensor("mT", [I3, T], F32, kind="ExternalOutput").ap()
    amax_m = nc.dram_tensor("amax_m", [1, 1], F32, kind="ExternalOutput").ap()
    scr2 = nc.dram_tensor("scr2", [1, 2], F32, kind="Internal").ap()
    am_bounce = nc.dram_tensor("am_bounce", [128], F32, kind="Internal").ap()

    sc = Script()
    with ExitStack() as es:
        A = lambda n, sh, dt: es.enter_context(nc.sbuf_tensor(n, sh, dt))
        gt = A("gt", [128, 2 * 512], F32)
        gq = A("gq", [128, 2 * 512], F32)
        sig = A("sig", [128, 2 * 512], F32)
        si2 = A("si2", [128, 2 * 512], F32)
        mm = A("mm_", [128, 2 * 512], F32)

        def sl2(buf, p):
            return buf[:, p * 512:(p + 1) * 512]
        aw2 = A("aw2", [128, 1], F32)
        macc = A("macc", [128, 1], F32)
        am8row = A("am8row", [1, 8], F32)
        am = A("am", [1, 1], F32)
        sc2 = A("sc2", [1, 2], F32)
        rsb = A("rsb", [128, 2], F32)
        amT = A("amT", [1, 128], F32)
        am1 = A("am1", [1, 1], F32)

        sc.c("vector", lambda v: v.memset(macc[:], 0.0))
        add_scalar_prep(sc, dict(am8row=am8row, am=am, sc2=sc2, rsb=rsb,
                                 scr2=scr2), amax8, 32767.0, 1e-12, "qdq")
        for idx in range(24 * 8):
            ib, tc = divmod(idx, 8)
            p = idx % 2
            kg, kq, ks, km = f"gt:{p}", f"gq:{p}", f"si:{p}", f"mm:{p}"
            sc.d("sync", lambda sync, ib=ib, tc=tc, p=p: sync.dma_start(
                out=sl2(gt, p), in_=gateT[ts(ib, 128), ts(tc, 512)]),
                r=[], w=[kg])
            sc.c("vector", lambda v, p=p: v._custom_dve(
                ROUND_SCALE, out=sl2(gq, p), in0=sl2(gt, p), s0=rsb[:, 0:1],
                s1=rsb[:, 1:2], imm2=MAGIC), r=[kg], w=[kq])
            sc.c("scalar", lambda s, p=p: s.activation(
                out=sl2(sig, p), in_=sl2(gq, p), func=AF.Sigmoid),
                r=[kq], w=[ks])
            sc.c("vector", lambda v, p=p: v._custom_dve(
                ROUND_SCALE, out=sl2(si2, p), in0=sl2(sig, p), s0=65536.0,
                s1=1.0, imm2=MAGIC), r=[ks], w=[ks + "b"])
            sc.c("vector", lambda v, p=p: v.tensor_scalar(
                sl2(si2, p), sl2(si2, p), 65535.0, 1.0 / 65536.0, ALU.min,
                ALU.mult), r=[ks + "b"], w=[ks + "b"])
            sc.c("vector", lambda v, p=p: v.tensor_tensor(
                sl2(mm, p), sl2(gq, p), sl2(si2, p), ALU.mult),
                r=[kq, ks + "b"], w=[km])
            sc.c("vector", lambda v, p=p: v.reduce_max(
                aw2[:], sl2(mm, p), axis=AX.X, apply_absolute_value=True),
                r=[km], w=["aw2"])
            sc.c("vector", lambda v: v.tensor_tensor(
                macc[:], macc[:], aw2[:], ALU.max), r=["aw2", "macc"], w=["macc"])
            sc.d("sync", lambda sync, ib=ib, tc=tc, p=p: sync.dma_start(
                out=mT[ts(ib, 128), ts(tc, 512)], in_=sl2(mm, p)),
                r=[km], w=["mT"])
        add_amax_finalize(sc, macc, amT, am1, am_bounce, amax_m)
        sc.run(nc)
    mybir.codegen_inst_isa_subclasses(nc)
    return nc



def build_l1_ffn_c():
    nc = bass.Bass("TRN2", target_bir_lowering=False, debug=False)
    upT = nc.dram_tensor("upT", [I3, T], F32, kind="ExternalInput").ap()
    mT = nc.dram_tensor("mT", [I3, T], F32, kind="ExternalInput").ap()
    amax8u = nc.dram_tensor("amax8u", [8, 1], F32, kind="ExternalInput").ap()
    amax8m = nc.dram_tensor("amax8m", [8, 1], F32, kind="ExternalInput").ap()
    oT = nc.dram_tensor("oT", [I3, T], F32, kind="ExternalOutput").ap()
    amax_o = nc.dram_tensor("amax_o", [1, 1], F32, kind="ExternalOutput").ap()
    scr2 = nc.dram_tensor("scr2", [1, 2], F32, kind="Internal").ap()
    scr2b = nc.dram_tensor("scr2b", [1, 2], F32, kind="Internal").ap()
    am_bounce = nc.dram_tensor("am_bounce", [128], F32, kind="Internal").ap()

    sc = Script()
    with ExitStack() as es:
        A = lambda n, sh, dt: es.enter_context(nc.sbuf_tensor(n, sh, dt))
        ut = A("ut", [128, 2 * 512], F32)
        mt = A("mt", [128, 2 * 512], F32)
        uq = A("uq", [128, 2 * 512], F32)
        mq = A("mq", [128, 2 * 512], F32)
        oo = A("oo", [128, 2 * 512], F32)

        def sl2(buf, p):
            return buf[:, p * 512:(p + 1) * 512]
        aw2 = A("aw2", [128, 1], F32)
        oacc = A("oacc", [128, 1], F32)
        am8row = A("am8row", [1, 8], F32)
        am = A("am", [1, 1], F32)
        sc2 = A("sc2", [1, 2], F32)
        rsbu = A("rsbu", [128, 2], F32)
        rsbm = A("rsbm", [128, 2], F32)
        amT = A("amT", [1, 128], F32)
        am1 = A("am1", [1, 1], F32)

        sc.c("vector", lambda v: v.memset(oacc[:], 0.0))
        add_scalar_prep(sc, dict(am8row=am8row, am=am, sc2=sc2, rsb=rsbu,
                                 scr2=scr2), amax8u, 32767.0, 1e-12, "qdq")
        add_scalar_prep(sc, dict(am8row=am8row, am=am, sc2=sc2, rsb=rsbm,
                                 scr2=scr2b), amax8m, 32767.0, 1e-12, "qdq")
        for idx in range(24 * 8):
            ib, tc = divmod(idx, 8)
            p = idx % 2
            ku, km_, kuq, kmq, ko = (f"ut:{p}", f"mt:{p}", f"uq:{p}",
                                     f"mq:{p}", f"oo:{p}")
            sc.d("sync", lambda sync, ib=ib, tc=tc, p=p: [
                sync.dma_start(out=sl2(ut, p), in_=upT[ts(ib, 128), ts(tc, 512)]),
                sync.dma_start(out=sl2(mt, p), in_=mT[ts(ib, 128), ts(tc, 512)])],
                n=2, r=[], w=[ku, km_])
            sc.c("vector", lambda v, p=p: v._custom_dve(
                ROUND_SCALE, out=sl2(uq, p), in0=sl2(ut, p), s0=rsbu[:, 0:1],
                s1=rsbu[:, 1:2], imm2=MAGIC), r=[ku], w=[kuq])
            sc.c("vector", lambda v, p=p: v._custom_dve(
                ROUND_SCALE, out=sl2(mq, p), in0=sl2(mt, p), s0=rsbm[:, 0:1],
                s1=rsbm[:, 1:2], imm2=MAGIC), r=[km_], w=[kmq])
            sc.c("vector", lambda v, p=p: v.tensor_tensor(
                sl2(oo, p), sl2(uq, p), sl2(mq, p), ALU.mult),
                r=[kuq, kmq], w=[ko])
            sc.c("vector", lambda v, p=p: v.reduce_max(
                aw2[:], sl2(oo, p), axis=AX.X, apply_absolute_value=True),
                r=[ko], w=["aw2"])
            sc.c("vector", lambda v: v.tensor_tensor(
                oacc[:], oacc[:], aw2[:], ALU.max), r=["aw2", "oacc"], w=["oacc"])
            sc.d("sync", lambda sync, ib=ib, tc=tc, p=p: sync.dma_start(
                out=oT[ts(ib, 128), ts(tc, 512)], in_=sl2(oo, p)),
                r=[ko], w=["oT"])
        add_amax_finalize(sc, oacc, amT, am1, am_bounce, amax_o)
        sc.run(nc)
    mybir.codegen_inst_isa_subclasses(nc)
    return nc



def build_l1_ffn_d():
    nc = bass.Bass("TRN2", target_bir_lowering=False, debug=False)
    x3T = nc.dram_tensor("x3T", [D, T], F32, kind="ExternalInput").ap()
    oT = nc.dram_tensor("oT", [I3, T], F32, kind="ExternalInput").ap()
    amax8 = nc.dram_tensor("amax8", [8, 1], F32, kind="ExternalInput").ap()
    w_down = nc.dram_tensor("w_down", [D, I3], F32, kind="ExternalInput").ap()
    x4T = nc.dram_tensor("x4T", [D, T], F32, kind="ExternalOutput").ap()
    scr2 = nc.dram_tensor("scr2", [1, 2], F32, kind="Internal").ap()

    sc = Script()
    with ExitStack() as es:
        A = lambda n, sh, dt: es.enter_context(nc.sbuf_tensor(n, sh, dt))
        PS = lambda n, sh, dt: es.enter_context(nc.psum_tensor(n, sh, dt))
        stage_f = A("stage_f", [128, 2 * 3072], F32)
        wq_t = A("wq_t", [128, 2 * 3072], BF16)
        wdq = A("wdq", [128, 24 * 1024], BF16)
        oqc = A("oqc", [128, 24 * 512], BF16)
        old = A("old", [128, 2 * 512], F32)
        yr = A("yr", [128, 8 * 512], F32)
        xtile = A("xtile", [128, 2 * 512], F32)
        sq = A("sq", [128, 2 * 512], F32)
        x4t = A("x4t", [128, 2 * 512], F32)
        tprod = A("tprod", [128, 2 * 512], F32)
        awb = A("awb", [128, 2 * 96], F32)
        sbv = A("sbv", [128, 2 * 96], F32)
        rbv = A("rbv", [128, 2 * 96], F32)

        def sl8s(buf, p):
            return buf[:, p * 3072:(p + 1) * 3072]

        def sl8h(buf, p):
            return buf[:, p * 512:(p + 1) * 512]

        def sl96(buf, p):
            return buf[:, p * 96:(p + 1) * 96]
        am8row = A("am8row", [1, 8], F32)
        am = A("am", [1, 1], F32)
        sc2 = A("sc2", [1, 2], F32)
        rsb = A("rsb", [128, 2], F32)
        lnrows = A("lnrows", [1, 512 * 6], F32)
        ones_col = A("ones_col", [128, 1], F32)
        ones_row1 = A("ones_row1", [1, 128], F32)
        cpool = A("cpool", [128, 1], F32)
        ident = A("ident", [128, 128], BF16)
        ps_m0 = PS("ps_m0", [128, 512], F32)
        ps_m1 = PS("ps_m1", [128, 512], F32)
        ps_tb0 = PS("ps_tb0", [128, 128], BF16)
        ps_tb1 = PS("ps_tb1", [128, 128], BF16)
        psm = [ps_m0, ps_m1]
        pstb = [ps_tb0, ps_tb1]
        ps_sum = PS("ps_sum", [1, 512], F32)
        ps_sq = PS("ps_sq", [1, 512], F32)
        ps_al = PS("ps_al", [128, 512], F32)
        ps_be = PS("ps_be", [128, 512], F32)

        add_const_pool(nc, sc, cpool, [1e-5])
        add_identity(sc, ident)
        sc.c("vector", lambda v: v.memset(ones_col[:], 1.0))
        sc.c("vector", lambda v: v.memset(ones_row1[:], 1.0))
        add_scalar_prep(sc, dict(am8row=am8row, am=am, sc2=sc2, rsb=rsb,
                                 scr2=scr2), amax8, 32767.0, 1e-12, "qdq")

        for rb in range(8):
            p = rb % 2
            kst, kaw, kwq = f"st:{p}", f"aw:{p}", f"wq:{p}"
            sc.d("sync", lambda sync, rb=rb, p=p: sync.dma_start(
                out=sl8s(stage_f, p), in_=w_down[ts(rb, 128), :]),
                r=[], w=[kst])
            sc.c("vector", lambda v, p=p: v.tensor_reduce(
                sl96(awb, p), sl8s(stage_f, p).rearrange("p (a b) -> p a b", b=32),
                axis=AX.X, op=ALU.max, apply_absolute_value=True),
                r=[kst], w=[kaw])
            sc.c("vector", lambda v, p=p: v.tensor_scalar(
                sl96(sbv, p), sl96(awb, p), 1.0 / 7.0, 1e-12, ALU.mult, ALU.max),
                r=[kaw], w=[kaw + "s"])
            sc.c("vector", lambda v, p=p: v.reciprocal(
                sl96(rbv, p), sl96(sbv, p)), r=[kaw + "s"], w=[kaw + "r"])

            def rounds(v, p=p):
                ins = []
                for bk in range(96):
                    ins.append(v._custom_dve(
                        ROUND_SCALE, out=sl8s(wq_t, p)[:, ts(bk, 32)],
                        in0=sl8s(stage_f, p)[:, ts(bk, 32)],
                        s0=sl96(rbv, p)[:, bk:bk + 1],
                        s1=sl96(sbv, p)[:, bk:bk + 1], imm2=MAGIC))
                return ins[-1]
            sc.c("vector", rounds, r=[kst, kaw + "s", kaw + "r"], w=[kwq])
            for j in range(24):
                jp = j % 2
                sc.c("tensor", lambda te, j=j, p=p, jp=jp: te.transpose(
                    pstb[jp][:], sl8s(wq_t, p)[:, ts(j, 128)], ident[:]),
                    r=[kwq], w=[f"pst:{jp}"])
                sc.c("vector", lambda v, j=j, rb=rb, jp=jp: v.tensor_copy(
                    wdq[:, j * 1024 + rb * 128: j * 1024 + rb * 128 + 128],
                    pstb[jp][:]), r=[f"pst:{jp}"], w=["wdq"])

        for tc in range(8):
            for ib in range(24):
                p = ib % 2
                sc.d("sync", lambda sync, ib=ib, tc=tc, p=p: sync.dma_start(
                    out=sl8h(old, p), in_=oT[ts(ib, 128), ts(tc, 512)]),
                    r=[], w=[f"old:{p}"])
                sc.c("vector", lambda v, ib=ib, p=p: v._custom_dve(
                    ROUND_SCALE, out=oqc[:, ts(ib, 512)], in0=sl8h(old, p),
                    s0=rsb[:, 0:1], s1=rsb[:, 1:2], imm2=MAGIC),
                    r=[f"old:{p}"], w=[f"oq:{ib}"])
            for do in range(8):
                p = do % 2

                def mmo(te, do=do, p=p):
                    ins = []
                    for ib in range(24):
                        ins.append(te.matmul(
                            psm[p][:],
                            wdq[:, ib * 1024 + do * 128: ib * 1024 + do * 128 + 128],
                            oqc[:, ts(ib, 512)],
                            start=(ib == 0), stop=(ib == 23)))
                    return ins[-1]
                sc.c("tensor", mmo, r=["wdq"] + [f"oq:{ib}" for ib in range(24)],
                     w=[f"psm:{p}"])
                sc.d("sync", lambda sync, do=do, tc=tc, p=p: sync.dma_start(
                    out=sl8h(xtile, p), in_=x3T[ts(do, 128), ts(tc, 512)]),
                    r=[], w=[f"xt:{p}"])
                sc.c("vector", lambda v, do=do, p=p: v.scalar_tensor_tensor(
                    yr[:, ts(do, 512)], psm[p][:], 1.0, sl8h(xtile, p),
                    ALU.mult, ALU.add), r=[f"psm:{p}", f"xt:{p}"], w=[f"yr:{do}"])
                sc.c("scalar", lambda s, do=do, p=p: s.activation(
                    out=sl8h(sq, p), in_=yr[:, ts(do, 512)], func=AF.Square),
                    r=[f"yr:{do}"], w=[f"sq:{p}"])

                def mmstat(te, do=do, p=p):
                    te.matmul(ps_sum[:], ones_col[:], yr[:, ts(do, 512)],
                              start=(do == 0), stop=(do == 7))
                    return te.matmul(ps_sq[:], ones_col[:], sl8h(sq, p),
                                     start=(do == 0), stop=(do == 7))
                sc.c("tensor", mmstat, r=[f"yr:{do}", f"sq:{p}"],
                     w=["pssum", "pssq"])
            add_ln_rows_k(sc, lnrows, ps_sum, ps_sq, ps_al, ps_be, ones_row1,
                          1.0 / D, 1e-5)
            for do in range(8):
                p = do % 2
                sc.c("vector", lambda v, do=do, p=p: v.tensor_tensor(
                    sl8h(tprod, p), yr[:, ts(do, 512)], ps_al[:], ALU.mult),
                    r=[f"yr:{do}", "psal"], w=[f"tp:{p}"])
                sc.c("vector", lambda v, p=p: v.tensor_tensor(
                    sl8h(x4t, p), sl8h(tprod, p), ps_be[:], ALU.add),
                    r=[f"tp:{p}", "psbe"], w=[f"x4:{p}"])
                sc.d("sync", lambda sync, do=do, tc=tc, p=p: sync.dma_start(
                    out=x4T[ts(do, 128), ts(tc, 512)], in_=sl8h(x4t, p)),
                    r=[f"x4:{p}"], w=["x4T"])
        sc.run(nc)
    mybir.codegen_inst_isa_subclasses(nc)
    return nc



_BF = ml_dtypes.bfloat16


def add_ln_rows(sc, rows, ps_sum, ps_sq, ps_al, ps_be, ones_row1, inv_n, eps_bias):
    R = lambda i: rows[:, i * 512:(i + 1) * 512]
    sc.c("vector", lambda v: [v.tensor_copy(R(0), ps_sum[:]),
                              v.tensor_copy(R(1), ps_sq[:])][-1])
    sc.c("vector", lambda v: [
        v.tensor_scalar(R(2), R(0), inv_n, None, ALU.mult),
        v.tensor_scalar(R(3), R(1), inv_n, None, ALU.mult)][-1])
    sc.c("vector", lambda v: v.tensor_tensor(R(5), R(2), R(2), ALU.mult))
    sc.c("vector", lambda v: v.tensor_tensor(R(4), R(3), R(5), ALU.subtract))
    sc.c("scalar", lambda s: s.activation(
        out=R(3), in_=R(4), func=AF.Sqrt, bias=eps_bias))
    sc.c("vector", lambda v: v.reciprocal(R(4), R(3)))
    sc.c("vector", lambda v: v.tensor_tensor(R(5), R(2), R(4), ALU.mult))
    sc.c("vector", lambda v: v.tensor_scalar(R(5), R(5), -1.0, None, ALU.mult))

    def bcasts(te):
        te.matmul(ps_al[:], ones_row1[:], R(4), start=True, stop=True)
        return te.matmul(ps_be[:], ones_row1[:], R(5), start=True, stop=True)
    sc.c("tensor", bcasts)


def add_ln_rows_k(sc, rows, ps_sum, ps_sq, ps_al, ps_be, ones_row1, inv_n,
                  eps_bias):
    R = lambda i: rows[:, i * 512:(i + 1) * 512]
    sc.c("vector", lambda v: [v.tensor_copy(R(0), ps_sum[:]),
                              v.tensor_copy(R(1), ps_sq[:])][-1],
         r=["pssum", "pssq"], w=["lnr"])
    sc.c("vector", lambda v: [
        v.tensor_scalar(R(2), R(0), inv_n, None, ALU.mult),
        v.tensor_scalar(R(3), R(1), inv_n, None, ALU.mult)][-1],
        r=["lnr"], w=["lnr"])
    sc.c("vector", lambda v: v.tensor_tensor(R(5), R(2), R(2), ALU.mult),
         r=["lnr"], w=["lnr"])
    sc.c("vector", lambda v: v.tensor_tensor(R(4), R(3), R(5), ALU.subtract),
         r=["lnr"], w=["lnr"])
    sc.c("scalar", lambda s: s.activation(
        out=R(3), in_=R(4), func=AF.Sqrt, bias=eps_bias),
        r=["lnr"], w=["lnr"])
    sc.c("vector", lambda v: v.reciprocal(R(4), R(3)), r=["lnr"], w=["lnr"])
    sc.c("vector", lambda v: v.tensor_tensor(R(5), R(2), R(4), ALU.mult),
         r=["lnr"], w=["lnr"])
    sc.c("vector", lambda v: v.tensor_scalar(R(5), R(5), -1.0, None, ALU.mult),
         r=["lnr"], w=["lnr"])

    def bcasts(te):
        te.matmul(ps_al[:], ones_row1[:], R(4), start=True, stop=True)
        return te.matmul(ps_be[:], ones_row1[:], R(5), start=True, stop=True)
    sc.c("tensor", bcasts, r=["lnr"], w=["psal", "psbe"])


def add_qdq_weight_lpbq(sc, nc, stage_f, wq_t, awb, sbv, rbv, w_hbm, rb, nblk):
    W = nblk * 32

    def load(sync):
        return sync.dma_start(out=stage_f[:, 0:W], in_=w_hbm[ts(rb, 128), :])
    sc.d("sync", load)
    sc.c("vector", lambda v: v.tensor_reduce(
        awb[:, 0:nblk], stage_f[:, 0:W].rearrange("p (a b) -> p a b", b=32),
        axis=AX.X, op=ALU.max, apply_absolute_value=True))
    sc.c("vector", lambda v: v.tensor_scalar(
        sbv[:, 0:nblk], awb[:, 0:nblk], 1.0 / 7.0, 1e-12, ALU.mult, ALU.max))
    sc.c("vector", lambda v: v.reciprocal(rbv[:, 0:nblk], sbv[:, 0:nblk]))

    def rounds(v):
        ins = []
        for b in range(nblk):
            ins.append(v._custom_dve(
                ROUND_SCALE, out=wq_t[:, ts(b, 32)], in0=stage_f[:, ts(b, 32)],
                s0=rbv[:, b:b + 1], s1=sbv[:, b:b + 1], imm2=MAGIC))
        return ins[-1]
    sc.c("vector", rounds)



_NC_CACHE = {}


def _get_nc(name, builder):
    if name not in _NC_CACHE:
        _NC_CACHE[name] = builder()
    return _NC_CACHE[name]


def _gather_amax_n(res, key, n):
    return np.ascontiguousarray(np.concatenate(
        [np.asarray(res.results[c][key]).reshape(1, 1) for c in range(n)]
        + [np.asarray(res.results[0][key]).reshape(1, 1)] * (8 - n),
        axis=0)).astype(np.float32)


def _gather_amax(res, key):
    return _gather_amax_n(res, key, NCORES)


def kernel(L0_attn_x, L0_attn_W_qkv, L0_ffn_W_in, L0_ffn_b_in, L0_ffn_W_out,
           L0_ffn_b_out, L1_attn_A_log, L1_attn_W_delta, L1_attn_b_delta,
           L1_attn_W_B, L1_attn_W_C, L1_attn_D_skip, L1_attn_norm_weight,
           L1_ffn_w_gate, L1_ffn_w_up, L1_ffn_w_down):
    cores = list(range(NCORES))
    x = np.ascontiguousarray(np.asarray(L0_attn_x, dtype=np.float32))
    wqkv = np.ascontiguousarray(np.asarray(L0_attn_W_qkv, np.float32))
    w_inT = np.ascontiguousarray(np.asarray(L0_ffn_W_in, np.float32).T)
    b_in_t = np.ascontiguousarray(np.asarray(L0_ffn_b_in, np.float32).reshape(32, 128).T)
    w_out = np.ascontiguousarray(np.asarray(L0_ffn_W_out, np.float32))
    b_out_t = np.ascontiguousarray(np.asarray(L0_ffn_b_out, np.float32).reshape(8, 128).T)
    a_log = np.ascontiguousarray(np.asarray(L1_attn_A_log, np.float32))
    w_delta = np.ascontiguousarray(np.asarray(L1_attn_W_delta, np.float32))
    b_delta_t = np.ascontiguousarray(np.asarray(L1_attn_b_delta, np.float32).reshape(8, 128).T)
    w_b = np.ascontiguousarray(np.asarray(L1_attn_W_B, np.float32))
    w_c = np.ascontiguousarray(np.asarray(L1_attn_W_C, np.float32))
    dskip_t = np.ascontiguousarray(np.asarray(L1_attn_D_skip, np.float32).reshape(8, 128).T)
    nw_t = np.ascontiguousarray(np.asarray(L1_attn_norm_weight, np.float32).reshape(8, 128).T)
    w_gate = np.ascontiguousarray(np.asarray(L1_ffn_w_gate, np.float32))
    w_up = np.ascontiguousarray(np.asarray(L1_ffn_w_up, np.float32))
    w_down = np.ascontiguousarray(np.asarray(L1_ffn_w_down, np.float32))

    nc1 = _get_nc("l0_attn", build_l0_attn)
    r1 = run_bass_kernel_spmd(nc1, [
        {"x_tok": np.ascontiguousarray(x[c]), "wqkv": wqkv} for c in cores],
        core_ids=cores)
    x1 = [np.asarray(r1.results[c]["x1T"]) for c in cores]
    am1 = _gather_amax(r1, "amax1")

    nc2 = _get_nc("l0_ffn_a", build_l0_ffn_a)
    r2 = run_bass_kernel_spmd(nc2, [
        {"x1T": x1[c], "amax8": am1, "w_inT": w_inT, "b_in_t": b_in_t}
        for c in cores], core_ids=cores)
    hid = [np.asarray(r2.results[c]["hidT"]) for c in cores]
    amh = _gather_amax(r2, "amax_h")

    nc3 = _get_nc("l0_ffn_b", build_l0_ffn_b)
    r3 = run_bass_kernel_spmd(nc3, [
        {"x1T": x1[c], "hidT": hid[c], "amax8": amh, "w_out": w_out,
         "b_out_t": b_out_t} for c in cores], core_ids=cores)
    x2 = [np.asarray(r3.results[c]["x2T"]) for c in cores]

    nc4 = _get_nc("ssm", build_ssm)
    r4 = run_bass_kernel_spmd(nc4, [
        {"x2T": x2[c], "a_log": a_log, "w_delta": w_delta,
         "b_delta_t": b_delta_t, "w_b": w_b, "w_c": w_c, "dskip_t": dskip_t,
         "nw_t": nw_t} for c in cores], core_ids=cores)
    x3 = [np.asarray(r4.results[c]["x3T"]) for c in cores]
    am3 = _gather_amax(r4, "amax3")

    nc5 = _get_nc("l1_ffn_a", build_l1_ffn_a)
    r5 = run_bass_kernel_spmd(nc5, [
        {"x3T": x3[c], "amax8": am3, "w_up": w_up, "w_gate": w_gate}
        for c in cores], core_ids=cores)
    up = [np.asarray(r5.results[c]["upT"]) for c in cores]
    gate = [np.asarray(r5.results[c]["gateT"]) for c in cores]
    amu = _gather_amax(r5, "amax_u")
    amg = _gather_amax(r5, "amax_g")

    nc6 = _get_nc("l1_ffn_b", build_l1_ffn_b)
    r6 = run_bass_kernel_spmd(nc6, [
        {"gateT": gate[c], "amax8": amg} for c in cores], core_ids=cores)
    mm_ = [np.asarray(r6.results[c]["mT"]) for c in cores]
    amm = _gather_amax(r6, "amax_m")

    nc7 = _get_nc("l1_ffn_c", build_l1_ffn_c)
    r7 = run_bass_kernel_spmd(nc7, [
        {"upT": up[c], "mT": mm_[c], "amax8u": amu, "amax8m": amm}
        for c in cores], core_ids=cores)
    oo = [np.asarray(r7.results[c]["oT"]) for c in cores]
    amo = _gather_amax(r7, "amax_o")

    nc8 = _get_nc("l1_ffn_d", build_l1_ffn_d)
    r8 = run_bass_kernel_spmd(nc8, [
        {"x3T": x3[c], "oT": oo[c], "amax8": amo, "w_down": w_down}
        for c in cores], core_ids=cores)
    out = np.stack([np.ascontiguousarray(np.asarray(r8.results[c]["x4T"]).T)
                    for c in cores])
    return out.astype(np.float32)


# version 06
# speedup vs baseline: 1.4293x; 1.0322x over previous
from contextlib import ExitStack

import numpy as np
import ml_dtypes
import concourse.bass as bass
import concourse.mybir as mybir
from concourse.bass_utils import run_bass_kernel_spmd
import concourse.dve_ops as dve_ops
from concourse.dve_op import DveOp
from concourse.dve_spec import Spec, Src0, Src1, C0, C1, C2, lower as spec_lower

F32 = mybir.dt.float32
BF16 = mybir.dt.bfloat16
AF = mybir.ActivationFunctionType
ALU = mybir.AluOpType
AX = mybir.AxisListType

T = 4096
D = 1024
FF = 4096
I3 = 3072
N16 = 16
NCORES = 8
MAGIC = float(3 << 22)
NEG = -1.0e30
INV32 = 1.0 / 32.0


def _register_dve(name, spec, reference):
    if name in dve_ops._SUB_OPCODE_FOR_NAME:
        for op in dve_ops.OPS:
            if op.name == name:
                return op
    sha = {"v3": spec_lower(spec, ver="v3", perf_en=False, name=name).sha("v3")}
    op = DveOp(name, spec, subdim=False, uops_sha=sha, reference=reference)
    dve_ops.OPS.append(op)
    dve_ops._SUB_OPCODE_FOR_NAME[name] = dve_ops._CUSTOM_DVE_ROW_BASE + len(dve_ops.OPS) - 1
    dve_ops.CUSTOM_DVE_SPECS[name] = spec
    dve_ops.CUSTOM_DVE_REFS[name] = reference
    return op


ROUND_SCALE = _register_dve(
    "ROUND_SCALE",
    Spec(body=((Src0 * C0 + C2) - C2) * C1),
    lambda in0, in1, s0, s1, imm2: np.rint(in0 * s0) * s1,
)


def ts(i, size):
    return slice(i * size, (i + 1) * size)



ENGINES = ("sync", "vector", "scalar", "tensor", "gpsimd")


class Script:

    def __init__(self):
        self.steps = []

    def c(self, eng, fn, r=None, w=None):
        self.steps.append((eng, "c", fn, 1, r, w))

    def d(self, eng, fn, n=1, r=None, w=None):
        self.steps.append((eng, "d", fn, n, r, w))

    def _compute_deps(self):
        n_steps = len(self.steps)
        deps = [set() for _ in range(n_steps)]
        last_w = {}
        readers = {}
        barrier = -1
        barrier_readers = []
        last_d = {}
        for i, (eng, kind, fn, n, r, w) in enumerate(self.steps):
            if kind == "d":
                if eng in last_d:
                    deps[i].add(last_d[eng])
                last_d[eng] = i
            if r is None or w is None:
                for j in barrier_readers:
                    deps[i].add(j)
                if barrier >= 0:
                    deps[i].add(barrier)
                barrier = i
                barrier_readers = []
                last_w = {}
                readers = {}
                continue
            if barrier >= 0:
                deps[i].add(barrier)
            for k in r:
                if k in last_w:
                    deps[i].add(last_w[k])
            for k in w:
                if k in last_w:
                    deps[i].add(last_w[k])
                for rd in readers.get(k, ()):
                    deps[i].add(rd)
            for k in r:
                readers.setdefault(k, []).append(i)
            for k in w:
                last_w[k] = i
                readers[k] = []
            barrier_readers.append(i)
        return deps

    FORCE_SERIAL = False

    def run(self, nc):
        if Script.FORCE_SERIAL:
            self.steps = [(e, k, f, n, None, None) for (e, k, f, n, r, w)
                          in self.steps]
        with ExitStack() as es:
            sems = {e: es.enter_context(nc.semaphore("sem_" + e)) for e in ENGINES}
            cnt = {e: 0 for e in ENGINES}
            plan = []
            for eng, kind, fn, n, r, w in self.steps:
                cnt[eng] += 1 if kind == "c" else 16 * n
                plan.append((eng, cnt[eng]))
            deps = self._compute_deps()
            waits = []
            for i, dd in enumerate(deps):
                per = {}
                for j in dd:
                    e, v = plan[j]
                    per[e] = max(per.get(e, 0), v)
                waits.append(per)

            def section(e, engobj):
                for i, (eng, kind, fn, n, r, w) in enumerate(self.steps):
                    if eng != e:
                        continue
                    for de, dv in sorted(waits[i].items()):
                        engobj.wait_ge(sems[de], dv)
                    instrs = fn(engobj)
                    if not isinstance(instrs, (list, tuple)):
                        instrs = [instrs]
                    if kind == "c":
                        instrs[-1].then_inc(sems[e], 1)
                    else:
                        assert len(instrs) == n, f"step {i}: expected {n} dmas, got {len(instrs)}"
                        for ins_ in instrs:
                            ins_.then_inc(sems[e], 16)

            import os
            if os.environ.get("SCRIPT_VERBOSE"):
                from collections import Counter
                print("script steps:", len(self.steps), Counter(s[0] for s in self.steps))
            block = es.enter_context(nc.Block())

            @block.sync
            def _(sync):
                section("sync", sync)
                for e in ENGINES:
                    if cnt[e] > 0:
                        sync.wait_ge(sems[e], cnt[e])

            @block.vector
            def _(v):
                section("vector", v)

            @block.scalar
            def _(s):
                section("scalar", s)

            @block.tensor
            def _(te):
                section("tensor", te)

            @block.gpsimd
            def _(g):
                section("gpsimd", g)



def add_const_pool(nc, sc, pool, values):
    for i, v in enumerate(values):
        nc.const_aps.register(F32, float(v), pool[:, i:i + 1])

    def f(vec):
        ins = []
        for i, v in enumerate(values):
            ins.append(vec.memset(pool[:, i:i + 1], float(v)))
        return ins[-1]
    sc.c("vector", f)


def add_identity(sc, ident_bf, ident_f32=None):
    def f0(g):
        ins = [g.memset(ident_bf[:], 0.0)]
        if ident_f32 is not None:
            ins.append(g.memset(ident_f32[:], 0.0))
        return ins[-1]
    sc.c("gpsimd", f0)

    def f1(g):
        ins = [g.affine_select(
            out=ident_bf[:], in_=ident_bf[:], compare_op=ALU.not_equal,
            fill=1.0, base=0, pattern=[[-1, 128]], channel_multiplier=1)]
        if ident_f32 is not None:
            ins.append(g.affine_select(
                out=ident_f32[:], in_=ident_f32[:], compare_op=ALU.not_equal,
                fill=1.0, base=0, pattern=[[-1, 128]], channel_multiplier=1))
        return ins[-1]
    sc.c("gpsimd", f1)


def add_scalar_prep(sc, bufs, amax_in, qmax, clip_lo, mode):
    am8row, am, sc2, rsb, scr2 = (bufs["am8row"], bufs["am"], bufs["sc2"],
                                  bufs["rsb"], bufs["scr2"])

    def load(sync):
        return sync.dma_start(out=am8row[:], in_=amax_in.rearrange("a one -> one a"))
    sc.d("sync", load)

    def red(v):
        return v.reduce_max(am[:], am8row[:], axis=AX.X)
    sc.c("vector", red)

    if mode == "a16":
        sc.c("vector", lambda v: v.tensor_scalar(am[:], am[:], clip_lo, None, ALU.max))
        def mk(v):
            v.reciprocal(sc2[:, 0:1], am[:])
            return v.tensor_scalar(sc2[:, 1:2], am[:], 1.0 / qmax, None, ALU.mult)
        sc.c("vector", mk)
        sc.c("vector", lambda v: v.tensor_scalar(
            sc2[:, 0:1], sc2[:, 0:1], qmax, None, ALU.mult))
    else:
        sc.c("vector", lambda v: v.tensor_scalar(
            sc2[:, 1:2], am[:], 1.0 / qmax, clip_lo, ALU.mult, ALU.max))
        sc.c("vector", lambda v: v.reciprocal(sc2[:, 0:1], sc2[:, 1:2]))

    def store(sync):
        return sync.dma_start(out=scr2, in_=sc2[:])
    sc.d("sync", store)

    def bcast(sync):
        return sync.dma_start(out=rsb[:], in_=scr2.broadcast_to([128, 2]))
    sc.d("sync", bcast)


def add_amax_finalize(sc, amax_acc, amT, am1, am_bounce, amax_out):
    def st1(sync):
        return sync.dma_start(out=am_bounce.rearrange("(p one) -> p one", one=1),
                              in_=amax_acc[:])
    sc.d("sync", st1)

    def ld1(sync):
        return sync.dma_start(out=amT[:],
                              in_=am_bounce.rearrange("(one p) -> one p", one=1))
    sc.d("sync", ld1)

    def red(v):
        return v.reduce_max(am1[:], amT[:], axis=AX.X)
    sc.c("vector", red)

    def st2(sync):
        return sync.dma_start(out=amax_out, in_=am1[:])
    sc.d("sync", st2)



def build_l0_attn():
    nc = bass.Bass("TRN2", target_bir_lowering=False, debug=False)
    x_tok = nc.dram_tensor("x_tok", [T, D], F32, kind="ExternalInput").ap()
    wqkv = nc.dram_tensor("wqkv", [D, 3 * D], F32, kind="ExternalInput").ap()
    x1T = nc.dram_tensor("x1T", [D, T], F32, kind="ExternalOutput").ap()
    amax1 = nc.dram_tensor("amax1", [1, 1], F32, kind="ExternalOutput").ap()
    xt_bf = nc.dram_tensor("xt_bf", [D, T], BF16, kind="Internal").ap()
    wq_bf = nc.dram_tensor("wq_bf", [D, 3 * D], BF16, kind="Internal").ap()
    qt_bf = nc.dram_tensor("qt_bf", [D, T], BF16, kind="Internal").ap()
    am_bounce = nc.dram_tensor("am_bounce", [128], F32, kind="Internal").ap()

    sc = Script()
    with ExitStack() as es:
        A = lambda n, sh, dt: es.enter_context(nc.sbuf_tensor(n, sh, dt))
        PS = lambda n, sh, dt: es.enter_context(nc.psum_tensor(n, sh, dt))
        stage_f = A("stage_f", [128, 2 * 1024], F32)
        stage_b = A("stage_b", [128, 2 * 1024], BF16)
        kT = A("kT", [128, 8 * T], BF16)
        vv = A("vv", [128, 32 * 1024], BF16)
        wvb = A("wvb", [128, 8 * 512], BF16)
        wqk_t = A("wqk_t", [128, 2 * 8 * 128], BF16)
        xc = A("xc", [128, 8 * 512], BF16)
        scb = A("scb", [128, T], F32)
        pbuf = A("pbuf", [128, T], BF16)
        pT2 = A("pT2", [128, 256], BF16)
        qTb = A("qTb", [128, 2 * 8 * 128], BF16)
        xrow = A("xrow", [128, 2 * 1024], F32)
        y1 = A("y1", [128, 1024], F32)
        x1tr = A("x1tr", [128, 1024], F32)
        sm = A("sm", [128, 16], F32)
        amax_acc = A("amax_acc", [128, 1], F32)
        amT = A("amT", [1, 128], F32)
        am1 = A("am1", [1, 1], F32)
        ident = A("ident", [128, 128], BF16)
        identf = A("identf", [128, 128], F32)
        cpool = A("cpool", [128, 2], F32)
        ps_a0 = PS("ps_a0", [128, 512], F32)
        ps_a1 = PS("ps_a1", [128, 512], F32)
        psa = [ps_a0, ps_a1]
        ps_o = PS("ps_o", [128, 1024], F32)
        ps_tb0 = PS("ps_tb0", [128, 128], BF16)
        ps_tb1 = PS("ps_tb1", [128, 128], BF16)
        pstb = [ps_tb0, ps_tb1]
        ps_tf0 = PS("ps_tf0", [128, 128], F32)
        ps_tf1 = PS("ps_tf1", [128, 128], F32)
        pstf = [ps_tf0, ps_tf1]

        COL = {k: i for i, k in enumerate(
            ["rowmax", "nbias", "lsum", "linv", "sumy", "sumsq", "m", "ey2",
             "mm", "var", "sd", "rstd", "mb", "negmb"])}

        def col(k):
            return sm[:, COL[k]:COL[k] + 1]

        def s2(buf, p, wdt):
            return buf[:, p * wdt:(p + 1) * wdt]

        add_const_pool(nc, sc, cpool, [1e-5])
        add_identity(sc, ident, identf)
        sc.c("vector", lambda v: v.memset(amax_acc[:], 0.0))

        for idx in range(24):
            r_, cc = divmod(idx, 3)
            p = idx % 2
            sc.d("sync", lambda sync, r_=r_, cc=cc, p=p: sync.dma_start(
                out=s2(stage_f, p, 1024), in_=wqkv[ts(r_, 128), ts(cc, 1024)]),
                r=[], w=[f"stf:{p}"])
            sc.c("scalar", lambda s, p=p: s.activation(
                out=s2(stage_b, p, 1024), in_=s2(stage_f, p, 1024), func=AF.Copy),
                r=[f"stf:{p}"], w=[f"stb:{p}"])
            sc.d("sync", lambda sync, r_=r_, cc=cc, p=p: sync.dma_start(
                out=wq_bf[ts(r_, 128), ts(cc, 1024)], in_=s2(stage_b, p, 1024)),
                r=[f"stb:{p}"], w=["wqbf"])

        for i in range(32):
            p = i % 2
            sc.d("sync", lambda sync, i=i, p=p: sync.dma_start(
                out=s2(xrow, p, 1024), in_=x_tok[ts(i, 128), :]),
                r=[], w=[f"xr:{p}"])
            sc.c("scalar", lambda s, p=p: s.activation(
                out=s2(stage_b, p, 1024), in_=s2(xrow, p, 1024), func=AF.Copy),
                r=[f"xr:{p}"], w=[f"stb:{p}"])
            for j in range(8):
                jp = j % 2
                sc.c("tensor", lambda te, j=j, p=p, jp=jp: te.transpose(
                    pstb[jp][:], s2(stage_b, p, 1024)[:, ts(j, 128)], ident[:]),
                    r=[f"stb:{p}"], w=[f"pstb:{jp}"])
                sc.c("vector", lambda v, j=j, p=p, jp=jp: v.tensor_copy(
                    s2(x1tr, p, 1024).bitcast(BF16)[:, ts(j, 128)] if False
                    else xc[:, j * 128 + p * 4096: j * 128 + p * 4096 + 128]
                    if False else s2(stage_f, p, 1024).bitcast(BF16)[:, ts(j, 128)],
                    pstb[jp][:]),
                    r=[f"pstb:{jp}"], w=[f"xco:{p}"])
            sc.d("sync", lambda sync, i=i, p=p: [sync.dma_start(
                out=xt_bf[ts(j, 128), ts(i, 128)],
                in_=s2(stage_f, p, 1024).bitcast(BF16)[:, ts(j, 128)])
                for j in range(8)], n=8, r=[f"xco:{p}"], w=["xtbf"])

        for dvh in range(2):
            sc.d("sync", lambda sync, dvh=dvh: [sync.dma_start(
                out=wvb[:, ts(db, 512)],
                in_=wq_bf[ts(db, 128), 2 * D + dvh * 512: 2 * D + dvh * 512 + 512])
                for db in range(8)], n=8, r=["wqbf"], w=["wvb"])
            for tc in range(8):
                p = tc % 2
                xcb = xc if p == 0 else pbuf
                kxc = f"xch:{p}"
                sc.d("sync", lambda sync, tc=tc, xcb=xcb: [sync.dma_start(
                    out=xcb[:, ts(db, 512)], in_=xt_bf[ts(db, 128), ts(tc, 512)])
                    for db in range(8)], n=8, r=["xtbf"], w=[kxc])
                for sub in range(4):
                    sp = sub % 2

                    def mmv(te, tc=tc, sub=sub, xcb=xcb, sp=sp):
                        ins = []
                        for db in range(8):
                            ins.append(te.matmul(
                                psa[sp][:],
                                xcb[:, db * 512 + sub * 128: db * 512 + sub * 128 + 128],
                                wvb[:, ts(db, 512)],
                                start=(db == 0), stop=(db == 7)))
                        return ins[-1]
                    sc.c("tensor", mmv, r=[kxc, "wvb"], w=[f"psa:{sp}"])
                    kt_ = tc * 4 + sub
                    sc.c("scalar", lambda s, kt_=kt_, dvh=dvh, sp=sp: s.activation(
                        out=vv[:, kt_ * 1024 + dvh * 512: kt_ * 1024 + dvh * 512 + 512],
                        in_=psa[sp][:], func=AF.Copy),
                        r=[f"psa:{sp}"], w=["vv"])

        for tc in range(8):
            p = tc % 2
            xcb = xc if p == 0 else pbuf
            kxc = f"xch:{p}"
            sc.d("sync", lambda sync, tc=tc, xcb=xcb: [sync.dma_start(
                out=xcb[:, ts(db, 512)], in_=xt_bf[ts(db, 128), ts(tc, 512)])
                for db in range(8)], n=8, r=["xtbf"], w=[kxc])
            for qk in range(2):
                for ob in range(8):
                    wp = (qk * 8 + ob) % 2
                    sc.d("sync", lambda sync, qk=qk, ob=ob, wp=wp: [sync.dma_start(
                        out=s2(wqk_t, wp, 1024)[:, ts(db, 128)],
                        in_=wq_bf[ts(db, 128), qk * D + ob * 128: qk * D + ob * 128 + 128])
                        for db in range(8)], n=8, r=["wqbf"], w=[f"wqk:{wp}"])

                    def mmqk(te, tc=tc, qk=qk, ob=ob, xcb=xcb, wp=wp):
                        ins = []
                        for db in range(8):
                            ins.append(te.matmul(
                                psa[wp][:], s2(wqk_t, wp, 1024)[:, ts(db, 128)],
                                xcb[:, ts(db, 512)],
                                start=(db == 0), stop=(db == 7)))
                        return ins[-1]
                    sc.c("tensor", mmqk, r=[kxc, f"wqk:{wp}"], w=[f"psa:{wp}"])
                    if qk == 1:
                        sc.c("scalar", lambda s, ob=ob, tc=tc, wp=wp: s.activation(
                            out=kT[:, ob * T + tc * 512: ob * T + tc * 512 + 512],
                            in_=psa[wp][:], func=AF.Copy),
                            r=[f"psa:{wp}"], w=["kTr"])
                    else:
                        sc.c("scalar", lambda s, wp=wp: s.activation(
                            out=s2(stage_b, wp, 1024)[:, 0:512], in_=psa[wp][:],
                            func=AF.Copy), r=[f"psa:{wp}"], w=[f"stb:{wp}"])
                        sc.d("sync", lambda sync, ob=ob, tc=tc, wp=wp: sync.dma_start(
                            out=qt_bf[ts(ob, 128), ts(tc, 512)],
                            in_=s2(stage_b, wp, 1024)[:, 0:512]),
                            r=[f"stb:{wp}"], w=["qtbf"])

        sc.c("vector", lambda v: v.engine_nop())

        for qb in range(32):
            klen = (qb + 1) * 128
            nkc = (klen + 511) // 512
            qp = qb % 2
            sc.d("sync", lambda sync, qb=qb, qp=qp: [sync.dma_start(
                out=s2(qTb, qp, 1024)[:, ts(db, 128)],
                in_=qt_bf[ts(db, 128), ts(qb, 128)]) for db in range(8)],
                n=8, r=[], w=[f"qTb:{qp}"])
            sc.d("sync", lambda sync, qb=qb, qp=qp: sync.dma_start(
                out=s2(xrow, qp, 1024), in_=x_tok[ts(qb, 128), :]),
                r=[], w=[f"xr:{qp}"])
            for kc in range(nkc):
                w_ = min(512, klen - kc * 512)
                kp = kc % 2

                def mms(te, qb=qb, kc=kc, w_=w_, qp=qp, kp=kp):
                    ins = []
                    for db in range(8):
                        ins.append(te.matmul(
                            psa[kp][:, 0:w_], s2(qTb, qp, 1024)[:, ts(db, 128)],
                            kT[:, db * T + kc * 512: db * T + kc * 512 + w_],
                            start=(db == 0), stop=(db == 7)))
                    return ins[-1]
                sc.c("tensor", mms, r=[f"qTb:{qp}", "kTr"], w=[f"psa:{kp}"])
                sc.c("scalar", lambda s, kc=kc, w_=w_, kp=kp: s.activation(
                    out=scb[:, kc * 512: kc * 512 + w_], in_=psa[kp][:, 0:w_],
                    func=AF.Copy), r=[f"psa:{kp}"], w=["scb"])
            sc.c("gpsimd", lambda g, qb=qb: g.affine_select(
                out=scb[:, ts(qb, 128)], in_=scb[:, ts(qb, 128)],
                compare_op=ALU.is_ge, fill=NEG, base=0,
                pattern=[[-1, 128]], channel_multiplier=1),
                r=["scb"], w=["scb"])
            sc.c("vector", lambda v, klen=klen: v.reduce_max(
                col("rowmax"), scb[:, 0:klen], axis=AX.X),
                r=["scb"], w=["smx"])
            sc.c("vector", lambda v: v.tensor_scalar(
                col("nbias"), col("rowmax"), -INV32, None, ALU.mult),
                r=["smx"], w=["smb"])
            sc.c("scalar", lambda s, klen=klen: s.activation(
                out=pbuf[:, 0:klen], in_=scb[:, 0:klen], func=AF.Exp,
                scale=INV32, bias=col("nbias"), accum_out=col("lsum")),
                r=["scb", "smb"], w=["pbuf", "sml"])
            sc.c("vector", lambda v: v.reciprocal(col("linv"), col("lsum")),
                 r=["sml"], w=["smv"])
            for kt in range(qb + 1):
                ktp = kt % 2
                sc.c("tensor", lambda te, kt=kt, ktp=ktp: te.transpose(
                    pstb[ktp][:], pbuf[:, ts(kt, 128)], ident[:]),
                    r=["pbuf"], w=[f"pstb:{ktp}"])
                sc.c("vector", lambda v, ktp=ktp: v.tensor_copy(
                    pT2[:, ts(ktp, 128)], pstb[ktp][:]),
                    r=[f"pstb:{ktp}"], w=[f"pT:{ktp}"])

                def mmav(te, kt=kt, qb=qb, ktp=ktp):
                    ins = []
                    for dvh in range(2):
                        ins.append(te.matmul(
                            ps_o[:, ts(dvh, 512)], pT2[:, ts(ktp, 128)],
                            vv[:, kt * 1024 + dvh * 512: kt * 1024 + dvh * 512 + 512],
                            start=(kt == 0), stop=(kt == qb)))
                    return ins[-1]
                sc.c("tensor", mmav, r=[f"pT:{ktp}", "vv"], w=["pso"])
            sc.c("vector", lambda v, qp=qp: v.scalar_tensor_tensor(
                y1[:], ps_o[:], col("linv"), s2(xrow, qp, 1024), ALU.mult,
                ALU.add, accum_out=col("sumy")),
                r=["pso", "smv", f"xr:{qp}"], w=["y1", "smy"])
            sc.c("scalar", lambda s: s.activation(
                out=scb[:, 0:1024], in_=y1[:], func=AF.Square,
                accum_out=col("sumsq")), r=["y1"], w=["scb", "smy2"])
            sc.c("vector", lambda v: [
                v.tensor_scalar(col("m"), col("sumy"), 1.0 / D, None, ALU.mult),
                v.tensor_scalar(col("ey2"), col("sumsq"), 1.0 / D, None,
                                ALU.mult)][-1], r=["smy", "smy2"], w=["smm"])
            sc.c("vector", lambda v: v.tensor_tensor(
                col("mm"), col("m"), col("m"), ALU.mult), r=["smm"], w=["smm2"])
            sc.c("vector", lambda v: v.tensor_tensor(
                col("var"), col("ey2"), col("mm"), ALU.subtract),
                r=["smm", "smm2"], w=["smvv"])
            sc.c("scalar", lambda s: s.activation(
                out=col("sd"), in_=col("var"), func=AF.Sqrt, bias=1e-5),
                r=["smvv"], w=["smsd"])
            sc.c("vector", lambda v: v.reciprocal(col("rstd"), col("sd")),
                 r=["smsd"], w=["smr"])
            sc.c("vector", lambda v: v.tensor_tensor(
                col("mb"), col("m"), col("rstd"), ALU.mult),
                r=["smm", "smr"], w=["smmb"])
            sc.c("vector", lambda v: v.tensor_scalar(
                col("negmb"), col("mb"), -1.0, None, ALU.mult),
                r=["smmb"], w=["smnb"])
            sc.c("vector", lambda v, qp=qp: v.tensor_scalar(
                s2(xrow, qp, 1024), y1[:], col("rstd"), col("negmb"), ALU.mult,
                ALU.add), r=["y1", "smr", "smnb"], w=[f"xr:{qp}"])
            sc.c("vector", lambda v, qp=qp: v.reduce_max(
                col("rowmax"), s2(xrow, qp, 1024), axis=AX.X,
                apply_absolute_value=True), r=[f"xr:{qp}"], w=["smx"])
            sc.c("vector", lambda v: v.tensor_tensor(
                amax_acc[:], amax_acc[:], col("rowmax"), ALU.max),
                r=["smx", "amacc"], w=["amacc"])
            for j in range(8):
                jp = j % 2
                sc.c("tensor", lambda te, j=j, jp=jp, qp=qp: te.transpose(
                    pstf[jp][:], s2(xrow, qp, 1024)[:, ts(j, 128)], identf[:]),
                    r=[f"xr:{qp}"], w=[f"pstf:{jp}"])
                sc.c("vector", lambda v, j=j, jp=jp: v.tensor_copy(
                    x1tr[:, ts(j, 128)], pstf[jp][:]),
                    r=[f"pstf:{jp}"], w=["x1t"])
            sc.d("sync", lambda sync, qb=qb: [sync.dma_start(
                out=x1T[ts(j, 128), ts(qb, 128)],
                in_=x1tr[:, ts(j, 128)]) for j in range(8)],
                n=8, r=["x1t"], w=["x1T", "x1t"])

        add_amax_finalize(sc, amax_acc, amT, am1, am_bounce, amax1)
        sc.run(nc)
    mybir.codegen_inst_isa_subclasses(nc)
    return nc



def build_l0_ffn_a():
    nc = bass.Bass("TRN2", target_bir_lowering=False, debug=False)
    x1T = nc.dram_tensor("x1T", [D, T], F32, kind="ExternalInput").ap()
    amax8 = nc.dram_tensor("amax8", [8, 1], F32, kind="ExternalInput").ap()
    w_inT = nc.dram_tensor("w_inT", [FF, D], F32, kind="ExternalInput").ap()
    b_in_t = nc.dram_tensor("b_in_t", [128, 32], F32, kind="ExternalInput").ap()
    hidT = nc.dram_tensor("hidT", [FF, T], F32, kind="ExternalOutput").ap()
    amax_h = nc.dram_tensor("amax_h", [1, 1], F32, kind="ExternalOutput").ap()
    scr2 = nc.dram_tensor("scr2", [1, 2], F32, kind="Internal").ap()
    am_bounce = nc.dram_tensor("am_bounce", [128], F32, kind="Internal").ap()

    sc = Script()
    with ExitStack() as es:
        A = lambda n, sh, dt: es.enter_context(nc.sbuf_tensor(n, sh, dt))
        PS = lambda n, sh, dt: es.enter_context(nc.psum_tensor(n, sh, dt))
        stage_f = A("stage_f", [128, 2 * 1024], F32)
        w_q_t = A("w_q_t", [128, 2 * 1024], BF16)
        w_in_q = A("w_in_q", [128, 8 * FF], BF16)
        xi = A("xi", [128, 8 * T], BF16)
        hb = A("hb", [128, 2 * 512], F32)

        def sl2f(buf, p):
            return buf[:, p * 1024:(p + 1) * 1024]

        def sl2b(buf, p):
            return buf[:, p * 1024:(p + 1) * 1024]

        def hbsl(p):
            return hb[:, p * 512:(p + 1) * 512]
        binb = A("binb", [128, 32], F32)
        sw = A("sw", [128, 32], F32)
        evsc = A("evsc", [128, 32], F32)
        aw = A("aw", [128, 2], F32)
        rw = A("rw", [128, 2], F32)
        aw2 = A("aw2", [128, 1], F32)
        hacc = A("hacc", [128, 1], F32)
        am8row = A("am8row", [1, 8], F32)
        am = A("am", [1, 1], F32)
        sc2 = A("sc2", [1, 2], F32)
        rsb = A("rsb", [128, 2], F32)
        amT = A("amT", [1, 128], F32)
        am1 = A("am1", [1, 1], F32)
        ident = A("ident", [128, 128], BF16)
        ps_m0 = PS("ps_m0", [128, 512], F32)
        ps_m1 = PS("ps_m1", [128, 512], F32)
        ps_tb0 = PS("ps_tb0", [128, 128], BF16)
        ps_tb1 = PS("ps_tb1", [128, 128], BF16)
        psm = [ps_m0, ps_m1]
        pstb = [ps_tb0, ps_tb1]

        add_identity(sc, ident)
        sc.c("vector", lambda v: v.memset(hacc[:], 0.0))
        add_scalar_prep(sc, dict(am8row=am8row, am=am, sc2=sc2, rsb=rsb,
                                 scr2=scr2), amax8, 32767.0, 1e-8, "a16")
        sc.d("sync", lambda sync: sync.dma_start(out=binb[:], in_=b_in_t))

        for fb in range(32):
            p = fb % 2
            kst, kaw, krw, kwq = f"st:{p}", f"aw:{p}", f"rw:{p}", f"wq:{p}"
            sc.d("sync", lambda sync, fb=fb, p=p: sync.dma_start(
                out=sl2f(stage_f, p), in_=w_inT[ts(fb, 128), :]),
                r=[], w=[kst])
            sc.c("vector", lambda v, p=p: v.reduce_max(
                aw[:, p:p + 1], sl2f(stage_f, p), axis=AX.X,
                apply_absolute_value=True), r=[kst], w=[kaw])
            sc.c("vector", lambda v, p=p: v.tensor_scalar(
                aw[:, p:p + 1], aw[:, p:p + 1], 1e-8, None, ALU.max),
                r=[kaw], w=[kaw])
            sc.c("vector", lambda v, p=p: v.reciprocal(
                rw[:, p:p + 1], aw[:, p:p + 1]), r=[kaw], w=[krw])
            sc.c("vector", lambda v, fb=fb, p=p: v.tensor_scalar(
                sw[:, fb:fb + 1], aw[:, p:p + 1], 1.0 / 127.0, None, ALU.mult),
                r=[kaw], w=["sw"])
            sc.c("vector", lambda v, p=p: v.tensor_scalar(
                rw[:, p:p + 1], rw[:, p:p + 1], 127.0, None, ALU.mult),
                r=[krw], w=[krw])
            sc.c("vector", lambda v, p=p: v._custom_dve(
                ROUND_SCALE, out=sl2b(w_q_t, p), in0=sl2f(stage_f, p),
                s0=rw[:, p:p + 1], s1=1.0, imm2=MAGIC),
                r=[kst, krw], w=[kwq])
            for j in range(8):
                jp = j % 2
                sc.c("tensor", lambda te, j=j, p=p, jp=jp: te.transpose(
                    pstb[jp][:], sl2b(w_q_t, p)[:, ts(j, 128)], ident[:]),
                    r=[kwq], w=[f"pst:{jp}"])
                sc.c("vector", lambda v, j=j, fb=fb, jp=jp: v.tensor_copy(
                    w_in_q[:, j * FF + fb * 128: j * FF + fb * 128 + 128],
                    pstb[jp][:]), r=[f"pst:{jp}"], w=["winq"])
        sc.c("vector", lambda v: v.tensor_scalar(
            evsc[:], sw[:], rsb[:, 1:2], None, ALU.mult),
            r=["sw"], w=["evsc"])

        for idx in range(64):
            db, tc = divmod(idx, 8)
            p = idx % 2
            sc.d("sync", lambda sync, db=db, tc=tc, p=p: sync.dma_start(
                out=sl2f(stage_f, p)[:, 0:512],
                in_=x1T[ts(db, 128), ts(tc, 512)]), r=[], w=[f"st:{p}"])
            sc.c("vector", lambda v, db=db, tc=tc, p=p: v._custom_dve(
                ROUND_SCALE, out=xi[:, db * T + tc * 512: db * T + tc * 512 + 512],
                in0=sl2f(stage_f, p)[:, 0:512], s0=rsb[:, 0:1], s1=1.0,
                imm2=MAGIC), r=[f"st:{p}"], w=["xi"])

        for idx in range(32 * 8):
            fb, tc = divmod(idx, 8)
            p = idx % 2
            def mmh(te, fb=fb, tc=tc, p=p):
                ins = []
                for db in range(8):
                    ins.append(te.matmul(
                        psm[p][:],
                        w_in_q[:, db * FF + fb * 128: db * FF + fb * 128 + 128],
                        xi[:, db * T + tc * 512: db * T + tc * 512 + 512],
                        start=(db == 0), stop=(db == 7)))
                return ins[-1]
            sc.c("tensor", mmh, r=["winq", "xi"], w=[f"psm:{p}"])
            sc.c("scalar", lambda s, fb=fb, p=p: s.activation(
                out=hbsl(p), in_=psm[p][:], func=AF.Gelu,
                scale=evsc[:, fb:fb + 1], bias=binb[:, fb:fb + 1]),
                r=[f"psm:{p}", "evsc"], w=[f"hb:{p}"])
            sc.c("vector", lambda v, p=p: v.reduce_max(
                aw2[:], hbsl(p), axis=AX.X,
                apply_absolute_value=True), r=[f"hb:{p}"], w=["aw2"])
            sc.c("vector", lambda v: v.tensor_tensor(
                hacc[:], hacc[:], aw2[:], ALU.max), r=["aw2", "hacc"], w=["hacc"])
            sc.d("sync", lambda sync, fb=fb, tc=tc, p=p: sync.dma_start(
                out=hidT[ts(fb, 128), ts(tc, 512)], in_=hbsl(p)),
                r=[f"hb:{p}"], w=["hidT"])

        add_amax_finalize(sc, hacc, amT, am1, am_bounce, amax_h)
        sc.run(nc)
    mybir.codegen_inst_isa_subclasses(nc)
    return nc



def build_l0_ffn_b():
    nc = bass.Bass("TRN2", target_bir_lowering=False, debug=False)
    x1T = nc.dram_tensor("x1T", [D, T], F32, kind="ExternalInput").ap()
    hidT = nc.dram_tensor("hidT", [FF, T], F32, kind="ExternalInput").ap()
    amax8 = nc.dram_tensor("amax8", [8, 1], F32, kind="ExternalInput").ap()
    w_out = nc.dram_tensor("w_out", [FF, D], F32, kind="ExternalInput").ap()
    b_out_t = nc.dram_tensor("b_out_t", [128, 8], F32, kind="ExternalInput").ap()
    x2T = nc.dram_tensor("x2T", [D, T], F32, kind="ExternalOutput").ap()
    scr2 = nc.dram_tensor("scr2", [1, 2], F32, kind="Internal").ap()

    sc = Script()
    with ExitStack() as es:
        A = lambda n, sh, dt: es.enter_context(nc.sbuf_tensor(n, sh, dt))
        PS = lambda n, sh, dt: es.enter_context(nc.psum_tensor(n, sh, dt))
        stage_f = A("stage_f", [128, 2 * 1024], F32)
        w_out_q = A("w_out_q", [128, 32 * 1024], BF16)
        hop = A("hop", [128, 32 * 512], BF16)
        hld = A("hld", [128, 2 * 512], F32)
        yr = A("yr", [128, 8 * 512], F32)
        xtile = A("xtile", [128, 2 * 512], F32)
        sq = A("sq", [128, 2 * 512], F32)
        x2t = A("x2t", [128, 2 * 512], F32)
        tprod = A("tprod", [128, 2 * 512], F32)

        def sl2f(buf, p):
            return buf[:, p * 1024:(p + 1) * 1024]

        def sl2h(buf, p):
            return buf[:, p * 512:(p + 1) * 512]
        bout = A("bout", [128, 8], F32)
        sw = A("sw", [128, 32], F32)
        cf = A("cf", [128, 32], F32)
        aw = A("aw", [128, 2], F32)
        rw = A("rw", [128, 2], F32)
        am8row = A("am8row", [1, 8], F32)
        am = A("am", [1, 1], F32)
        sc2 = A("sc2", [1, 2], F32)
        rsb = A("rsb", [128, 2], F32)
        lnrows = A("lnrows", [1, 512 * 6], F32)
        ones_col = A("ones_col", [128, 1], F32)
        ones_row1 = A("ones_row1", [1, 128], F32)
        cpool = A("cpool", [128, 1], F32)
        ps_m0 = PS("ps_m0", [128, 512], F32)
        ps_m1 = PS("ps_m1", [128, 512], F32)
        psm = [ps_m0, ps_m1]
        ps_sum = PS("ps_sum", [1, 512], F32)
        ps_sq = PS("ps_sq", [1, 512], F32)
        ps_al = PS("ps_al", [128, 512], F32)
        ps_be = PS("ps_be", [128, 512], F32)

        add_const_pool(nc, sc, cpool, [1e-5])
        sc.c("vector", lambda v: v.memset(ones_col[:], 1.0))
        sc.c("vector", lambda v: v.memset(ones_row1[:], 1.0))
        add_scalar_prep(sc, dict(am8row=am8row, am=am, sc2=sc2, rsb=rsb,
                                 scr2=scr2), amax8, 32767.0, 1e-8, "a16")
        sc.d("sync", lambda sync: sync.dma_start(out=bout[:], in_=b_out_t))

        for fb in range(32):
            p = fb % 2
            kst, kaw, krw = f"st:{p}", f"aw:{p}", f"rw:{p}"
            sc.d("sync", lambda sync, fb=fb, p=p: sync.dma_start(
                out=sl2f(stage_f, p), in_=w_out[ts(fb, 128), :]),
                r=[], w=[kst])
            sc.c("vector", lambda v, p=p: v.reduce_max(
                aw[:, p:p + 1], sl2f(stage_f, p), axis=AX.X,
                apply_absolute_value=True), r=[kst], w=[kaw])
            sc.c("vector", lambda v, p=p: v.tensor_scalar(
                aw[:, p:p + 1], aw[:, p:p + 1], 1e-8, None, ALU.max),
                r=[kaw], w=[kaw])
            sc.c("vector", lambda v, p=p: v.reciprocal(
                rw[:, p:p + 1], aw[:, p:p + 1]), r=[kaw], w=[krw])
            sc.c("vector", lambda v, fb=fb, p=p: v.tensor_scalar(
                sw[:, fb:fb + 1], aw[:, p:p + 1], 1.0 / 127.0, None, ALU.mult),
                r=[kaw], w=["sw"])
            sc.c("vector", lambda v, p=p: v.tensor_scalar(
                rw[:, p:p + 1], rw[:, p:p + 1], 127.0, None, ALU.mult),
                r=[krw], w=[krw])
            sc.c("vector", lambda v, fb=fb, p=p: v._custom_dve(
                ROUND_SCALE, out=w_out_q[:, ts(fb, 1024)], in0=sl2f(stage_f, p),
                s0=rw[:, p:p + 1], s1=1.0, imm2=MAGIC),
                r=[kst, krw], w=["woq"])
        sc.c("vector", lambda v: v.tensor_scalar(
            cf[:], sw[:], rsb[:, 1:2], None, ALU.mult), r=["sw"], w=["cf"])

        for tc in range(8):
            for fb in range(32):
                p = fb % 2
                sc.d("sync", lambda sync, fb=fb, tc=tc, p=p: sync.dma_start(
                    out=sl2h(hld, p), in_=hidT[ts(fb, 128), ts(tc, 512)]),
                    r=[], w=[f"hld:{p}"])
                sc.c("vector", lambda v, fb=fb, p=p: v._custom_dve(
                    ROUND_SCALE, out=hop[:, ts(fb, 512)], in0=sl2h(hld, p),
                    s0=rsb[:, 0:1], s1=cf[:, fb:fb + 1], imm2=MAGIC),
                    r=[f"hld:{p}", "cf"], w=[f"hop:{fb}"])
            for do in range(8):
                p = do % 2

                def mmo(te, do=do, p=p):
                    ins = []
                    for fb in range(32):
                        ins.append(te.matmul(
                            psm[p][:],
                            w_out_q[:, fb * 1024 + do * 128: fb * 1024 + do * 128 + 128],
                            hop[:, ts(fb, 512)],
                            start=(fb == 0), stop=(fb == 31)))
                    return ins[-1]
                sc.c("tensor", mmo, r=["woq"] + [f"hop:{fb}" for fb in range(32)],
                     w=[f"psm:{p}"])
                sc.d("sync", lambda sync, do=do, tc=tc, p=p: sync.dma_start(
                    out=sl2h(xtile, p), in_=x1T[ts(do, 128), ts(tc, 512)]),
                    r=[], w=[f"xt:{p}"])
                sc.c("vector", lambda v, do=do, p=p: v.scalar_tensor_tensor(
                    yr[:, ts(do, 512)], psm[p][:], bout[:, do:do + 1],
                    sl2h(xtile, p), ALU.add, ALU.add),
                    r=[f"psm:{p}", f"xt:{p}"], w=[f"yr:{do}"])
                sc.c("scalar", lambda s, do=do, p=p: s.activation(
                    out=sl2h(sq, p), in_=yr[:, ts(do, 512)], func=AF.Square),
                    r=[f"yr:{do}"], w=[f"sq:{p}"])

                def mmstat(te, do=do, p=p):
                    te.matmul(ps_sum[:], ones_col[:], yr[:, ts(do, 512)],
                              start=(do == 0), stop=(do == 7))
                    return te.matmul(ps_sq[:], ones_col[:], sl2h(sq, p),
                                     start=(do == 0), stop=(do == 7))
                sc.c("tensor", mmstat, r=[f"yr:{do}", f"sq:{p}"],
                     w=["pssum", "pssq"])
            add_ln_rows_k(sc, lnrows, ps_sum, ps_sq, ps_al, ps_be, ones_row1,
                          1.0 / D, 1e-5)
            for do in range(8):
                p = do % 2
                sc.c("vector", lambda v, do=do, p=p: v.tensor_tensor(
                    sl2h(tprod, p), yr[:, ts(do, 512)], ps_al[:], ALU.mult),
                    r=[f"yr:{do}", "psal"], w=[f"tp:{p}"])
                sc.c("vector", lambda v, p=p: v.tensor_tensor(
                    sl2h(x2t, p), sl2h(tprod, p), ps_be[:], ALU.add),
                    r=[f"tp:{p}", "psbe"], w=[f"x2t:{p}"])
                sc.d("sync", lambda sync, do=do, tc=tc, p=p: sync.dma_start(
                    out=x2T[ts(do, 128), ts(tc, 512)], in_=sl2h(x2t, p)),
                    r=[f"x2t:{p}"], w=["x2T"])
        sc.run(nc)
    mybir.codegen_inst_isa_subclasses(nc)
    return nc



def build_ssm(phases="123", dbg=False):
    nc = bass.Bass("TRN2", target_bir_lowering=False, debug=False)
    x2T = nc.dram_tensor("x2T", [D, T], F32, kind="ExternalInput").ap()
    a_log = nc.dram_tensor("a_log", [D, N16], F32, kind="ExternalInput").ap()
    w_delta = nc.dram_tensor("w_delta", [D, D], F32, kind="ExternalInput").ap()
    b_delta_t = nc.dram_tensor("b_delta_t", [128, 8], F32, kind="ExternalInput").ap()
    w_b = nc.dram_tensor("w_b", [D, N16], F32, kind="ExternalInput").ap()
    w_c = nc.dram_tensor("w_c", [D, N16], F32, kind="ExternalInput").ap()
    dskip_t = nc.dram_tensor("dskip_t", [128, 8], F32, kind="ExternalInput").ap()
    nw_t = nc.dram_tensor("nw_t", [128, 8], F32, kind="ExternalInput").ap()
    x3T = nc.dram_tensor("x3T", [D, T], F32, kind="ExternalOutput").ap()
    amax3 = nc.dram_tensor("amax3", [1, 1], F32, kind="ExternalOutput").ap()
    ik = "ExternalOutput" if dbg else "Internal"
    xn_hbm = nc.dram_tensor("xn_hbm", [D, T], F32, kind=ik).ap()
    xnb_hbm = nc.dram_tensor("xnb_hbm", [D, T], BF16, kind="Internal").ap()
    _dl_hbm = nc.dram_tensor("dl_hbm", [D, T], F32, kind=ik).ap()
    bp_hbm = nc.dram_tensor("bp_hbm", [N16, T], F32, kind=ik).ap()
    c_hbm = nc.dram_tensor("c_hbm", [N16, T], F32, kind=ik).ap()
    am_bounce = nc.dram_tensor("am_bounce", [128], F32, kind="Internal").ap()

    DT_MAX, DT_MIN = 10.0, 1e-4
    CEN = (DT_MAX + DT_MIN) / 2.0
    HALF = (DT_MAX - DT_MIN) / 2.0

    sc = Script()
    with ExitStack() as es:
        A = lambda n, sh, dt: es.enter_context(nc.sbuf_tensor(n, sh, dt))
        PS = lambda n, sh, dt: es.enter_context(nc.psum_tensor(n, sh, dt))
        stage_f = A("stage_f", [128, 1024], F32)
        wdel_b = A("wdel_b", [128, 8 * 1024], BF16)
        wb_b = A("wb_b", [128, 8 * N16], BF16)
        wc_b = A("wc_b", [128, 8 * N16], BF16)
        ahalf = A("ahalf", [128, 8 * N16], F32)
        aexp = A("aexp", [128, N16], F32)
        bdel = A("bdel", [128, 8], F32)
        dskip = A("dskip", [128, 8], F32)
        nww = A("nww", [128, 8], F32)
        xch = A("xch", [128, 8 * 512], F32)
        xnch = A("xnch", [128, 8 * 512], F32)
        dlch = A("dlch", [128, 8 * 512], F32)
        uch = A("uch", [128, 8 * 512], F32)
        ytc = A("ytc", [128, 8 * 512], F32)
        yrb = A("yrb", [128, 8 * 512], F32)
        sq = A("sq", [128, 512], F32)
        sqd = A("sqd", [128, 512], F32)
        x3t = A("x3t", [128, 512], F32)
        tprod = A("tprod", [128, 512], F32)
        xbf = A("xbf", [128, 512], BF16)
        rhs_bf = A("rhs_bf", [128, 2 * 8 * 512], BF16)
        sp1 = A("sp1", [128, 2 * 512], F32)
        sp2 = A("sp2", [128, 2 * 512], F32)
        sp3 = A("sp3", [128, 2 * 512], F32)
        dnm = A("dnm", [128, 2 * 512], F32)
        wre = A("wre", [128, 2 * 512], F32)
        are = A("are", [128, 2 * 512], F32)
        qre = A("qre", [128, 2 * 512], F32)
        bre = A("bre", [128, 2 * 512], F32)
        bp1 = A("bp1", [128, 2 * 512], F32)
        c1 = A("c1", [128, 2 * 512], F32)
        hn = A("hn", [128, 512], F32)
        hst = A("hst", [128, 128], F32)
        bprow = A("bprow", [16, 512], F32)
        lnrows = A("lnrows", [1, 512 * 6], F32)
        ones_col = A("ones_col", [128, 1], F32)
        ones_row1 = A("ones_row1", [1, 128], F32)
        aw2 = A("aw2", [128, 1], F32)
        amax_acc = A("amax_acc", [128, 1], F32)
        amT = A("amT", [1, 128], F32)
        am1 = A("am1", [1, 1], F32)
        cpool = A("cpool", [128, 6], F32)
        ps_m0 = PS("ps_m0", [128, 512], F32)
        ps_m1 = PS("ps_m1", [128, 512], F32)
        ps_bp = PS("ps_bp", [16, 512], F32)
        ps_sum = PS("ps_sum", [1, 512], F32)
        ps_sq = PS("ps_sq", [1, 512], F32)
        ps_al = PS("ps_al", [128, 512], F32)
        ps_be = PS("ps_be", [128, 512], F32)
        psm = [ps_m0, ps_m1]

        S2 = 512

        def sl(buf, p, wdt=S2):
            return buf[:, p * wdt:(p + 1) * wdt]

        add_const_pool(nc, sc, cpool, [1e-5, 1e-6, 1.0, -CEN / HALF, -1.0])
        sc.c("vector", lambda v: v.memset(ones_col[:], 1.0))
        sc.c("vector", lambda v: v.memset(ones_row1[:], 1.0))
        sc.c("vector", lambda v: v.memset(amax_acc[:], 0.0))
        sc.c("vector", lambda v: v.memset(hst[:], 0.0))
        sc.d("sync", lambda sync: [sync.dma_start(out=bdel[:], in_=b_delta_t),
                                   sync.dma_start(out=dskip[:], in_=dskip_t),
                                   sync.dma_start(out=nww[:], in_=nw_t)], n=3)
        for db in range(8):
            sc.d("sync", lambda sync, db=db: sync.dma_start(
                out=aexp[:], in_=a_log[ts(db, 128), :]))
            sc.c("scalar", lambda s: s.activation(
                out=aexp[:], in_=aexp[:], func=AF.Exp))
            sc.c("vector", lambda v, db=db: v.tensor_scalar(
                ahalf[:, ts(db, N16)], aexp[:], 0.5, None, ALU.mult))
        for db in range(8):
            sc.d("sync", lambda sync, db=db: sync.dma_start(
                out=stage_f[:], in_=w_delta[ts(db, 128), :]))
            sc.c("scalar", lambda s, db=db: s.activation(
                out=wdel_b[:, ts(db, 1024)], in_=stage_f[:], func=AF.Copy))
            sc.d("sync", lambda sync, db=db: sync.dma_start(
                out=stage_f[:, 0:N16], in_=w_b[ts(db, 128), :]))
            sc.c("scalar", lambda s, db=db: s.activation(
                out=wb_b[:, ts(db, N16)], in_=stage_f[:, 0:N16], func=AF.Copy))
            sc.d("sync", lambda sync, db=db: sync.dma_start(
                out=stage_f[:, 0:N16], in_=w_c[ts(db, 128), :]))
            sc.c("scalar", lambda s, db=db: s.activation(
                out=wc_b[:, ts(db, N16)], in_=stage_f[:, 0:N16], func=AF.Copy))

        for tc in (range(8) if "1" in phases else []):
            sc.d("sync", lambda sync, tc=tc: [sync.dma_start(
                out=xch[:, ts(db, 512)], in_=x2T[ts(db, 128), ts(tc, 512)])
                for db in range(8)], n=8)
            for db in range(8):
                sc.c("scalar", lambda s, db=db: s.activation(
                    out=sq[:], in_=xch[:, ts(db, 512)], func=AF.Square))
                sc.c("tensor", lambda te, db=db: te.matmul(
                    ps_sum[:], ones_col[:], sq[:],
                    start=(db == 0), stop=(db == 7)))
            sc.c("vector", lambda v: v.tensor_copy(lnrows[:, 0:512], ps_sum[:]))
            sc.c("vector", lambda v: v.tensor_scalar(
                lnrows[:, 512:1024], lnrows[:, 0:512], 1.0 / D, None, ALU.mult))
            sc.c("scalar", lambda s: s.activation(
                out=lnrows[:, 0:512], in_=lnrows[:, 512:1024], func=AF.Sqrt,
                bias=1e-6))
            sc.c("vector", lambda v: v.reciprocal(
                lnrows[:, 512:1024], lnrows[:, 0:512]))
            sc.c("tensor", lambda te: te.matmul(
                ps_al[:], ones_row1[:], lnrows[:, 512:1024], start=True, stop=True))
            for db in range(8):
                sc.c("vector", lambda v, db=db: v.scalar_tensor_tensor(
                    xnch[:, ts(db, 512)], xch[:, ts(db, 512)], nww[:, db:db + 1],
                    ps_al[:], ALU.mult, ALU.mult))
                sc.c("scalar", lambda s, db=db: s.activation(
                    out=xbf[:], in_=xnch[:, ts(db, 512)], func=AF.Copy))
                sc.d("sync", lambda sync, db=db, tc=tc: [
                    sync.dma_start(out=xn_hbm[ts(db, 128), ts(tc, 512)],
                                   in_=xnch[:, ts(db, 512)]),
                    sync.dma_start(out=xnb_hbm[ts(db, 128), ts(tc, 512)],
                                   in_=xbf[:])], n=2)

        DLH = lambda tc: f"dlh:{tc}"

        def emit_p2(tc):
            p2 = tc % 2
            rkey = f"rhs:{p2}"
            sc.d("sync", lambda sync, tc=tc, p2=p2: [sync.dma_start(
                out=rhs_bf[:, (p2 * 8 + db) * 512:(p2 * 8 + db) * 512 + 512],
                in_=xnb_hbm[ts(db, 128), ts(tc, 512)]) for db in range(8)],
                n=8, r=["xnbh"], w=[rkey])
            for ob in range(8):
                obp = ob % 2
                pk = f"psm:{obp}"
                k1, k2, k3 = f"sp1:{obp}", f"sp2:{obp}", f"sp3:{obp}"

                def mmd(te, ob=ob, p2=p2, obp=obp):
                    ins = []
                    for db in range(8):
                        ins.append(te.matmul(
                            psm[obp][:],
                            wdel_b[:, db * 1024 + ob * 128: db * 1024 + ob * 128 + 128],
                            rhs_bf[:, (p2 * 8 + db) * 512:(p2 * 8 + db) * 512 + 512],
                            start=(db == 0), stop=(db == 7)))
                    return ins[-1]
                sc.c("tensor", mmd, r=[rkey], w=[pk])
                sc.c("vector", lambda v, ob=ob, obp=obp: v.tensor_scalar(
                    sl(sp1, obp), psm[obp][:], bdel[:, ob:ob + 1], None, ALU.add),
                    r=[pk], w=[k1])
                sc.c("scalar", lambda s, obp=obp: s.activation(
                    out=sl(sp2, obp), in_=sl(sp1, obp), func=AF.Abs),
                    r=[k1], w=[k2])
                sc.c("scalar", lambda s, obp=obp: s.activation(
                    out=sl(sp2, obp), in_=sl(sp2, obp), func=AF.Exp, scale=-1.0),
                    r=[k2], w=[k2])
                sc.c("scalar", lambda s, obp=obp: s.activation(
                    out=sl(sp2, obp), in_=sl(sp2, obp), func=AF.Ln, bias=1.0),
                    r=[k2], w=[k2])
                sc.c("vector", lambda v, obp=obp: v.tensor_scalar(
                    sl(sp3, obp), sl(sp1, obp), 0.0, None, ALU.max),
                    r=[k1], w=[k3])
                sc.c("vector", lambda v, obp=obp: v.tensor_tensor(
                    sl(sp2, obp), sl(sp2, obp), sl(sp3, obp), ALU.add),
                    r=[k2, k3], w=[k2])
                sc.c("scalar", lambda s, obp=obp: s.activation(
                    out=sl(sp3, obp), in_=sl(sp2, obp), func=AF.Tanh,
                    scale=1.0 / HALF, bias=-CEN / HALF),
                    r=[k2], w=[k3])
                sc.c("vector", lambda v, obp=obp: v.tensor_scalar(
                    sl(sp3, obp), sl(sp3, obp), HALF, CEN, ALU.mult, ALU.add),
                    r=[k3], w=[k3])
                sc.d("sync", lambda sync, ob=ob, tc=tc, obp=obp: sync.dma_start(
                    out=_dl_hbm[ts(ob, 128), ts(tc, 512)], in_=sl(sp3, obp)),
                    r=[k3], w=[DLH(tc)])
            for wsb, hbm in ((wb_b, bp_hbm), (wc_b, c_hbm)):
                def mmb(te, wsb=wsb, p2=p2):
                    ins = []
                    for db in range(8):
                        ins.append(te.matmul(
                            ps_bp[:], wsb[:, ts(db, N16)],
                            rhs_bf[:, (p2 * 8 + db) * 512:(p2 * 8 + db) * 512 + 512],
                            start=(db == 0), stop=(db == 7)))
                    return ins[-1]
                sc.c("tensor", mmb, r=[rkey], w=["psbp"])
                sc.c("vector", lambda v: v.tensor_copy(bprow[:], ps_bp[:]),
                     r=["psbp"], w=["bprow"])
                sc.d("sync", lambda sync, hbm=hbm, tc=tc: sync.dma_start(
                    out=hbm[:, ts(tc, 512)], in_=bprow[:]),
                    r=["bprow"], w=[f"bpch:{tc}"])

        def emit_p3(tc):
            sc.d("sync", lambda sync, tc=tc: [sync.dma_start(
                out=dlch[:, ts(db, 512)], in_=_dl_hbm[ts(db, 128), ts(tc, 512)])
                for db in range(8)], n=8, r=[DLH(tc)], w=["dlch"])
            sc.d("sync", lambda sync, tc=tc: [sync.dma_start(
                out=xnch[:, ts(db, 512)], in_=xn_hbm[ts(db, 128), ts(tc, 512)])
                for db in range(8)], n=8, r=["xnh"], w=["xnch"])
            for db in range(8):
                sc.c("scalar", lambda s, db=db: s.activation(
                    out=sqd[:], in_=dlch[:, ts(db, 512)], func=AF.Sqrt),
                    r=["dlch"], w=["sqd"])
                sc.c("vector", lambda v, db=db: v.scalar_tensor_tensor(
                    uch[:, ts(db, 512)], sqd[:], 0.5, xnch[:, ts(db, 512)],
                    ALU.mult, ALU.mult), r=["sqd", "xnch"], w=["uch"])

            def emit_bcast(n):
                np_ = n % 2
                sc.d("sync", lambda sync, n=n, np_=np_: [
                    sync.dma_start(out=sl(bp1, np_),
                                   in_=bp_hbm[n:n + 1, ts(tc, 512)].broadcast_to([128, 512])),
                    sync.dma_start(out=sl(c1, np_),
                                   in_=c_hbm[n:n + 1, ts(tc, 512)].broadcast_to([128, 512]))],
                    n=2, r=[f"bpch:{tc}"], w=[f"bp:{np_}", f"c:{np_}"])

            def emit_denom(i):
                n, db = divmod(i, 8)
                ip = i % 2
                sc.c("scalar", lambda s, n=n, db=db, ip=ip: s.activation(
                    out=sl(dnm, ip), in_=dlch[:, ts(db, 512)], func=AF.Identity,
                    scale=ahalf[:, db * N16 + n: db * N16 + n + 1], bias=1.0),
                    r=["dlch"], w=[f"dnm:{ip}"])

            emit_bcast(0)
            emit_bcast(1)
            emit_denom(0)
            emit_denom(1)
            for n in range(N16):
                nb = n % 2
                for db in range(8):
                    i = n * 8 + db
                    ip = i % 2
                    kd, kw, ka, kq, kb = (f"dnm:{ip}", f"w:{ip}", f"a:{ip}",
                                          f"q:{ip}", f"b:{ip}")
                    sc.c("vector", lambda v, ip=ip: v.reciprocal_approx_fast(
                        out=sl(wre, ip), in_=sl(dnm, ip)), r=[kd], w=[kw])
                    sc.c("vector", lambda v, db=db, ip=ip, nb=nb: v.tensor_tensor(
                        sl(qre, ip), uch[:, ts(db, 512)], sl(bp1, nb), ALU.mult),
                        r=["uch", f"bp:{nb}"], w=[kq])
                    sc.c("scalar", lambda s, ip=ip: s.activation(
                        out=sl(are, ip), in_=sl(wre, ip), func=AF.Identity,
                        scale=2.0, bias=-1.0), r=[kw], w=[ka])
                    if i + 2 < N16 * 8:
                        emit_denom(i + 2)
                    sc.c("vector", lambda v, ip=ip: v.scalar_tensor_tensor(
                        sl(bre, ip), sl(are, ip), 1.0, sl(qre, ip),
                        ALU.add, ALU.mult), r=[ka, kq], w=[kb])
                    sc.c("vector", lambda v, db=db, n=n, ip=ip: v.tensor_tensor_scan(
                        hn[:], sl(are, ip), sl(bre, ip),
                        hst[:, db * N16 + n: db * N16 + n + 1], ALU.mult, ALU.add),
                        r=[ka, kb, "hst"], w=["hn"])
                    sc.c("vector", lambda v, db=db, n=n: v.tensor_copy(
                        hst[:, db * N16 + n: db * N16 + n + 1], hn[:, 511:512]),
                        r=["hn"], w=["hst"])
                    sc.c("vector", lambda v, ip=ip, nb=nb: v.tensor_tensor(
                        sl(qre, ip), hn[:], sl(c1, nb), ALU.mult),
                        r=["hn", f"c:{nb}"], w=[kq])
                    if n == 0:
                        sc.c("vector", lambda v, db=db, ip=ip: v.tensor_copy(
                            ytc[:, ts(db, 512)], sl(qre, ip)),
                            r=[kq], w=[f"yt:{db}"])
                    else:
                        sc.c("vector", lambda v, db=db, ip=ip: v.tensor_tensor(
                            ytc[:, ts(db, 512)], ytc[:, ts(db, 512)], sl(qre, ip),
                            ALU.add), r=[kq, f"yt:{db}"], w=[f"yt:{db}"])
                if n + 2 < N16:
                    emit_bcast(n + 2)
            sc.d("sync", lambda sync, tc=tc: [sync.dma_start(
                out=xch[:, ts(db, 512)], in_=x2T[ts(db, 128), ts(tc, 512)])
                for db in range(8)], n=8, r=["xch2"], w=["xch2"])
            for db in range(8):
                sc.c("vector", lambda v, db=db: v.scalar_tensor_tensor(
                    tprod[:], xnch[:, ts(db, 512)], dskip[:, db:db + 1],
                    ytc[:, ts(db, 512)], ALU.mult, ALU.add),
                    r=["xnch", f"yt:{db}"], w=["tp"])
                sc.c("vector", lambda v, db=db: v.tensor_tensor(
                    yrb[:, ts(db, 512)], tprod[:], xch[:, ts(db, 512)], ALU.add),
                    r=["tp", "xch2"], w=[f"yr:{db}"])
                sc.c("scalar", lambda s, db=db: s.activation(
                    out=sq[:], in_=yrb[:, ts(db, 512)], func=AF.Square),
                    r=[f"yr:{db}"], w=["sq1"])

                def mmstat(te, db=db):
                    te.matmul(ps_sum[:], ones_col[:], yrb[:, ts(db, 512)],
                              start=(db == 0), stop=(db == 7))
                    return te.matmul(ps_sq[:], ones_col[:], sq[:],
                                     start=(db == 0), stop=(db == 7))
                sc.c("tensor", mmstat, r=[f"yr:{db}", "sq1"], w=["pssum", "pssq"])
            add_ln_rows_k(sc, lnrows, ps_sum, ps_sq, ps_al, ps_be, ones_row1,
                          1.0 / D, 1e-5)
            for db in range(8):
                sc.c("vector", lambda v, db=db: v.tensor_tensor(
                    tprod[:], yrb[:, ts(db, 512)], ps_al[:], ALU.mult),
                    r=[f"yr:{db}", "psal"], w=["tp"])
                sc.c("vector", lambda v: v.tensor_tensor(
                    x3t[:], tprod[:], ps_be[:], ALU.add),
                    r=["tp", "psbe"], w=["x3t"])
                sc.c("vector", lambda v: v.reduce_max(
                    aw2[:], x3t[:], axis=AX.X, apply_absolute_value=True),
                    r=["x3t"], w=["aw2"])
                sc.c("vector", lambda v: v.tensor_tensor(
                    amax_acc[:], amax_acc[:], aw2[:], ALU.max),
                    r=["aw2", "amacc"], w=["amacc"])
                sc.d("sync", lambda sync, db=db, tc=tc: sync.dma_start(
                    out=x3T[ts(db, 128), ts(tc, 512)], in_=x3t[:]),
                    r=["x3t"], w=["x3T"])

        if "3" in phases:
            emit_p2(0)
            for tc in range(8):
                if tc + 1 < 8:
                    emit_p2(tc + 1)
                emit_p3(tc)

        add_amax_finalize(sc, amax_acc, amT, am1, am_bounce, amax3)
        sc.run(nc)
    mybir.codegen_inst_isa_subclasses(nc)
    return nc



def build_l1_ffn_a():
    nc = bass.Bass("TRN2", target_bir_lowering=False, debug=False)
    x3T = nc.dram_tensor("x3T", [D, T], F32, kind="ExternalInput").ap()
    amax8 = nc.dram_tensor("amax8", [8, 1], F32, kind="ExternalInput").ap()
    w_up = nc.dram_tensor("w_up", [I3, D], F32, kind="ExternalInput").ap()
    w_gate = nc.dram_tensor("w_gate", [I3, D], F32, kind="ExternalInput").ap()
    upT = nc.dram_tensor("upT", [I3, T], F32, kind="ExternalOutput").ap()
    gateT = nc.dram_tensor("gateT", [I3, T], F32, kind="ExternalOutput").ap()
    amax_u = nc.dram_tensor("amax_u", [1, 1], F32, kind="ExternalOutput").ap()
    amax_g = nc.dram_tensor("amax_g", [1, 1], F32, kind="ExternalOutput").ap()
    scr2 = nc.dram_tensor("scr2", [1, 2], F32, kind="Internal").ap()
    am_bounce = nc.dram_tensor("am_bounce", [128], F32, kind="Internal").ap()

    sc = Script()
    with ExitStack() as es:
        A = lambda n, sh, dt: es.enter_context(nc.sbuf_tensor(n, sh, dt))
        PS = lambda n, sh, dt: es.enter_context(nc.psum_tensor(n, sh, dt))
        stage_f = A("stage_f", [128, 2 * 1024], F32)
        wq_t = A("wq_t", [128, 2 * 1024], BF16)
        wuq = A("wuq", [128, 8 * I3], BF16)
        wgq = A("wgq", [128, 8 * I3], BF16)
        xq = A("xq", [128, 8 * T], BF16)
        ot = A("ot", [128, 2 * 512], F32)
        awb = A("awb", [128, 2 * 32], F32)
        sbv = A("sbv", [128, 2 * 32], F32)
        rbv = A("rbv", [128, 2 * 32], F32)

        def sl5(buf, p):
            return buf[:, p * 1024:(p + 1) * 1024]

        def sl5b(buf, p):
            return buf[:, p * 1024:(p + 1) * 1024]

        def sl5h(buf, p):
            return buf[:, p * 512:(p + 1) * 512]

        def sl32(buf, p):
            return buf[:, p * 32:(p + 1) * 32]
        aw2 = A("aw2", [128, 1], F32)
        acc_u = A("acc_u", [128, 1], F32)
        acc_g = A("acc_g", [128, 1], F32)
        am8row = A("am8row", [1, 8], F32)
        am = A("am", [1, 1], F32)
        sc2 = A("sc2", [1, 2], F32)
        rsb = A("rsb", [128, 2], F32)
        amT = A("amT", [1, 128], F32)
        am1 = A("am1", [1, 1], F32)
        ident = A("ident", [128, 128], BF16)
        ps_m0 = PS("ps_m0", [128, 512], F32)
        ps_m1 = PS("ps_m1", [128, 512], F32)
        ps_tb0 = PS("ps_tb0", [128, 128], BF16)
        ps_tb1 = PS("ps_tb1", [128, 128], BF16)
        psm = [ps_m0, ps_m1]
        pstb = [ps_tb0, ps_tb1]

        add_identity(sc, ident)
        sc.c("vector", lambda v: v.memset(acc_u[:], 0.0))
        sc.c("vector", lambda v: v.memset(acc_g[:], 0.0))
        add_scalar_prep(sc, dict(am8row=am8row, am=am, sc2=sc2, rsb=rsb,
                                 scr2=scr2), amax8, 32767.0, 1e-12, "qdq")

        for idx in range(64):
            db, tc = divmod(idx, 8)
            p = idx % 2
            sc.d("sync", lambda sync, db=db, tc=tc, p=p: sync.dma_start(
                out=sl5(stage_f, p)[:, 0:512],
                in_=x3T[ts(db, 128), ts(tc, 512)]), r=[], w=[f"st:{p}"])
            sc.c("vector", lambda v, db=db, tc=tc, p=p: v._custom_dve(
                ROUND_SCALE, out=xq[:, db * T + tc * 512: db * T + tc * 512 + 512],
                in0=sl5(stage_f, p)[:, 0:512], s0=rsb[:, 0:1], s1=1.0,
                imm2=MAGIC), r=[f"st:{p}"], w=["xq"])

        for wi, (w_hbm, wq_dst) in enumerate(((w_up, wuq), (w_gate, wgq))):
            for rb in range(24):
                p = (wi * 24 + rb) % 2
                kst, kaw, kwq = f"st:{p}", f"aw:{p}", f"wq:{p}"
                sc.d("sync", lambda sync, rb=rb, p=p, w_hbm=w_hbm: sync.dma_start(
                    out=sl5(stage_f, p), in_=w_hbm[ts(rb, 128), :]),
                    r=[], w=[kst])
                sc.c("vector", lambda v, p=p: v.tensor_reduce(
                    sl32(awb, p), sl5(stage_f, p).rearrange("p (a b) -> p a b", b=32),
                    axis=AX.X, op=ALU.max, apply_absolute_value=True),
                    r=[kst], w=[kaw])
                sc.c("vector", lambda v, p=p: v.tensor_scalar(
                    sl32(sbv, p), sl32(awb, p), 1.0 / 7.0, 1e-12, ALU.mult,
                    ALU.max), r=[kaw], w=[kaw + "s"])
                sc.c("vector", lambda v, p=p: v.reciprocal(
                    sl32(rbv, p), sl32(sbv, p)), r=[kaw + "s"], w=[kaw + "r"])

                def rounds(v, p=p):
                    ins = []
                    for bk in range(32):
                        ins.append(v._custom_dve(
                            ROUND_SCALE, out=sl5b(wq_t, p)[:, ts(bk, 32)],
                            in0=sl5(stage_f, p)[:, ts(bk, 32)],
                            s0=sl32(rbv, p)[:, bk:bk + 1],
                            s1=sl32(sbv, p)[:, bk:bk + 1], imm2=MAGIC))
                    return ins[-1]
                sc.c("vector", rounds, r=[kst, kaw + "s", kaw + "r"], w=[kwq])
                for j in range(8):
                    jp = j % 2
                    sc.c("tensor", lambda te, j=j, p=p, jp=jp: te.transpose(
                        pstb[jp][:], sl5b(wq_t, p)[:, ts(j, 128)], ident[:]),
                        r=[kwq], w=[f"pst:{jp}"])
                    sc.c("vector", lambda v, j=j, rb=rb, jp=jp, wq_dst=wq_dst:
                         v.tensor_copy(
                             wq_dst[:, j * I3 + rb * 128: j * I3 + rb * 128 + 128],
                             pstb[jp][:]), r=[f"pst:{jp}"], w=["wq_res"])

        for wi, (wq_src, out_hbm, acck) in enumerate(((wuq, upT, "accu"),
                                                      (wgq, gateT, "accg"))):
            accb = acc_u if wi == 0 else acc_g
            for idx in range(24 * 8):
                ib, tc = divmod(idx, 8)
                p = idx % 2

                def mmu(te, wq_src=wq_src, ib=ib, tc=tc, p=p):
                    ins = []
                    for db in range(8):
                        ins.append(te.matmul(
                            psm[p][:],
                            wq_src[:, db * I3 + ib * 128: db * I3 + ib * 128 + 128],
                            xq[:, db * T + tc * 512: db * T + tc * 512 + 512],
                            start=(db == 0), stop=(db == 7)))
                    return ins[-1]
                sc.c("tensor", mmu, r=["wq_res", "xq"], w=[f"psm:{p}"])
                sc.c("scalar", lambda s, p=p: s.activation(
                    out=sl5h(ot, p), in_=psm[p][:], func=AF.Copy,
                    scale=rsb[:, 1:2]), r=[f"psm:{p}"], w=[f"ot:{p}"])
                sc.c("vector", lambda v, p=p: v.reduce_max(
                    aw2[:], sl5h(ot, p), axis=AX.X, apply_absolute_value=True),
                    r=[f"ot:{p}"], w=["aw2"])
                sc.c("vector", lambda v, accb=accb, acck=acck: v.tensor_tensor(
                    accb[:], accb[:], aw2[:], ALU.max),
                    r=["aw2", acck], w=[acck])
                sc.d("sync", lambda sync, out_hbm=out_hbm, ib=ib, tc=tc, p=p:
                     sync.dma_start(out=out_hbm[ts(ib, 128), ts(tc, 512)],
                                    in_=sl5h(ot, p)), r=[f"ot:{p}"], w=["outh"])

        add_amax_finalize(sc, acc_u, amT, am1, am_bounce, amax_u)
        add_amax_finalize(sc, acc_g, amT, am1, am_bounce, amax_g)
        sc.run(nc)
    mybir.codegen_inst_isa_subclasses(nc)
    return nc



def build_l1_ffn_b():
    nc = bass.Bass("TRN2", target_bir_lowering=False, debug=False)
    gateT = nc.dram_tensor("gateT", [I3, T], F32, kind="ExternalInput").ap()
    amax8 = nc.dram_tensor("amax8", [8, 1], F32, kind="ExternalInput").ap()
    mT = nc.dram_tensor("mT", [I3, T], F32, kind="ExternalOutput").ap()
    amax_m = nc.dram_tensor("amax_m", [1, 1], F32, kind="ExternalOutput").ap()
    scr2 = nc.dram_tensor("scr2", [1, 2], F32, kind="Internal").ap()
    am_bounce = nc.dram_tensor("am_bounce", [128], F32, kind="Internal").ap()

    sc = Script()
    with ExitStack() as es:
        A = lambda n, sh, dt: es.enter_context(nc.sbuf_tensor(n, sh, dt))
        gt = A("gt", [128, 2 * 512], F32)
        gq = A("gq", [128, 2 * 512], F32)
        sig = A("sig", [128, 2 * 512], F32)
        si2 = A("si2", [128, 2 * 512], F32)
        mm = A("mm_", [128, 2 * 512], F32)

        def sl2(buf, p):
            return buf[:, p * 512:(p + 1) * 512]
        aw2 = A("aw2", [128, 1], F32)
        macc = A("macc", [128, 1], F32)
        am8row = A("am8row", [1, 8], F32)
        am = A("am", [1, 1], F32)
        sc2 = A("sc2", [1, 2], F32)
        rsb = A("rsb", [128, 2], F32)
        amT = A("amT", [1, 128], F32)
        am1 = A("am1", [1, 1], F32)

        sc.c("vector", lambda v: v.memset(macc[:], 0.0))
        add_scalar_prep(sc, dict(am8row=am8row, am=am, sc2=sc2, rsb=rsb,
                                 scr2=scr2), amax8, 32767.0, 1e-12, "qdq")
        for idx in range(24 * 8):
            ib, tc = divmod(idx, 8)
            p = idx % 2
            kg, kq, ks, km = f"gt:{p}", f"gq:{p}", f"si:{p}", f"mm:{p}"
            sc.d("sync", lambda sync, ib=ib, tc=tc, p=p: sync.dma_start(
                out=sl2(gt, p), in_=gateT[ts(ib, 128), ts(tc, 512)]),
                r=[], w=[kg])
            sc.c("vector", lambda v, p=p: v._custom_dve(
                ROUND_SCALE, out=sl2(gq, p), in0=sl2(gt, p), s0=rsb[:, 0:1],
                s1=rsb[:, 1:2], imm2=MAGIC), r=[kg], w=[kq])
            sc.c("scalar", lambda s, p=p: s.activation(
                out=sl2(sig, p), in_=sl2(gq, p), func=AF.Sigmoid),
                r=[kq], w=[ks])
            sc.c("vector", lambda v, p=p: v._custom_dve(
                ROUND_SCALE, out=sl2(si2, p), in0=sl2(sig, p), s0=65536.0,
                s1=1.0, imm2=MAGIC), r=[ks], w=[ks + "b"])
            sc.c("vector", lambda v, p=p: v.tensor_scalar(
                sl2(si2, p), sl2(si2, p), 65535.0, 1.0 / 65536.0, ALU.min,
                ALU.mult), r=[ks + "b"], w=[ks + "b"])
            sc.c("vector", lambda v, p=p: v.tensor_tensor(
                sl2(mm, p), sl2(gq, p), sl2(si2, p), ALU.mult),
                r=[kq, ks + "b"], w=[km])
            sc.c("vector", lambda v, p=p: v.reduce_max(
                aw2[:], sl2(mm, p), axis=AX.X, apply_absolute_value=True),
                r=[km], w=["aw2"])
            sc.c("vector", lambda v: v.tensor_tensor(
                macc[:], macc[:], aw2[:], ALU.max), r=["aw2", "macc"], w=["macc"])
            sc.d("sync", lambda sync, ib=ib, tc=tc, p=p: sync.dma_start(
                out=mT[ts(ib, 128), ts(tc, 512)], in_=sl2(mm, p)),
                r=[km], w=["mT"])
        add_amax_finalize(sc, macc, amT, am1, am_bounce, amax_m)
        sc.run(nc)
    mybir.codegen_inst_isa_subclasses(nc)
    return nc



def build_l1_ffn_c():
    nc = bass.Bass("TRN2", target_bir_lowering=False, debug=False)
    upT = nc.dram_tensor("upT", [I3, T], F32, kind="ExternalInput").ap()
    mT = nc.dram_tensor("mT", [I3, T], F32, kind="ExternalInput").ap()
    amax8u = nc.dram_tensor("amax8u", [8, 1], F32, kind="ExternalInput").ap()
    amax8m = nc.dram_tensor("amax8m", [8, 1], F32, kind="ExternalInput").ap()
    oT = nc.dram_tensor("oT", [I3, T], F32, kind="ExternalOutput").ap()
    amax_o = nc.dram_tensor("amax_o", [1, 1], F32, kind="ExternalOutput").ap()
    scr2 = nc.dram_tensor("scr2", [1, 2], F32, kind="Internal").ap()
    scr2b = nc.dram_tensor("scr2b", [1, 2], F32, kind="Internal").ap()
    am_bounce = nc.dram_tensor("am_bounce", [128], F32, kind="Internal").ap()

    sc = Script()
    with ExitStack() as es:
        A = lambda n, sh, dt: es.enter_context(nc.sbuf_tensor(n, sh, dt))
        ut = A("ut", [128, 2 * 512], F32)
        mt = A("mt", [128, 2 * 512], F32)
        uq = A("uq", [128, 2 * 512], F32)
        mq = A("mq", [128, 2 * 512], F32)
        oo = A("oo", [128, 2 * 512], F32)

        def sl2(buf, p):
            return buf[:, p * 512:(p + 1) * 512]
        aw2 = A("aw2", [128, 1], F32)
        oacc = A("oacc", [128, 1], F32)
        am8row = A("am8row", [1, 8], F32)
        am = A("am", [1, 1], F32)
        sc2 = A("sc2", [1, 2], F32)
        rsbu = A("rsbu", [128, 2], F32)
        rsbm = A("rsbm", [128, 2], F32)
        amT = A("amT", [1, 128], F32)
        am1 = A("am1", [1, 1], F32)

        sc.c("vector", lambda v: v.memset(oacc[:], 0.0))
        add_scalar_prep(sc, dict(am8row=am8row, am=am, sc2=sc2, rsb=rsbu,
                                 scr2=scr2), amax8u, 32767.0, 1e-12, "qdq")
        add_scalar_prep(sc, dict(am8row=am8row, am=am, sc2=sc2, rsb=rsbm,
                                 scr2=scr2b), amax8m, 32767.0, 1e-12, "qdq")
        for idx in range(24 * 8):
            ib, tc = divmod(idx, 8)
            p = idx % 2
            ku, km_, kuq, kmq, ko = (f"ut:{p}", f"mt:{p}", f"uq:{p}",
                                     f"mq:{p}", f"oo:{p}")
            sc.d("sync", lambda sync, ib=ib, tc=tc, p=p: [
                sync.dma_start(out=sl2(ut, p), in_=upT[ts(ib, 128), ts(tc, 512)]),
                sync.dma_start(out=sl2(mt, p), in_=mT[ts(ib, 128), ts(tc, 512)])],
                n=2, r=[], w=[ku, km_])
            sc.c("vector", lambda v, p=p: v._custom_dve(
                ROUND_SCALE, out=sl2(uq, p), in0=sl2(ut, p), s0=rsbu[:, 0:1],
                s1=rsbu[:, 1:2], imm2=MAGIC), r=[ku], w=[kuq])
            sc.c("vector", lambda v, p=p: v._custom_dve(
                ROUND_SCALE, out=sl2(mq, p), in0=sl2(mt, p), s0=rsbm[:, 0:1],
                s1=rsbm[:, 1:2], imm2=MAGIC), r=[km_], w=[kmq])
            sc.c("vector", lambda v, p=p: v.tensor_tensor(
                sl2(oo, p), sl2(uq, p), sl2(mq, p), ALU.mult),
                r=[kuq, kmq], w=[ko])
            sc.c("vector", lambda v, p=p: v.reduce_max(
                aw2[:], sl2(oo, p), axis=AX.X, apply_absolute_value=True),
                r=[ko], w=["aw2"])
            sc.c("vector", lambda v: v.tensor_tensor(
                oacc[:], oacc[:], aw2[:], ALU.max), r=["aw2", "oacc"], w=["oacc"])
            sc.d("sync", lambda sync, ib=ib, tc=tc, p=p: sync.dma_start(
                out=oT[ts(ib, 128), ts(tc, 512)], in_=sl2(oo, p)),
                r=[ko], w=["oT"])
        add_amax_finalize(sc, oacc, amT, am1, am_bounce, amax_o)
        sc.run(nc)
    mybir.codegen_inst_isa_subclasses(nc)
    return nc



def build_l1_ffn_d():
    nc = bass.Bass("TRN2", target_bir_lowering=False, debug=False)
    x3T = nc.dram_tensor("x3T", [D, T], F32, kind="ExternalInput").ap()
    oT = nc.dram_tensor("oT", [I3, T], F32, kind="ExternalInput").ap()
    amax8 = nc.dram_tensor("amax8", [8, 1], F32, kind="ExternalInput").ap()
    w_down = nc.dram_tensor("w_down", [D, I3], F32, kind="ExternalInput").ap()
    x4T = nc.dram_tensor("x4T", [D, T], F32, kind="ExternalOutput").ap()
    scr2 = nc.dram_tensor("scr2", [1, 2], F32, kind="Internal").ap()

    sc = Script()
    with ExitStack() as es:
        A = lambda n, sh, dt: es.enter_context(nc.sbuf_tensor(n, sh, dt))
        PS = lambda n, sh, dt: es.enter_context(nc.psum_tensor(n, sh, dt))
        stage_f = A("stage_f", [128, 2 * 3072], F32)
        wq_t = A("wq_t", [128, 2 * 3072], BF16)
        wdq = A("wdq", [128, 24 * 1024], BF16)
        oqc = A("oqc", [128, 24 * 512], BF16)
        old = A("old", [128, 2 * 512], F32)
        yr = A("yr", [128, 8 * 512], F32)
        xtile = A("xtile", [128, 2 * 512], F32)
        sq = A("sq", [128, 2 * 512], F32)
        x4t = A("x4t", [128, 2 * 512], F32)
        tprod = A("tprod", [128, 2 * 512], F32)
        awb = A("awb", [128, 2 * 96], F32)
        sbv = A("sbv", [128, 2 * 96], F32)
        rbv = A("rbv", [128, 2 * 96], F32)

        def sl8s(buf, p):
            return buf[:, p * 3072:(p + 1) * 3072]

        def sl8h(buf, p):
            return buf[:, p * 512:(p + 1) * 512]

        def sl96(buf, p):
            return buf[:, p * 96:(p + 1) * 96]
        am8row = A("am8row", [1, 8], F32)
        am = A("am", [1, 1], F32)
        sc2 = A("sc2", [1, 2], F32)
        rsb = A("rsb", [128, 2], F32)
        lnrows = A("lnrows", [1, 512 * 6], F32)
        ones_col = A("ones_col", [128, 1], F32)
        ones_row1 = A("ones_row1", [1, 128], F32)
        cpool = A("cpool", [128, 1], F32)
        ident = A("ident", [128, 128], BF16)
        ps_m0 = PS("ps_m0", [128, 512], F32)
        ps_m1 = PS("ps_m1", [128, 512], F32)
        ps_tb0 = PS("ps_tb0", [128, 128], BF16)
        ps_tb1 = PS("ps_tb1", [128, 128], BF16)
        psm = [ps_m0, ps_m1]
        pstb = [ps_tb0, ps_tb1]
        ps_sum = PS("ps_sum", [1, 512], F32)
        ps_sq = PS("ps_sq", [1, 512], F32)
        ps_al = PS("ps_al", [128, 512], F32)
        ps_be = PS("ps_be", [128, 512], F32)

        add_const_pool(nc, sc, cpool, [1e-5])
        add_identity(sc, ident)
        sc.c("vector", lambda v: v.memset(ones_col[:], 1.0))
        sc.c("vector", lambda v: v.memset(ones_row1[:], 1.0))
        add_scalar_prep(sc, dict(am8row=am8row, am=am, sc2=sc2, rsb=rsb,
                                 scr2=scr2), amax8, 32767.0, 1e-12, "qdq")

        for rb in range(8):
            p = rb % 2
            kst, kaw, kwq = f"st:{p}", f"aw:{p}", f"wq:{p}"
            sc.d("sync", lambda sync, rb=rb, p=p: sync.dma_start(
                out=sl8s(stage_f, p), in_=w_down[ts(rb, 128), :]),
                r=[], w=[kst])
            sc.c("vector", lambda v, p=p: v.tensor_reduce(
                sl96(awb, p), sl8s(stage_f, p).rearrange("p (a b) -> p a b", b=32),
                axis=AX.X, op=ALU.max, apply_absolute_value=True),
                r=[kst], w=[kaw])
            sc.c("vector", lambda v, p=p: v.tensor_scalar(
                sl96(sbv, p), sl96(awb, p), 1.0 / 7.0, 1e-12, ALU.mult, ALU.max),
                r=[kaw], w=[kaw + "s"])
            sc.c("vector", lambda v, p=p: v.reciprocal(
                sl96(rbv, p), sl96(sbv, p)), r=[kaw + "s"], w=[kaw + "r"])

            def rounds(v, p=p):
                ins = []
                for bk in range(96):
                    ins.append(v._custom_dve(
                        ROUND_SCALE, out=sl8s(wq_t, p)[:, ts(bk, 32)],
                        in0=sl8s(stage_f, p)[:, ts(bk, 32)],
                        s0=sl96(rbv, p)[:, bk:bk + 1],
                        s1=sl96(sbv, p)[:, bk:bk + 1], imm2=MAGIC))
                return ins[-1]
            sc.c("vector", rounds, r=[kst, kaw + "s", kaw + "r"], w=[kwq])
            for j in range(24):
                jp = j % 2
                sc.c("tensor", lambda te, j=j, p=p, jp=jp: te.transpose(
                    pstb[jp][:], sl8s(wq_t, p)[:, ts(j, 128)], ident[:]),
                    r=[kwq], w=[f"pst:{jp}"])
                sc.c("vector", lambda v, j=j, rb=rb, jp=jp: v.tensor_copy(
                    wdq[:, j * 1024 + rb * 128: j * 1024 + rb * 128 + 128],
                    pstb[jp][:]), r=[f"pst:{jp}"], w=["wdq"])

        for tc in range(8):
            for ib in range(24):
                p = ib % 2
                sc.d("sync", lambda sync, ib=ib, tc=tc, p=p: sync.dma_start(
                    out=sl8h(old, p), in_=oT[ts(ib, 128), ts(tc, 512)]),
                    r=[], w=[f"old:{p}"])
                sc.c("vector", lambda v, ib=ib, p=p: v._custom_dve(
                    ROUND_SCALE, out=oqc[:, ts(ib, 512)], in0=sl8h(old, p),
                    s0=rsb[:, 0:1], s1=rsb[:, 1:2], imm2=MAGIC),
                    r=[f"old:{p}"], w=[f"oq:{ib}"])
            for do in range(8):
                p = do % 2

                def mmo(te, do=do, p=p):
                    ins = []
                    for ib in range(24):
                        ins.append(te.matmul(
                            psm[p][:],
                            wdq[:, ib * 1024 + do * 128: ib * 1024 + do * 128 + 128],
                            oqc[:, ts(ib, 512)],
                            start=(ib == 0), stop=(ib == 23)))
                    return ins[-1]
                sc.c("tensor", mmo, r=["wdq"] + [f"oq:{ib}" for ib in range(24)],
                     w=[f"psm:{p}"])
                sc.d("sync", lambda sync, do=do, tc=tc, p=p: sync.dma_start(
                    out=sl8h(xtile, p), in_=x3T[ts(do, 128), ts(tc, 512)]),
                    r=[], w=[f"xt:{p}"])
                sc.c("vector", lambda v, do=do, p=p: v.scalar_tensor_tensor(
                    yr[:, ts(do, 512)], psm[p][:], 1.0, sl8h(xtile, p),
                    ALU.mult, ALU.add), r=[f"psm:{p}", f"xt:{p}"], w=[f"yr:{do}"])
                sc.c("scalar", lambda s, do=do, p=p: s.activation(
                    out=sl8h(sq, p), in_=yr[:, ts(do, 512)], func=AF.Square),
                    r=[f"yr:{do}"], w=[f"sq:{p}"])

                def mmstat(te, do=do, p=p):
                    te.matmul(ps_sum[:], ones_col[:], yr[:, ts(do, 512)],
                              start=(do == 0), stop=(do == 7))
                    return te.matmul(ps_sq[:], ones_col[:], sl8h(sq, p),
                                     start=(do == 0), stop=(do == 7))
                sc.c("tensor", mmstat, r=[f"yr:{do}", f"sq:{p}"],
                     w=["pssum", "pssq"])
            add_ln_rows_k(sc, lnrows, ps_sum, ps_sq, ps_al, ps_be, ones_row1,
                          1.0 / D, 1e-5)
            for do in range(8):
                p = do % 2
                sc.c("vector", lambda v, do=do, p=p: v.tensor_tensor(
                    sl8h(tprod, p), yr[:, ts(do, 512)], ps_al[:], ALU.mult),
                    r=[f"yr:{do}", "psal"], w=[f"tp:{p}"])
                sc.c("vector", lambda v, p=p: v.tensor_tensor(
                    sl8h(x4t, p), sl8h(tprod, p), ps_be[:], ALU.add),
                    r=[f"tp:{p}", "psbe"], w=[f"x4:{p}"])
                sc.d("sync", lambda sync, do=do, tc=tc, p=p: sync.dma_start(
                    out=x4T[ts(do, 128), ts(tc, 512)], in_=sl8h(x4t, p)),
                    r=[f"x4:{p}"], w=["x4T"])
        sc.run(nc)
    mybir.codegen_inst_isa_subclasses(nc)
    return nc



_BF = ml_dtypes.bfloat16


def add_ln_rows(sc, rows, ps_sum, ps_sq, ps_al, ps_be, ones_row1, inv_n, eps_bias):
    R = lambda i: rows[:, i * 512:(i + 1) * 512]
    sc.c("vector", lambda v: [v.tensor_copy(R(0), ps_sum[:]),
                              v.tensor_copy(R(1), ps_sq[:])][-1])
    sc.c("vector", lambda v: [
        v.tensor_scalar(R(2), R(0), inv_n, None, ALU.mult),
        v.tensor_scalar(R(3), R(1), inv_n, None, ALU.mult)][-1])
    sc.c("vector", lambda v: v.tensor_tensor(R(5), R(2), R(2), ALU.mult))
    sc.c("vector", lambda v: v.tensor_tensor(R(4), R(3), R(5), ALU.subtract))
    sc.c("scalar", lambda s: s.activation(
        out=R(3), in_=R(4), func=AF.Sqrt, bias=eps_bias))
    sc.c("vector", lambda v: v.reciprocal(R(4), R(3)))
    sc.c("vector", lambda v: v.tensor_tensor(R(5), R(2), R(4), ALU.mult))
    sc.c("vector", lambda v: v.tensor_scalar(R(5), R(5), -1.0, None, ALU.mult))

    def bcasts(te):
        te.matmul(ps_al[:], ones_row1[:], R(4), start=True, stop=True)
        return te.matmul(ps_be[:], ones_row1[:], R(5), start=True, stop=True)
    sc.c("tensor", bcasts)


def add_ln_rows_k(sc, rows, ps_sum, ps_sq, ps_al, ps_be, ones_row1, inv_n,
                  eps_bias):
    R = lambda i: rows[:, i * 512:(i + 1) * 512]
    sc.c("vector", lambda v: [v.tensor_copy(R(0), ps_sum[:]),
                              v.tensor_copy(R(1), ps_sq[:])][-1],
         r=["pssum", "pssq"], w=["lnr"])
    sc.c("vector", lambda v: [
        v.tensor_scalar(R(2), R(0), inv_n, None, ALU.mult),
        v.tensor_scalar(R(3), R(1), inv_n, None, ALU.mult)][-1],
        r=["lnr"], w=["lnr"])
    sc.c("vector", lambda v: v.tensor_tensor(R(5), R(2), R(2), ALU.mult),
         r=["lnr"], w=["lnr"])
    sc.c("vector", lambda v: v.tensor_tensor(R(4), R(3), R(5), ALU.subtract),
         r=["lnr"], w=["lnr"])
    sc.c("scalar", lambda s: s.activation(
        out=R(3), in_=R(4), func=AF.Sqrt, bias=eps_bias),
        r=["lnr"], w=["lnr"])
    sc.c("vector", lambda v: v.reciprocal(R(4), R(3)), r=["lnr"], w=["lnr"])
    sc.c("vector", lambda v: v.tensor_tensor(R(5), R(2), R(4), ALU.mult),
         r=["lnr"], w=["lnr"])
    sc.c("vector", lambda v: v.tensor_scalar(R(5), R(5), -1.0, None, ALU.mult),
         r=["lnr"], w=["lnr"])

    def bcasts(te):
        te.matmul(ps_al[:], ones_row1[:], R(4), start=True, stop=True)
        return te.matmul(ps_be[:], ones_row1[:], R(5), start=True, stop=True)
    sc.c("tensor", bcasts, r=["lnr"], w=["psal", "psbe"])


def add_qdq_weight_lpbq(sc, nc, stage_f, wq_t, awb, sbv, rbv, w_hbm, rb, nblk):
    W = nblk * 32

    def load(sync):
        return sync.dma_start(out=stage_f[:, 0:W], in_=w_hbm[ts(rb, 128), :])
    sc.d("sync", load)
    sc.c("vector", lambda v: v.tensor_reduce(
        awb[:, 0:nblk], stage_f[:, 0:W].rearrange("p (a b) -> p a b", b=32),
        axis=AX.X, op=ALU.max, apply_absolute_value=True))
    sc.c("vector", lambda v: v.tensor_scalar(
        sbv[:, 0:nblk], awb[:, 0:nblk], 1.0 / 7.0, 1e-12, ALU.mult, ALU.max))
    sc.c("vector", lambda v: v.reciprocal(rbv[:, 0:nblk], sbv[:, 0:nblk]))

    def rounds(v):
        ins = []
        for b in range(nblk):
            ins.append(v._custom_dve(
                ROUND_SCALE, out=wq_t[:, ts(b, 32)], in0=stage_f[:, ts(b, 32)],
                s0=rbv[:, b:b + 1], s1=sbv[:, b:b + 1], imm2=MAGIC))
        return ins[-1]
    sc.c("vector", rounds)



_NC_CACHE = {}


def _get_nc(name, builder):
    if name not in _NC_CACHE:
        _NC_CACHE[name] = builder()
    return _NC_CACHE[name]


def _gather_amax_n(res, key, n):
    return np.ascontiguousarray(np.concatenate(
        [np.asarray(res.results[c][key]).reshape(1, 1) for c in range(n)]
        + [np.asarray(res.results[0][key]).reshape(1, 1)] * (8 - n),
        axis=0)).astype(np.float32)


def _gather_amax(res, key):
    return _gather_amax_n(res, key, NCORES)


def kernel(L0_attn_x, L0_attn_W_qkv, L0_ffn_W_in, L0_ffn_b_in, L0_ffn_W_out,
           L0_ffn_b_out, L1_attn_A_log, L1_attn_W_delta, L1_attn_b_delta,
           L1_attn_W_B, L1_attn_W_C, L1_attn_D_skip, L1_attn_norm_weight,
           L1_ffn_w_gate, L1_ffn_w_up, L1_ffn_w_down):
    cores = list(range(NCORES))
    x = np.ascontiguousarray(np.asarray(L0_attn_x, dtype=np.float32))
    wqkv = np.ascontiguousarray(np.asarray(L0_attn_W_qkv, np.float32))
    w_inT = np.ascontiguousarray(np.asarray(L0_ffn_W_in, np.float32).T)
    b_in_t = np.ascontiguousarray(np.asarray(L0_ffn_b_in, np.float32).reshape(32, 128).T)
    w_out = np.ascontiguousarray(np.asarray(L0_ffn_W_out, np.float32))
    b_out_t = np.ascontiguousarray(np.asarray(L0_ffn_b_out, np.float32).reshape(8, 128).T)
    a_log = np.ascontiguousarray(np.asarray(L1_attn_A_log, np.float32))
    w_delta = np.ascontiguousarray(np.asarray(L1_attn_W_delta, np.float32))
    b_delta_t = np.ascontiguousarray(np.asarray(L1_attn_b_delta, np.float32).reshape(8, 128).T)
    w_b = np.ascontiguousarray(np.asarray(L1_attn_W_B, np.float32))
    w_c = np.ascontiguousarray(np.asarray(L1_attn_W_C, np.float32))
    dskip_t = np.ascontiguousarray(np.asarray(L1_attn_D_skip, np.float32).reshape(8, 128).T)
    nw_t = np.ascontiguousarray(np.asarray(L1_attn_norm_weight, np.float32).reshape(8, 128).T)
    w_gate = np.ascontiguousarray(np.asarray(L1_ffn_w_gate, np.float32))
    w_up = np.ascontiguousarray(np.asarray(L1_ffn_w_up, np.float32))
    w_down = np.ascontiguousarray(np.asarray(L1_ffn_w_down, np.float32))

    nc1 = _get_nc("l0_attn", build_l0_attn)
    r1 = run_bass_kernel_spmd(nc1, [
        {"x_tok": np.ascontiguousarray(x[c]), "wqkv": wqkv} for c in cores],
        core_ids=cores)
    x1 = [np.asarray(r1.results[c]["x1T"]) for c in cores]
    am1 = _gather_amax(r1, "amax1")

    nc2 = _get_nc("l0_ffn_a", build_l0_ffn_a)
    r2 = run_bass_kernel_spmd(nc2, [
        {"x1T": x1[c], "amax8": am1, "w_inT": w_inT, "b_in_t": b_in_t}
        for c in cores], core_ids=cores)
    hid = [np.asarray(r2.results[c]["hidT"]) for c in cores]
    amh = _gather_amax(r2, "amax_h")

    nc3 = _get_nc("l0_ffn_b", build_l0_ffn_b)
    r3 = run_bass_kernel_spmd(nc3, [
        {"x1T": x1[c], "hidT": hid[c], "amax8": amh, "w_out": w_out,
         "b_out_t": b_out_t} for c in cores], core_ids=cores)
    x2 = [np.asarray(r3.results[c]["x2T"]) for c in cores]

    nc4 = _get_nc("ssm", build_ssm)
    r4 = run_bass_kernel_spmd(nc4, [
        {"x2T": x2[c], "a_log": a_log, "w_delta": w_delta,
         "b_delta_t": b_delta_t, "w_b": w_b, "w_c": w_c, "dskip_t": dskip_t,
         "nw_t": nw_t} for c in cores], core_ids=cores)
    x3 = [np.asarray(r4.results[c]["x3T"]) for c in cores]
    am3 = _gather_amax(r4, "amax3")

    nc5 = _get_nc("l1_ffn_a", build_l1_ffn_a)
    r5 = run_bass_kernel_spmd(nc5, [
        {"x3T": x3[c], "amax8": am3, "w_up": w_up, "w_gate": w_gate}
        for c in cores], core_ids=cores)
    up = [np.asarray(r5.results[c]["upT"]) for c in cores]
    gate = [np.asarray(r5.results[c]["gateT"]) for c in cores]
    amu = _gather_amax(r5, "amax_u")
    amg = _gather_amax(r5, "amax_g")

    nc6 = _get_nc("l1_ffn_b", build_l1_ffn_b)
    r6 = run_bass_kernel_spmd(nc6, [
        {"gateT": gate[c], "amax8": amg} for c in cores], core_ids=cores)
    mm_ = [np.asarray(r6.results[c]["mT"]) for c in cores]
    amm = _gather_amax(r6, "amax_m")

    nc7 = _get_nc("l1_ffn_c", build_l1_ffn_c)
    r7 = run_bass_kernel_spmd(nc7, [
        {"upT": up[c], "mT": mm_[c], "amax8u": amu, "amax8m": amm}
        for c in cores], core_ids=cores)
    oo = [np.asarray(r7.results[c]["oT"]) for c in cores]
    amo = _gather_amax(r7, "amax_o")

    nc8 = _get_nc("l1_ffn_d", build_l1_ffn_d)
    r8 = run_bass_kernel_spmd(nc8, [
        {"x3T": x3[c], "oT": oo[c], "amax8": amo, "w_down": w_down}
        for c in cores], core_ids=cores)
    out = np.stack([np.ascontiguousarray(np.asarray(r8.results[c]["x4T"]).T)
                    for c in cores])
    return out.astype(np.float32)


# version 07
# speedup vs baseline: 1.7107x; 1.1969x over previous
from contextlib import ExitStack

import numpy as np
import ml_dtypes
import concourse.bass as bass
import concourse.mybir as mybir
from concourse.bass_utils import run_bass_kernel_spmd
import concourse.dve_ops as dve_ops
from concourse.dve_op import DveOp
from concourse.dve_spec import Spec, Src0, Src1, C0, C1, C2, lower as spec_lower

F32 = mybir.dt.float32
BF16 = mybir.dt.bfloat16
AF = mybir.ActivationFunctionType
ALU = mybir.AluOpType
AX = mybir.AxisListType

T = 4096
D = 1024
FF = 4096
I3 = 3072
N16 = 16
NCORES = 8
MAGIC = float(3 << 22)
NEG = -1.0e30
INV32 = 1.0 / 32.0


def _register_dve(name, spec, reference):
    if name in dve_ops._SUB_OPCODE_FOR_NAME:
        for op in dve_ops.OPS:
            if op.name == name:
                return op
    sha = {"v3": spec_lower(spec, ver="v3", perf_en=False, name=name).sha("v3")}
    op = DveOp(name, spec, subdim=False, uops_sha=sha, reference=reference)
    dve_ops.OPS.append(op)
    dve_ops._SUB_OPCODE_FOR_NAME[name] = dve_ops._CUSTOM_DVE_ROW_BASE + len(dve_ops.OPS) - 1
    dve_ops.CUSTOM_DVE_SPECS[name] = spec
    dve_ops.CUSTOM_DVE_REFS[name] = reference
    return op


ROUND_SCALE = _register_dve(
    "ROUND_SCALE",
    Spec(body=((Src0 * C0 + C2) - C2) * C1),
    lambda in0, in1, s0, s1, imm2: np.rint(in0 * s0) * s1,
)


def ts(i, size):
    return slice(i * size, (i + 1) * size)



ENGINES = ("sync", "vector", "scalar", "tensor", "gpsimd")


class Script:

    def __init__(self):
        self.steps = []

    def c(self, eng, fn, r=None, w=None):
        self.steps.append((eng, "c", fn, 1, r, w))

    def d(self, eng, fn, n=1, r=None, w=None):
        self.steps.append((eng, "d", fn, n, r, w))

    def _compute_deps(self):
        n_steps = len(self.steps)
        deps = [set() for _ in range(n_steps)]
        last_w = {}
        readers = {}
        barrier = -1
        barrier_readers = []
        last_d = {}
        for i, (eng, kind, fn, n, r, w) in enumerate(self.steps):
            if kind == "d":
                if eng in last_d:
                    deps[i].add(last_d[eng])
                last_d[eng] = i
            if r is None or w is None:
                for j in barrier_readers:
                    deps[i].add(j)
                if barrier >= 0:
                    deps[i].add(barrier)
                barrier = i
                barrier_readers = []
                last_w = {}
                readers = {}
                continue
            if barrier >= 0:
                deps[i].add(barrier)
            for k in r:
                if k in last_w:
                    deps[i].add(last_w[k])
            for k in w:
                if k in last_w:
                    deps[i].add(last_w[k])
                for rd in readers.get(k, ()):
                    deps[i].add(rd)
            for k in r:
                readers.setdefault(k, []).append(i)
            for k in w:
                last_w[k] = i
                readers[k] = []
            barrier_readers.append(i)
        return deps

    FORCE_SERIAL = False

    def run(self, nc):
        if Script.FORCE_SERIAL:
            self.steps = [(e, k, f, n, None, None) for (e, k, f, n, r, w)
                          in self.steps]
        with ExitStack() as es:
            sems = {e: es.enter_context(nc.semaphore("sem_" + e)) for e in ENGINES}
            cnt = {e: 0 for e in ENGINES}
            plan = []
            for eng, kind, fn, n, r, w in self.steps:
                cnt[eng] += 1 if kind == "c" else 16 * n
                plan.append((eng, cnt[eng]))
            deps = self._compute_deps()
            waits = []
            for i, dd in enumerate(deps):
                per = {}
                for j in dd:
                    e, v = plan[j]
                    per[e] = max(per.get(e, 0), v)
                waits.append(per)

            def section(e, engobj):
                for i, (eng, kind, fn, n, r, w) in enumerate(self.steps):
                    if eng != e:
                        continue
                    for de, dv in sorted(waits[i].items()):
                        engobj.wait_ge(sems[de], dv)
                    instrs = fn(engobj)
                    if not isinstance(instrs, (list, tuple)):
                        instrs = [instrs]
                    if kind == "c":
                        instrs[-1].then_inc(sems[e], 1)
                    else:
                        assert len(instrs) == n, f"step {i}: expected {n} dmas, got {len(instrs)}"
                        for ins_ in instrs:
                            ins_.then_inc(sems[e], 16)

            import os
            if os.environ.get("SCRIPT_VERBOSE"):
                from collections import Counter
                print("script steps:", len(self.steps), Counter(s[0] for s in self.steps))
            block = es.enter_context(nc.Block())

            @block.sync
            def _(sync):
                section("sync", sync)
                for e in ENGINES:
                    if cnt[e] > 0:
                        sync.wait_ge(sems[e], cnt[e])

            @block.vector
            def _(v):
                section("vector", v)

            @block.scalar
            def _(s):
                section("scalar", s)

            @block.tensor
            def _(te):
                section("tensor", te)

            @block.gpsimd
            def _(g):
                section("gpsimd", g)



def add_const_pool(nc, sc, pool, values):
    for i, v in enumerate(values):
        nc.const_aps.register(F32, float(v), pool[:, i:i + 1])

    def f(vec):
        ins = []
        for i, v in enumerate(values):
            ins.append(vec.memset(pool[:, i:i + 1], float(v)))
        return ins[-1]
    sc.c("vector", f)


def add_identity(sc, ident_bf, ident_f32=None):
    def f0(g):
        ins = [g.memset(ident_bf[:], 0.0)]
        if ident_f32 is not None:
            ins.append(g.memset(ident_f32[:], 0.0))
        return ins[-1]
    sc.c("gpsimd", f0)

    def f1(g):
        ins = [g.affine_select(
            out=ident_bf[:], in_=ident_bf[:], compare_op=ALU.not_equal,
            fill=1.0, base=0, pattern=[[-1, 128]], channel_multiplier=1)]
        if ident_f32 is not None:
            ins.append(g.affine_select(
                out=ident_f32[:], in_=ident_f32[:], compare_op=ALU.not_equal,
                fill=1.0, base=0, pattern=[[-1, 128]], channel_multiplier=1))
        return ins[-1]
    sc.c("gpsimd", f1)


def add_scalar_prep(sc, bufs, amax_in, qmax, clip_lo, mode):
    am8row, am, sc2, rsb, scr2 = (bufs["am8row"], bufs["am"], bufs["sc2"],
                                  bufs["rsb"], bufs["scr2"])

    def load(sync):
        return sync.dma_start(out=am8row[:], in_=amax_in.rearrange("a one -> one a"))
    sc.d("sync", load)

    def red(v):
        return v.reduce_max(am[:], am8row[:], axis=AX.X)
    sc.c("vector", red)

    if mode == "a16":
        sc.c("vector", lambda v: v.tensor_scalar(am[:], am[:], clip_lo, None, ALU.max))
        def mk(v):
            v.reciprocal(sc2[:, 0:1], am[:])
            return v.tensor_scalar(sc2[:, 1:2], am[:], 1.0 / qmax, None, ALU.mult)
        sc.c("vector", mk)
        sc.c("vector", lambda v: v.tensor_scalar(
            sc2[:, 0:1], sc2[:, 0:1], qmax, None, ALU.mult))
    else:
        sc.c("vector", lambda v: v.tensor_scalar(
            sc2[:, 1:2], am[:], 1.0 / qmax, clip_lo, ALU.mult, ALU.max))
        sc.c("vector", lambda v: v.reciprocal(sc2[:, 0:1], sc2[:, 1:2]))

    def store(sync):
        return sync.dma_start(out=scr2, in_=sc2[:])
    sc.d("sync", store)

    def bcast(sync):
        return sync.dma_start(out=rsb[:], in_=scr2.broadcast_to([128, 2]))
    sc.d("sync", bcast)


def add_amax_finalize(sc, amax_acc, amT, am1, am_bounce, amax_out):
    def st1(sync):
        return sync.dma_start(out=am_bounce.rearrange("(p one) -> p one", one=1),
                              in_=amax_acc[:])
    sc.d("sync", st1)

    def ld1(sync):
        return sync.dma_start(out=amT[:],
                              in_=am_bounce.rearrange("(one p) -> one p", one=1))
    sc.d("sync", ld1)

    def red(v):
        return v.reduce_max(am1[:], amT[:], axis=AX.X)
    sc.c("vector", red)

    def st2(sync):
        return sync.dma_start(out=amax_out, in_=am1[:])
    sc.d("sync", st2)



def build_l0_attn():
    nc = bass.Bass("TRN2", target_bir_lowering=False, debug=False)
    x_tok = nc.dram_tensor("x_tok", [T, D], F32, kind="ExternalInput").ap()
    wqkv = nc.dram_tensor("wqkv", [D, 3 * D], F32, kind="ExternalInput").ap()
    x1T = nc.dram_tensor("x1T", [D, T], F32, kind="ExternalOutput").ap()
    amax1 = nc.dram_tensor("amax1", [1, 1], F32, kind="ExternalOutput").ap()
    xt_bf = nc.dram_tensor("xt_bf", [D, T], BF16, kind="Internal").ap()
    wq_bf = nc.dram_tensor("wq_bf", [D, 3 * D], BF16, kind="Internal").ap()
    qt_bf = nc.dram_tensor("qt_bf", [D, T], BF16, kind="Internal").ap()
    am_bounce = nc.dram_tensor("am_bounce", [128], F32, kind="Internal").ap()

    sc = Script()
    with ExitStack() as es:
        A = lambda n, sh, dt: es.enter_context(nc.sbuf_tensor(n, sh, dt))
        PS = lambda n, sh, dt: es.enter_context(nc.psum_tensor(n, sh, dt))
        stage_f = A("stage_f", [128, 2 * 1024], F32)
        stage_b = A("stage_b", [128, 2 * 1024], BF16)
        kT = A("kT", [128, 8 * T], BF16)
        vv = A("vv", [128, 32 * 1024], BF16)
        wvb = A("wvb", [128, 8 * 512], BF16)
        wqk_t = A("wqk_t", [128, 2 * 8 * 128], BF16)
        xc = A("xc", [128, 8 * 512], BF16)
        scb = A("scb", [128, T], F32)
        pbuf = A("pbuf", [128, T], BF16)
        pT2 = A("pT2", [128, 256], BF16)
        qTb = A("qTb", [128, 2 * 8 * 128], BF16)
        xrow = A("xrow", [128, 2 * 1024], F32)
        y1 = A("y1", [128, 1024], F32)
        x1tr = A("x1tr", [128, 1024], F32)
        sm = A("sm", [128, 16], F32)
        amax_acc = A("amax_acc", [128, 1], F32)
        amT = A("amT", [1, 128], F32)
        am1 = A("am1", [1, 1], F32)
        ident = A("ident", [128, 128], BF16)
        identf = A("identf", [128, 128], F32)
        cpool = A("cpool", [128, 2], F32)
        ps_a0 = PS("ps_a0", [128, 512], F32)
        ps_a1 = PS("ps_a1", [128, 512], F32)
        psa = [ps_a0, ps_a1]
        ps_o = PS("ps_o", [128, 1024], F32)
        ps_tb0 = PS("ps_tb0", [128, 128], BF16)
        ps_tb1 = PS("ps_tb1", [128, 128], BF16)
        pstb = [ps_tb0, ps_tb1]
        ps_tf0 = PS("ps_tf0", [128, 128], F32)
        ps_tf1 = PS("ps_tf1", [128, 128], F32)
        pstf = [ps_tf0, ps_tf1]

        COL = {k: i for i, k in enumerate(
            ["rowmax", "nbias", "lsum", "linv", "sumy", "sumsq", "m", "ey2",
             "mm", "var", "sd", "rstd", "mb", "negmb"])}

        def col(k):
            return sm[:, COL[k]:COL[k] + 1]

        def s2(buf, p, wdt):
            return buf[:, p * wdt:(p + 1) * wdt]

        add_const_pool(nc, sc, cpool, [1e-5])
        add_identity(sc, ident, identf)
        sc.c("vector", lambda v: v.memset(amax_acc[:], 0.0))

        for idx in range(24):
            r_, cc = divmod(idx, 3)
            p = idx % 2
            sc.d("sync", lambda sync, r_=r_, cc=cc, p=p: sync.dma_start(
                out=s2(stage_f, p, 1024), in_=wqkv[ts(r_, 128), ts(cc, 1024)]),
                r=[], w=[f"stf:{p}"])
            sc.c("scalar", lambda s, p=p: s.activation(
                out=s2(stage_b, p, 1024), in_=s2(stage_f, p, 1024), func=AF.Copy),
                r=[f"stf:{p}"], w=[f"stb:{p}"])
            sc.d("sync", lambda sync, r_=r_, cc=cc, p=p: sync.dma_start(
                out=wq_bf[ts(r_, 128), ts(cc, 1024)], in_=s2(stage_b, p, 1024)),
                r=[f"stb:{p}"], w=["wqbf"])

        for i in range(32):
            p = i % 2
            sc.d("sync", lambda sync, i=i, p=p: sync.dma_start(
                out=s2(xrow, p, 1024), in_=x_tok[ts(i, 128), :]),
                r=[], w=[f"xr:{p}"])
            sc.c("scalar", lambda s, p=p: s.activation(
                out=s2(stage_b, p, 1024), in_=s2(xrow, p, 1024), func=AF.Copy),
                r=[f"xr:{p}"], w=[f"stb:{p}"])
            for j in range(8):
                jp = j % 2
                sc.c("tensor", lambda te, j=j, p=p, jp=jp: te.transpose(
                    pstb[jp][:], s2(stage_b, p, 1024)[:, ts(j, 128)], ident[:]),
                    r=[f"stb:{p}"], w=[f"pstb:{jp}"])
                sc.c("vector", lambda v, j=j, p=p, jp=jp: v.tensor_copy(
                    s2(x1tr, p, 1024).bitcast(BF16)[:, ts(j, 128)] if False
                    else xc[:, j * 128 + p * 4096: j * 128 + p * 4096 + 128]
                    if False else s2(stage_f, p, 1024).bitcast(BF16)[:, ts(j, 128)],
                    pstb[jp][:]),
                    r=[f"pstb:{jp}"], w=[f"xco:{p}"])
            sc.d("sync", lambda sync, i=i, p=p: [sync.dma_start(
                out=xt_bf[ts(j, 128), ts(i, 128)],
                in_=s2(stage_f, p, 1024).bitcast(BF16)[:, ts(j, 128)])
                for j in range(8)], n=8, r=[f"xco:{p}"], w=["xtbf"])

        for dvh in range(2):
            sc.d("sync", lambda sync, dvh=dvh: [sync.dma_start(
                out=wvb[:, ts(db, 512)],
                in_=wq_bf[ts(db, 128), 2 * D + dvh * 512: 2 * D + dvh * 512 + 512])
                for db in range(8)], n=8, r=["wqbf"], w=["wvb"])
            for tc in range(8):
                p = tc % 2
                xcb = xc if p == 0 else pbuf
                kxc = f"xch:{p}"
                sc.d("sync", lambda sync, tc=tc, xcb=xcb: [sync.dma_start(
                    out=xcb[:, ts(db, 512)], in_=xt_bf[ts(db, 128), ts(tc, 512)])
                    for db in range(8)], n=8, r=["xtbf"], w=[kxc])
                for sub in range(4):
                    sp = sub % 2

                    def mmv(te, tc=tc, sub=sub, xcb=xcb, sp=sp):
                        ins = []
                        for db in range(8):
                            ins.append(te.matmul(
                                psa[sp][:],
                                xcb[:, db * 512 + sub * 128: db * 512 + sub * 128 + 128],
                                wvb[:, ts(db, 512)],
                                start=(db == 0), stop=(db == 7)))
                        return ins[-1]
                    sc.c("tensor", mmv, r=[kxc, "wvb"], w=[f"psa:{sp}"])
                    kt_ = tc * 4 + sub
                    sc.c("scalar", lambda s, kt_=kt_, dvh=dvh, sp=sp: s.activation(
                        out=vv[:, kt_ * 1024 + dvh * 512: kt_ * 1024 + dvh * 512 + 512],
                        in_=psa[sp][:], func=AF.Copy),
                        r=[f"psa:{sp}"], w=["vv"])

        for tc in range(8):
            p = tc % 2
            xcb = xc if p == 0 else pbuf
            kxc = f"xch:{p}"
            sc.d("sync", lambda sync, tc=tc, xcb=xcb: [sync.dma_start(
                out=xcb[:, ts(db, 512)], in_=xt_bf[ts(db, 128), ts(tc, 512)])
                for db in range(8)], n=8, r=["xtbf"], w=[kxc])
            for qk in range(2):
                for ob in range(8):
                    wp = (qk * 8 + ob) % 2
                    sc.d("sync", lambda sync, qk=qk, ob=ob, wp=wp: [sync.dma_start(
                        out=s2(wqk_t, wp, 1024)[:, ts(db, 128)],
                        in_=wq_bf[ts(db, 128), qk * D + ob * 128: qk * D + ob * 128 + 128])
                        for db in range(8)], n=8, r=["wqbf"], w=[f"wqk:{wp}"])

                    def mmqk(te, tc=tc, qk=qk, ob=ob, xcb=xcb, wp=wp):
                        ins = []
                        for db in range(8):
                            ins.append(te.matmul(
                                psa[wp][:], s2(wqk_t, wp, 1024)[:, ts(db, 128)],
                                xcb[:, ts(db, 512)],
                                start=(db == 0), stop=(db == 7)))
                        return ins[-1]
                    sc.c("tensor", mmqk, r=[kxc, f"wqk:{wp}"], w=[f"psa:{wp}"])
                    if qk == 1:
                        sc.c("scalar", lambda s, ob=ob, tc=tc, wp=wp: s.activation(
                            out=kT[:, ob * T + tc * 512: ob * T + tc * 512 + 512],
                            in_=psa[wp][:], func=AF.Copy),
                            r=[f"psa:{wp}"], w=["kTr"])
                    else:
                        sc.c("scalar", lambda s, wp=wp: s.activation(
                            out=s2(stage_b, wp, 1024)[:, 0:512], in_=psa[wp][:],
                            func=AF.Copy), r=[f"psa:{wp}"], w=[f"stb:{wp}"])
                        sc.d("sync", lambda sync, ob=ob, tc=tc, wp=wp: sync.dma_start(
                            out=qt_bf[ts(ob, 128), ts(tc, 512)],
                            in_=s2(stage_b, wp, 1024)[:, 0:512]),
                            r=[f"stb:{wp}"], w=["qtbf"])

        sc.c("vector", lambda v: v.engine_nop())

        for qb in range(32):
            klen = (qb + 1) * 128
            nkc = (klen + 511) // 512
            qp = qb % 2
            sc.d("sync", lambda sync, qb=qb, qp=qp: [sync.dma_start(
                out=s2(qTb, qp, 1024)[:, ts(db, 128)],
                in_=qt_bf[ts(db, 128), ts(qb, 128)]) for db in range(8)],
                n=8, r=[], w=[f"qTb:{qp}"])
            sc.d("sync", lambda sync, qb=qb, qp=qp: sync.dma_start(
                out=s2(xrow, qp, 1024), in_=x_tok[ts(qb, 128), :]),
                r=[], w=[f"xr:{qp}"])
            for kc in range(nkc):
                w_ = min(512, klen - kc * 512)
                kp = kc % 2

                def mms(te, qb=qb, kc=kc, w_=w_, qp=qp, kp=kp):
                    ins = []
                    for db in range(8):
                        ins.append(te.matmul(
                            psa[kp][:, 0:w_], s2(qTb, qp, 1024)[:, ts(db, 128)],
                            kT[:, db * T + kc * 512: db * T + kc * 512 + w_],
                            start=(db == 0), stop=(db == 7)))
                    return ins[-1]
                sc.c("tensor", mms, r=[f"qTb:{qp}", "kTr"], w=[f"psa:{kp}"])
                sc.c("scalar", lambda s, kc=kc, w_=w_, kp=kp: s.activation(
                    out=scb[:, kc * 512: kc * 512 + w_], in_=psa[kp][:, 0:w_],
                    func=AF.Copy), r=[f"psa:{kp}"], w=["scb"])
            sc.c("gpsimd", lambda g, qb=qb: g.affine_select(
                out=scb[:, ts(qb, 128)], in_=scb[:, ts(qb, 128)],
                compare_op=ALU.is_ge, fill=NEG, base=0,
                pattern=[[-1, 128]], channel_multiplier=1),
                r=["scb"], w=["scb"])
            sc.c("vector", lambda v, klen=klen: v.reduce_max(
                col("rowmax"), scb[:, 0:klen], axis=AX.X),
                r=["scb"], w=["smx"])
            sc.c("vector", lambda v: v.tensor_scalar(
                col("nbias"), col("rowmax"), -INV32, None, ALU.mult),
                r=["smx"], w=["smb"])
            sc.c("scalar", lambda s, klen=klen: s.activation(
                out=pbuf[:, 0:klen], in_=scb[:, 0:klen], func=AF.Exp,
                scale=INV32, bias=col("nbias"), accum_out=col("lsum")),
                r=["scb", "smb"], w=["pbuf", "sml"])
            sc.c("vector", lambda v: v.reciprocal(col("linv"), col("lsum")),
                 r=["sml"], w=["smv"])
            for kt in range(qb + 1):
                ktp = kt % 2
                sc.c("tensor", lambda te, kt=kt, ktp=ktp: te.transpose(
                    pstb[ktp][:], pbuf[:, ts(kt, 128)], ident[:]),
                    r=["pbuf"], w=[f"pstb:{ktp}"])
                sc.c("vector", lambda v, ktp=ktp: v.tensor_copy(
                    pT2[:, ts(ktp, 128)], pstb[ktp][:]),
                    r=[f"pstb:{ktp}"], w=[f"pT:{ktp}"])

                def mmav(te, kt=kt, qb=qb, ktp=ktp):
                    ins = []
                    for dvh in range(2):
                        ins.append(te.matmul(
                            ps_o[:, ts(dvh, 512)], pT2[:, ts(ktp, 128)],
                            vv[:, kt * 1024 + dvh * 512: kt * 1024 + dvh * 512 + 512],
                            start=(kt == 0), stop=(kt == qb)))
                    return ins[-1]
                sc.c("tensor", mmav, r=[f"pT:{ktp}", "vv"], w=["pso"])
            sc.c("vector", lambda v, qp=qp: v.scalar_tensor_tensor(
                y1[:], ps_o[:], col("linv"), s2(xrow, qp, 1024), ALU.mult,
                ALU.add, accum_out=col("sumy")),
                r=["pso", "smv", f"xr:{qp}"], w=["y1", "smy"])
            sc.c("scalar", lambda s: s.activation(
                out=scb[:, 0:1024], in_=y1[:], func=AF.Square,
                accum_out=col("sumsq")), r=["y1"], w=["scb", "smy2"])
            sc.c("vector", lambda v: [
                v.tensor_scalar(col("m"), col("sumy"), 1.0 / D, None, ALU.mult),
                v.tensor_scalar(col("ey2"), col("sumsq"), 1.0 / D, None,
                                ALU.mult)][-1], r=["smy", "smy2"], w=["smm"])
            sc.c("vector", lambda v: v.tensor_tensor(
                col("mm"), col("m"), col("m"), ALU.mult), r=["smm"], w=["smm2"])
            sc.c("vector", lambda v: v.tensor_tensor(
                col("var"), col("ey2"), col("mm"), ALU.subtract),
                r=["smm", "smm2"], w=["smvv"])
            sc.c("scalar", lambda s: s.activation(
                out=col("sd"), in_=col("var"), func=AF.Sqrt, bias=1e-5),
                r=["smvv"], w=["smsd"])
            sc.c("vector", lambda v: v.reciprocal(col("rstd"), col("sd")),
                 r=["smsd"], w=["smr"])
            sc.c("vector", lambda v: v.tensor_tensor(
                col("mb"), col("m"), col("rstd"), ALU.mult),
                r=["smm", "smr"], w=["smmb"])
            sc.c("vector", lambda v: v.tensor_scalar(
                col("negmb"), col("mb"), -1.0, None, ALU.mult),
                r=["smmb"], w=["smnb"])
            sc.c("vector", lambda v, qp=qp: v.tensor_scalar(
                s2(xrow, qp, 1024), y1[:], col("rstd"), col("negmb"), ALU.mult,
                ALU.add), r=["y1", "smr", "smnb"], w=[f"xr:{qp}"])
            sc.c("vector", lambda v, qp=qp: v.reduce_max(
                col("rowmax"), s2(xrow, qp, 1024), axis=AX.X,
                apply_absolute_value=True), r=[f"xr:{qp}"], w=["smx"])
            sc.c("vector", lambda v: v.tensor_tensor(
                amax_acc[:], amax_acc[:], col("rowmax"), ALU.max),
                r=["smx", "amacc"], w=["amacc"])
            for j in range(8):
                jp = j % 2
                sc.c("tensor", lambda te, j=j, jp=jp, qp=qp: te.transpose(
                    pstf[jp][:], s2(xrow, qp, 1024)[:, ts(j, 128)], identf[:]),
                    r=[f"xr:{qp}"], w=[f"pstf:{jp}"])
                sc.c("vector", lambda v, j=j, jp=jp: v.tensor_copy(
                    x1tr[:, ts(j, 128)], pstf[jp][:]),
                    r=[f"pstf:{jp}"], w=["x1t"])
            sc.d("sync", lambda sync, qb=qb: [sync.dma_start(
                out=x1T[ts(j, 128), ts(qb, 128)],
                in_=x1tr[:, ts(j, 128)]) for j in range(8)],
                n=8, r=["x1t"], w=["x1T", "x1t"])

        add_amax_finalize(sc, amax_acc, amT, am1, am_bounce, amax1)
        sc.run(nc)
    mybir.codegen_inst_isa_subclasses(nc)
    return nc



def build_l0_ffn_a():
    nc = bass.Bass("TRN2", target_bir_lowering=False, debug=False)
    x1T = nc.dram_tensor("x1T", [D, T], F32, kind="ExternalInput").ap()
    amax8 = nc.dram_tensor("amax8", [8, 1], F32, kind="ExternalInput").ap()
    w_inT = nc.dram_tensor("w_inT", [FF, D], F32, kind="ExternalInput").ap()
    b_in_t = nc.dram_tensor("b_in_t", [128, 32], F32, kind="ExternalInput").ap()
    hidT = nc.dram_tensor("hidT", [FF, T], F32, kind="ExternalOutput").ap()
    amax_h = nc.dram_tensor("amax_h", [1, 1], F32, kind="ExternalOutput").ap()
    scr2 = nc.dram_tensor("scr2", [1, 2], F32, kind="Internal").ap()
    am_bounce = nc.dram_tensor("am_bounce", [128], F32, kind="Internal").ap()

    sc = Script()
    with ExitStack() as es:
        A = lambda n, sh, dt: es.enter_context(nc.sbuf_tensor(n, sh, dt))
        PS = lambda n, sh, dt: es.enter_context(nc.psum_tensor(n, sh, dt))
        stage_f = A("stage_f", [128, 2 * 1024], F32)
        w_q_t = A("w_q_t", [128, 2 * 1024], BF16)
        w_in_q = A("w_in_q", [128, 8 * FF], BF16)
        xi = A("xi", [128, 8 * T], BF16)
        hb = A("hb", [128, 2 * 512], F32)

        def sl2f(buf, p):
            return buf[:, p * 1024:(p + 1) * 1024]

        def sl2b(buf, p):
            return buf[:, p * 1024:(p + 1) * 1024]

        def hbsl(p):
            return hb[:, p * 512:(p + 1) * 512]
        binb = A("binb", [128, 32], F32)
        sw = A("sw", [128, 32], F32)
        evsc = A("evsc", [128, 32], F32)
        aw = A("aw", [128, 2], F32)
        rw = A("rw", [128, 2], F32)
        aw2 = A("aw2", [128, 1], F32)
        hacc = A("hacc", [128, 1], F32)
        am8row = A("am8row", [1, 8], F32)
        am = A("am", [1, 1], F32)
        sc2 = A("sc2", [1, 2], F32)
        rsb = A("rsb", [128, 2], F32)
        amT = A("amT", [1, 128], F32)
        am1 = A("am1", [1, 1], F32)
        ident = A("ident", [128, 128], BF16)
        ps_m0 = PS("ps_m0", [128, 512], F32)
        ps_m1 = PS("ps_m1", [128, 512], F32)
        ps_tb0 = PS("ps_tb0", [128, 128], BF16)
        ps_tb1 = PS("ps_tb1", [128, 128], BF16)
        psm = [ps_m0, ps_m1]
        pstb = [ps_tb0, ps_tb1]

        add_identity(sc, ident)
        sc.c("vector", lambda v: v.memset(hacc[:], 0.0))
        add_scalar_prep(sc, dict(am8row=am8row, am=am, sc2=sc2, rsb=rsb,
                                 scr2=scr2), amax8, 32767.0, 1e-8, "a16")
        sc.d("sync", lambda sync: sync.dma_start(out=binb[:], in_=b_in_t))

        for fb in range(32):
            p = fb % 2
            kst, kaw, krw, kwq = f"st:{p}", f"aw:{p}", f"rw:{p}", f"wq:{p}"
            sc.d("sync", lambda sync, fb=fb, p=p: sync.dma_start(
                out=sl2f(stage_f, p), in_=w_inT[ts(fb, 128), :]),
                r=[], w=[kst])
            sc.c("vector", lambda v, p=p: v.reduce_max(
                aw[:, p:p + 1], sl2f(stage_f, p), axis=AX.X,
                apply_absolute_value=True), r=[kst], w=[kaw])
            sc.c("vector", lambda v, p=p: v.tensor_scalar(
                aw[:, p:p + 1], aw[:, p:p + 1], 1e-8, None, ALU.max),
                r=[kaw], w=[kaw])
            sc.c("vector", lambda v, p=p: v.reciprocal(
                rw[:, p:p + 1], aw[:, p:p + 1]), r=[kaw], w=[krw])
            sc.c("vector", lambda v, fb=fb, p=p: v.tensor_scalar(
                sw[:, fb:fb + 1], aw[:, p:p + 1], 1.0 / 127.0, None, ALU.mult),
                r=[kaw], w=["sw"])
            sc.c("vector", lambda v, p=p: v.tensor_scalar(
                rw[:, p:p + 1], rw[:, p:p + 1], 127.0, None, ALU.mult),
                r=[krw], w=[krw])
            sc.c("vector", lambda v, p=p: v._custom_dve(
                ROUND_SCALE, out=sl2b(w_q_t, p), in0=sl2f(stage_f, p),
                s0=rw[:, p:p + 1], s1=1.0, imm2=MAGIC),
                r=[kst, krw], w=[kwq])
            for j in range(8):
                jp = j % 2
                sc.c("tensor", lambda te, j=j, p=p, jp=jp: te.transpose(
                    pstb[jp][:], sl2b(w_q_t, p)[:, ts(j, 128)], ident[:]),
                    r=[kwq], w=[f"pst:{jp}"])
                sc.c("vector", lambda v, j=j, fb=fb, jp=jp: v.tensor_copy(
                    w_in_q[:, j * FF + fb * 128: j * FF + fb * 128 + 128],
                    pstb[jp][:]), r=[f"pst:{jp}"], w=["winq"])
        sc.c("vector", lambda v: v.tensor_scalar(
            evsc[:], sw[:], rsb[:, 1:2], None, ALU.mult),
            r=["sw"], w=["evsc"])

        for idx in range(64):
            db, tc = divmod(idx, 8)
            p = idx % 2
            sc.d("sync", lambda sync, db=db, tc=tc, p=p: sync.dma_start(
                out=sl2f(stage_f, p)[:, 0:512],
                in_=x1T[ts(db, 128), ts(tc, 512)]), r=[], w=[f"st:{p}"])
            sc.c("vector", lambda v, db=db, tc=tc, p=p: v._custom_dve(
                ROUND_SCALE, out=xi[:, db * T + tc * 512: db * T + tc * 512 + 512],
                in0=sl2f(stage_f, p)[:, 0:512], s0=rsb[:, 0:1], s1=1.0,
                imm2=MAGIC), r=[f"st:{p}"], w=["xi"])

        for idx in range(32 * 8):
            fb, tc = divmod(idx, 8)
            p = idx % 2
            def mmh(te, fb=fb, tc=tc, p=p):
                ins = []
                for db in range(8):
                    ins.append(te.matmul(
                        psm[p][:],
                        w_in_q[:, db * FF + fb * 128: db * FF + fb * 128 + 128],
                        xi[:, db * T + tc * 512: db * T + tc * 512 + 512],
                        start=(db == 0), stop=(db == 7)))
                return ins[-1]
            sc.c("tensor", mmh, r=["winq", "xi"], w=[f"psm:{p}"])
            sc.c("scalar", lambda s, fb=fb, p=p: s.activation(
                out=hbsl(p), in_=psm[p][:], func=AF.Gelu,
                scale=evsc[:, fb:fb + 1], bias=binb[:, fb:fb + 1]),
                r=[f"psm:{p}", "evsc"], w=[f"hb:{p}"])
            sc.c("vector", lambda v, p=p: v.reduce_max(
                aw2[:], hbsl(p), axis=AX.X,
                apply_absolute_value=True), r=[f"hb:{p}"], w=["aw2"])
            sc.c("vector", lambda v: v.tensor_tensor(
                hacc[:], hacc[:], aw2[:], ALU.max), r=["aw2", "hacc"], w=["hacc"])
            sc.d("sync", lambda sync, fb=fb, tc=tc, p=p: sync.dma_start(
                out=hidT[ts(fb, 128), ts(tc, 512)], in_=hbsl(p)),
                r=[f"hb:{p}"], w=["hidT"])

        add_amax_finalize(sc, hacc, amT, am1, am_bounce, amax_h)
        sc.run(nc)
    mybir.codegen_inst_isa_subclasses(nc)
    return nc



def build_l0_ffn_b():
    nc = bass.Bass("TRN2", target_bir_lowering=False, debug=False)
    x1T = nc.dram_tensor("x1T", [D, T], F32, kind="ExternalInput").ap()
    hidT = nc.dram_tensor("hidT", [FF, T], F32, kind="ExternalInput").ap()
    amax8 = nc.dram_tensor("amax8", [8, 1], F32, kind="ExternalInput").ap()
    w_out = nc.dram_tensor("w_out", [FF, D], F32, kind="ExternalInput").ap()
    b_out_t = nc.dram_tensor("b_out_t", [128, 8], F32, kind="ExternalInput").ap()
    x2T = nc.dram_tensor("x2T", [D, T], F32, kind="ExternalOutput").ap()
    scr2 = nc.dram_tensor("scr2", [1, 2], F32, kind="Internal").ap()

    sc = Script()
    with ExitStack() as es:
        A = lambda n, sh, dt: es.enter_context(nc.sbuf_tensor(n, sh, dt))
        PS = lambda n, sh, dt: es.enter_context(nc.psum_tensor(n, sh, dt))
        stage_f = A("stage_f", [128, 2 * 1024], F32)
        w_out_q = A("w_out_q", [128, 32 * 1024], BF16)
        hop = A("hop", [128, 32 * 512], BF16)
        hld = A("hld", [128, 2 * 512], F32)
        yr = A("yr", [128, 8 * 512], F32)
        xtile = A("xtile", [128, 2 * 512], F32)
        sq = A("sq", [128, 2 * 512], F32)
        x2t = A("x2t", [128, 2 * 512], F32)
        tprod = A("tprod", [128, 2 * 512], F32)

        def sl2f(buf, p):
            return buf[:, p * 1024:(p + 1) * 1024]

        def sl2h(buf, p):
            return buf[:, p * 512:(p + 1) * 512]
        bout = A("bout", [128, 8], F32)
        sw = A("sw", [128, 32], F32)
        cf = A("cf", [128, 32], F32)
        aw = A("aw", [128, 2], F32)
        rw = A("rw", [128, 2], F32)
        am8row = A("am8row", [1, 8], F32)
        am = A("am", [1, 1], F32)
        sc2 = A("sc2", [1, 2], F32)
        rsb = A("rsb", [128, 2], F32)
        lnrows = A("lnrows", [1, 512 * 6], F32)
        ones_col = A("ones_col", [128, 1], F32)
        ones_row1 = A("ones_row1", [1, 128], F32)
        cpool = A("cpool", [128, 1], F32)
        ps_m0 = PS("ps_m0", [128, 512], F32)
        ps_m1 = PS("ps_m1", [128, 512], F32)
        psm = [ps_m0, ps_m1]
        ps_sum = PS("ps_sum", [1, 512], F32)
        ps_sq = PS("ps_sq", [1, 512], F32)
        ps_al = PS("ps_al", [128, 512], F32)
        ps_be = PS("ps_be", [128, 512], F32)

        add_const_pool(nc, sc, cpool, [1e-5])
        sc.c("vector", lambda v: v.memset(ones_col[:], 1.0))
        sc.c("vector", lambda v: v.memset(ones_row1[:], 1.0))
        add_scalar_prep(sc, dict(am8row=am8row, am=am, sc2=sc2, rsb=rsb,
                                 scr2=scr2), amax8, 32767.0, 1e-8, "a16")
        sc.d("sync", lambda sync: sync.dma_start(out=bout[:], in_=b_out_t))

        for fb in range(32):
            p = fb % 2
            kst, kaw, krw = f"st:{p}", f"aw:{p}", f"rw:{p}"
            sc.d("sync", lambda sync, fb=fb, p=p: sync.dma_start(
                out=sl2f(stage_f, p), in_=w_out[ts(fb, 128), :]),
                r=[], w=[kst])
            sc.c("vector", lambda v, p=p: v.reduce_max(
                aw[:, p:p + 1], sl2f(stage_f, p), axis=AX.X,
                apply_absolute_value=True), r=[kst], w=[kaw])
            sc.c("vector", lambda v, p=p: v.tensor_scalar(
                aw[:, p:p + 1], aw[:, p:p + 1], 1e-8, None, ALU.max),
                r=[kaw], w=[kaw])
            sc.c("vector", lambda v, p=p: v.reciprocal(
                rw[:, p:p + 1], aw[:, p:p + 1]), r=[kaw], w=[krw])
            sc.c("vector", lambda v, fb=fb, p=p: v.tensor_scalar(
                sw[:, fb:fb + 1], aw[:, p:p + 1], 1.0 / 127.0, None, ALU.mult),
                r=[kaw], w=["sw"])
            sc.c("vector", lambda v, p=p: v.tensor_scalar(
                rw[:, p:p + 1], rw[:, p:p + 1], 127.0, None, ALU.mult),
                r=[krw], w=[krw])
            sc.c("vector", lambda v, fb=fb, p=p: v._custom_dve(
                ROUND_SCALE, out=w_out_q[:, ts(fb, 1024)], in0=sl2f(stage_f, p),
                s0=rw[:, p:p + 1], s1=1.0, imm2=MAGIC),
                r=[kst, krw], w=["woq"])
        sc.c("vector", lambda v: v.tensor_scalar(
            cf[:], sw[:], rsb[:, 1:2], None, ALU.mult), r=["sw"], w=["cf"])

        for tc in range(8):
            for fb in range(32):
                p = fb % 2
                sc.d("sync", lambda sync, fb=fb, tc=tc, p=p: sync.dma_start(
                    out=sl2h(hld, p), in_=hidT[ts(fb, 128), ts(tc, 512)]),
                    r=[], w=[f"hld:{p}"])
                sc.c("vector", lambda v, fb=fb, p=p: v._custom_dve(
                    ROUND_SCALE, out=hop[:, ts(fb, 512)], in0=sl2h(hld, p),
                    s0=rsb[:, 0:1], s1=cf[:, fb:fb + 1], imm2=MAGIC),
                    r=[f"hld:{p}", "cf"], w=[f"hop:{fb}"])
            for do in range(8):
                p = do % 2

                def mmo(te, do=do, p=p):
                    ins = []
                    for fb in range(32):
                        ins.append(te.matmul(
                            psm[p][:],
                            w_out_q[:, fb * 1024 + do * 128: fb * 1024 + do * 128 + 128],
                            hop[:, ts(fb, 512)],
                            start=(fb == 0), stop=(fb == 31)))
                    return ins[-1]
                sc.c("tensor", mmo, r=["woq"] + [f"hop:{fb}" for fb in range(32)],
                     w=[f"psm:{p}"])
                sc.d("sync", lambda sync, do=do, tc=tc, p=p: sync.dma_start(
                    out=sl2h(xtile, p), in_=x1T[ts(do, 128), ts(tc, 512)]),
                    r=[], w=[f"xt:{p}"])
                sc.c("vector", lambda v, do=do, p=p: v.scalar_tensor_tensor(
                    yr[:, ts(do, 512)], psm[p][:], bout[:, do:do + 1],
                    sl2h(xtile, p), ALU.add, ALU.add),
                    r=[f"psm:{p}", f"xt:{p}"], w=[f"yr:{do}"])
                sc.c("scalar", lambda s, do=do, p=p: s.activation(
                    out=sl2h(sq, p), in_=yr[:, ts(do, 512)], func=AF.Square),
                    r=[f"yr:{do}"], w=[f"sq:{p}"])

                def mmstat(te, do=do, p=p):
                    te.matmul(ps_sum[:], ones_col[:], yr[:, ts(do, 512)],
                              start=(do == 0), stop=(do == 7))
                    return te.matmul(ps_sq[:], ones_col[:], sl2h(sq, p),
                                     start=(do == 0), stop=(do == 7))
                sc.c("tensor", mmstat, r=[f"yr:{do}", f"sq:{p}"],
                     w=["pssum", "pssq"])
            add_ln_rows_k(sc, lnrows, ps_sum, ps_sq, ps_al, ps_be, ones_row1,
                          1.0 / D, 1e-5)
            for do in range(8):
                p = do % 2
                sc.c("vector", lambda v, do=do, p=p: v.tensor_tensor(
                    sl2h(tprod, p), yr[:, ts(do, 512)], ps_al[:], ALU.mult),
                    r=[f"yr:{do}", "psal"], w=[f"tp:{p}"])
                sc.c("vector", lambda v, p=p: v.tensor_tensor(
                    sl2h(x2t, p), sl2h(tprod, p), ps_be[:], ALU.add),
                    r=[f"tp:{p}", "psbe"], w=[f"x2t:{p}"])
                sc.d("sync", lambda sync, do=do, tc=tc, p=p: sync.dma_start(
                    out=x2T[ts(do, 128), ts(tc, 512)], in_=sl2h(x2t, p)),
                    r=[f"x2t:{p}"], w=["x2T"])
        sc.run(nc)
    mybir.codegen_inst_isa_subclasses(nc)
    return nc



def build_ssm(phases="123", dbg=False):
    nc = bass.Bass("TRN2", target_bir_lowering=False, debug=False)
    x2T = nc.dram_tensor("x2T", [D, T], F32, kind="ExternalInput").ap()
    a_log = nc.dram_tensor("a_log", [D, N16], F32, kind="ExternalInput").ap()
    w_delta = nc.dram_tensor("w_delta", [D, D], F32, kind="ExternalInput").ap()
    b_delta_t = nc.dram_tensor("b_delta_t", [128, 8], F32, kind="ExternalInput").ap()
    w_b = nc.dram_tensor("w_b", [D, N16], F32, kind="ExternalInput").ap()
    w_c = nc.dram_tensor("w_c", [D, N16], F32, kind="ExternalInput").ap()
    dskip_t = nc.dram_tensor("dskip_t", [128, 8], F32, kind="ExternalInput").ap()
    nw_t = nc.dram_tensor("nw_t", [128, 8], F32, kind="ExternalInput").ap()
    x3T = nc.dram_tensor("x3T", [D, T], F32, kind="ExternalOutput").ap()
    amax3 = nc.dram_tensor("amax3", [1, 1], F32, kind="ExternalOutput").ap()
    ik = "ExternalOutput" if dbg else "Internal"
    xn_hbm = nc.dram_tensor("xn_hbm", [D, T], F32, kind=ik).ap()
    xnb_hbm = nc.dram_tensor("xnb_hbm", [D, T], BF16, kind="Internal").ap()
    _dl_hbm = nc.dram_tensor("dl_hbm", [D, T], F32, kind=ik).ap()
    bp_hbm = nc.dram_tensor("bp_hbm", [N16, T], F32, kind=ik).ap()
    c_hbm = nc.dram_tensor("c_hbm", [N16, T], F32, kind=ik).ap()
    am_bounce = nc.dram_tensor("am_bounce", [128], F32, kind="Internal").ap()

    DT_MAX, DT_MIN = 10.0, 1e-4
    CEN = (DT_MAX + DT_MIN) / 2.0
    HALF = (DT_MAX - DT_MIN) / 2.0

    sc = Script()
    with ExitStack() as es:
        A = lambda n, sh, dt: es.enter_context(nc.sbuf_tensor(n, sh, dt))
        PS = lambda n, sh, dt: es.enter_context(nc.psum_tensor(n, sh, dt))
        stage_f = A("stage_f", [128, 1024], F32)
        wdel_b = A("wdel_b", [128, 8 * 1024], BF16)
        wb_b = A("wb_b", [128, 8 * N16], BF16)
        wc_b = A("wc_b", [128, 8 * N16], BF16)
        ahalf = A("ahalf", [128, 8 * N16], F32)
        aexp = A("aexp", [128, N16], F32)
        bdel = A("bdel", [128, 8], F32)
        dskip = A("dskip", [128, 8], F32)
        nww = A("nww", [128, 8], F32)
        xch = A("xch", [128, 2 * 512], F32)
        xnch = A("xnch", [128, 8 * 512], F32)
        dlch = A("dlch", [128, 8 * 512], F32)
        uch = A("uch", [128, 8 * 512], F32)
        ytc = A("ytc", [128, 8 * 512], F32)
        yrb = A("yrb", [128, 8 * 512], F32)
        sq = A("sq", [128, 512], F32)
        sqd = A("sqd", [128, 512], F32)
        x3t = A("x3t", [128, 512], F32)
        tprod = A("tprod", [128, 512], F32)
        xbf = A("xbf", [128, 512], BF16)
        rhs_bf = A("rhs_bf", [128, 2 * 8 * 512], BF16)
        sp1 = A("sp1", [128, 2 * 512], F32)
        sp2 = A("sp2", [128, 2 * 512], F32)
        sp3 = A("sp3", [128, 2 * 512], F32)
        dnm = A("dnm", [128, 2 * 4 * 512], F32)
        wre = A("wre", [128, 4 * 512], F32)
        are = A("are", [128, 4 * 512], F32)
        qre = A("qre", [128, 4 * 512], F32)
        bp1 = A("bp1", [128, 2 * 512], F32)
        c1 = A("c1", [128, 2 * 512], F32)
        hst = A("hst", [128, 128], F32)
        bprow = A("bprow", [16, 512], F32)
        lnrows = A("lnrows", [1, 512 * 6], F32)
        ones_col = A("ones_col", [128, 1], F32)
        ones_row1 = A("ones_row1", [1, 128], F32)
        aw2 = A("aw2", [128, 1], F32)
        amax_acc = A("amax_acc", [128, 1], F32)
        amT = A("amT", [1, 128], F32)
        am1 = A("am1", [1, 1], F32)
        cpool = A("cpool", [128, 6], F32)
        ps_m0 = PS("ps_m0", [128, 512], F32)
        ps_m1 = PS("ps_m1", [128, 512], F32)
        ps_bp = PS("ps_bp", [16, 512], F32)
        ps_sum = PS("ps_sum", [1, 512], F32)
        ps_sq = PS("ps_sq", [1, 512], F32)
        ps_al = PS("ps_al", [128, 512], F32)
        ps_be = PS("ps_be", [128, 512], F32)
        psm = [ps_m0, ps_m1]

        S2 = 512

        def sl(buf, p, wdt=S2):
            return buf[:, p * wdt:(p + 1) * wdt]

        add_const_pool(nc, sc, cpool, [1e-5, 1e-6, 1.0, -CEN / HALF, -1.0])
        sc.c("vector", lambda v: v.memset(ones_col[:], 1.0))
        sc.c("vector", lambda v: v.memset(ones_row1[:], 1.0))
        sc.c("vector", lambda v: v.memset(amax_acc[:], 0.0))
        sc.c("vector", lambda v: v.memset(hst[:], 0.0))
        sc.d("sync", lambda sync: [sync.dma_start(out=bdel[:], in_=b_delta_t),
                                   sync.dma_start(out=dskip[:], in_=dskip_t),
                                   sync.dma_start(out=nww[:], in_=nw_t)], n=3)
        for db in range(8):
            sc.d("sync", lambda sync, db=db: sync.dma_start(
                out=aexp[:], in_=a_log[ts(db, 128), :]))
            sc.c("scalar", lambda s: s.activation(
                out=aexp[:], in_=aexp[:], func=AF.Exp))
            sc.c("vector", lambda v, db=db: v.tensor_scalar(
                ahalf[:, ts(db, N16)], aexp[:], 0.5, None, ALU.mult))
        for db in range(8):
            sc.d("sync", lambda sync, db=db: sync.dma_start(
                out=stage_f[:], in_=w_delta[ts(db, 128), :]))
            sc.c("scalar", lambda s, db=db: s.activation(
                out=wdel_b[:, ts(db, 1024)], in_=stage_f[:], func=AF.Copy))
            sc.d("sync", lambda sync, db=db: sync.dma_start(
                out=stage_f[:, 0:N16], in_=w_b[ts(db, 128), :]))
            sc.c("scalar", lambda s, db=db: s.activation(
                out=wb_b[:, ts(db, N16)], in_=stage_f[:, 0:N16], func=AF.Copy))
            sc.d("sync", lambda sync, db=db: sync.dma_start(
                out=stage_f[:, 0:N16], in_=w_c[ts(db, 128), :]))
            sc.c("scalar", lambda s, db=db: s.activation(
                out=wc_b[:, ts(db, N16)], in_=stage_f[:, 0:N16], func=AF.Copy))

        for tc in (range(8) if "1" in phases else []):
            sc.d("sync", lambda sync, tc=tc: [sync.dma_start(
                out=yrb[:, ts(db, 512)], in_=x2T[ts(db, 128), ts(tc, 512)])
                for db in range(8)], n=8)
            for db in range(8):
                sc.c("scalar", lambda s, db=db: s.activation(
                    out=sq[:], in_=yrb[:, ts(db, 512)], func=AF.Square))
                sc.c("tensor", lambda te, db=db: te.matmul(
                    ps_sum[:], ones_col[:], sq[:],
                    start=(db == 0), stop=(db == 7)))
            sc.c("vector", lambda v: v.tensor_copy(lnrows[:, 0:512], ps_sum[:]))
            sc.c("vector", lambda v: v.tensor_scalar(
                lnrows[:, 512:1024], lnrows[:, 0:512], 1.0 / D, None, ALU.mult))
            sc.c("scalar", lambda s: s.activation(
                out=lnrows[:, 0:512], in_=lnrows[:, 512:1024], func=AF.Sqrt,
                bias=1e-6))
            sc.c("vector", lambda v: v.reciprocal(
                lnrows[:, 512:1024], lnrows[:, 0:512]))
            sc.c("tensor", lambda te: te.matmul(
                ps_al[:], ones_row1[:], lnrows[:, 512:1024], start=True, stop=True))
            for db in range(8):
                sc.c("vector", lambda v, db=db: v.scalar_tensor_tensor(
                    xnch[:, ts(db, 512)], yrb[:, ts(db, 512)], nww[:, db:db + 1],
                    ps_al[:], ALU.mult, ALU.mult))
                sc.c("scalar", lambda s, db=db: s.activation(
                    out=xbf[:], in_=xnch[:, ts(db, 512)], func=AF.Copy))
                sc.d("sync", lambda sync, db=db, tc=tc: [
                    sync.dma_start(out=xn_hbm[ts(db, 128), ts(tc, 512)],
                                   in_=xnch[:, ts(db, 512)]),
                    sync.dma_start(out=xnb_hbm[ts(db, 128), ts(tc, 512)],
                                   in_=xbf[:])], n=2)

        DLH = lambda tc: f"dlh:{tc}"

        def emit_p2(tc):
            p2 = tc % 2
            rkey = f"rhs:{p2}"
            sc.d("sync", lambda sync, tc=tc, p2=p2: [sync.dma_start(
                out=rhs_bf[:, (p2 * 8 + db) * 512:(p2 * 8 + db) * 512 + 512],
                in_=xnb_hbm[ts(db, 128), ts(tc, 512)]) for db in range(8)],
                n=8, r=["xnbh"], w=[rkey])
            for ob in range(8):
                obp = ob % 2
                pk = f"psm:{obp}"
                k1, k2, k3 = f"sp1:{obp}", f"sp2:{obp}", f"sp3:{obp}"

                def mmd(te, ob=ob, p2=p2, obp=obp):
                    ins = []
                    for db in range(8):
                        ins.append(te.matmul(
                            psm[obp][:],
                            wdel_b[:, db * 1024 + ob * 128: db * 1024 + ob * 128 + 128],
                            rhs_bf[:, (p2 * 8 + db) * 512:(p2 * 8 + db) * 512 + 512],
                            start=(db == 0), stop=(db == 7)))
                    return ins[-1]
                sc.c("tensor", mmd, r=[rkey], w=[pk])
                sc.c("vector", lambda v, ob=ob, obp=obp: v.tensor_scalar(
                    sl(sp1, obp), psm[obp][:], bdel[:, ob:ob + 1], None, ALU.add),
                    r=[pk], w=[k1])
                sc.c("scalar", lambda s, obp=obp: s.activation(
                    out=sl(sp2, obp), in_=sl(sp1, obp), func=AF.Abs),
                    r=[k1], w=[k2])
                sc.c("scalar", lambda s, obp=obp: s.activation(
                    out=sl(sp2, obp), in_=sl(sp2, obp), func=AF.Exp, scale=-1.0),
                    r=[k2], w=[k2])
                sc.c("scalar", lambda s, obp=obp: s.activation(
                    out=sl(sp2, obp), in_=sl(sp2, obp), func=AF.Ln, bias=1.0),
                    r=[k2], w=[k2])
                sc.c("vector", lambda v, obp=obp: v.tensor_scalar(
                    sl(sp3, obp), sl(sp1, obp), 0.0, None, ALU.max),
                    r=[k1], w=[k3])
                sc.c("vector", lambda v, obp=obp: v.tensor_tensor(
                    sl(sp2, obp), sl(sp2, obp), sl(sp3, obp), ALU.add),
                    r=[k2, k3], w=[k2])
                sc.c("scalar", lambda s, obp=obp: s.activation(
                    out=sl(sp3, obp), in_=sl(sp2, obp), func=AF.Tanh,
                    scale=1.0 / HALF, bias=-CEN / HALF),
                    r=[k2], w=[k3])
                sc.c("vector", lambda v, obp=obp: v.tensor_scalar(
                    sl(sp3, obp), sl(sp3, obp), HALF, CEN, ALU.mult, ALU.add),
                    r=[k3], w=[k3])
                sc.d("sync", lambda sync, ob=ob, tc=tc, obp=obp: sync.dma_start(
                    out=_dl_hbm[ts(ob, 128), ts(tc, 512)], in_=sl(sp3, obp)),
                    r=[k3], w=[DLH(tc)])
            for wsb, hbm in ((wb_b, bp_hbm), (wc_b, c_hbm)):
                def mmb(te, wsb=wsb, p2=p2):
                    ins = []
                    for db in range(8):
                        ins.append(te.matmul(
                            ps_bp[:], wsb[:, ts(db, N16)],
                            rhs_bf[:, (p2 * 8 + db) * 512:(p2 * 8 + db) * 512 + 512],
                            start=(db == 0), stop=(db == 7)))
                    return ins[-1]
                sc.c("tensor", mmb, r=[rkey], w=["psbp"])
                sc.c("vector", lambda v: v.tensor_copy(bprow[:], ps_bp[:]),
                     r=["psbp"], w=["bprow"])
                sc.d("sync", lambda sync, hbm=hbm, tc=tc: sync.dma_start(
                    out=hbm[:, ts(tc, 512)], in_=bprow[:]),
                    r=["bprow"], w=[f"bpch:{tc}"])

        def emit_p3(tc):
            sc.d("sync", lambda sync, tc=tc: [sync.dma_start(
                out=dlch[:, ts(db, 512)], in_=_dl_hbm[ts(db, 128), ts(tc, 512)])
                for db in range(8)], n=8, r=[DLH(tc)], w=["dlch"])
            sc.d("sync", lambda sync, tc=tc: [sync.dma_start(
                out=xnch[:, ts(db, 512)], in_=xn_hbm[ts(db, 128), ts(tc, 512)])
                for db in range(8)], n=8, r=["xnh"], w=["xnch"])
            for db in range(8):
                sc.c("scalar", lambda s, db=db: s.activation(
                    out=sqd[:], in_=dlch[:, ts(db, 512)], func=AF.Sqrt),
                    r=["dlch"], w=["sqd"])
                sc.c("vector", lambda v, db=db: v.scalar_tensor_tensor(
                    uch[:, ts(db, 512)], sqd[:], 0.5, xnch[:, ts(db, 512)],
                    ALU.mult, ALU.mult), r=["sqd", "xnch"], w=["uch"])

            def emit_bcast(n):
                np_ = n % 2
                sc.d("sync", lambda sync, n=n, np_=np_: [
                    sync.dma_start(out=sl(bp1, np_),
                                   in_=bp_hbm[n:n + 1, ts(tc, 512)].broadcast_to([128, 512])),
                    sync.dma_start(out=sl(c1, np_),
                                   in_=c_hbm[n:n + 1, ts(tc, 512)].broadcast_to([128, 512]))],
                    n=2, r=[f"bpch:{tc}"], w=[f"bp:{np_}", f"c:{np_}"])

            def emit_denoms(i):
                n, hb_ = divmod(i, 2)
                ip = i % 2

                def f(s, n=n, hb_=hb_, ip=ip):
                    ins = []
                    for k in range(4):
                        db = hb_ * 4 + k
                        ins.append(s.activation(
                            out=dnm[:, (ip * 4 + k) * 512:(ip * 4 + k) * 512 + 512],
                            in_=dlch[:, ts(db, 512)], func=AF.Identity,
                            scale=ahalf[:, db * N16 + n: db * N16 + n + 1],
                            bias=1.0))
                    return ins[-1]
                sc.c("scalar", f, r=["dlch"], w=[f"dnm:{ip}"])

            def B4(buf, k):
                return buf[:, k * 512:(k + 1) * 512]

            def DN(ip, k):
                return dnm[:, (ip * 4 + k) * 512:(ip * 4 + k) * 512 + 512]

            emit_bcast(0)
            emit_bcast(1)
            emit_denoms(0)
            emit_denoms(1)
            for n in range(N16):
                nb = n % 2
                for hb_ in range(2):
                    i = n * 2 + hb_
                    ip = i % 2
                    db0 = hb_ * 4

                    def vrecip(v, ip=ip):
                        return [v.reciprocal_approx_fast(
                            out=B4(wre, k), in_=DN(ip, k)) for k in range(4)][-1]
                    sc.c("vector", vrecip, r=[f"dnm:{ip}"], w=["wall"])

                    def vq(v, db0=db0, nb=nb):
                        return [v.tensor_tensor(
                            B4(qre, k), uch[:, ts(db0 + k, 512)], sl(bp1, nb),
                            ALU.mult) for k in range(4)][-1]
                    sc.c("vector", vq, r=["uch", f"bp:{nb}"], w=["qall"])

                    def sa(s):
                        return [s.activation(
                            out=B4(are, k), in_=B4(wre, k), func=AF.Identity,
                            scale=2.0, bias=-1.0) for k in range(4)][-1]
                    sc.c("scalar", sa, r=["wall"], w=["aall"])
                    if i + 2 < N16 * 2:
                        emit_denoms(i + 2)

                    def vb(v):
                        return [v.scalar_tensor_tensor(
                            B4(qre, k), B4(wre, k), 2.0, B4(qre, k),
                            ALU.mult, ALU.mult) for k in range(4)][-1]
                    sc.c("vector", vb, r=["wall", "qall"], w=["qall"])

                    def vscan(v, n=n, db0=db0):
                        return [v.tensor_tensor_scan(
                            B4(wre, k), B4(are, k), B4(qre, k),
                            hst[:, (db0 + k) * N16 + n: (db0 + k) * N16 + n + 1],
                            ALU.mult, ALU.add) for k in range(4)][-1]
                    sc.c("vector", vscan, r=["aall", "qall", "hst"], w=["wall"])

                    def vcopy(v, n=n, db0=db0):
                        return [v.tensor_copy(
                            hst[:, (db0 + k) * N16 + n: (db0 + k) * N16 + n + 1],
                            B4(wre, k)[:, 511:512]) for k in range(4)][-1]
                    sc.c("vector", vcopy, r=["wall"], w=["hst"])

                    def vm(v, nb=nb):
                        return [v.tensor_tensor(
                            B4(are, k), B4(wre, k), sl(c1, nb), ALU.mult)
                            for k in range(4)][-1]
                    sc.c("vector", vm, r=["wall", f"c:{nb}"], w=["aall"])

                    if n == 0:
                        def vy(v, db0=db0):
                            return [v.tensor_copy(
                                ytc[:, ts(db0 + k, 512)], B4(are, k))
                                for k in range(4)][-1]
                    else:
                        def vy(v, db0=db0):
                            return [v.tensor_tensor(
                                ytc[:, ts(db0 + k, 512)], ytc[:, ts(db0 + k, 512)],
                                B4(are, k), ALU.add) for k in range(4)][-1]
                    sc.c("vector", vy, r=["aall", "ytc"], w=["ytc"])
                if n + 2 < N16:
                    emit_bcast(n + 2)
            for db in range(8):
                p = db % 2
                sc.d("sync", lambda sync, db=db, tc=tc, p=p: sync.dma_start(
                    out=xch[:, ts(p, 512)], in_=x2T[ts(db, 128), ts(tc, 512)]),
                    r=[f"xch:{p}"], w=[f"xch:{p}"])
                sc.c("vector", lambda v, db=db: v.scalar_tensor_tensor(
                    tprod[:], xnch[:, ts(db, 512)], dskip[:, db:db + 1],
                    ytc[:, ts(db, 512)], ALU.mult, ALU.add),
                    r=["xnch", "ytc"], w=["tp"])
                sc.c("vector", lambda v, db=db, p=p: v.tensor_tensor(
                    yrb[:, ts(db, 512)], tprod[:], xch[:, ts(p, 512)], ALU.add),
                    r=["tp", f"xch:{p}"], w=[f"yr:{db}"])
                sc.c("scalar", lambda s, db=db: s.activation(
                    out=sq[:], in_=yrb[:, ts(db, 512)], func=AF.Square),
                    r=[f"yr:{db}"], w=["sq1"])

                def mmstat(te, db=db):
                    te.matmul(ps_sum[:], ones_col[:], yrb[:, ts(db, 512)],
                              start=(db == 0), stop=(db == 7))
                    return te.matmul(ps_sq[:], ones_col[:], sq[:],
                                     start=(db == 0), stop=(db == 7))
                sc.c("tensor", mmstat, r=[f"yr:{db}", "sq1"], w=["pssum", "pssq"])
            add_ln_rows_k(sc, lnrows, ps_sum, ps_sq, ps_al, ps_be, ones_row1,
                          1.0 / D, 1e-5)
            for db in range(8):
                sc.c("vector", lambda v, db=db: v.tensor_tensor(
                    tprod[:], yrb[:, ts(db, 512)], ps_al[:], ALU.mult),
                    r=[f"yr:{db}", "psal"], w=["tp"])
                sc.c("vector", lambda v: v.tensor_tensor(
                    x3t[:], tprod[:], ps_be[:], ALU.add),
                    r=["tp", "psbe"], w=["x3t"])
                sc.c("vector", lambda v: v.reduce_max(
                    aw2[:], x3t[:], axis=AX.X, apply_absolute_value=True),
                    r=["x3t"], w=["aw2"])
                sc.c("vector", lambda v: v.tensor_tensor(
                    amax_acc[:], amax_acc[:], aw2[:], ALU.max),
                    r=["aw2", "amacc"], w=["amacc"])
                sc.d("sync", lambda sync, db=db, tc=tc: sync.dma_start(
                    out=x3T[ts(db, 128), ts(tc, 512)], in_=x3t[:]),
                    r=["x3t"], w=["x3T"])

        if "3" in phases:
            emit_p2(0)
            for tc in range(8):
                if tc + 1 < 8:
                    emit_p2(tc + 1)
                emit_p3(tc)

        add_amax_finalize(sc, amax_acc, amT, am1, am_bounce, amax3)
        sc.run(nc)
    mybir.codegen_inst_isa_subclasses(nc)
    return nc



def build_l1_ffn_a():
    nc = bass.Bass("TRN2", target_bir_lowering=False, debug=False)
    x3T = nc.dram_tensor("x3T", [D, T], F32, kind="ExternalInput").ap()
    amax8 = nc.dram_tensor("amax8", [8, 1], F32, kind="ExternalInput").ap()
    w_up = nc.dram_tensor("w_up", [I3, D], F32, kind="ExternalInput").ap()
    w_gate = nc.dram_tensor("w_gate", [I3, D], F32, kind="ExternalInput").ap()
    upT = nc.dram_tensor("upT", [I3, T], F32, kind="ExternalOutput").ap()
    gateT = nc.dram_tensor("gateT", [I3, T], F32, kind="ExternalOutput").ap()
    amax_u = nc.dram_tensor("amax_u", [1, 1], F32, kind="ExternalOutput").ap()
    amax_g = nc.dram_tensor("amax_g", [1, 1], F32, kind="ExternalOutput").ap()
    scr2 = nc.dram_tensor("scr2", [1, 2], F32, kind="Internal").ap()
    am_bounce = nc.dram_tensor("am_bounce", [128], F32, kind="Internal").ap()

    sc = Script()
    with ExitStack() as es:
        A = lambda n, sh, dt: es.enter_context(nc.sbuf_tensor(n, sh, dt))
        PS = lambda n, sh, dt: es.enter_context(nc.psum_tensor(n, sh, dt))
        stage_f = A("stage_f", [128, 2 * 1024], F32)
        wq_t = A("wq_t", [128, 2 * 1024], BF16)
        wuq = A("wuq", [128, 8 * I3], BF16)
        wgq = A("wgq", [128, 8 * I3], BF16)
        xq = A("xq", [128, 8 * T], BF16)
        ot = A("ot", [128, 2 * 512], F32)
        awb = A("awb", [128, 2 * 32], F32)
        sbv = A("sbv", [128, 2 * 32], F32)
        rbv = A("rbv", [128, 2 * 32], F32)

        def sl5(buf, p):
            return buf[:, p * 1024:(p + 1) * 1024]

        def sl5b(buf, p):
            return buf[:, p * 1024:(p + 1) * 1024]

        def sl5h(buf, p):
            return buf[:, p * 512:(p + 1) * 512]

        def sl32(buf, p):
            return buf[:, p * 32:(p + 1) * 32]
        aw2 = A("aw2", [128, 1], F32)
        acc_u = A("acc_u", [128, 1], F32)
        acc_g = A("acc_g", [128, 1], F32)
        am8row = A("am8row", [1, 8], F32)
        am = A("am", [1, 1], F32)
        sc2 = A("sc2", [1, 2], F32)
        rsb = A("rsb", [128, 2], F32)
        amT = A("amT", [1, 128], F32)
        am1 = A("am1", [1, 1], F32)
        ident = A("ident", [128, 128], BF16)
        ps_m0 = PS("ps_m0", [128, 512], F32)
        ps_m1 = PS("ps_m1", [128, 512], F32)
        ps_tb0 = PS("ps_tb0", [128, 128], BF16)
        ps_tb1 = PS("ps_tb1", [128, 128], BF16)
        psm = [ps_m0, ps_m1]
        pstb = [ps_tb0, ps_tb1]

        add_identity(sc, ident)
        sc.c("vector", lambda v: v.memset(acc_u[:], 0.0))
        sc.c("vector", lambda v: v.memset(acc_g[:], 0.0))
        add_scalar_prep(sc, dict(am8row=am8row, am=am, sc2=sc2, rsb=rsb,
                                 scr2=scr2), amax8, 32767.0, 1e-12, "qdq")

        for idx in range(64):
            db, tc = divmod(idx, 8)
            p = idx % 2
            sc.d("sync", lambda sync, db=db, tc=tc, p=p: sync.dma_start(
                out=sl5(stage_f, p)[:, 0:512],
                in_=x3T[ts(db, 128), ts(tc, 512)]), r=[], w=[f"st:{p}"])
            sc.c("vector", lambda v, db=db, tc=tc, p=p: v._custom_dve(
                ROUND_SCALE, out=xq[:, db * T + tc * 512: db * T + tc * 512 + 512],
                in0=sl5(stage_f, p)[:, 0:512], s0=rsb[:, 0:1], s1=1.0,
                imm2=MAGIC), r=[f"st:{p}"], w=["xq"])

        for wi, (w_hbm, wq_dst) in enumerate(((w_up, wuq), (w_gate, wgq))):
            for rb in range(24):
                p = (wi * 24 + rb) % 2
                kst, kaw, kwq = f"st:{p}", f"aw:{p}", f"wq:{p}"
                sc.d("sync", lambda sync, rb=rb, p=p, w_hbm=w_hbm: sync.dma_start(
                    out=sl5(stage_f, p), in_=w_hbm[ts(rb, 128), :]),
                    r=[], w=[kst])
                sc.c("vector", lambda v, p=p: v.tensor_reduce(
                    sl32(awb, p), sl5(stage_f, p).rearrange("p (a b) -> p a b", b=32),
                    axis=AX.X, op=ALU.max, apply_absolute_value=True),
                    r=[kst], w=[kaw])
                sc.c("vector", lambda v, p=p: v.tensor_scalar(
                    sl32(sbv, p), sl32(awb, p), 1.0 / 7.0, 1e-12, ALU.mult,
                    ALU.max), r=[kaw], w=[kaw + "s"])
                sc.c("vector", lambda v, p=p: v.reciprocal(
                    sl32(rbv, p), sl32(sbv, p)), r=[kaw + "s"], w=[kaw + "r"])

                def rounds(v, p=p):
                    ins = []
                    for bk in range(32):
                        ins.append(v._custom_dve(
                            ROUND_SCALE, out=sl5b(wq_t, p)[:, ts(bk, 32)],
                            in0=sl5(stage_f, p)[:, ts(bk, 32)],
                            s0=sl32(rbv, p)[:, bk:bk + 1],
                            s1=sl32(sbv, p)[:, bk:bk + 1], imm2=MAGIC))
                    return ins[-1]
                sc.c("vector", rounds, r=[kst, kaw + "s", kaw + "r"], w=[kwq])
                for j in range(8):
                    jp = j % 2
                    sc.c("tensor", lambda te, j=j, p=p, jp=jp: te.transpose(
                        pstb[jp][:], sl5b(wq_t, p)[:, ts(j, 128)], ident[:]),
                        r=[kwq], w=[f"pst:{jp}"])
                    sc.c("vector", lambda v, j=j, rb=rb, jp=jp, wq_dst=wq_dst:
                         v.tensor_copy(
                             wq_dst[:, j * I3 + rb * 128: j * I3 + rb * 128 + 128],
                             pstb[jp][:]), r=[f"pst:{jp}"], w=["wq_res"])

        for wi, (wq_src, out_hbm, acck) in enumerate(((wuq, upT, "accu"),
                                                      (wgq, gateT, "accg"))):
            accb = acc_u if wi == 0 else acc_g
            for idx in range(24 * 8):
                ib, tc = divmod(idx, 8)
                p = idx % 2

                def mmu(te, wq_src=wq_src, ib=ib, tc=tc, p=p):
                    ins = []
                    for db in range(8):
                        ins.append(te.matmul(
                            psm[p][:],
                            wq_src[:, db * I3 + ib * 128: db * I3 + ib * 128 + 128],
                            xq[:, db * T + tc * 512: db * T + tc * 512 + 512],
                            start=(db == 0), stop=(db == 7)))
                    return ins[-1]
                sc.c("tensor", mmu, r=["wq_res", "xq"], w=[f"psm:{p}"])
                sc.c("scalar", lambda s, p=p: s.activation(
                    out=sl5h(ot, p), in_=psm[p][:], func=AF.Copy,
                    scale=rsb[:, 1:2]), r=[f"psm:{p}"], w=[f"ot:{p}"])
                sc.c("vector", lambda v, p=p: v.reduce_max(
                    aw2[:], sl5h(ot, p), axis=AX.X, apply_absolute_value=True),
                    r=[f"ot:{p}"], w=["aw2"])
                sc.c("vector", lambda v, accb=accb, acck=acck: v.tensor_tensor(
                    accb[:], accb[:], aw2[:], ALU.max),
                    r=["aw2", acck], w=[acck])
                sc.d("sync", lambda sync, out_hbm=out_hbm, ib=ib, tc=tc, p=p:
                     sync.dma_start(out=out_hbm[ts(ib, 128), ts(tc, 512)],
                                    in_=sl5h(ot, p)), r=[f"ot:{p}"], w=["outh"])

        add_amax_finalize(sc, acc_u, amT, am1, am_bounce, amax_u)
        add_amax_finalize(sc, acc_g, amT, am1, am_bounce, amax_g)
        sc.run(nc)
    mybir.codegen_inst_isa_subclasses(nc)
    return nc



def build_l1_ffn_b():
    nc = bass.Bass("TRN2", target_bir_lowering=False, debug=False)
    gateT = nc.dram_tensor("gateT", [I3, T], F32, kind="ExternalInput").ap()
    amax8 = nc.dram_tensor("amax8", [8, 1], F32, kind="ExternalInput").ap()
    mT = nc.dram_tensor("mT", [I3, T], F32, kind="ExternalOutput").ap()
    amax_m = nc.dram_tensor("amax_m", [1, 1], F32, kind="ExternalOutput").ap()
    scr2 = nc.dram_tensor("scr2", [1, 2], F32, kind="Internal").ap()
    am_bounce = nc.dram_tensor("am_bounce", [128], F32, kind="Internal").ap()

    sc = Script()
    with ExitStack() as es:
        A = lambda n, sh, dt: es.enter_context(nc.sbuf_tensor(n, sh, dt))
        gt = A("gt", [128, 2 * T], F32)
        gq = A("gq", [128, 2 * T], F32)
        sig = A("sig", [128, 2 * T], F32)
        si2 = A("si2", [128, 2 * T], F32)
        mm = A("mm_", [128, 2 * T], F32)

        def sl2(buf, p):
            return buf[:, p * T:(p + 1) * T]
        aw2 = A("aw2", [128, 1], F32)
        macc = A("macc", [128, 1], F32)
        am8row = A("am8row", [1, 8], F32)
        am = A("am", [1, 1], F32)
        sc2 = A("sc2", [1, 2], F32)
        rsb = A("rsb", [128, 2], F32)
        amT = A("amT", [1, 128], F32)
        am1 = A("am1", [1, 1], F32)

        sc.c("vector", lambda v: v.memset(macc[:], 0.0))
        add_scalar_prep(sc, dict(am8row=am8row, am=am, sc2=sc2, rsb=rsb,
                                 scr2=scr2), amax8, 32767.0, 1e-12, "qdq")
        for ib in range(24):
            p = ib % 2
            kg, kq, ks, km = f"gt:{p}", f"gq:{p}", f"si:{p}", f"mm:{p}"
            sc.d("sync", lambda sync, ib=ib, p=p: sync.dma_start(
                out=sl2(gt, p), in_=gateT[ts(ib, 128), :]), r=[], w=[kg])
            sc.c("vector", lambda v, p=p: v._custom_dve(
                ROUND_SCALE, out=sl2(gq, p), in0=sl2(gt, p), s0=rsb[:, 0:1],
                s1=rsb[:, 1:2], imm2=MAGIC), r=[kg], w=[kq])
            sc.c("scalar", lambda s, p=p: s.activation(
                out=sl2(sig, p), in_=sl2(gq, p), func=AF.Sigmoid),
                r=[kq], w=[ks])
            sc.c("vector", lambda v, p=p: v._custom_dve(
                ROUND_SCALE, out=sl2(si2, p), in0=sl2(sig, p), s0=65536.0,
                s1=1.0, imm2=MAGIC), r=[ks], w=[ks + "b"])
            sc.c("vector", lambda v, p=p: v.tensor_scalar(
                sl2(si2, p), sl2(si2, p), 65535.0, 1.0 / 65536.0, ALU.min,
                ALU.mult), r=[ks + "b"], w=[ks + "b"])
            sc.c("vector", lambda v, p=p: v.tensor_tensor(
                sl2(mm, p), sl2(gq, p), sl2(si2, p), ALU.mult),
                r=[kq, ks + "b"], w=[km])
            sc.c("vector", lambda v, p=p: v.reduce_max(
                aw2[:], sl2(mm, p), axis=AX.X, apply_absolute_value=True),
                r=[km], w=["aw2"])
            sc.c("vector", lambda v: v.tensor_tensor(
                macc[:], macc[:], aw2[:], ALU.max), r=["aw2", "macc"], w=["macc"])
            sc.d("gpsimd", lambda g, ib=ib, p=p: g.dma_start(
                out=mT[ts(ib, 128), :], in_=sl2(mm, p)), r=[km], w=["mT"])
        add_amax_finalize(sc, macc, amT, am1, am_bounce, amax_m)
        sc.run(nc)
    mybir.codegen_inst_isa_subclasses(nc)
    return nc



def build_l1_ffn_c():
    nc = bass.Bass("TRN2", target_bir_lowering=False, debug=False)
    upT = nc.dram_tensor("upT", [I3, T], F32, kind="ExternalInput").ap()
    mT = nc.dram_tensor("mT", [I3, T], F32, kind="ExternalInput").ap()
    amax8u = nc.dram_tensor("amax8u", [8, 1], F32, kind="ExternalInput").ap()
    amax8m = nc.dram_tensor("amax8m", [8, 1], F32, kind="ExternalInput").ap()
    oT = nc.dram_tensor("oT", [I3, T], F32, kind="ExternalOutput").ap()
    amax_o = nc.dram_tensor("amax_o", [1, 1], F32, kind="ExternalOutput").ap()
    scr2 = nc.dram_tensor("scr2", [1, 2], F32, kind="Internal").ap()
    scr2b = nc.dram_tensor("scr2b", [1, 2], F32, kind="Internal").ap()
    am_bounce = nc.dram_tensor("am_bounce", [128], F32, kind="Internal").ap()

    sc = Script()
    with ExitStack() as es:
        A = lambda n, sh, dt: es.enter_context(nc.sbuf_tensor(n, sh, dt))
        ut = A("ut", [128, 2 * T], F32)
        mt = A("mt", [128, 2 * T], F32)
        uq = A("uq", [128, 2 * T], F32)
        mq = A("mq", [128, 2 * T], F32)
        oo = A("oo", [128, 2 * T], F32)

        def sl2(buf, p):
            return buf[:, p * T:(p + 1) * T]
        aw2 = A("aw2", [128, 1], F32)
        oacc = A("oacc", [128, 1], F32)
        am8row = A("am8row", [1, 8], F32)
        am = A("am", [1, 1], F32)
        sc2 = A("sc2", [1, 2], F32)
        rsbu = A("rsbu", [128, 2], F32)
        rsbm = A("rsbm", [128, 2], F32)
        amT = A("amT", [1, 128], F32)
        am1 = A("am1", [1, 1], F32)

        sc.c("vector", lambda v: v.memset(oacc[:], 0.0))
        add_scalar_prep(sc, dict(am8row=am8row, am=am, sc2=sc2, rsb=rsbu,
                                 scr2=scr2), amax8u, 32767.0, 1e-12, "qdq")
        add_scalar_prep(sc, dict(am8row=am8row, am=am, sc2=sc2, rsb=rsbm,
                                 scr2=scr2b), amax8m, 32767.0, 1e-12, "qdq")
        for ib in range(24):
            p = ib % 2
            ku, km_, kuq, kmq, ko = (f"ut:{p}", f"mt:{p}", f"uq:{p}",
                                     f"mq:{p}", f"oo:{p}")
            sc.d("sync", lambda sync, ib=ib, p=p: [
                sync.dma_start(out=sl2(ut, p), in_=upT[ts(ib, 128), :]),
                sync.dma_start(out=sl2(mt, p), in_=mT[ts(ib, 128), :])],
                n=2, r=[], w=[ku, km_])
            sc.c("vector", lambda v, p=p: v._custom_dve(
                ROUND_SCALE, out=sl2(uq, p), in0=sl2(ut, p), s0=rsbu[:, 0:1],
                s1=rsbu[:, 1:2], imm2=MAGIC), r=[ku], w=[kuq])
            sc.c("vector", lambda v, p=p: v._custom_dve(
                ROUND_SCALE, out=sl2(mq, p), in0=sl2(mt, p), s0=rsbm[:, 0:1],
                s1=rsbm[:, 1:2], imm2=MAGIC), r=[km_], w=[kmq])
            sc.c("vector", lambda v, p=p: v.tensor_tensor(
                sl2(oo, p), sl2(uq, p), sl2(mq, p), ALU.mult),
                r=[kuq, kmq], w=[ko])
            sc.c("vector", lambda v, p=p: v.reduce_max(
                aw2[:], sl2(oo, p), axis=AX.X, apply_absolute_value=True),
                r=[ko], w=["aw2"])
            sc.c("vector", lambda v: v.tensor_tensor(
                oacc[:], oacc[:], aw2[:], ALU.max), r=["aw2", "oacc"], w=["oacc"])
            sc.d("gpsimd", lambda g, ib=ib, p=p: g.dma_start(
                out=oT[ts(ib, 128), :], in_=sl2(oo, p)), r=[ko], w=["oT"])
        add_amax_finalize(sc, oacc, amT, am1, am_bounce, amax_o)
        sc.run(nc)
    mybir.codegen_inst_isa_subclasses(nc)
    return nc



def build_l1_ffn_d():
    nc = bass.Bass("TRN2", target_bir_lowering=False, debug=False)
    x3T = nc.dram_tensor("x3T", [D, T], F32, kind="ExternalInput").ap()
    oT = nc.dram_tensor("oT", [I3, T], F32, kind="ExternalInput").ap()
    amax8 = nc.dram_tensor("amax8", [8, 1], F32, kind="ExternalInput").ap()
    w_down = nc.dram_tensor("w_down", [D, I3], F32, kind="ExternalInput").ap()
    x4T = nc.dram_tensor("x4T", [D, T], F32, kind="ExternalOutput").ap()
    scr2 = nc.dram_tensor("scr2", [1, 2], F32, kind="Internal").ap()

    sc = Script()
    with ExitStack() as es:
        A = lambda n, sh, dt: es.enter_context(nc.sbuf_tensor(n, sh, dt))
        PS = lambda n, sh, dt: es.enter_context(nc.psum_tensor(n, sh, dt))
        stage_f = A("stage_f", [128, 2 * 3072], F32)
        wq_t = A("wq_t", [128, 2 * 3072], BF16)
        wdq = A("wdq", [128, 24 * 1024], BF16)
        oqc = A("oqc", [128, 24 * 512], BF16)
        old = A("old", [128, 2 * 512], F32)
        yr = A("yr", [128, 8 * 512], F32)
        xtile = A("xtile", [128, 2 * 512], F32)
        sq = A("sq", [128, 2 * 512], F32)
        x4t = A("x4t", [128, 2 * 512], F32)
        tprod = A("tprod", [128, 2 * 512], F32)
        awb = A("awb", [128, 2 * 96], F32)
        sbv = A("sbv", [128, 2 * 96], F32)
        rbv = A("rbv", [128, 2 * 96], F32)

        def sl8s(buf, p):
            return buf[:, p * 3072:(p + 1) * 3072]

        def sl8h(buf, p):
            return buf[:, p * 512:(p + 1) * 512]

        def sl96(buf, p):
            return buf[:, p * 96:(p + 1) * 96]
        am8row = A("am8row", [1, 8], F32)
        am = A("am", [1, 1], F32)
        sc2 = A("sc2", [1, 2], F32)
        rsb = A("rsb", [128, 2], F32)
        lnrows = A("lnrows", [1, 512 * 6], F32)
        ones_col = A("ones_col", [128, 1], F32)
        ones_row1 = A("ones_row1", [1, 128], F32)
        cpool = A("cpool", [128, 1], F32)
        ident = A("ident", [128, 128], BF16)
        ps_m0 = PS("ps_m0", [128, 512], F32)
        ps_m1 = PS("ps_m1", [128, 512], F32)
        ps_tb0 = PS("ps_tb0", [128, 128], BF16)
        ps_tb1 = PS("ps_tb1", [128, 128], BF16)
        psm = [ps_m0, ps_m1]
        pstb = [ps_tb0, ps_tb1]
        ps_sum = PS("ps_sum", [1, 512], F32)
        ps_sq = PS("ps_sq", [1, 512], F32)
        ps_al = PS("ps_al", [128, 512], F32)
        ps_be = PS("ps_be", [128, 512], F32)

        add_const_pool(nc, sc, cpool, [1e-5])
        add_identity(sc, ident)
        sc.c("vector", lambda v: v.memset(ones_col[:], 1.0))
        sc.c("vector", lambda v: v.memset(ones_row1[:], 1.0))
        add_scalar_prep(sc, dict(am8row=am8row, am=am, sc2=sc2, rsb=rsb,
                                 scr2=scr2), amax8, 32767.0, 1e-12, "qdq")

        for rb in range(8):
            p = rb % 2
            kst, kaw, kwq = f"st:{p}", f"aw:{p}", f"wq:{p}"
            sc.d("sync", lambda sync, rb=rb, p=p: sync.dma_start(
                out=sl8s(stage_f, p), in_=w_down[ts(rb, 128), :]),
                r=[], w=[kst])
            sc.c("vector", lambda v, p=p: v.tensor_reduce(
                sl96(awb, p), sl8s(stage_f, p).rearrange("p (a b) -> p a b", b=32),
                axis=AX.X, op=ALU.max, apply_absolute_value=True),
                r=[kst], w=[kaw])
            sc.c("vector", lambda v, p=p: v.tensor_scalar(
                sl96(sbv, p), sl96(awb, p), 1.0 / 7.0, 1e-12, ALU.mult, ALU.max),
                r=[kaw], w=[kaw + "s"])
            sc.c("vector", lambda v, p=p: v.reciprocal(
                sl96(rbv, p), sl96(sbv, p)), r=[kaw + "s"], w=[kaw + "r"])

            def rounds(v, p=p):
                ins = []
                for bk in range(96):
                    ins.append(v._custom_dve(
                        ROUND_SCALE, out=sl8s(wq_t, p)[:, ts(bk, 32)],
                        in0=sl8s(stage_f, p)[:, ts(bk, 32)],
                        s0=sl96(rbv, p)[:, bk:bk + 1],
                        s1=sl96(sbv, p)[:, bk:bk + 1], imm2=MAGIC))
                return ins[-1]
            sc.c("vector", rounds, r=[kst, kaw + "s", kaw + "r"], w=[kwq])
            for j in range(24):
                jp = j % 2
                sc.c("tensor", lambda te, j=j, p=p, jp=jp: te.transpose(
                    pstb[jp][:], sl8s(wq_t, p)[:, ts(j, 128)], ident[:]),
                    r=[kwq], w=[f"pst:{jp}"])
                sc.c("vector", lambda v, j=j, rb=rb, jp=jp: v.tensor_copy(
                    wdq[:, j * 1024 + rb * 128: j * 1024 + rb * 128 + 128],
                    pstb[jp][:]), r=[f"pst:{jp}"], w=["wdq"])

        for tc in range(8):
            for ib in range(24):
                p = ib % 2
                sc.d("sync", lambda sync, ib=ib, tc=tc, p=p: sync.dma_start(
                    out=sl8h(old, p), in_=oT[ts(ib, 128), ts(tc, 512)]),
                    r=[], w=[f"old:{p}"])
                sc.c("vector", lambda v, ib=ib, p=p: v._custom_dve(
                    ROUND_SCALE, out=oqc[:, ts(ib, 512)], in0=sl8h(old, p),
                    s0=rsb[:, 0:1], s1=rsb[:, 1:2], imm2=MAGIC),
                    r=[f"old:{p}"], w=[f"oq:{ib}"])
            for do in range(8):
                p = do % 2

                def mmo(te, do=do, p=p):
                    ins = []
                    for ib in range(24):
                        ins.append(te.matmul(
                            psm[p][:],
                            wdq[:, ib * 1024 + do * 128: ib * 1024 + do * 128 + 128],
                            oqc[:, ts(ib, 512)],
                            start=(ib == 0), stop=(ib == 23)))
                    return ins[-1]
                sc.c("tensor", mmo, r=["wdq"] + [f"oq:{ib}" for ib in range(24)],
                     w=[f"psm:{p}"])
                sc.d("sync", lambda sync, do=do, tc=tc, p=p: sync.dma_start(
                    out=sl8h(xtile, p), in_=x3T[ts(do, 128), ts(tc, 512)]),
                    r=[], w=[f"xt:{p}"])
                sc.c("vector", lambda v, do=do, p=p: v.scalar_tensor_tensor(
                    yr[:, ts(do, 512)], psm[p][:], 1.0, sl8h(xtile, p),
                    ALU.mult, ALU.add), r=[f"psm:{p}", f"xt:{p}"], w=[f"yr:{do}"])
                sc.c("scalar", lambda s, do=do, p=p: s.activation(
                    out=sl8h(sq, p), in_=yr[:, ts(do, 512)], func=AF.Square),
                    r=[f"yr:{do}"], w=[f"sq:{p}"])

                def mmstat(te, do=do, p=p):
                    te.matmul(ps_sum[:], ones_col[:], yr[:, ts(do, 512)],
                              start=(do == 0), stop=(do == 7))
                    return te.matmul(ps_sq[:], ones_col[:], sl8h(sq, p),
                                     start=(do == 0), stop=(do == 7))
                sc.c("tensor", mmstat, r=[f"yr:{do}", f"sq:{p}"],
                     w=["pssum", "pssq"])
            add_ln_rows_k(sc, lnrows, ps_sum, ps_sq, ps_al, ps_be, ones_row1,
                          1.0 / D, 1e-5)
            for do in range(8):
                p = do % 2
                sc.c("vector", lambda v, do=do, p=p: v.tensor_tensor(
                    sl8h(tprod, p), yr[:, ts(do, 512)], ps_al[:], ALU.mult),
                    r=[f"yr:{do}", "psal"], w=[f"tp:{p}"])
                sc.c("vector", lambda v, p=p: v.tensor_tensor(
                    sl8h(x4t, p), sl8h(tprod, p), ps_be[:], ALU.add),
                    r=[f"tp:{p}", "psbe"], w=[f"x4:{p}"])
                sc.d("sync", lambda sync, do=do, tc=tc, p=p: sync.dma_start(
                    out=x4T[ts(do, 128), ts(tc, 512)], in_=sl8h(x4t, p)),
                    r=[f"x4:{p}"], w=["x4T"])
        sc.run(nc)
    mybir.codegen_inst_isa_subclasses(nc)
    return nc



_BF = ml_dtypes.bfloat16


def add_ln_rows(sc, rows, ps_sum, ps_sq, ps_al, ps_be, ones_row1, inv_n, eps_bias):
    R = lambda i: rows[:, i * 512:(i + 1) * 512]
    sc.c("vector", lambda v: [v.tensor_copy(R(0), ps_sum[:]),
                              v.tensor_copy(R(1), ps_sq[:])][-1])
    sc.c("vector", lambda v: [
        v.tensor_scalar(R(2), R(0), inv_n, None, ALU.mult),
        v.tensor_scalar(R(3), R(1), inv_n, None, ALU.mult)][-1])
    sc.c("vector", lambda v: v.tensor_tensor(R(5), R(2), R(2), ALU.mult))
    sc.c("vector", lambda v: v.tensor_tensor(R(4), R(3), R(5), ALU.subtract))
    sc.c("scalar", lambda s: s.activation(
        out=R(3), in_=R(4), func=AF.Sqrt, bias=eps_bias))
    sc.c("vector", lambda v: v.reciprocal(R(4), R(3)))
    sc.c("vector", lambda v: v.tensor_tensor(R(5), R(2), R(4), ALU.mult))
    sc.c("vector", lambda v: v.tensor_scalar(R(5), R(5), -1.0, None, ALU.mult))

    def bcasts(te):
        te.matmul(ps_al[:], ones_row1[:], R(4), start=True, stop=True)
        return te.matmul(ps_be[:], ones_row1[:], R(5), start=True, stop=True)
    sc.c("tensor", bcasts)


def add_ln_rows_k(sc, rows, ps_sum, ps_sq, ps_al, ps_be, ones_row1, inv_n,
                  eps_bias):
    R = lambda i: rows[:, i * 512:(i + 1) * 512]
    sc.c("vector", lambda v: [v.tensor_copy(R(0), ps_sum[:]),
                              v.tensor_copy(R(1), ps_sq[:])][-1],
         r=["pssum", "pssq"], w=["lnr"])
    sc.c("vector", lambda v: [
        v.tensor_scalar(R(2), R(0), inv_n, None, ALU.mult),
        v.tensor_scalar(R(3), R(1), inv_n, None, ALU.mult)][-1],
        r=["lnr"], w=["lnr"])
    sc.c("vector", lambda v: v.tensor_tensor(R(5), R(2), R(2), ALU.mult),
         r=["lnr"], w=["lnr"])
    sc.c("vector", lambda v: v.tensor_tensor(R(4), R(3), R(5), ALU.subtract),
         r=["lnr"], w=["lnr"])
    sc.c("scalar", lambda s: s.activation(
        out=R(3), in_=R(4), func=AF.Sqrt, bias=eps_bias),
        r=["lnr"], w=["lnr"])
    sc.c("vector", lambda v: v.reciprocal(R(4), R(3)), r=["lnr"], w=["lnr"])
    sc.c("vector", lambda v: v.tensor_tensor(R(5), R(2), R(4), ALU.mult),
         r=["lnr"], w=["lnr"])
    sc.c("vector", lambda v: v.tensor_scalar(R(5), R(5), -1.0, None, ALU.mult),
         r=["lnr"], w=["lnr"])

    def bcasts(te):
        te.matmul(ps_al[:], ones_row1[:], R(4), start=True, stop=True)
        return te.matmul(ps_be[:], ones_row1[:], R(5), start=True, stop=True)
    sc.c("tensor", bcasts, r=["lnr"], w=["psal", "psbe"])


def add_qdq_weight_lpbq(sc, nc, stage_f, wq_t, awb, sbv, rbv, w_hbm, rb, nblk):
    W = nblk * 32

    def load(sync):
        return sync.dma_start(out=stage_f[:, 0:W], in_=w_hbm[ts(rb, 128), :])
    sc.d("sync", load)
    sc.c("vector", lambda v: v.tensor_reduce(
        awb[:, 0:nblk], stage_f[:, 0:W].rearrange("p (a b) -> p a b", b=32),
        axis=AX.X, op=ALU.max, apply_absolute_value=True))
    sc.c("vector", lambda v: v.tensor_scalar(
        sbv[:, 0:nblk], awb[:, 0:nblk], 1.0 / 7.0, 1e-12, ALU.mult, ALU.max))
    sc.c("vector", lambda v: v.reciprocal(rbv[:, 0:nblk], sbv[:, 0:nblk]))

    def rounds(v):
        ins = []
        for b in range(nblk):
            ins.append(v._custom_dve(
                ROUND_SCALE, out=wq_t[:, ts(b, 32)], in0=stage_f[:, ts(b, 32)],
                s0=rbv[:, b:b + 1], s1=sbv[:, b:b + 1], imm2=MAGIC))
        return ins[-1]
    sc.c("vector", rounds)



_NC_CACHE = {}


def _get_nc(name, builder):
    if name not in _NC_CACHE:
        _NC_CACHE[name] = builder()
    return _NC_CACHE[name]


def _gather_amax_n(res, key, n):
    return np.ascontiguousarray(np.concatenate(
        [np.asarray(res.results[c][key]).reshape(1, 1) for c in range(n)]
        + [np.asarray(res.results[0][key]).reshape(1, 1)] * (8 - n),
        axis=0)).astype(np.float32)


def _gather_amax(res, key):
    return _gather_amax_n(res, key, NCORES)


def kernel(L0_attn_x, L0_attn_W_qkv, L0_ffn_W_in, L0_ffn_b_in, L0_ffn_W_out,
           L0_ffn_b_out, L1_attn_A_log, L1_attn_W_delta, L1_attn_b_delta,
           L1_attn_W_B, L1_attn_W_C, L1_attn_D_skip, L1_attn_norm_weight,
           L1_ffn_w_gate, L1_ffn_w_up, L1_ffn_w_down):
    cores = list(range(NCORES))
    x = np.ascontiguousarray(np.asarray(L0_attn_x, dtype=np.float32))
    wqkv = np.ascontiguousarray(np.asarray(L0_attn_W_qkv, np.float32))
    w_inT = np.ascontiguousarray(np.asarray(L0_ffn_W_in, np.float32).T)
    b_in_t = np.ascontiguousarray(np.asarray(L0_ffn_b_in, np.float32).reshape(32, 128).T)
    w_out = np.ascontiguousarray(np.asarray(L0_ffn_W_out, np.float32))
    b_out_t = np.ascontiguousarray(np.asarray(L0_ffn_b_out, np.float32).reshape(8, 128).T)
    a_log = np.ascontiguousarray(np.asarray(L1_attn_A_log, np.float32))
    w_delta = np.ascontiguousarray(np.asarray(L1_attn_W_delta, np.float32))
    b_delta_t = np.ascontiguousarray(np.asarray(L1_attn_b_delta, np.float32).reshape(8, 128).T)
    w_b = np.ascontiguousarray(np.asarray(L1_attn_W_B, np.float32))
    w_c = np.ascontiguousarray(np.asarray(L1_attn_W_C, np.float32))
    dskip_t = np.ascontiguousarray(np.asarray(L1_attn_D_skip, np.float32).reshape(8, 128).T)
    nw_t = np.ascontiguousarray(np.asarray(L1_attn_norm_weight, np.float32).reshape(8, 128).T)
    w_gate = np.ascontiguousarray(np.asarray(L1_ffn_w_gate, np.float32))
    w_up = np.ascontiguousarray(np.asarray(L1_ffn_w_up, np.float32))
    w_down = np.ascontiguousarray(np.asarray(L1_ffn_w_down, np.float32))

    nc1 = _get_nc("l0_attn", build_l0_attn)
    r1 = run_bass_kernel_spmd(nc1, [
        {"x_tok": np.ascontiguousarray(x[c]), "wqkv": wqkv} for c in cores],
        core_ids=cores)
    x1 = [np.asarray(r1.results[c]["x1T"]) for c in cores]
    am1 = _gather_amax(r1, "amax1")

    nc2 = _get_nc("l0_ffn_a", build_l0_ffn_a)
    r2 = run_bass_kernel_spmd(nc2, [
        {"x1T": x1[c], "amax8": am1, "w_inT": w_inT, "b_in_t": b_in_t}
        for c in cores], core_ids=cores)
    hid = [np.asarray(r2.results[c]["hidT"]) for c in cores]
    amh = _gather_amax(r2, "amax_h")

    nc3 = _get_nc("l0_ffn_b", build_l0_ffn_b)
    r3 = run_bass_kernel_spmd(nc3, [
        {"x1T": x1[c], "hidT": hid[c], "amax8": amh, "w_out": w_out,
         "b_out_t": b_out_t} for c in cores], core_ids=cores)
    x2 = [np.asarray(r3.results[c]["x2T"]) for c in cores]

    nc4 = _get_nc("ssm", build_ssm)
    r4 = run_bass_kernel_spmd(nc4, [
        {"x2T": x2[c], "a_log": a_log, "w_delta": w_delta,
         "b_delta_t": b_delta_t, "w_b": w_b, "w_c": w_c, "dskip_t": dskip_t,
         "nw_t": nw_t} for c in cores], core_ids=cores)
    x3 = [np.asarray(r4.results[c]["x3T"]) for c in cores]
    am3 = _gather_amax(r4, "amax3")

    nc5 = _get_nc("l1_ffn_a", build_l1_ffn_a)
    r5 = run_bass_kernel_spmd(nc5, [
        {"x3T": x3[c], "amax8": am3, "w_up": w_up, "w_gate": w_gate}
        for c in cores], core_ids=cores)
    up = [np.asarray(r5.results[c]["upT"]) for c in cores]
    gate = [np.asarray(r5.results[c]["gateT"]) for c in cores]
    amu = _gather_amax(r5, "amax_u")
    amg = _gather_amax(r5, "amax_g")

    nc6 = _get_nc("l1_ffn_b", build_l1_ffn_b)
    r6 = run_bass_kernel_spmd(nc6, [
        {"gateT": gate[c], "amax8": amg} for c in cores], core_ids=cores)
    mm_ = [np.asarray(r6.results[c]["mT"]) for c in cores]
    amm = _gather_amax(r6, "amax_m")

    nc7 = _get_nc("l1_ffn_c", build_l1_ffn_c)
    r7 = run_bass_kernel_spmd(nc7, [
        {"upT": up[c], "mT": mm_[c], "amax8u": amu, "amax8m": amm}
        for c in cores], core_ids=cores)
    oo = [np.asarray(r7.results[c]["oT"]) for c in cores]
    amo = _gather_amax(r7, "amax_o")

    nc8 = _get_nc("l1_ffn_d", build_l1_ffn_d)
    r8 = run_bass_kernel_spmd(nc8, [
        {"x3T": x3[c], "oT": oo[c], "amax8": amo, "w_down": w_down}
        for c in cores], core_ids=cores)
    out = np.stack([np.ascontiguousarray(np.asarray(r8.results[c]["x4T"]).T)
                    for c in cores])
    return out.astype(np.float32)
